# version 03
# baseline (speedup 1.0000x reference)
import numpy as np
from contextlib import ExitStack

import concourse.bass as bass
import concourse.mybir as mybir
import bass_rust
from concourse.bass_utils import run_bass_kernel_spmd

F32 = mybir.dt.float32
BF16 = mybir.dt.bfloat16
I16 = mybir.dt.int16
I32 = mybir.dt.int32
U16 = mybir.dt.uint16
AF = mybir.ActivationFunctionType
ALU = mybir.AluOpType
AX = mybir.AxisListType

B, T, D, NS, DFF, DI = 4, 4096, 1024, 16, 4096, 3072
DH, SCN, DCN = 512, 4, 8
FC, ICN = DFF // 128, DI // 128
NW, WT, OT = 8, 512, 256
TOK, QT, NQ = 2048, 512, 4
QMAX = 32767.0
EPS_LN, EPS_RMS = 1e-5, 1e-6
PAIRS = [[0, 1], [2, 3], [4, 5], [6, 7]]
ALL8 = [list(range(8))]
ENGS = ("sync", "scalar", "vector", "gpsimd", "tensor")
V, A_, P_, G, SY, TE = "vector", "scalar", "gpsimd", "gpsimd", "sync", "tensor"


def tok_max(*tokens):
    outd = {}
    for t in tokens:
        if t is None:
            continue
        items = t.items() if isinstance(t, dict) else [t]
        for k, v in items:
            if outd.get(k, -1) < v:
                outd[k] = v
    return outd


class Prog:
    SEMS = ["sv", "ss", "sp", "st", "dy", "dq", "dg", "cc"]

    def __init__(self, nc):
        self.nc = nc
        self.buf = {e: [] for e in ENGS}
        self.cnt = {}
        self.sems = {}
        self.marks = {}

    def _semname(self, eng, meth):
        if meth == "collective_compute":
            return "cc"
        if meth is not None and "dma" in meth:
            return {"sync": "dy", "scalar": "dq", "gpsimd": "dg"}.get(eng, "dy")
        return {"vector": "sv", "scalar": "ss", "gpsimd": "sp", "tensor": "st", "sync": "dy"}[eng]

    def _filter(self, eng, waits):
        need = tok_max(*waits)
        fil = {}
        for k, v in need.items():
            if self.marks.get((eng, k), 0) < v:
                fil[k] = v
                self.marks[(eng, k)] = v
        return fil

    def op(self, eng, meth, *args, waits=(), argfn=None, **kw):
        fil = self._filter(eng, waits)
        semname = self._semname(eng, meth)
        inc = 16 if semname in ("dy", "dq", "dg") else 1
        self.cnt[semname] = self.cnt.get(semname, 0) + inc
        self.buf[eng].append((fil, meth, args, kw, semname, inc, argfn))
        return (semname, self.cnt[semname])

    def raw(self, eng, fn, waits=()):
        fil = self._filter(eng, waits)
        self.buf[eng].append((fil, None, None, None, None, 0, fn))

    def fence(self):
        return dict(self.cnt)

    def flush(self, block):
        for eng in ENGS:
            items = self.buf[eng]
            self.buf[eng] = []
            if not items:
                continue

            def make(items=items):
                def body(e):
                    for fil, meth, args, kw, semname, inc, argfn in items:
                        for k, v in fil.items():
                            e.wait_ge(self.sems[k], v)
                        if meth is None:
                            argfn(e)
                            continue
                        a, k2 = (args, kw) if argfn is None else argfn(e)
                        getattr(e, meth)(*a, **k2).then_inc(self.sems[semname], inc)
                return body
            getattr(block, eng)(make())


def pack_w(w, KC, MC):
    a = np.asarray(w, np.float32).reshape(KC, 128, MC, 128).transpose(2, 0, 1, 3)
    return np.ascontiguousarray(a.reshape(MC * KC * 128, 128))


def slab_ap(dram_ap, mc0, nmc, KC):
    rows = dram_ap[mc0 * KC * 128:(mc0 + nmc) * KC * 128, :]
    return rows.rearrange("(o kc p) f -> p o kc f", o=nmc, kc=KC, p=128)


def slab_out(tile, nmc, KC):
    return tile.ap().rearrange("p (o kc f) -> p o kc f", o=nmc, kc=KC)


class Builder:
    def __init__(self, dbg=(), stop=99):
        self.nc = bass.Bass("TRN2", target_bir_lowering=False)
        self.stop = stop
        self.dbg = list(dbg)
        self.prog = Prog(self.nc)
        self.dr = {}
        self.taps = {}
        self.hcell = {}

    def din(self, name, shape, dt=F32):
        self.dr[name] = self.nc.dram_tensor(name, list(shape), dt, kind="ExternalInput").ap()
        return self.dr[name]

    def dint(self, name, shape, dt=F32):
        self.dr[name] = self.nc.dram_tensor(name, list(shape), dt)
        return self.dr[name]

    def tap(self, name, shape, dt=F32):
        self.taps[name] = self.nc.dram_tensor("tap_" + name, list(shape), dt, kind="ExternalOutput").ap()
        return self.taps[name]

    def build(self):
        nc, prog = self.nc, self.prog
        self.x_scan = self.din("x_scan", [DH, T])
        self.x_own = self.din("x_own", [D, TOK])
        for n, s in (("A_s", [128, SCN * NS]), ("B_s", [128, SCN * NS]), ("C_s", [128, SCN * NS]),
                     ("dtb", [128, SCN * NS]), ("D_s", [128, SCN]), ("s1_s", [128, SCN]),
                     ("s2c", [128, DCN]), ("b1c", [128, FC]), ("b2c", [128, DCN])):
            self.din(n, s)
        for n, s in (("wp1", [FC * DCN * 128, 128]), ("wp2", [DCN * FC * 128, 128])):
            self.din(n, s)
        self.din("wun", [DI, D])
        self.din("wgn", [DI, D])
        self.din("wdn", [D, DI])
        self.dr["identb"] = self.nc.dram_tensor("identb", [128, 128], BF16, kind="ExternalInput").ap()
        self.dint("wqu", [ICN * DCN * 128, 128], BF16)
        self.dint("wqg", [ICN * DCN * 128, 128], BF16)
        self.dint("wqd", [DCN * ICN * 128, 128], BF16)
        self.out = self.nc.dram_tensor("out", [D, TOK], F32, kind="ExternalOutput").ap()
        self.stats_in = self.dint("stats_in", [2, T])
        self.stats_out = self.dint("stats_out", [4, T])
        self.yag_in = [self.dint(f"yag_in{w}", [DH, WT]) for w in range(NW)]
        self.yag_out = [self.dint(f"yag_out{w}", [D, WT]) for w in range(NW)]
        self.m_in = {k: self.dint(f"min_{k}", [1, 2]) for k in ("m1", "mug", "mg2", "mo")}
        self.m_out = {k: self.dint(f"mout_{k}", [1, 2]) for k in ("m1", "mug", "mg2", "mo")}
        self.sp_up = self.dint("sp_up", [DI, TOK], BF16)
        self.sp_gate = self.dint("sp_gate", [DI, TOK], BF16)

        with ExitStack() as ctx:
            self.ctx = ctx
            S = lambda st, n, sh, dt=F32: st.enter_context(nc.sbuf_tensor(n, list(sh), dt))
            self.S = S
            self.ones = S(ctx, "ones", [128, WT])
            self.onesbf = S(ctx, "onesbf", [128, 1], BF16)
            self.dAv = S(ctx, "dAv", [128, SCN * NS])
            self.CdB = S(ctx, "CdB", [128, SCN * NS])
            self.sD = S(ctx, "sD", [128, SCN])
            self.s_s2 = S(ctx, "s_s2", [128, DCN])
            self.s_b1 = S(ctx, "s_b1", [128, FC])
            self.s_b2 = S(ctx, "s_b2", [128, DCN])
            self.states = S(ctx, "states", [128, SCN * NS])
            self.x1 = [S(ctx, f"x1_{dc}", [128, TOK]) for dc in range(DCN)]
            self.rows = S(ctx, "rows", [1, 6144])
            self.maxc = S(ctx, "maxc", [128, 256])
            self.scal = S(ctx, "scal", [128, 8])
            self.scal2 = S(ctx, "scal2", [128, 2])
            self.srow1 = S(ctx, "srow1", [1, 32])
            self.cbias = S(ctx, "cbias", [128, 4])
            self.drt = S(ctx, "drt", [128, 8])
            for name in Prog.SEMS:
                prog.sems[name] = ctx.enter_context(nc.semaphore(name))
            self.block = ctx.enter_context(nc.Block())

            def mk_h(e):
                pid = e.partition_id()
                h = e.compute_val(pid % 2)
                reg = e.alloc_register("hreg")
                e.reg_mov(reg, h)
                self.hcell["hrv"] = bass_rust.RuntimeValue(reg, min_val=0, max_val=1)
            prog.raw(SY, mk_h)

            self.t_ones = prog.op(V, "memset", self.ones[:], 1.0)
            tb0 = prog.op(V, "memset", self.cbias[:, 0:1], EPS_LN)
            tb1 = prog.op(V, "memset", self.cbias[:, 1:2], EPS_LN * EPS_RMS)
            tb2 = prog.op(V, "memset", self.cbias[:, 2:3], EPS_RMS)
            self.t_cbias = tok_max(tb0, tb1, tb2)
            self.t_onesbf = prog.op(V, "memset", self.onesbf[:], 1.0)
            self.t_st0 = prog.op(V, "memset", self.states[:], 0.0)

            self.ph_params()
            if self.stop >= 2:
                self.run_main()
            prog.flush(self.block)
        return nc

    def run_main(self):
        nc, prog = self.nc, self.prog
        ctx = self.ctx
        if True:
            with ExitStack() as rgx:
                self.x1ln = [self.S(rgx, f"x1ln_{dc}", [128, TOK], BF16) for dc in range(DCN)]
                self.cstg = [self.S(rgx, f"cstg{i}", [128, WT], BF16) for i in range(2)]
                with ExitStack() as psc:
                    self.alloc_psum(psc)
                    with ExitStack() as rg:
                        self.alloc_scan(rg)
                        self.ph_stats_u()
                        if self.stop >= 3:
                            self.ph_scan()
                        if self.stop >= 4:
                            self.ph_x1()
                        prog.flush(self.block)
                    if self.stop >= 5:
                        with ExitStack() as rga:
                            self.alloc_attn(rga)
                            self.ph_attn()
                            prog.flush(self.block)
                    if self.stop >= 6:
                        self.ph_ln2_m1_xq_pre()
                    prog.flush(self.block)
            if self.stop < 7:
                return
            self.fence_ffn = prog.fence()
            with ExitStack() as rgf0:
                self.xqb = [self.S(rgf0, f"xq_{dc}", [128, TOK], BF16) for dc in range(DCN)]
                with ExitStack() as psc:
                    self.alloc_psum(psc)
                    with ExitStack() as rgf1:
                        self.ln2 = [self.S(rgf1, f"ln2_{dc}", [128, TOK]) for dc in range(DCN)]
                        self.ql = [self.S(rgf1, f"qlx{i}", [128, QT], I16) for i in range(4)]
                        self.qf = [self.S(rgf1, f"qfx{i}", [128, QT]) for i in range(2)]
                        self.ph_ln2_m1_xq()
                        prog.flush(self.block)
                    self.fence_q = prog.fence()
                    with ExitStack() as rgf2:
                        self.alloc_ffn(rgf2)
                        if self.stop >= 8:
                            self.ph_upgate()
                        if self.stop >= 9:
                            self.ph_passA()
                        if self.stop >= 10:
                            self.ph_passB()
                        if self.stop >= 11:
                            self.ph_passC_down()
                        prog.flush(self.block)

    _psum_gen = [0]

    def alloc_psum(self, psc):
        nc = self.nc
        g = self._psum_gen[0]
        self._psum_gen[0] += 1
        self.ps_a = psc.enter_context(nc.psum_tensor(f"ps_a{g}", [128, QT], F32))
        self.ps_b = psc.enter_context(nc.psum_tensor(f"ps_b{g}", [128, QT], F32))
        self.ps_o1 = psc.enter_context(nc.psum_tensor(f"ps_o1{g}", [128, WT], F32))
        self.ps_o2 = psc.enter_context(nc.psum_tensor(f"ps_o2{g}", [128, WT], F32))
        self.ps_o3 = psc.enter_context(nc.psum_tensor(f"ps_o3{g}", [128, OT], F32))
        self.ps_s1 = psc.enter_context(nc.psum_tensor(f"ps_s1{g}", [1, WT], F32))
        self.ps_s2 = psc.enter_context(nc.psum_tensor(f"ps_s2{g}", [1, WT], F32))
        self.ps_aux = psc.enter_context(nc.psum_tensor(f"ps_aux{g}", [128, 512], F32))
        self.ps_t = self.ps_aux.ap()[:, 0:64].bitcast(BF16)
        self.ps_dmy = self.ps_aux.ap()[:, 128:136]

    def alloc_scan(self, rg):
        S = self.S
        self.xs_t = [S(rg, f"xs{i}", [128, WT]) for i in range(4)]
        self.z_t = [S(rg, f"z{i}", [128, WT]) for i in range(4)]
        self.zscr = [S(rg, f"zscr{i}", [128, WT]) for i in range(2)]
        self.dab = [S(rg, f"dab{i}", [128, WT]) for i in range(4)]
        self.prb = [S(rg, f"prb{i}", [128, WT]) for i in range(4)]
        self.s_out = [S(rg, f"sout{i}", [128, WT]) for i in range(4)]
        self.y_pp = [[S(rg, f"ypp{k}_{sc}", [128, WT]) for sc in range(SCN)] for k in range(2)]
        self.yall = S(rg, "yall", [128, DCN * OT])
        self.xoall = S(rg, "xoall", [128, DCN * OT])
        self.xscr = [S(rg, f"xscr{i}", [128, OT]) for i in range(4)]

    def alloc_attn(self, rg):
        S = self.S
        self.dq_w = S(rg, "dq_w", [128, D])
        self.dq_l = S(rg, "dq_l", [128, D], I16)
        self.dq_q = S(rg, "dq_q", [128, D], BF16)
        self.dq_s = S(rg, "dq_s", [128, 3 * 32])
        self.dq_st = S(rg, "dq_st", [128, DCN * 128], BF16)
        self.identb = S(rg, "identb_sb", [128, 128], BF16)
        self.w1s = [S(rg, f"w1s{i}", [128, 2 * DCN * 128], BF16) for i in range(2)]
        self.w2s = [S(rg, f"w2s{i}", [128, FC * 128], BF16) for i in range(2)]
        self.g1bf = S(rg, "g1bf", [128, FC * QT], BF16)
        self.evscr = [S(rg, f"ev{i}", [128, QT]) for i in range(2)]
        self.xmscr = [S(rg, f"xm{i}", [128, QT]) for i in range(2)]
        self.xo2 = [S(rg, f"xo2_{i}", [128, QT]) for i in range(2)]

    def alloc_ffn(self, rg):
        S = self.S
        self.wus = [S(rg, f"wus{i}", [128, 2 * DCN * 128], BF16) for i in range(2)]
        self.wgs = [S(rg, f"wgs{i}", [128, 2 * DCN * 128], BF16) for i in range(2)]
        self.stg = [S(rg, f"stg{i}", [128, TOK], BF16) for i in range(2)]
        self.qa = [S(rg, f"qa{i}", [128, QT], BF16) for i in range(4)]
        self.ql = [S(rg, f"ql{i}", [128, QT], I16) for i in range(4)]
        self.qm = [S(rg, f"qm{i}", [128, QT], U16) for i in range(1)]
        self.qf = [S(rg, f"qf{i}", [128, QT]) for i in range(6)]
        self.obf = [S(rg, f"obf{i}", [128, QT], BF16) for i in range(ICN)]
        self.wds = [S(rg, f"wds{i}", [128, ICN * 128], BF16) for i in range(2)]
        self.oxo = [S(rg, f"oxo{i}", [128, QT]) for i in range(1)]

    R_M, R_VAR, R_RSTD, R_GFAC, R_MG, R_MRSTD, R_TMP = (i * WT for i in range(7))
    R_SR4 = 7 * WT
    L_S1, L_S2, L_M2R = 0, TOK, 2 * TOK

    def row(self, off, n=WT):
        return self.rows[:, off:off + n]

    def eng_drain(self, eng, *toks):
        if eng == V:
            t = self.prog.op(V, "memset", self.drt[:, 0:4], 0.0, waits=list(toks))
        elif eng == A_:
            t = self.prog.op(A_, "activation", self.drt[:, 4:8], self.drt[:, 0:4], AF.Identity,
                             waits=list(toks))
        else:
            t = self.prog.op(P_, "memset", self.drt[:, 0:4], 0.0, waits=list(toks))
        return tok_max(t, *toks)

    def pe_drain(self, *toks):
        t = self.prog.op(TE, "matmul", self.ps_dmy, self.ones[0:1, 0:128],
                         self.ones[0:1, 0:8], waits=list(toks))
        return tok_max(t, *toks)

    def ph_params(self):
        prog, S = self.prog, self.S
        with ExitStack() as rg:
            sA = S(rg, "sA", [128, SCN * NS]); sB = S(rg, "sB", [128, SCN * NS])
            sC = S(rg, "sC", [128, SCN * NS]); sdt = S(rg, "sdt", [128, SCN * NS])
            sp1 = S(rg, "sp1", [128, SCN * NS]); s_s1 = S(rg, "s_s1", [128, SCN])
            dr = self.dr
            tc = tok_max(*[prog.op(SY, "dma_start", out=d[:], in_=dr[n]) for n, d in
                           (("A_s", sA), ("B_s", sB), ("C_s", sC), ("dtb", sdt),
                            ("D_s", self.sD), ("s1_s", s_s1), ("s2c", self.s_s2),
                            ("b1c", self.s_b1), ("b2c", self.s_b2))])
            t_expA = prog.op(A_, "activation", sp1[:], sA[:], AF.Exp, waits=[tc])
            t_r = prog.op(V, "reciprocal", self.CdB[:], sp1[:], waits=[t_expA])
            t_m = prog.op(V, "tensor_tensor", self.dAv[:], sp1[:], sdt[:], op=ALU.mult, waits=[t_expA, tc])
            self.t_dA = prog.op(A_, "activation", self.dAv[:], self.dAv[:], AF.Exp, scale=-1.0, waits=[t_m])
            t_1m = prog.op(V, "tensor_scalar", sp1[:], self.dAv[:], -1.0, 1.0,
                           op0=ALU.mult, op1=ALU.add, waits=[self.t_dA, t_r])
            t_d1 = prog.op(V, "tensor_tensor", self.CdB[:], self.CdB[:], sp1[:], op=ALU.mult, waits=[t_1m])
            t_d2 = prog.op(V, "tensor_tensor", self.CdB[:], self.CdB[:], sB[:], op=ALU.mult, waits=[t_d1])
            t_d3 = prog.op(V, "tensor_tensor", self.CdB[:], self.CdB[:], sC[:], op=ALU.mult, waits=[t_d2])
            tl = []
            for sc in range(SCN):
                tl.append(prog.op(V, "tensor_scalar", self.CdB[:, sc * NS:(sc + 1) * NS],
                                  self.CdB[:, sc * NS:(sc + 1) * NS], s_s1[:, sc:sc + 1], None,
                                  op0=ALU.mult, waits=[t_d3]))
            self.t_cdb = tok_max(*tl)
            self.t_Df = prog.op(V, "tensor_tensor", self.sD[:], self.sD[:], s_s1[:], op=ALU.mult, waits=[tc])
            self.fence_params = prog.fence()
            prog.flush(self.block)

    def ph_stats_u(self):
        prog = self.prog
        row, R_TMP, R_SR4 = self.row, self.R_TMP, self.R_SR4
        self.t_xs = {}
        self.use_cstg = {}
        use_ps1 = use_ps2 = None
        stat_d = []
        xs_free = {}
        for w in range(NW):
            mm1 = mm2 = None
            for sc in range(SCN):
                t_l = prog.op(SY, "dma_start", out=self.xs_t[sc][:],
                              in_=self.x_scan[sc * 128:(sc + 1) * 128, w * WT:(w + 1) * WT],
                              waits=[xs_free.get(sc), self.fence_params if w == 0 else None])
                t_c = prog.op(A_, "activation", self.cstg[0][:], self.xs_t[sc][:], AF.Identity,
                              waits=[t_l, self.use_cstg.get(0)])
                t_q = prog.op(A_, "activation", self.cstg[1][:], self.xs_t[sc][:], AF.Square,
                              waits=[t_l, self.use_cstg.get(1)])
                xs_free[sc] = tok_max(t_c, t_q)
                mm1 = prog.op(TE, "matmul", self.ps_s1[:], self.onesbf[:], self.cstg[0][:],
                              start=(sc == 0), stop=(sc == SCN - 1),
                              waits=[t_c, self.t_onesbf] + ([use_ps1] if sc == 0 else []))
                mm2 = prog.op(TE, "matmul", self.ps_s2[:], self.onesbf[:], self.cstg[1][:],
                              start=(sc == 0), stop=(sc == SCN - 1),
                              waits=[t_q, self.t_onesbf] + ([use_ps2] if sc == 0 else []))
                self.use_cstg[0] = mm1
                self.use_cstg[1] = mm2
            t_mm = self.pe_drain(mm1, mm2)
            t_e1 = prog.op(A_, "activation", row(R_TMP), self.ps_s1[:], AF.Identity,
                           waits=[t_mm] + ([stat_d[-2]] if stat_d else []))
            t_e2 = prog.op(A_, "activation", row(R_SR4), self.ps_s2[:], AF.Identity,
                           waits=[t_mm] + ([stat_d[-1]] if stat_d else []))
            use_ps1, use_ps2 = t_e1, t_e2
            stat_d.append(prog.op(SY, "dma_start", out=self.stats_in[0:1, w * WT:(w + 1) * WT],
                                  in_=row(R_TMP), waits=[t_e1]))
            stat_d.append(prog.op(SY, "dma_start", out=self.stats_in[1:2, w * WT:(w + 1) * WT],
                                  in_=row(R_SR4), waits=[t_e2]))
        self.xs_free = xs_free
        self.use_ps1, self.use_ps2 = use_ps1, use_ps2
        self.t_sag = prog.op(G, "collective_compute", "AllGather", ALU.bypass,
                             replica_groups=PAIRS, ins=[self.stats_in.ap()], outs=[self.stats_out.ap()],
                             waits=[tok_max(*stat_d)])
        self.rows_use = tok_max(*stat_d)

    def ph_scan(self):
        prog = self.prog
        row = self.row
        R_M, R_VAR, R_RSTD, R_GFAC, R_MG, R_MRSTD, R_TMP, R_SR4 = (
            self.R_M, self.R_VAR, self.R_RSTD, self.R_GFAC, self.R_MG, self.R_MRSTD, self.R_TMP, self.R_SR4)
        self.ywtok = [None] * NW
        stuse = {}
        dause, pruse, sduse = {}, {}, {}
        zuse = {}
        ypfree = {}
        combo = 0
        for w in range(NW):
            rr = []
            for r in range(4):
                rr.append(prog.op(SY, "dma_start",
                                  out=self.rows[:, R_SR4 + r * WT:R_SR4 + (r + 1) * WT],
                                  in_=self.stats_out[r:r + 1, w * WT:(w + 1) * WT],
                                  waits=[self.t_sag, self.rows_use]))
            trd = tok_max(*rr)
            tS1 = prog.op(V, "tensor_tensor", row(R_TMP), self.rows[:, R_SR4:R_SR4 + WT],
                          self.rows[:, R_SR4 + 2 * WT:R_SR4 + 3 * WT], op=ALU.add, waits=[trd])
            tS2 = prog.op(V, "tensor_tensor", row(R_VAR), self.rows[:, R_SR4 + WT:R_SR4 + 2 * WT],
                          self.rows[:, R_SR4 + 3 * WT:R_SR4 + 4 * WT], op=ALU.add, waits=[trd])
            tm = prog.op(V, "tensor_scalar", row(R_M), row(R_TMP), 1.0 / D, None, op0=ALU.mult, waits=[tS1])
            tex = prog.op(V, "tensor_scalar", row(R_VAR), row(R_VAR), 1.0 / D, None, op0=ALU.mult, waits=[tS2])
            tm2 = prog.op(V, "tensor_tensor", row(R_TMP), row(R_M), row(R_M), op=ALU.mult, waits=[tm])
            tvar = prog.op(V, "tensor_tensor", row(R_VAR), row(R_VAR), row(R_TMP), op=ALU.subtract,
                           waits=[tex, tm2])
            tl1 = prog.op(A_, "activation", row(R_RSTD), row(R_VAR), AF.Ln, bias=self.cbias[0:1, 0:1], waits=[tvar, self.t_cbias])
            trstd = prog.op(A_, "activation", row(R_RSTD), row(R_RSTD), AF.Exp, scale=-0.5, waits=[tl1])
            tl2 = prog.op(A_, "activation", row(R_GFAC), row(R_VAR), AF.Ln, scale=1.0 + EPS_RMS,
                          bias=self.cbias[0:1, 1:2], waits=[tvar, self.t_cbias])
            tgf = prog.op(A_, "activation", row(R_GFAC), row(R_GFAC), AF.Exp, scale=-0.5, waits=[tl2])
            tmg = prog.op(V, "tensor_tensor", row(R_MG), row(R_M), row(R_GFAC), op=ALU.mult, waits=[tm, tgf])

            t_og = prog.op(TE, "matmul", self.ps_o1[:], self.ones[0:1, 0:128], row(R_GFAC),
                           waits=[tgf, self.t_ones, zuse.get("o1")])
            t_omg = prog.op(TE, "matmul", self.ps_o2[:], self.ones[0:1, 0:128], row(R_MG),
                            waits=[tmg, self.t_ones, zuse.get("o2")])
            t_og = t_omg = self.pe_drain(t_og, t_omg)
            tz_w = {}
            for sc in range(SCN):
                t_l = prog.op(SY, "dma_start", out=self.xs_t[sc][:],
                              in_=self.x_scan[sc * 128:(sc + 1) * 128, w * WT:(w + 1) * WT],
                              waits=[self.xs_free.get(sc)])
                zs = self.zscr[sc % 2]
                t1 = prog.op(V, "tensor_tensor", zs[:], self.xs_t[sc][:], self.ps_o1[:], op=ALU.mult,
                             waits=[t_l, t_og, zuse.get(("zscr", sc % 2))])
                t2 = prog.op(V, "tensor_tensor", self.z_t[sc][:], zs[:], self.ps_o2[:], op=ALU.subtract,
                             waits=[t1, t_omg, zuse.get(("z", sc))])
                zuse[("zscr", sc % 2)] = t2
                tz_w[sc] = t2
                self.xs_free[sc] = t1
                if "z" in self.dbg:
                    if "z" not in self.taps:
                        self.tap("z", [DH, T])
                    tz_d = prog.op(SY, "dma_start", out=self.taps["z"][sc * 128:(sc + 1) * 128, w * WT:(w + 1) * WT],
                                   in_=self.z_t[sc][:], waits=[t2])
                    zuse[("z", sc)] = tok_max(zuse.get(("z", sc)), tz_d)
            zuse["o1"] = tok_max(*tz_w.values())
            zuse["o2"] = zuse["o1"]
            self.rows_use = tok_max(zuse["o1"], t_og, t_omg)

            ybounce = []
            for sc in range(SCN):
                acc_prev = None
                for n in range(NS):
                    ci = sc * NS + n
                    k = combo % 4
                    t_da = prog.op(A_, "activation", self.dab[k][:], self.ones[:], AF.Identity,
                                   scale=self.dAv[:, ci:ci + 1],
                                   waits=[self.t_dA, self.t_ones, dause.get(k)])
                    t_pr = prog.op(A_, "activation", self.prb[k][:], self.z_t[sc][:], AF.Identity,
                                   scale=self.CdB[:, ci:ci + 1],
                                   waits=[tz_w[sc], self.t_cdb, pruse.get(k)])
                    init = 0.0 if w == 0 else self.states[:, ci:ci + 1]
                    t_sc = prog.op(V, "tensor_tensor_scan", self.s_out[k][:], self.dab[k][:],
                                   self.prb[k][:], init, op0=ALU.mult, op1=ALU.add,
                                   waits=[t_da, t_pr, sduse.get(k), self.t_st0] +
                                         ([stuse[ci]] if ci in stuse else []))
                    t_cp = prog.op(V, "tensor_scalar", self.states[:, ci:ci + 1],
                                   self.s_out[k][:, WT - 1:WT], 1.0, None, op0=ALU.mult, waits=[t_sc])
                    stuse[ci] = t_cp
                    dause[k] = t_sc
                    pruse[k] = t_sc
                    dst = self.y_pp[n % 2][sc]
                    src = self.y_pp[(n + 1) % 2][sc]
                    if n == 0:
                        t_ac = prog.op(P_, "tensor_copy", dst[:], self.s_out[k][:],
                                       waits=[t_sc, ypfree.get((0, sc))])
                    else:
                        t_ac = prog.op(P_, "tensor_tensor", dst[:], src[:], self.s_out[k][:], op=ALU.add,
                                       waits=[t_sc, acc_prev, ypfree.get((n % 2, sc))])
                    sduse[k] = tok_max(t_ac, t_cp)
                    acc_prev = t_ac
                    combo += 1
                dst = self.y_pp[NS % 2][sc]
                src = self.y_pp[(NS + 1) % 2][sc]
                t_dk = prog.op(V, "scalar_tensor_tensor", dst[:], self.z_t[sc][:], self.sD[:, sc:sc + 1],
                               src[:], op0=ALU.mult, op1=ALU.add,
                               waits=[acc_prev, self.t_Df, ypfree.get((NS % 2, sc))])
                zuse[("z", sc)] = t_dk
                t_yd = prog.op(SY, "dma_start", out=self.yag_in[w][sc * 128:(sc + 1) * 128, :],
                               in_=dst[:], waits=[t_dk])
                ypfree[(NS % 2, sc)] = t_yd
                ypfree[((NS + 1) % 2, sc)] = t_dk
                ybounce.append(t_yd)
            self.ywtok[w] = prog.op(G, "collective_compute", "AllGather", ALU.bypass,
                                    replica_groups=PAIRS, ins=[self.yag_in[w].ap()],
                                    outs=[self.yag_out[w].ap()], waits=[tok_max(*ybounce)])
            if "yag" in self.dbg:
                if "yag" not in self.taps:
                    self.tap("yag", [D, T])
                prog.op(SY, "dma_start", out=self.taps["yag"][:, w * WT:(w + 1) * WT],
                        in_=self.yag_out[w].ap(), waits=[self.ywtok[w]])
        self.zuse_final = dict(zuse)

    def ph_x1(self):
        prog = self.prog
        row = self.row
        R_M, R_VAR, R_RSTD, R_MRSTD, R_TMP, R_SR4 = (
            self.R_M, self.R_VAR, self.R_RSTD, self.R_MRSTD, self.R_TMP, self.R_SR4)
        self.t_x1 = {}
        self.t_x1ln = {}
        yx_use = {}
        o_use = {"o1": self.zuse_final.get("o1"), "o2": self.zuse_final.get("o2")}
        for w in range(NW):
            def mk_yread(w=w):
                def fn(e):
                    src_ap = self.yag_out[w].rearrange("(dc p) t -> p dc t", p=128)
                    return ((), dict(out=self.yall.ap().rearrange("p (dc t) -> p dc t", dc=DCN),
                                     in_=src_ap[:, :, bass.ts(self.hcell["hrv"], OT)]))
                return fn
            t_yall = prog.op(SY, "dma_start", argfn=mk_yread(),
                             waits=[self.ywtok[w], yx_use.get("yall")])
            t_xoall = prog.op(SY, "dma_start",
                              out=self.xoall.ap().rearrange("p (dc t) -> p dc t", dc=DCN),
                              in_=self.x_own.rearrange("(dc p) t -> p dc t", p=128)[:, :, w * OT:(w + 1) * OT],
                              waits=[yx_use.get("xoall")])
            t_y = {dc: t_yall for dc in range(DCN)}
            t_xo = {dc: t_xoall for dc in range(DCN)}
            def mk_r(w=w):
                def fn(e):
                    return ((), dict(
                        out=self.rows[:, R_SR4:R_SR4 + 4 * OT].rearrange("p (r c) -> p r c", r=4),
                        in_=self.stats_out.rearrange("(a r) c -> a r c", a=1)[:, :, bass.ts(self.hcell["hrv"] + 2 * w, OT)]))
                return fn
            trd = prog.op(SY, "dma_start", argfn=mk_r(), waits=[self.t_sag, self.rows_use])
            tS1 = prog.op(V, "tensor_tensor", row(R_TMP, OT), self.rows[:, R_SR4:R_SR4 + OT],
                          self.rows[:, R_SR4 + 2 * OT:R_SR4 + 3 * OT], op=ALU.add, waits=[trd])
            tS2 = prog.op(V, "tensor_tensor", row(R_VAR, OT), self.rows[:, R_SR4 + OT:R_SR4 + 2 * OT],
                          self.rows[:, R_SR4 + 3 * OT:R_SR4 + 4 * OT], op=ALU.add, waits=[trd])
            tm = prog.op(V, "tensor_scalar", row(R_M, OT), row(R_TMP, OT), 1.0 / D, None,
                         op0=ALU.mult, waits=[tS1])
            tex = prog.op(V, "tensor_scalar", row(R_VAR, OT), row(R_VAR, OT), 1.0 / D, None,
                          op0=ALU.mult, waits=[tS2])
            tm2 = prog.op(V, "tensor_tensor", row(R_TMP, OT), row(R_M, OT), row(R_M, OT),
                          op=ALU.mult, waits=[tm])
            tvar = prog.op(V, "tensor_tensor", row(R_VAR, OT), row(R_VAR, OT), row(R_TMP, OT),
                           op=ALU.subtract, waits=[tex, tm2])
            tl1 = prog.op(A_, "activation", row(R_RSTD, OT), row(R_VAR, OT), AF.Ln, bias=self.cbias[0:1, 0:1], waits=[tvar, self.t_cbias])
            trstd = prog.op(A_, "activation", row(R_RSTD, OT), row(R_RSTD, OT), AF.Exp, scale=-0.5, waits=[tl1])
            tmr = prog.op(V, "tensor_tensor", row(R_MRSTD, OT), row(R_M, OT), row(R_RSTD, OT),
                          op=ALU.mult, waits=[tm, trstd])
            t_or = prog.op(TE, "matmul", self.ps_o1[:, 0:OT], self.ones[0:1, 0:128], row(R_RSTD, OT),
                           waits=[trstd, o_use.get("o1")])
            t_omr = prog.op(TE, "matmul", self.ps_o2[:, 0:OT], self.ones[0:1, 0:128], row(R_MRSTD, OT),
                            waits=[tmr, o_use.get("o2")])
            t_or = t_omr = self.pe_drain(t_or, t_omr)
            self.rows_use = tok_max(t_or, t_omr, tvar)
            sq_mm = None
            p1l = []
            for dc in range(DCN):
                xr = self.xscr[dc % 2]
                yr = self.xscr[2 + dc % 2]
                p1 = prog.op(V, "tensor_tensor", xr[:], self.xoall[:, dc * OT:(dc + 1) * OT],
                             self.ps_o1[:, 0:OT],
                             op=ALU.mult, waits=[t_xo[dc], t_or, yx_use.get(("s", dc % 2))])
                p2 = prog.op(V, "tensor_tensor", yr[:], self.yall[:, dc * OT:(dc + 1) * OT],
                             self.ps_o2[:, 0:OT],
                             op=ALU.subtract, waits=[t_y[dc], t_omr, yx_use.get(("s2", dc % 2))])
                tx1 = prog.op(P_, "tensor_tensor", self.x1[dc][:, w * OT:(w + 1) * OT], xr[:], yr[:],
                              op=ALU.add, waits=[p1, p2])
                self.t_x1[(w, dc)] = tx1
                yx_use["yall"] = tok_max(yx_use.get("yall"), p2)
                yx_use["xoall"] = tok_max(yx_use.get("xoall"), p1)
                yx_use[("s", dc % 2)] = tx1
                yx_use[("s2", dc % 2)] = tx1
                p1l += [p1, p2]
                t_sq = prog.op(A_, "activation", self.cstg[dc % 2][:, 0:OT],
                               self.x1[dc][:, w * OT:(w + 1) * OT], AF.Square,
                               waits=[tx1, self.use_cstg.get(dc % 2)])
                sq_mm = prog.op(TE, "matmul", self.ps_s2[:, 0:OT], self.onesbf[:],
                                self.cstg[dc % 2][:, 0:OT],
                                start=(dc == 0), stop=(dc == DCN - 1),
                                waits=[t_sq] + ([self.use_ps2] if dc == 0 else []))
                self.use_cstg[dc % 2] = sq_mm
            o_use["o1"] = tok_max(*p1l)
            o_use["o2"] = o_use["o1"]
            sq_mm = self.pe_drain(sq_mm)
            t_es = prog.op(A_, "activation", row(R_TMP, OT), self.ps_s2[:, 0:OT], AF.Identity, waits=[sq_mm])
            self.use_ps2 = t_es
            tlf = prog.op(A_, "activation", row(R_TMP, OT), row(R_TMP, OT), AF.Ln, scale=1.0 / D,
                          bias=self.cbias[0:1, 2:3], waits=[t_es, self.t_cbias])
            tfac = prog.op(A_, "activation", row(R_TMP, OT), row(R_TMP, OT), AF.Exp, scale=-0.5, waits=[tlf])
            t_of = prog.op(TE, "matmul", self.ps_o3[:, 0:OT], self.ones[0:1, 0:128], row(R_TMP, OT),
                           waits=[tfac, o_use.get("o3")])
            t_of = self.pe_drain(t_of)
            lnt = []
            for dc in range(DCN):
                t_ln = prog.op(V, "scalar_tensor_tensor", self.x1ln[dc][:, w * OT:(w + 1) * OT],
                               self.x1[dc][:, w * OT:(w + 1) * OT], self.s_s2[:, dc:dc + 1],
                               self.ps_o3[:, 0:OT], op0=ALU.mult, op1=ALU.mult,
                               waits=[self.t_x1[(w, dc)], t_of])
                self.t_x1ln[(w, dc)] = t_ln
                lnt.append(t_ln)
            o_use["o3"] = tok_max(*lnt)
        if "x1" in self.dbg:
            tp = self.tap("x1", [D, TOK])
            for dc in range(DCN):
                prog.op(SY, "dma_start", out=tp[dc * 128:(dc + 1) * 128, :], in_=self.x1[dc][:],
                        waits=[tok_max(*[self.t_x1[(w, dc)] for w in range(NW)])])
        if "x1ln" in self.dbg:
            tp = self.tap("x1ln", [D, TOK], BF16)
            for dc in range(DCN):
                prog.op(SY, "dma_start", out=tp[dc * 128:(dc + 1) * 128, :], in_=self.x1ln[dc][:],
                        waits=[tok_max(*[self.t_x1ln[(w, dc)] for w in range(NW)])])

    def dequant_chunks(self):
        units = []
        for key, nat, outp, KCtot, MC, I in (
                ("u", self.dr["wun"], self.dr["wqu"], DCN, ICN, D),
                ("g", self.dr["wgn"], self.dr["wqg"], DCN, ICN, D),
                ("d", self.dr["wdn"], self.dr["wqd"], ICN, DCN, DI)):
            nsub = I // D
            for oc in range(MC):
                for s in range(nsub):
                    units.append((key, nat, outp, KCtot, s * DCN, oc, s == nsub - 1))
        return units

    def emit_dequant(self, unit):
        prog = self.prog
        key, nat, outp, KCtot, kc0, oc, last = unit
        nb = D // 32
        st = self.dq_state
        t_l = prog.op(SY, "dma_start", out=self.dq_w[:],
                      in_=nat[oc * 128:(oc + 1) * 128, kc0 * 128:kc0 * 128 + D],
                      waits=[st.get("w")])
        w3 = self.dq_w.ap().rearrange("p (b k) -> p b k", k=32)
        t_bm = prog.op(V, "reduce_max", self.dq_s[:, 0:nb], w3, axis=AX.X,
                       apply_absolute_value=True, waits=[t_l, st.get("s")])
        t_s = prog.op(V, "tensor_scalar", self.dq_s[:, 32:32 + nb], self.dq_s[:, 0:nb],
                      1.0 / 7.0, 1e-12, op0=ALU.mult, op1=ALU.max, waits=[t_bm, st.get("s2")])
        t_r = prog.op(V, "reciprocal", self.dq_s[:, 64:64 + nb], self.dq_s[:, 32:32 + nb],
                      waits=[t_s, st.get("r")])
        rb = self.dq_s.ap()[:, 64:64 + nb].unsqueeze(2).broadcast_to([128, nb, 32])
        sb_ = self.dq_s.ap()[:, 32:32 + nb].unsqueeze(2).broadcast_to([128, nb, 32])
        l3 = self.dq_l.ap().rearrange("p (b k) -> p b k", k=32)
        q3 = self.dq_q.ap().rearrange("p (b k) -> p b k", k=32)
        t_lv = prog.op(V, "tensor_tensor", l3, w3, rb, op=ALU.mult, waits=[t_r, st.get("l")])
        t_q = prog.op(P_, "tensor_tensor", q3, l3, sb_, op=ALU.mult, waits=[t_lv, st.get("q")])
        st["w"] = t_lv
        st["s"] = t_lv
        st["s2"] = tok_max(t_lv, t_q)
        st["r"] = t_lv
        st["l"] = t_q
        t_q = self.eng_drain(P_, t_q)
        evs = []
        for kc in range(DCN):
            t_t = prog.op(TE, "matmul", self.ps_t, self.dq_q[:, kc * 128:(kc + 1) * 128],
                          self.identb[:], is_transpose=True,
                          waits=[t_q, self.t_identb, st.get(("pt",))])
            t_t = self.pe_drain(t_t)
            t_e = prog.op(A_, "activation", self.dq_st[:, kc * 128:(kc + 1) * 128], self.ps_t,
                          AF.Identity, waits=[t_t, st.get(("st", kc))])
            st[("pt",)] = t_e
            evs.append(t_e)
            st[("st", kc)] = t_e
        rows0 = (oc * KCtot + kc0) * 128
        t_o = prog.op(SY, "dma_start",
                      out=outp[rows0:rows0 + DCN * 128, :].rearrange("(kc p) f -> p kc f", p=128),
                      in_=self.dq_st.ap().rearrange("p (kc f) -> p kc f", kc=DCN),
                      waits=[tok_max(*evs)])
        st["q"] = t_o
        for kc in range(DCN):
            st[("st", kc)] = tok_max(st[("st", kc)], t_o)
        if last:
            self.t_wq[(key, oc)] = t_o
        else:
            self.t_wq[(key, oc)] = tok_max(self.t_wq.get((key, oc)), t_o)

    def ph_attn(self):
        prog = self.prog
        row = self.row
        fence_a = prog.fence()
        self.dq_state = {}
        self.t_wq = {}
        self.t_identb = prog.op(SY, "dma_start", out=self.identb[:], in_=self.dr["identb"],
                                waits=[fence_a])
        dq_units = self.dequant_chunks()
        dq_pos = 0
        w1_use, w2_use, g1_use, ps_use, ev_use, xm_use, xo_use = {}, {}, {}, {}, {}, {}, {}
        self.t_xm = {}
        use_ps1l = self.use_ps1
        use_ps2l = self.use_ps2
        self.lnS_tok = []
        for q in range(NQ):
            x1ln_tok = self.eng_drain(V, *[self.t_x1ln[(w, dc)] for dc in range(DCN) for w in (2 * q, 2 * q + 1)])
            for fg in range(FC // 2):
                sl = fg % 2
                t_w = prog.op(G, "dma_start", out=slab_out(self.w1s[sl], 2, DCN), in_=slab_ap(self.dr["wp1"], fg * 2, 2, DCN),
                              waits=[w1_use.get(sl)] + ([fence_a] if (q == 0 and fg < 2) else []))
                gel = []
                for mc in range(2):
                    ffc = fg * 2 + mc
                    pb = [self.ps_a, self.ps_b][ffc % 2]
                    mml = None
                    for dc in range(DCN):
                        mml = prog.op(TE, "matmul", pb[:],
                                      self.w1s[sl][:, (mc * DCN + dc) * 128:(mc * DCN + dc + 1) * 128],
                                      self.x1ln[dc][:, q * QT:(q + 1) * QT],
                                      start=(dc == 0), stop=(dc == DCN - 1),
                                      waits=[t_w, x1ln_tok] + ([ps_use.get(ffc % 2)] if dc == 0 else []))
                    mml = self.pe_drain(mml)
                    t_g = prog.op(A_, "activation", self.g1bf[:, ffc * QT:(ffc + 1) * QT], pb[:],
                                  AF.Gelu, bias=self.s_b1[:, ffc:ffc + 1],
                                  waits=[mml, g1_use.get(ffc)])
                    ps_use[ffc % 2] = t_g
                    g1_use[ffc] = t_g
                    gel.append(t_g)
                w1_use[sl] = tok_max(*gel)
            g1_all = self.eng_drain(A_, *[g1_use[f] for f in range(FC)])
            mmA = mmB = None
            for dc in range(DCN):
                sl = dc % 2
                t_w = prog.op(G, "dma_start", out=slab_out(self.w2s[sl], 1, FC), in_=slab_ap(self.dr["wp2"], dc, 1, FC),
                              waits=[w2_use.get(sl)] + ([fence_a] if (q == 0 and dc < 2) else []))
                pb = [self.ps_a, self.ps_b][dc % 2]
                mml = None
                for kc in range(FC):
                    mml = prog.op(TE, "matmul", pb[:], self.w2s[sl][:, kc * 128:(kc + 1) * 128],
                                  self.g1bf[:, kc * QT:(kc + 1) * QT],
                                  start=(kc == 0), stop=(kc == FC - 1),
                                  waits=[t_w, g1_all] + ([ps_use.get(dc % 2)] if kc == 0 else []))
                w2_use[sl] = mml
                mml = self.pe_drain(mml)
                t_ev = prog.op(A_, "activation", self.evscr[sl][:], pb[:], AF.Identity,
                               bias=self.s_b2[:, dc:dc + 1], waits=[mml, ev_use.get(sl)])
                ps_use[dc % 2] = t_ev
                t_xo = prog.op(SY, "dma_start", out=self.xo2[sl][:],
                               in_=self.x_own[dc * 128:(dc + 1) * 128, q * QT:(q + 1) * QT],
                               waits=[xo_use.get(sl)] + ([fence_a] if (q == 0 and dc < 2) else []))
                x1tok = tok_max(self.t_x1ln[(2 * q, dc)], self.t_x1ln[(2 * q + 1, dc)])
                t_p = prog.op(P_, "tensor_tensor", self.xmscr[sl][:], self.evscr[sl][:],
                              self.x1[dc][:, q * QT:(q + 1) * QT], op=ALU.add,
                              waits=[t_ev, x1tok, xm_use.get(sl)])
                txm = prog.op(V, "tensor_tensor", self.x1[dc][:, q * QT:(q + 1) * QT],
                              self.xmscr[sl][:], self.xo2[sl][:], op=ALU.add,
                              waits=[t_p, t_xo])
                self.t_xm[(q, dc)] = txm
                ev_use[sl] = t_p
                xm_use[sl] = txm
                xo_use[sl] = txm
                t_cb = prog.op(A_, "activation", self.cstg[dc % 2][:], self.x1[dc][:, q * QT:(q + 1) * QT],
                               AF.Identity, waits=[txm, self.use_cstg.get(dc % 2)])
                mmA = prog.op(TE, "matmul", self.ps_s1[:], self.onesbf[:], self.cstg[dc % 2][:],
                              start=(dc == 0), stop=(dc == DCN - 1),
                              waits=[t_cb] + ([use_ps1l] if dc == 0 else []))
                t_sq = prog.op(A_, "activation", self.cstg[dc % 2][:], self.x1[dc][:, q * QT:(q + 1) * QT],
                               AF.Square, waits=[txm, mmA])
                mmB = prog.op(TE, "matmul", self.ps_s2[:], self.onesbf[:], self.cstg[dc % 2][:],
                              start=(dc == 0), stop=(dc == DCN - 1),
                              waits=[t_sq] + ([use_ps2l] if dc == 0 else []))
                self.use_cstg[dc % 2] = mmB
            t_mm = self.pe_drain(mmA, mmB)
            tE1 = prog.op(A_, "activation", self.rows[:, self.L_S1 + q * QT:self.L_S1 + (q + 1) * QT],
                          self.ps_s1[:], AF.Identity, waits=[t_mm, self.rows_use])
            tE2 = prog.op(A_, "activation", self.rows[:, self.L_S2 + q * QT:self.L_S2 + (q + 1) * QT],
                          self.ps_s2[:], AF.Identity, waits=[t_mm])
            use_ps1l, use_ps2l = tE1, tE2
            self.lnS_tok += [tE1, tE2]
            n_do = 18 if q < NQ - 1 else len(dq_units) - dq_pos
            for _ in range(n_do):
                if dq_pos < len(dq_units):
                    self.emit_dequant(dq_units[dq_pos])
                    dq_pos += 1
        if "xmid" in self.dbg:
            tp = self.tap("xmid", [D, TOK])
            for dc in range(DCN):
                prog.op(SY, "dma_start", out=tp[dc * 128:(dc + 1) * 128, :], in_=self.x1[dc][:],
                        waits=[tok_max(*[self.t_xm[(q, dc)] for q in range(NQ)])])

    def ph_ln2_m1_xq_pre(self):
        prog = self.prog
        L_S1, L_S2, L_M2R = self.L_S1, self.L_S2, self.L_M2R
        tS = tok_max(*self.lnS_tok)
        t1 = prog.op(V, "tensor_scalar", self.rows[:, L_S1:L_S1 + TOK], self.rows[:, L_S1:L_S1 + TOK],
                     1.0 / D, None, op0=ALU.mult, waits=[tS])
        t2 = prog.op(V, "tensor_scalar", self.rows[:, L_S2:L_S2 + TOK], self.rows[:, L_S2:L_S2 + TOK],
                     1.0 / D, None, op0=ALU.mult, waits=[tS])
        t3 = prog.op(V, "tensor_tensor", self.rows[:, L_M2R:L_M2R + TOK], self.rows[:, L_S1:L_S1 + TOK],
                     self.rows[:, L_S1:L_S1 + TOK], op=ALU.mult, waits=[t1])
        t4 = prog.op(V, "tensor_tensor", self.rows[:, L_S2:L_S2 + TOK], self.rows[:, L_S2:L_S2 + TOK],
                     self.rows[:, L_M2R:L_M2R + TOK], op=ALU.subtract, waits=[t2, t3])
        t5 = prog.op(A_, "activation", self.rows[:, L_S2:L_S2 + TOK], self.rows[:, L_S2:L_S2 + TOK],
                     AF.Ln, bias=self.cbias[0:1, 0:1], waits=[t4, self.t_cbias])
        t6 = prog.op(A_, "activation", self.rows[:, L_S2:L_S2 + TOK], self.rows[:, L_S2:L_S2 + TOK],
                     AF.Exp, scale=-0.5, waits=[t5])
        t7 = prog.op(V, "tensor_tensor", self.rows[:, L_M2R:L_M2R + TOK], self.rows[:, L_S1:L_S1 + TOK],
                     self.rows[:, L_S2:L_S2 + TOK], op=ALU.mult, waits=[t6, t3])
        self.t_ln2rows = tok_max(t6, t7)

    def _bcast_scales(self, key, dst, col0, n=2, extra=()):
        prog = self.prog
        t_d = prog.op(SY, "dma_start", out=self.m_in[key].ap(), in_=self.srow1[0:1, 0:2],
                      waits=[self.t_maxloc])
        t_c = prog.op(G, "collective_compute", "AllReduce", ALU.max, replica_groups=ALL8,
                      ins=[self.m_in[key].ap()], outs=[self.m_out[key].ap()], waits=[t_d])
        t_r = prog.op(SY, "dma_start", out=self.srow1[:, 4:6], in_=self.m_out[key].ap(), waits=[t_c])
        ts_ = prog.op(V, "tensor_scalar", self.srow1[:, 8:8 + n], self.srow1[:, 4:4 + n],
                      1.0 / QMAX, 1e-12, op0=ALU.mult, op1=ALU.max, waits=[t_r])
        tr_ = prog.op(V, "reciprocal", self.srow1[:, 8 + n:8 + 2 * n], self.srow1[:, 8:8 + n], waits=[ts_])
        t_o = prog.op(TE, "matmul", self.ps_o3[:, 0:2 * n], self.ones[0:1, 0:128],
                      self.srow1[0:1, 8:8 + 2 * n],
                      waits=[tr_, self.t_ones, self.ps_o3_use] + list(extra))
        t_o = self.pe_drain(t_o)
        t_e = prog.op(A_, "activation", dst[:, col0:col0 + 2 * n], self.ps_o3[:, 0:2 * n],
                      AF.Identity, waits=[t_o])
        self.ps_o3_use = t_e
        return t_e

    def ph_ln2_m1_xq(self):
        prog = self.prog
        L_S2, L_M2R = self.L_S2, self.L_M2R
        self.ps_o3_use = None
        fence = self.fence_ffn
        ln_use = {}
        o_use = {}
        mx = []
        ps_use = {}
        for q in range(NQ):
            t_or = prog.op(TE, "matmul", self.ps_o1[:], self.ones[0:1, 0:128],
                           self.rows[:, L_S2 + q * QT:L_S2 + (q + 1) * QT],
                           waits=[self.t_ln2rows, o_use.get("o1"), fence])
            t_om = prog.op(TE, "matmul", self.ps_o2[:], self.ones[0:1, 0:128],
                           self.rows[:, L_M2R + q * QT:L_M2R + (q + 1) * QT],
                           waits=[self.t_ln2rows, o_use.get("o2")])
            t_or = t_om = self.pe_drain(t_or, t_om)
            tq = []
            for dc in range(DCN):
                t1 = prog.op(V, "tensor_tensor", self.qf[dc % 2][:], self.x1[dc][:, q * QT:(q + 1) * QT],
                             self.ps_o1[:], op=ALU.mult,
                             waits=[self.t_xm[(q, dc)], t_or, ln_use.get(("qf", dc % 2)), fence])
                t2 = prog.op(V, "tensor_tensor", self.ln2[dc][:, q * QT:(q + 1) * QT], self.qf[dc % 2][:],
                             self.ps_o2[:], op=ALU.subtract, waits=[t1, t_om])
                ln_use[("qf", dc % 2)] = t2
                t3 = prog.op(V, "reduce_max", self.maxc[:, q * DCN + dc:q * DCN + dc + 1],
                             self.ln2[dc][:, q * QT:(q + 1) * QT], axis=AX.X,
                             apply_absolute_value=True, waits=[t2])
                ln_use[(q, dc)] = t2
                tq.append(t3)
                mx.append(t3)
            o_use["o1"] = tok_max(*tq)
            o_use["o2"] = o_use["o1"]
        t_mr = prog.op(V, "reduce_max", self.maxc[:, 32:33], self.maxc[:, 0:32], axis=AX.X,
                       waits=[tok_max(*mx)])
        t_pc = prog.op(G, "tensor_reduce", self.srow1[0:1, 0:1], self.maxc[:, 32:33], axis=AX.C,
                       op=ALU.max, waits=[t_mr])
        t_cp = prog.op(V, "tensor_scalar", self.srow1[:, 1:2], self.srow1[:, 0:1], 1.0, None,
                       op0=ALU.mult, waits=[t_pc])
        self.t_maxloc = tok_max(t_pc, t_cp)
        t_sc = self._bcast_scales("m1", self.scal, 0)
        self.t_xq = {}
        lvl_use = {}
        for q in range(NQ):
            for dc in range(DCN):
                r = dc % 4
                tl = prog.op(V, "tensor_scalar", self.ql[r][:], self.ln2[dc][:, q * QT:(q + 1) * QT],
                             self.scal[:, 2:3], None, op0=ALU.mult,
                             waits=[ln_use[(q, dc)], t_sc, lvl_use.get(r)])
                tb = prog.op(V, "tensor_scalar", self.xqb[dc][:, q * QT:(q + 1) * QT], self.ql[r][:],
                             self.scal[:, 0:1], None, op0=ALU.mult, waits=[tl])
                lvl_use[r] = tb
                self.t_xq[(q, dc)] = tb
        if "ln2" in self.dbg:
            tp = self.tap("ln2", [D, TOK])
            for dc in range(DCN):
                prog.op(SY, "dma_start", out=tp[dc * 128:(dc + 1) * 128, :], in_=self.ln2[dc][:],
                        waits=[tok_max(*[self.t_xq[(q, dc)] for q in range(NQ)])])
        if "xq" in self.dbg:
            tp = self.tap("xq", [D, TOK], BF16)
            for dc in range(DCN):
                prog.op(SY, "dma_start", out=tp[dc * 128:(dc + 1) * 128, :], in_=self.xqb[dc][:],
                        waits=[tok_max(*[self.t_xq[(q, dc)] for q in range(NQ)])])

    def ph_upgate(self):
        prog = self.prog
        sub = self.stop - 7 if self.stop < 9 else 99
        wu_use, wg_use, ps_use, st_use = {}, {}, {}, {}
        mx = []
        xq_all = self.eng_drain(V, *self.t_xq.values())
        for ig in range(ICN // 2):
            sl = ig % 2
            t_wu = prog.op(SY, "dma_start", out=slab_out(self.wus[sl], 2, DCN),
                           in_=slab_ap(self.dr["wqu"], ig * 2, 2, DCN),
                           waits=[wu_use.get(sl), self.t_wq[("u", ig * 2)], self.t_wq[("u", ig * 2 + 1)],
                                  self.fence_q if ig < 2 else None])
            t_wg = prog.op(SY, "dma_start", out=slab_out(self.wgs[sl], 2, DCN),
                           in_=slab_ap(self.dr["wqg"], ig * 2, 2, DCN),
                           waits=[wg_use.get(sl), self.t_wq[("g", ig * 2)], self.t_wq[("g", ig * 2 + 1)],
                                  self.fence_q if ig < 2 else None])
            lastu, lastg = [], []
            if sub < 0.4:
                wu_use[sl] = t_wu
                wg_use[sl] = t_wg
                continue
            for mc in range(2):
                ic = ig * 2 + mc
                for q in range(NQ):
                    mmu = mmg = None
                    for dc in range(DCN):
                        mmu = prog.op(TE, "matmul", self.ps_a[:],
                                      self.wus[sl][:, (mc * DCN + dc) * 128:(mc * DCN + dc + 1) * 128],
                                      self.xqb[dc][:, q * QT:(q + 1) * QT],
                                      start=(dc == 0), stop=(dc == DCN - 1),
                                      waits=[t_wu, xq_all] + ([ps_use.get(0)] if dc == 0 else []))
                    for dc in range(DCN):
                        mmg = prog.op(TE, "matmul", self.ps_b[:],
                                      self.wgs[sl][:, (mc * DCN + dc) * 128:(mc * DCN + dc + 1) * 128],
                                      self.xqb[dc][:, q * QT:(q + 1) * QT],
                                      start=(dc == 0), stop=(dc == DCN - 1),
                                      waits=[t_wg, xq_all] + ([ps_use.get(1)] if dc == 0 else []))
                    if sub < 0.6:
                        ev = prog.op(A_, "activation", self.stg[0][:], self.ps_a[:], AF.Identity,
                                     waits=[mmu, st_use.get("x"), self.fence_q])
                        ev2 = prog.op(A_, "activation", self.stg[1][:], self.ps_b[:], AF.Identity,
                                      waits=[mmg, st_use.get("y"), self.fence_q])
                        st_use["x"], st_use["y"] = ev, ev2
                        ps_use[0], ps_use[1] = ev, ev2
                        lastu.append(mmu)
                        lastg.append(mmg)
                        continue
                    t_dr = self.pe_drain(mmu, mmg)
                    mmu = tok_max(mmu, t_dr)
                    mmg = tok_max(mmg, t_dr)
                    su = self.stg[0]
                    sg = self.stg[1]
                    teu = prog.op(A_, "activation", su[:, q * QT:(q + 1) * QT], self.ps_a[:], AF.Identity,
                                  waits=[mmu, (st_use.get(0) if q == 0 else None), self.fence_q])
                    teg = prog.op(A_, "activation", sg[:, q * QT:(q + 1) * QT], self.ps_b[:], AF.Identity,
                                  waits=[mmg, (st_use.get(1) if q == 0 else None), self.fence_q])
                    cu = 64 + ic * NQ + q
                    cg = 160 + ic * NQ + q
                    tmu = prog.op(V, "reduce_max", self.maxc[:, cu:cu + 1], self.ps_a[:], axis=AX.X,
                                  apply_absolute_value=True, waits=[mmu, teu])
                    tmg = prog.op(V, "reduce_max", self.maxc[:, cg:cg + 1], self.ps_b[:], axis=AX.X,
                                  apply_absolute_value=True, waits=[mmg, teg])
                    ps_use[0] = tok_max(teu, tmu)
                    ps_use[1] = tok_max(teg, tmg)
                    mx += [tmu, tmg]
                    lastu.append(mmu)
                    lastg.append(mmg)
                    if sub < 0.8:
                        continue
                    if q == NQ - 1:
                        du = prog.op(SY, "dma_start", out=self.sp_up[ic * 128:(ic + 1) * 128, :],
                                     in_=su[:], waits=[teu])
                        dg = prog.op(SY, "dma_start", out=self.sp_gate[ic * 128:(ic + 1) * 128, :],
                                     in_=sg[:], waits=[teg])
                        st_use[0] = du
                        st_use[1] = dg
            wu_use[sl] = tok_max(*lastu)
            wg_use[sl] = tok_max(*lastg)
        if sub < 0.6:
            return
        tmx = tok_max(*mx)
        t1 = prog.op(V, "reduce_max", self.maxc[:, 33:34], self.maxc[:, 64:160], axis=AX.X, waits=[tmx])
        t2 = prog.op(V, "reduce_max", self.maxc[:, 34:35], self.maxc[:, 160:256], axis=AX.X, waits=[tmx])
        t3 = prog.op(G, "tensor_reduce", self.srow1[0:1, 0:1], self.maxc[:, 33:34], axis=AX.C,
                     op=ALU.max, waits=[t1])
        t4 = prog.op(G, "tensor_reduce", self.srow1[0:1, 1:2], self.maxc[:, 34:35], axis=AX.C,
                     op=ALU.max, waits=[t2])
        self.t_maxloc = tok_max(t3, t4)
        self.maxc_free = tok_max(t1, t2)
        self.t_scug = self._bcast_scales("mug", self.scal, 0, n=2, extra=[xq_all])
        if "gate" in self.dbg:
            tpg = self.tap("gate", [DI, TOK], BF16)
            tpu = self.tap("up", [DI, TOK], BF16)
            fs = prog.fence()
            prog.op(SY, "dma_start", out=tpg[:, :], in_=self.sp_gate.ap(), waits=[fs])
            prog.op(SY, "dma_start", out=tpu[:, :], in_=self.sp_up.ap(), waits=[fs])

    def _gate_chain(self, ic, q, r, want_t2, waits_extra=()):
        prog = self.prog
        tg = prog.op(SY, "dma_start", out=self.qa[r][:],
                     in_=self.sp_gate[ic * 128:(ic + 1) * 128, q * QT:(q + 1) * QT],
                     waits=[self.qa_use.get(r)] + list(waits_extra))
        tl = prog.op(V, "tensor_scalar", self.ql[r][:], self.qa[r][:], self.scal[:, 3:4], None,
                     op0=ALU.mult, waits=[tg, self.t_scug, self.ql_use.get(r)])
        self.qa_use[r] = tl
        tgq = prog.op(A_, "activation", self.qf[r][:], self.ql[r][:], AF.Identity,
                      scale=self.scal[:, 1:2], waits=[tl, self.qf_use.get(r)])
        tsg = prog.op(A_, "activation", self.qf[2 + r][:], self.ql[r][:], AF.Sigmoid,
                      scale=self.scal[:, 1:2], waits=[tl, self.qf_use.get(2 + r)])
        self.ql_use[r] = tsg
        tsm = prog.op(V, "tensor_scalar", self.qm[0][:], self.qf[2 + r][:], 65536.0, None,
                      op0=ALU.mult, waits=[tsg, self.qm_use.get(0)])
        tsq = prog.op(A_, "activation", self.qf[2 + r][:], self.qm[0][:], AF.Identity,
                      scale=1.0 / 65536.0, waits=[tsm])
        self.qm_use[0] = tsq
        return tgq, tsq

    def _signed_minmax_combine(self, waits=()):
        prog = self.prog
        tM = prog.op(V, "reduce_max", self.maxc[:, 50:51], self.maxc[:, 64:160], axis=AX.X,
                     waits=list(waits))
        tN = prog.op(V, "tensor_reduce", self.maxc[:, 51:52], self.maxc[:, 160:256], axis=AX.X,
                     op=ALU.min, waits=list(waits))
        tC = prog.op(V, "scalar_tensor_tensor", self.maxc[:, 52:53], self.maxc[:, 51:52], -1.0,
                     self.maxc[:, 50:51], op0=ALU.mult, op1=ALU.max, waits=[tM, tN])
        tP = prog.op(G, "tensor_reduce", self.srow1[0:1, 0:1], self.maxc[:, 52:53], axis=AX.C,
                     op=ALU.max, waits=[tC])
        tcp = prog.op(V, "tensor_scalar", self.srow1[:, 1:2], self.srow1[:, 0:1], 1.0, None,
                      op0=ALU.mult, waits=[tP])
        self.t_maxloc = tok_max(tP, tcp)
        self.maxc_free = tok_max(tM, tN)

    def ph_passA(self):
        prog = self.prog
        self.qa_use, self.ql_use, self.qf_use, self.qm_use = {}, {}, {}, {}
        last = []
        for ic in range(ICN):
            for q in range(NQ):
                idx = ic * NQ + q
                r = idx % 2
                tgq, tsq = self._gate_chain(ic, q, r, True)
                tt2 = prog.op(P_, "tensor_tensor", self.qf[4][:], self.qf[r][:], self.qf[2 + r][:],
                              op=ALU.mult, waits=[tgq, tsq, self.qf_use.get(4), self.maxc_free])
                self.qf_use[r] = tt2
                self.qf_use[2 + r] = tt2
                tmx = prog.op(V, "tensor_reduce", self.maxc[:, 64 + idx:65 + idx], self.qf[4][:],
                              axis=AX.X, op=ALU.max, waits=[tt2, self.maxc_free])
                tmn = prog.op(V, "tensor_reduce", self.maxc[:, 160 + idx:161 + idx], self.qf[4][:],
                              axis=AX.X, op=ALU.min, waits=[tt2])
                self.qf_use[4] = tok_max(tmx, tmn)
                last.append(tok_max(tmx, tmn))
        self._signed_minmax_combine(waits=[tok_max(*last)])
        self.t_sc2 = self._bcast_scales("mg2", self.scal, 4, n=2)

    def ph_passB(self):
        prog = self.prog
        lastB = []
        for ic in range(ICN):
            for q in range(NQ):
                r = (ic * NQ + q) % 2
                tgq, tsq = self._gate_chain(ic, q, r, True, waits_extra=[self.t_sc2])
                tt2 = prog.op(P_, "tensor_tensor", self.qf[4][:], self.qf[r][:], self.qf[2 + r][:],
                              op=ALU.mult, waits=[tgq, tsq, self.qf_use.get(4)])
                self.qf_use[r] = tt2
                self.qf_use[2 + r] = tt2
                tl2 = prog.op(V, "tensor_scalar", self.ql[2 + r][:], self.qf[4][:], self.scal[:, 6:7],
                              None, op0=ALU.mult, waits=[tt2, self.t_sc2, self.ql_use.get(2 + r)])
                self.qf_use[4] = tl2
                tg2 = prog.op(A_, "activation", self.qf[4][:], self.ql[2 + r][:], AF.Identity,
                              scale=self.scal[:, 4:5], waits=[tl2])
                self.ql_use[2 + r] = tg2
                tu = prog.op(SY, "dma_start", out=self.qa[2 + r][:],
                             in_=self.sp_up[ic * 128:(ic + 1) * 128, q * QT:(q + 1) * QT],
                             waits=[self.qa_use.get(2 + r)])
                tlu = prog.op(V, "tensor_scalar", self.ql[r][:], self.qa[2 + r][:], self.scal[:, 2:3],
                              None, op0=ALU.mult, waits=[tu, self.ql_use.get(r)])
                self.qa_use[2 + r] = tlu
                tuq = prog.op(A_, "activation", self.qf[5][:], self.ql[r][:], AF.Identity,
                              scale=self.scal[:, 0:1], waits=[tlu, self.qf_use.get(5)])
                self.ql_use[r] = tuq
                idx = ic * NQ + q
                tt3 = prog.op(P_, "tensor_tensor", self.qf[r][:], self.qf[4][:], self.qf[5][:],
                              op=ALU.mult, waits=[tg2, tuq, self.qf_use.get(r), self.maxc_free])
                self.qf_use[4] = tt3
                self.qf_use[5] = tt3
                tmx = prog.op(V, "tensor_reduce", self.maxc[:, 64 + idx:65 + idx], self.qf[r][:],
                              axis=AX.X, op=ALU.max, waits=[tt3, self.maxc_free])
                tmn = prog.op(V, "tensor_reduce", self.maxc[:, 160 + idx:161 + idx], self.qf[r][:],
                              axis=AX.X, op=ALU.min, waits=[tt3])
                self.qf_use[r] = tok_max(tmx, tmn)
                lastB.append(tok_max(tmx, tmn))
        self._signed_minmax_combine(waits=[tok_max(*lastB)])
        self.t_sco = self._bcast_scales("mo", self.scal2, 0, n=1)

    def ph_passC_down(self):
        prog = self.prog
        wd_use = {}
        obf_use = {}
        out_d = []
        ps_use = {}
        oxo_use = {}
        for q in range(NQ):
            ot = []
            for ic in range(ICN):
                r = (ic) % 2
                tgq, tsq = self._gate_chain(ic, q, r, True, waits_extra=[self.t_sco])
                tt2 = prog.op(P_, "tensor_tensor", self.qf[4][:], self.qf[r][:], self.qf[2 + r][:],
                              op=ALU.mult, waits=[tgq, tsq, self.qf_use.get(4)])
                self.qf_use[r] = tt2
                self.qf_use[2 + r] = tt2
                tl2 = prog.op(V, "tensor_scalar", self.ql[2 + r][:], self.qf[4][:], self.scal[:, 6:7],
                              None, op0=ALU.mult, waits=[tt2, self.ql_use.get(2 + r)])
                self.qf_use[4] = tl2
                tg2 = prog.op(A_, "activation", self.qf[4][:], self.ql[2 + r][:], AF.Identity,
                              scale=self.scal[:, 4:5], waits=[tl2])
                self.ql_use[2 + r] = tg2
                tu = prog.op(SY, "dma_start", out=self.qa[2 + r][:],
                             in_=self.sp_up[ic * 128:(ic + 1) * 128, q * QT:(q + 1) * QT],
                             waits=[self.qa_use.get(2 + r)])
                tlu = prog.op(V, "tensor_scalar", self.ql[r][:], self.qa[2 + r][:], self.scal[:, 2:3],
                              None, op0=ALU.mult, waits=[tu, self.ql_use.get(r)])
                self.qa_use[2 + r] = tlu
                tuq = prog.op(A_, "activation", self.qf[5][:], self.ql[r][:], AF.Identity,
                              scale=self.scal[:, 0:1], waits=[tlu, self.qf_use.get(5)])
                self.ql_use[r] = tuq
                tt3 = prog.op(P_, "tensor_tensor", self.qf[r][:], self.qf[4][:], self.qf[5][:],
                              op=ALU.mult, waits=[tg2, tuq, self.qf_use.get(r)])
                self.qf_use[4] = tt3
                self.qf_use[5] = tt3
                tlo = prog.op(V, "tensor_scalar", self.ql[2 + r][:], self.qf[r][:], self.scal2[:, 1:2],
                              None, op0=ALU.mult, waits=[tt3, self.t_sco, self.ql_use.get(2 + r)])
                self.qf_use[r] = tlo
                tob = prog.op(V, "tensor_scalar", self.obf[ic][:], self.ql[2 + r][:], self.scal2[:, 0:1],
                              None, op0=ALU.mult, waits=[tlo, obf_use.get(ic)])
                self.ql_use[2 + r] = tob
                ot.append(tob)
            o_all = self.eng_drain(V, *ot)
            for dc in range(DCN):
                sl = dc % 2
                t_w = prog.op(SY, "dma_start", out=slab_out(self.wds[sl], 1, ICN),
                              in_=slab_ap(self.dr["wqd"], dc, 1, ICN),
                              waits=[wd_use.get(sl), self.t_wq[("d", dc)],
                                     self.fence_q if (q == 0 and dc < 2) else None])
                pb = [self.ps_a, self.ps_b][dc % 2]
                mml = None
                for kc in range(ICN):
                    mml = prog.op(TE, "matmul", pb[:], self.wds[sl][:, kc * 128:(kc + 1) * 128],
                                  self.obf[kc][:], start=(kc == 0), stop=(kc == ICN - 1),
                                  waits=[t_w, o_all] + ([ps_use.get(dc % 2)] if kc == 0 else []))
                wd_use[sl] = mml
                for ic in range(ICN):
                    obf_use[ic] = tok_max(obf_use.get(ic), mml)
                mml = self.pe_drain(mml)
                t_ev = prog.op(A_, "activation", self.oxo[0][:], pb[:], AF.Identity,
                               waits=[mml, oxo_use.get(0)])
                ps_use[dc % 2] = t_ev
                t_fin = prog.op(V, "tensor_tensor", self.qf[dc % 2][:], self.oxo[0][:],
                                self.x1[dc][:, q * QT:(q + 1) * QT], op=ALU.add,
                                waits=[t_ev, self.t_xm[(q, dc)], self.qf_use.get(dc % 2)])
                oxo_use[0] = t_fin
                if "ffnout" in self.dbg:
                    if "ffnout" not in self.taps:
                        self.tap("ffnout", [D, TOK])
                    tfo = prog.op(SY, "dma_start",
                                  out=self.taps["ffnout"][dc * 128:(dc + 1) * 128, q * QT:(q + 1) * QT],
                                  in_=self.oxo[0][:], waits=[t_ev])
                    oxo_use[0] = tok_max(t_fin, tfo)
                self.qf_use[dc % 2] = t_fin
                out_d.append(prog.op(SY, "dma_start", out=self.out[dc * 128:(dc + 1) * 128, q * QT:(q + 1) * QT],
                                     in_=self.qf[dc % 2][:], waits=[t_fin]))
                self.qf_use[dc % 2] = tok_max(self.qf_use[dc % 2], out_d[-1])
        fin = tok_max(*out_d)
        prog.raw(SY, lambda e: None, waits=[fin])


def build_program(dbg=(), stop=99):
    b = Builder(dbg, stop=stop)
    nc = b.build()
    return nc, b


def _host_inputs(inputs, cid):
    b, h = cid // 2, cid % 2
    x = np.asarray(inputs["L0_attn_x"], np.float32)
    xT = np.ascontiguousarray(x[b].T)
    x_scan = np.ascontiguousarray(xT[h * DH:(h + 1) * DH])
    cols = np.concatenate([np.arange((2 * j + h) * OT, (2 * j + h) * OT + OT) for j in range(NW)])
    x_own = np.ascontiguousarray(xT[:, cols])

    def dhalf_lane(a):
        return np.ascontiguousarray(
            np.asarray(a, np.float32)[h * DH:(h + 1) * DH].reshape(SCN, 128, NS).transpose(1, 0, 2).reshape(128, SCN * NS))

    def dhalf_col(v):
        return np.ascontiguousarray(np.asarray(v, np.float32)[h * DH:(h + 1) * DH].reshape(SCN, 128).T)

    def full_col(v, n):
        return np.ascontiguousarray(np.asarray(v, np.float32).reshape(n, 128).T)

    dt16 = np.exp(np.linspace(np.log(0.001), np.log(0.1), NS)).astype(np.float32)
    dtb = np.ascontiguousarray(np.broadcast_to(np.tile(dt16, SCN), (128, SCN * NS)).astype(np.float32))

    return {
        "x_scan": x_scan,
        "x_own": x_own,
        "A_s": dhalf_lane(inputs["L0_attn_A_ssm"]),
        "B_s": dhalf_lane(inputs["L0_attn_B_ssm"]),
        "C_s": dhalf_lane(inputs["L0_attn_C_ssm"]),
        "dtb": dtb,
        "D_s": dhalf_col(inputs["L0_attn_D_ssm"]),
        "s1_s": dhalf_col(inputs["L0_attn_scale1"]),
        "s2c": full_col(inputs["L0_attn_scale2"], DCN),
        "b1c": full_col(inputs["L0_attn_b1"], FC),
        "b2c": full_col(inputs["L0_attn_b2"], DCN),
    }


_WCACHE = {}


def _host_weights(inputs):
    import ml_dtypes
    w1 = np.asarray(inputs["L0_attn_w1"], np.float32)
    w2 = np.asarray(inputs["L0_attn_w2"], np.float32)
    return {
        "wp1": pack_w(w1, DCN, FC),
        "wp2": pack_w(w2, FC, DCN),
        "wun": np.ascontiguousarray(np.asarray(inputs["L0_ffn_w_up"], np.float32)),
        "wgn": np.ascontiguousarray(np.asarray(inputs["L0_ffn_w_gate"], np.float32)),
        "wdn": np.ascontiguousarray(np.asarray(inputs["L0_ffn_w_down"], np.float32)),
        "identb": np.eye(128, dtype=np.float32).astype(ml_dtypes.bfloat16),
    }


def kernel(**inputs):
    nc, _b = build_program()
    wts = _host_weights(inputs)
    in_maps = []
    for cid in range(8):
        m = _host_inputs(inputs, cid)
        m.update(wts)
        in_maps.append(m)
    res = run_bass_kernel_spmd(nc, in_maps, core_ids=list(range(8)))
    outp = np.zeros((B, D, T), np.float32)
    for cid in range(8):
        b, h = cid // 2, cid % 2
        o = np.asarray(res.results[cid]["out"])
        for j in range(NW):
            outp[b][:, (2 * j + h) * OT:(2 * j + h) * OT + OT] = o[:, j * OT:(j + 1) * OT]
    return np.ascontiguousarray(outp.transpose(0, 2, 1))


# version 04
# speedup vs baseline: 1.0561x; 1.0561x over previous
import numpy as np
from contextlib import ExitStack

import concourse.bass as bass
import concourse.mybir as mybir
import bass_rust
from concourse.bass_utils import run_bass_kernel_spmd

F32 = mybir.dt.float32
BF16 = mybir.dt.bfloat16
I16 = mybir.dt.int16
I32 = mybir.dt.int32
U16 = mybir.dt.uint16
AF = mybir.ActivationFunctionType
ALU = mybir.AluOpType
AX = mybir.AxisListType

B, T, D, NS, DFF, DI = 4, 4096, 1024, 16, 4096, 3072
DH, SCN, DCN = 512, 4, 8
FC, ICN = DFF // 128, DI // 128
NW, WT, OT = 8, 512, 256
TOK, QT, NQ = 2048, 512, 4
QMAX = 32767.0
EPS_LN, EPS_RMS = 1e-5, 1e-6
PAIRS = [[0, 1], [2, 3], [4, 5], [6, 7]]
ALL8 = [list(range(8))]
ENGS = ("sync", "scalar", "vector", "gpsimd", "tensor")
V, A_, P_, G, SY, TE = "vector", "scalar", "gpsimd", "gpsimd", "sync", "tensor"


def tok_max(*tokens):
    outd = {}
    for t in tokens:
        if t is None:
            continue
        items = t.items() if isinstance(t, dict) else [t]
        for k, v in items:
            if outd.get(k, -1) < v:
                outd[k] = v
    return outd


class Prog:
    SEMS = ["sv", "ss", "sp", "st", "dy", "dq", "dg", "cc"]

    def __init__(self, nc):
        self.nc = nc
        self.buf = {e: [] for e in ENGS}
        self.cnt = {}
        self.sems = {}
        self.marks = {}

    def _semname(self, eng, meth):
        if meth == "collective_compute":
            return "cc"
        if meth is not None and "dma" in meth:
            return {"sync": "dy", "scalar": "dq", "gpsimd": "dg"}.get(eng, "dy")
        return {"vector": "sv", "scalar": "ss", "gpsimd": "sp", "tensor": "st", "sync": "dy"}[eng]

    def _filter(self, eng, waits):
        need = tok_max(*waits)
        fil = {}
        for k, v in need.items():
            if self.marks.get((eng, k), 0) < v:
                fil[k] = v
                self.marks[(eng, k)] = v
        return fil

    def op(self, eng, meth, *args, waits=(), argfn=None, **kw):
        fil = self._filter(eng, waits)
        semname = self._semname(eng, meth)
        inc = 16 if semname in ("dy", "dq", "dg") else 1
        self.cnt[semname] = self.cnt.get(semname, 0) + inc
        self.buf[eng].append((fil, meth, args, kw, semname, inc, argfn))
        return (semname, self.cnt[semname])

    def raw(self, eng, fn, waits=()):
        fil = self._filter(eng, waits)
        self.buf[eng].append((fil, None, None, None, None, 0, fn))

    def fence(self):
        return dict(self.cnt)

    def flush(self, block):
        for eng in ENGS:
            items = self.buf[eng]
            self.buf[eng] = []
            if not items:
                continue

            def make(items=items):
                def body(e):
                    for fil, meth, args, kw, semname, inc, argfn in items:
                        for k, v in fil.items():
                            e.wait_ge(self.sems[k], v)
                        if meth is None:
                            argfn(e)
                            continue
                        a, k2 = (args, kw) if argfn is None else argfn(e)
                        getattr(e, meth)(*a, **k2).then_inc(self.sems[semname], inc)
                return body
            getattr(block, eng)(make())


def pack_w(w, KC, MC):
    a = np.asarray(w, np.float32).reshape(KC, 128, MC, 128).transpose(2, 0, 1, 3)
    return np.ascontiguousarray(a.reshape(MC * KC * 128, 128))


def slab_ap(dram_ap, mc0, nmc, KC):
    rows = dram_ap[mc0 * KC * 128:(mc0 + nmc) * KC * 128, :]
    return rows.rearrange("(o kc p) f -> p o kc f", o=nmc, kc=KC, p=128)


def slab_out(tile, nmc, KC):
    return tile.ap().rearrange("p (o kc f) -> p o kc f", o=nmc, kc=KC)


class Builder:
    def __init__(self, dbg=(), stop=99):
        self.nc = bass.Bass("TRN2", target_bir_lowering=False)
        self.stop = stop
        self.dbg = list(dbg)
        self.prog = Prog(self.nc)
        self.dr = {}
        self.taps = {}
        self.hcell = {}

    def din(self, name, shape, dt=F32):
        self.dr[name] = self.nc.dram_tensor(name, list(shape), dt, kind="ExternalInput").ap()
        return self.dr[name]

    def dint(self, name, shape, dt=F32):
        self.dr[name] = self.nc.dram_tensor(name, list(shape), dt)
        return self.dr[name]

    def tap(self, name, shape, dt=F32):
        self.taps[name] = self.nc.dram_tensor("tap_" + name, list(shape), dt, kind="ExternalOutput").ap()
        return self.taps[name]

    def build(self):
        nc, prog = self.nc, self.prog
        self.x_scan = self.din("x_scan", [DH, T])
        self.x_own = self.din("x_own", [D, TOK])
        for n, s in (("A_s", [128, SCN * NS]), ("B_s", [128, SCN * NS]), ("C_s", [128, SCN * NS]),
                     ("dtb", [128, SCN * NS]), ("D_s", [128, SCN]), ("s1_s", [128, SCN]),
                     ("s2c", [128, DCN]), ("b1c", [128, FC]), ("b2c", [128, DCN])):
            self.din(n, s)
        for n, s in (("wp1", [FC * DCN * 128, 128]), ("wp2", [DCN * FC * 128, 128])):
            self.din(n, s)
        self.din("wun", [DI, D])
        self.din("wgn", [DI, D])
        self.din("wdn", [D, DI])
        self.dr["identb"] = self.nc.dram_tensor("identb", [128, 128], BF16, kind="ExternalInput").ap()
        self.dint("wqu", [ICN * DCN * 128, 128], BF16)
        self.dint("wqg", [ICN * DCN * 128, 128], BF16)
        self.dint("wqd", [DCN * ICN * 128, 128], BF16)
        self.out = self.nc.dram_tensor("out", [D, TOK], F32, kind="ExternalOutput").ap()
        self.stats_in = self.dint("stats_in", [2, T])
        self.stats_out = self.dint("stats_out", [4, T])
        self.yag_in = [self.dint(f"yag_in{w}", [DH, WT]) for w in range(NW)]
        self.yag_out = [self.dint(f"yag_out{w}", [D, WT]) for w in range(NW)]
        self.m_in = {k: self.dint(f"min_{k}", [1, 2]) for k in ("m1", "mug", "mg2", "mo")}
        self.m_out = {k: self.dint(f"mout_{k}", [1, 2]) for k in ("m1", "mug", "mg2", "mo")}
        self.sp_up = self.dint("sp_up", [DI, TOK], BF16)
        self.sp_gate = self.dint("sp_gate", [DI, TOK], BF16)

        with ExitStack() as ctx:
            self.ctx = ctx
            S = lambda st, n, sh, dt=F32: st.enter_context(nc.sbuf_tensor(n, list(sh), dt))
            self.S = S
            self.ones = S(ctx, "ones", [128, WT])
            self.onesbf = S(ctx, "onesbf", [128, 1], BF16)
            self.dAv = S(ctx, "dAv", [128, SCN * NS])
            self.CdB = S(ctx, "CdB", [128, SCN * NS])
            self.sD = S(ctx, "sD", [128, SCN])
            self.s_s2 = S(ctx, "s_s2", [128, DCN])
            self.s_b1 = S(ctx, "s_b1", [128, FC])
            self.s_b2 = S(ctx, "s_b2", [128, DCN])
            self.states = S(ctx, "states", [128, SCN * NS])
            self.x1 = [S(ctx, f"x1_{dc}", [128, TOK]) for dc in range(DCN)]
            self.rows = S(ctx, "rows", [1, 6144])
            self.maxc = S(ctx, "maxc", [128, 256])
            self.scal = S(ctx, "scal", [128, 8])
            self.scal2 = S(ctx, "scal2", [128, 2])
            self.srow1 = S(ctx, "srow1", [1, 32])
            self.cbias = S(ctx, "cbias", [128, 4])
            self.drt = S(ctx, "drt", [128, 8])
            for name in Prog.SEMS:
                prog.sems[name] = ctx.enter_context(nc.semaphore(name))
            self.block = ctx.enter_context(nc.Block())

            def mk_h(e):
                pid = e.partition_id()
                h = e.compute_val(pid % 2)
                reg = e.alloc_register("hreg")
                e.reg_mov(reg, h)
                self.hcell["hrv"] = bass_rust.RuntimeValue(reg, min_val=0, max_val=1)
            prog.raw(SY, mk_h)

            self.t_ones = prog.op(V, "memset", self.ones[:], 1.0)
            tb0 = prog.op(V, "memset", self.cbias[:, 0:1], EPS_LN)
            tb1 = prog.op(V, "memset", self.cbias[:, 1:2], EPS_LN * EPS_RMS)
            tb2 = prog.op(V, "memset", self.cbias[:, 2:3], EPS_RMS)
            self.t_cbias = tok_max(tb0, tb1, tb2)
            self.t_onesbf = prog.op(V, "memset", self.onesbf[:], 1.0)
            self.t_st0 = prog.op(V, "memset", self.states[:], 0.0)

            self.ph_params()
            if self.stop >= 2:
                self.run_main()
            prog.flush(self.block)
        return nc

    def run_main(self):
        nc, prog = self.nc, self.prog
        ctx = self.ctx
        if True:
            with ExitStack() as rgx:
                self.x1ln = [self.S(rgx, f"x1ln_{dc}", [128, TOK], BF16) for dc in range(DCN)]
                self.cstg = [self.S(rgx, f"cstg{i}", [128, WT], BF16) for i in range(2)]
                with ExitStack() as psc:
                    self.alloc_psum(psc)
                    with ExitStack() as rg:
                        self.alloc_scan(rg)
                        self.ph_stats_u()
                        if self.stop >= 3:
                            self.ph_scan()
                        if self.stop >= 4:
                            self.ph_x1()
                        prog.flush(self.block)
                    if self.stop >= 5:
                        with ExitStack() as rga:
                            self.alloc_attn(rga)
                            self.ph_attn()
                            prog.flush(self.block)
                    if self.stop >= 6:
                        self.ph_ln2_m1_xq_pre()
                    prog.flush(self.block)
            if self.stop < 7:
                return
            self.fence_ffn = prog.fence()
            with ExitStack() as rgf0:
                self.xqb = [self.S(rgf0, f"xq_{dc}", [128, TOK], BF16) for dc in range(DCN)]
                with ExitStack() as psc:
                    self.alloc_psum(psc)
                    with ExitStack() as rgf1:
                        self.ln2 = [self.S(rgf1, f"ln2_{dc}", [128, TOK]) for dc in range(DCN)]
                        self.ql = [self.S(rgf1, f"qlx{i}", [128, QT], I16) for i in range(4)]
                        self.qf = [self.S(rgf1, f"qfx{i}", [128, QT]) for i in range(2)]
                        self.ph_ln2_m1_xq()
                        prog.flush(self.block)
                    self.fence_q = prog.fence()
                    with ExitStack() as rgf2:
                        self.alloc_ffn(rgf2)
                        if self.stop >= 8:
                            self.ph_upgate()
                        if self.stop >= 9:
                            self.ph_passA()
                        if self.stop >= 10:
                            self.ph_passB()
                        if self.stop >= 11:
                            self.ph_passC_down()
                        prog.flush(self.block)

    _psum_gen = [0]

    def alloc_psum(self, psc):
        nc = self.nc
        g = self._psum_gen[0]
        self._psum_gen[0] += 1
        self.ps_a = psc.enter_context(nc.psum_tensor(f"ps_a{g}", [128, QT], F32))
        self.ps_b = psc.enter_context(nc.psum_tensor(f"ps_b{g}", [128, QT], F32))
        self.ps_o1 = psc.enter_context(nc.psum_tensor(f"ps_o1{g}", [128, WT], F32))
        self.ps_o2 = psc.enter_context(nc.psum_tensor(f"ps_o2{g}", [128, WT], F32))
        self.ps_o3 = psc.enter_context(nc.psum_tensor(f"ps_o3{g}", [128, OT], F32))
        self.ps_s1 = psc.enter_context(nc.psum_tensor(f"ps_s1{g}", [1, WT], F32))
        self.ps_s2 = psc.enter_context(nc.psum_tensor(f"ps_s2{g}", [1, WT], F32))
        self.ps_aux = psc.enter_context(nc.psum_tensor(f"ps_aux{g}", [128, 512], F32))
        self.ps_t = self.ps_aux.ap()[:, 0:64].bitcast(BF16)
        self.ps_dmy = self.ps_aux.ap()[:, 128:136]

    def alloc_scan(self, rg):
        S = self.S
        WP = 2 * WT
        self.z_t = [S(rg, f"z{i}", [128, WP]) for i in range(4)]
        self.dab = [S(rg, f"dab{i}", [128, WP]) for i in range(2)]
        self.prb = [S(rg, f"prb{i}", [128, WP]) for i in range(2)]
        self.s_out = [S(rg, f"sout{i}", [128, WP]) for i in range(2)]
        self.y_t = [S(rg, f"yt{i}", [128, WP]) for i in range(2)]
        self.yall = S(rg, "yall", [128, DCN * OT])
        self.xoall = S(rg, "xoall", [128, DCN * OT])
        self.xscr = [S(rg, f"xscr{i}", [128, OT]) for i in range(4)]
        self.gstat = [S(rg, f"gstat{i}", [128, 32]) for i in range(2)]
        self.gd = [S(rg, f"gd{i}", [128, 32]) for i in range(2)]
        self.rgf = S(rg, "rgf", [1, WP])
        self.rmg = S(rg, "rmg", [1, WP])
        self.rrow = S(rg, "rrow", [1, OT])
        self.mrow = S(rg, "mrow", [1, OT])
        self.ostat = [S(rg, f"ostat{i}", [32, 32]) for i in range(2)]
        self.od = [S(rg, f"od{i}", [32, 32]) for i in range(2)]


    def alloc_attn(self, rg):
        S = self.S
        self.dq_w = S(rg, "dq_w", [128, D])
        self.dq_l = S(rg, "dq_l", [128, D], I16)
        self.dq_q = S(rg, "dq_q", [128, D], BF16)
        self.dq_s = S(rg, "dq_s", [128, 3 * 32])
        self.dq_st = S(rg, "dq_st", [128, DCN * 128], BF16)
        self.identb = S(rg, "identb_sb", [128, 128], BF16)
        self.w1s = [S(rg, f"w1s{i}", [128, 2 * DCN * 128], BF16) for i in range(2)]
        self.w2s = [S(rg, f"w2s{i}", [128, FC * 128], BF16) for i in range(2)]
        self.g1bf = S(rg, "g1bf", [128, FC * QT], BF16)
        self.evscr = [S(rg, f"ev{i}", [128, QT]) for i in range(2)]
        self.xmscr = [S(rg, f"xm{i}", [128, QT]) for i in range(2)]
        self.xo2 = [S(rg, f"xo2_{i}", [128, QT]) for i in range(2)]

    def alloc_ffn(self, rg):
        S = self.S
        self.wus = [S(rg, f"wus{i}", [128, 2 * DCN * 128], BF16) for i in range(2)]
        self.wgs = [S(rg, f"wgs{i}", [128, 2 * DCN * 128], BF16) for i in range(2)]
        self.stg = [S(rg, f"stg{i}", [128, TOK], BF16) for i in range(2)]
        self.qa = [S(rg, f"qa{i}", [128, QT], BF16) for i in range(4)]
        self.ql = [S(rg, f"ql{i}", [128, QT], I16) for i in range(4)]
        self.qm = [S(rg, f"qm{i}", [128, QT], U16) for i in range(1)]
        self.qf = [S(rg, f"qf{i}", [128, QT]) for i in range(6)]
        self.obf = [S(rg, f"obf{i}", [128, QT], BF16) for i in range(ICN)]
        self.wds = [S(rg, f"wds{i}", [128, ICN * 128], BF16) for i in range(2)]
        self.oxo = [S(rg, f"oxo{i}", [128, QT]) for i in range(1)]

    R_M, R_VAR, R_RSTD, R_GFAC, R_MG, R_MRSTD, R_TMP = (i * WT for i in range(7))
    R_SR4 = 7 * WT
    L_S1, L_S2, L_M2R = 0, TOK, 2 * TOK

    def row(self, off, n=WT):
        return self.rows[:, off:off + n]

    def eng_drain(self, eng, *toks):
        if eng == V:
            t = self.prog.op(V, "memset", self.drt[:, 0:4], 0.0, waits=list(toks))
        elif eng == A_:
            t = self.prog.op(A_, "activation", self.drt[:, 4:8], self.drt[:, 0:4], AF.Identity,
                             waits=list(toks))
        else:
            t = self.prog.op(P_, "memset", self.drt[:, 0:4], 0.0, waits=list(toks))
        return tok_max(t, *toks)

    def pe_drain(self, *toks):
        t = self.prog.op(TE, "matmul", self.ps_dmy, self.ones[0:1, 0:128],
                         self.ones[0:1, 0:8], waits=list(toks))
        return tok_max(t, *toks)

    def ph_params(self):
        prog, S = self.prog, self.S
        with ExitStack() as rg:
            sA = S(rg, "sA", [128, SCN * NS]); sB = S(rg, "sB", [128, SCN * NS])
            sC = S(rg, "sC", [128, SCN * NS]); sdt = S(rg, "sdt", [128, SCN * NS])
            sp1 = S(rg, "sp1", [128, SCN * NS]); s_s1 = S(rg, "s_s1", [128, SCN])
            dr = self.dr
            tc = tok_max(*[prog.op(SY, "dma_start", out=d[:], in_=dr[n]) for n, d in
                           (("A_s", sA), ("B_s", sB), ("C_s", sC), ("dtb", sdt),
                            ("D_s", self.sD), ("s1_s", s_s1), ("s2c", self.s_s2),
                            ("b1c", self.s_b1), ("b2c", self.s_b2))])
            t_expA = prog.op(A_, "activation", sp1[:], sA[:], AF.Exp, waits=[tc])
            t_r = prog.op(V, "reciprocal", self.CdB[:], sp1[:], waits=[t_expA])
            t_m = prog.op(V, "tensor_tensor", self.dAv[:], sp1[:], sdt[:], op=ALU.mult, waits=[t_expA, tc])
            self.t_dA = prog.op(A_, "activation", self.dAv[:], self.dAv[:], AF.Exp, scale=-1.0, waits=[t_m])
            t_1m = prog.op(V, "tensor_scalar", sp1[:], self.dAv[:], -1.0, 1.0,
                           op0=ALU.mult, op1=ALU.add, waits=[self.t_dA, t_r])
            t_d1 = prog.op(V, "tensor_tensor", self.CdB[:], self.CdB[:], sp1[:], op=ALU.mult, waits=[t_1m])
            t_d2 = prog.op(V, "tensor_tensor", self.CdB[:], self.CdB[:], sB[:], op=ALU.mult, waits=[t_d1])
            t_d3 = prog.op(V, "tensor_tensor", self.CdB[:], self.CdB[:], sC[:], op=ALU.mult, waits=[t_d2])
            tl = []
            for sc in range(SCN):
                tl.append(prog.op(V, "tensor_scalar", self.CdB[:, sc * NS:(sc + 1) * NS],
                                  self.CdB[:, sc * NS:(sc + 1) * NS], s_s1[:, sc:sc + 1], None,
                                  op0=ALU.mult, waits=[t_d3]))
            self.t_cdb = tok_max(*tl)
            self.t_Df = prog.op(V, "tensor_tensor", self.sD[:], self.sD[:], s_s1[:], op=ALU.mult, waits=[tc])
            self.fence_params = prog.fence()
            prog.flush(self.block)

    def ph_stats_u(self):
        prog = self.prog
        row, R_TMP, R_SR4 = self.row, 0, T
        self.t_xs = {}
        self.use_cstg = {}
        use_ps1 = use_ps2 = None
        stat_d = []
        xs_free = {}
        for w in range(NW):
            mm1 = mm2 = None
            for sc in range(SCN):
                xst = self.z_t[sc].ap()[:, (w % 2) * WT:(w % 2) * WT + WT]
                t_l = prog.op(SY, "dma_start", out=xst,
                              in_=self.x_scan[sc * 128:(sc + 1) * 128, w * WT:(w + 1) * WT],
                              waits=[xs_free.get((sc, w % 2)), self.fence_params if w == 0 else None])
                t_c = prog.op(A_, "activation", self.cstg[0][:], xst, AF.Identity,
                              waits=[t_l, self.use_cstg.get(0)])
                t_q = prog.op(A_, "activation", self.cstg[1][:], xst, AF.Square,
                              waits=[t_l, self.use_cstg.get(1)])
                xs_free[(sc, w % 2)] = tok_max(t_c, t_q)
                mm1 = prog.op(TE, "matmul", self.ps_s1[:], self.onesbf[:], self.cstg[0][:],
                              start=(sc == 0), stop=(sc == SCN - 1),
                              waits=[t_c, self.t_onesbf] + ([use_ps1] if sc == 0 else []))
                mm2 = prog.op(TE, "matmul", self.ps_s2[:], self.onesbf[:], self.cstg[1][:],
                              start=(sc == 0), stop=(sc == SCN - 1),
                              waits=[t_q, self.t_onesbf] + ([use_ps2] if sc == 0 else []))
                self.use_cstg[0] = mm1
                self.use_cstg[1] = mm2
            t_mm = self.pe_drain(mm1, mm2)
            t_e1 = prog.op(A_, "activation", row(R_TMP), self.ps_s1[:], AF.Identity,
                           waits=[t_mm] + ([stat_d[-2]] if stat_d else []))
            t_e2 = prog.op(A_, "activation", row(R_SR4), self.ps_s2[:], AF.Identity,
                           waits=[t_mm] + ([stat_d[-1]] if stat_d else []))
            use_ps1, use_ps2 = t_e1, t_e2
            stat_d.append(prog.op(SY, "dma_start", out=self.stats_in[0:1, w * WT:(w + 1) * WT],
                                  in_=row(R_TMP), waits=[t_e1]))
            stat_d.append(prog.op(SY, "dma_start", out=self.stats_in[1:2, w * WT:(w + 1) * WT],
                                  in_=row(R_SR4), waits=[t_e2]))
        self.xs_free = xs_free
        self.use_ps1, self.use_ps2 = use_ps1, use_ps2
        self.t_sag = prog.op(G, "collective_compute", "AllGather", ALU.bypass,
                             replica_groups=PAIRS, ins=[self.stats_in.ap()], outs=[self.stats_out.ap()],
                             waits=[tok_max(*stat_d)])
        self.rows_use = tok_max(*stat_d)

    def ph_scan(self):
        prog = self.prog
        WP = 2 * WT
        self.ywtok = [None] * NW
        stuse = {}
        dause, pruse, sduse = {}, {}, {}
        zuse = {}
        yfree = {}
        gstat_use, gd_use, rgf_use = {}, {}, {}
        t_gload, t_wb = {}, {}
        combo = 0

        def gload(wp):
            sl = wp % 2
            rr = []
            for r in range(4):
                rr.append(prog.op(SY, "dma_start", out=self.gstat[sl][:, r * 8:(r + 1) * 8],
                                  in_=self.stats_out[r:r + 1, wp * WP:(wp + 1) * WP],
                                  waits=[self.t_sag, gstat_use.get(sl)]))
            t_gload[wp] = tok_max(*rr)

        def derive(wp, extra=()):
            sl = wp % 2
            g, d = self.gstat[sl], self.gd[sl]
            tS1 = prog.op(V, "tensor_tensor", d[:, 0:8], g[:, 0:8], g[:, 16:24], op=ALU.add,
                          waits=[t_gload[wp], gd_use.get(sl)] + list(extra))
            tS2 = prog.op(V, "tensor_tensor", d[:, 8:16], g[:, 8:16], g[:, 24:32], op=ALU.add,
                          waits=[t_gload[wp], gd_use.get(sl)])
            gstat_use[sl] = tok_max(tS1, tS2)
            tm = prog.op(V, "tensor_scalar", d[:, 0:8], d[:, 0:8], 1.0 / D, None, op0=ALU.mult,
                         waits=[tS1])
            tex = prog.op(V, "tensor_scalar", d[:, 8:16], d[:, 8:16], 1.0 / D, None, op0=ALU.mult,
                          waits=[tS2])
            tm2 = prog.op(V, "tensor_tensor", d[:, 16:24], d[:, 0:8], d[:, 0:8], op=ALU.mult,
                          waits=[tm, tS2])
            tvar = prog.op(V, "tensor_tensor", d[:, 8:16], d[:, 8:16], d[:, 16:24], op=ALU.subtract,
                           waits=[tex, tm2])
            tl2 = prog.op(A_, "activation", d[:, 24:32], d[:, 8:16], AF.Ln, scale=1.0 + EPS_RMS,
                          bias=self.cbias[:, 1:2], waits=[tvar, self.t_cbias])
            tgf = prog.op(A_, "activation", d[:, 24:32], d[:, 24:32], AF.Exp, scale=-0.5, waits=[tl2])
            tmg = prog.op(V, "tensor_tensor", d[:, 8:16], d[:, 0:8], d[:, 24:32], op=ALU.mult,
                          waits=[tgf, tm])
            t_gfr = prog.op(SY, "dma_start", out=self.rgf[0:1, :], in_=d[:, 24:32],
                            waits=[tgf, rgf_use.get("gf")])
            t_mgr = prog.op(SY, "dma_start", out=self.rmg[0:1, :], in_=d[:, 8:16],
                            waits=[tmg, rgf_use.get("mg")])
            gd_use[sl] = tok_max(t_gfr, t_mgr)
            t_wb[wp] = (t_gfr, t_mgr)

        def xload(wp, init=None):
            t_lx = {}
            for sc in range(SCN):
                t_lx[sc] = prog.op(SY if sc % 2 == 0 else G, "dma_start", out=self.z_t[sc][:],
                                   in_=self.x_scan[sc * 128:(sc + 1) * 128, wp * WP:(wp + 1) * WP],
                                   waits=[init[sc] if init is not None else zuse.get(("z", sc))])
            return t_lx

        gload(0)
        init0 = {sc: tok_max(self.xs_free.get((sc, 0)), self.xs_free.get((sc, 1)))
                 for sc in range(SCN)}
        t_lx = xload(0, init=init0)
        t_lx_next = {}
        derive(0, extra=list(t_lx.values()))

        for wp in range(NW // 2):
            if wp > 0:
                t_lx = t_lx_next
                t_lx_next = {}
            t_gfr, t_mgr = t_wb[wp]
            z_parts = {}
            for h in range(2):
                t_og = prog.op(TE, "matmul", self.ps_o1[:], self.ones[0:1, 0:128],
                               self.rgf[0:1, h * WT:(h + 1) * WT],
                               waits=[t_gfr, self.t_ones, zuse.get("o1")])
                t_omg = prog.op(TE, "matmul", self.ps_o2[:], self.ones[0:1, 0:128],
                                self.rmg[0:1, h * WT:(h + 1) * WT],
                                waits=[t_mgr, self.t_ones, zuse.get("o2")])
                t_og = t_omg = self.pe_drain(t_og, t_omg)
                rgf_use["gf"] = t_og
                rgf_use["mg"] = t_og
                tcons = []
                for sc in range(SCN):
                    zh = self.z_t[sc].ap()[:, h * WT:(h + 1) * WT]
                    t1 = prog.op(V, "tensor_tensor", zh, zh, self.ps_o1[:],
                                 op=ALU.mult, waits=[t_lx[sc], t_og])
                    t2 = prog.op(V, "tensor_tensor", zh, zh, self.ps_o2[:], op=ALU.subtract,
                                 waits=[t1])
                    tcons.append(t2)
                    z_parts[(sc, h)] = t2
                zuse["o1"] = tok_max(*tcons)
                zuse["o2"] = zuse["o1"]
            tz_w = {sc: tok_max(z_parts[(sc, 0)], z_parts[(sc, 1)]) for sc in range(SCN)}
            if "z" in self.dbg:
                if "z" not in self.taps:
                    self.tap("z", [DH, T])
                for sc in range(SCN):
                    tz_d = prog.op(SY, "dma_start",
                                   out=self.taps["z"][sc * 128:(sc + 1) * 128, wp * WP:(wp + 1) * WP],
                                   in_=self.z_t[sc][:], waits=[tz_w[sc]])
                    tz_w[sc] = tok_max(tz_w[sc], tz_d)

            if wp + 1 < NW // 2:
                gload(wp + 1)
                derive(wp + 1)

            ybounce = []
            for sc in range(SCN):
                yt = self.y_t[sc % 2]
                acc_prev = None
                prb_toks = []
                for n in range(NS):
                    ci = sc * NS + n
                    k = combo % 2
                    t_pr = prog.op(A_, "activation", self.prb[k][:], self.z_t[sc][:], AF.Identity,
                                   scale=self.CdB[:, ci:ci + 1],
                                   waits=[tz_w[sc], self.t_cdb, pruse.get(k)])
                    prb_toks.append(t_pr)
                    t_da = prog.op(A_, "activation", self.dab[k][:], self.prb[k][:], AF.Identity,
                                   scale=0.0, bias=self.dAv[:, ci:ci + 1],
                                   waits=[t_pr, self.t_dA, dause.get(k)])
                    init = 0.0 if wp == 0 else self.states[:, ci:ci + 1]
                    t_sc = prog.op(V, "tensor_tensor_scan", self.s_out[k][:], self.dab[k][:],
                                   self.prb[k][:], init, op0=ALU.mult, op1=ALU.add,
                                   waits=[t_da, t_pr, sduse.get(k), self.t_st0] +
                                         ([stuse[ci]] if ci in stuse else []))
                    t_cp = prog.op(V, "tensor_scalar", self.states[:, ci:ci + 1],
                                   self.s_out[k][:, WP - 1:WP], 1.0, None, op0=ALU.mult,
                                   waits=[t_sc])
                    stuse[ci] = t_cp
                    dause[k] = t_sc
                    pruse[k] = t_sc
                    if n == 0:
                        t_ac = prog.op(V, "scalar_tensor_tensor", yt[:], self.z_t[sc][:],
                                       self.sD[:, sc:sc + 1], self.s_out[k][:],
                                       op0=ALU.mult, op1=ALU.add,
                                       waits=[t_sc, tz_w[sc], self.t_Df, yfree.get(sc % 2)])
                    else:
                        t_ac = prog.op(V, "tensor_tensor", yt[:], yt[:], self.s_out[k][:],
                                       op=ALU.add, waits=[t_sc, acc_prev])
                    sduse[k] = tok_max(t_ac, t_cp)
                    acc_prev = t_ac
                    combo += 1
                zuse[("z", sc)] = tok_max(acc_prev, tz_w[sc], *prb_toks)
                ybt = []
                for h in range(2):
                    w = 2 * wp + h
                    t_yd = prog.op(SY, "dma_start",
                                   out=self.yag_in[w][sc * 128:(sc + 1) * 128, :],
                                   in_=yt[:, h * WT:(h + 1) * WT], waits=[acc_prev])
                    ybt.append(t_yd)
                    ybounce.append((w, t_yd))
                yfree[sc % 2] = tok_max(*ybt)
                if wp + 1 < NW // 2:
                    t_lx_next[sc] = prog.op(
                        SY if sc % 2 == 0 else G, "dma_start", out=self.z_t[sc][:],
                        in_=self.x_scan[sc * 128:(sc + 1) * 128, (wp + 1) * WP:(wp + 2) * WP],
                        waits=[zuse.get(("z", sc))])
            for h in range(2):
                w = 2 * wp + h
                toks = [t for (ww, t) in ybounce if ww == w]
                self.ywtok[w] = prog.op(G, "collective_compute", "AllGather", ALU.bypass,
                                        replica_groups=PAIRS, ins=[self.yag_in[w].ap()],
                                        outs=[self.yag_out[w].ap()], waits=[tok_max(*toks)])
                if "yag" in self.dbg:
                    if "yag" not in self.taps:
                        self.tap("yag", [D, T])
                    prog.op(SY, "dma_start", out=self.taps["yag"][:, w * WT:(w + 1) * WT],
                            in_=self.yag_out[w].ap(), waits=[self.ywtok[w]])
        self.zuse_final = dict(zuse)

    def ph_x1(self):
        prog = self.prog
        row = self.row
        R_TMP = self.R_TMP
        self.t_x1 = {}
        self.t_x1ln = {}
        yx_use = {}
        ost_use, od_use, rr_use = {}, {}, {}
        rowtmp_use = {}
        t_trd = {}
        o_use = {"o1": self.zuse_final.get("o1"), "o2": self.zuse_final.get("o2")}

        def oload(w):
            sl = w % 2
            base = T + 1024 * sl
            def mk_r(w=w, base=base):
                def fn(e):
                    return ((), dict(
                        out=self.rows[:, base:base + 4 * OT].rearrange("p (r c) -> p r c", r=4),
                        in_=self.stats_out.rearrange("(a r) c -> a r c", a=1)[:, :, bass.ts(self.hcell["hrv"] + 2 * w, OT)]))
                return fn
            t_row = prog.op(SY, "dma_start", argfn=mk_r(),
                            waits=[self.t_sag, rowtmp_use.get(sl, self.rows_use)])
            rr = []
            for r in range(4):
                rr.append(prog.op(SY, "dma_start", out=self.ostat[sl][0:32, r * 8:(r + 1) * 8],
                                  in_=self.rows[0:1, base + r * OT:base + (r + 1) * OT],
                                  waits=[t_row, ost_use.get(sl)]))
            t_trd[w] = tok_max(*rr)
            rowtmp_use[sl] = t_trd[w]

        oload(0)
        for w in range(NW):
            sl = w % 2
            def mk_yread(w=w):
                def fn(e):
                    src_ap = self.yag_out[w].rearrange("(dc p) t -> p dc t", p=128)
                    return ((), dict(out=self.yall.ap().rearrange("p (dc t) -> p dc t", dc=DCN),
                                     in_=src_ap[:, :, bass.ts(self.hcell["hrv"], OT)]))
                return fn
            t_yall = prog.op(SY, "dma_start", argfn=mk_yread(),
                             waits=[self.ywtok[w], yx_use.get("yall")])
            t_xoall = prog.op(SY, "dma_start",
                              out=self.xoall.ap().rearrange("p (dc t) -> p dc t", dc=DCN),
                              in_=self.x_own.rearrange("(dc p) t -> p dc t", p=128)[:, :, w * OT:(w + 1) * OT],
                              waits=[yx_use.get("xoall")])
            t_y = {dc: t_yall for dc in range(DCN)}
            t_xo = {dc: t_xoall for dc in range(DCN)}
            ost, od = self.ostat[sl], self.od[sl]
            trd = t_trd[w]
            if w + 1 < NW:
                oload(w + 1)
            tS1 = prog.op(V, "tensor_tensor", od[:, 0:8], ost[:, 0:8], ost[:, 16:24],
                          op=ALU.add, waits=[trd, od_use.get(sl)] +
                          ([t_xoall] if w == 0 else []))
            tS2 = prog.op(V, "tensor_tensor", od[:, 8:16], ost[:, 8:16], ost[:, 24:32],
                          op=ALU.add, waits=[trd, od_use.get(sl)])
            ost_use[sl] = tok_max(tS1, tS2)
            tm = prog.op(V, "tensor_scalar", od[:, 0:8], od[:, 0:8], 1.0 / D, None,
                         op0=ALU.mult, waits=[tS1])
            tex = prog.op(V, "tensor_scalar", od[:, 8:16], od[:, 8:16], 1.0 / D, None,
                          op0=ALU.mult, waits=[tS2])
            tm2 = prog.op(V, "tensor_tensor", od[:, 16:24], od[:, 0:8], od[:, 0:8],
                          op=ALU.mult, waits=[tm, tS2])
            tvar = prog.op(V, "tensor_tensor", od[:, 8:16], od[:, 8:16], od[:, 16:24],
                           op=ALU.subtract, waits=[tex, tm2])
            tl1 = prog.op(A_, "activation", od[:, 24:32], od[:, 8:16], AF.Ln,
                          bias=self.cbias[0:32, 0:1], waits=[tvar, self.t_cbias])
            trstd = prog.op(A_, "activation", od[:, 24:32], od[:, 24:32], AF.Exp, scale=-0.5,
                            waits=[tl1])
            tmr = prog.op(V, "tensor_tensor", od[:, 8:16], od[:, 0:8], od[:, 24:32],
                          op=ALU.mult, waits=[trstd, tm])
            rrow = self.rows[0:1, T + 4 * OT:T + 5 * OT]
            mrow = self.rows[0:1, T + 5 * OT:T + 6 * OT]
            t_rr = prog.op(SY, "dma_start", out=rrow, in_=od[:, 24:32],
                           waits=[trstd, rr_use.get("r")])
            t_mr = prog.op(SY, "dma_start", out=mrow, in_=od[:, 8:16],
                           waits=[tmr, rr_use.get("m")])
            od_use[sl] = tok_max(t_rr, t_mr)
            t_or = prog.op(TE, "matmul", self.ps_o1[:, 0:OT], self.ones[0:1, 0:128],
                           rrow, waits=[t_rr, o_use.get("o1")])
            t_omr = prog.op(TE, "matmul", self.ps_o2[:, 0:OT], self.ones[0:1, 0:128],
                            mrow, waits=[t_mr, o_use.get("o2")])
            t_or = t_omr = self.pe_drain(t_or, t_omr)
            rr_use["r"] = t_or
            rr_use["m"] = t_or
            sq_mm = None
            p1l = []
            for dc in range(DCN):
                xr = self.xscr[dc % 2]
                yr = self.xscr[2 + dc % 2]
                p1 = prog.op(V, "tensor_tensor", xr[:], self.xoall[:, dc * OT:(dc + 1) * OT],
                             self.ps_o1[:, 0:OT],
                             op=ALU.mult, waits=[t_xo[dc], t_or, yx_use.get(("s", dc % 2))])
                p2 = prog.op(V, "tensor_tensor", yr[:], self.yall[:, dc * OT:(dc + 1) * OT],
                             self.ps_o2[:, 0:OT],
                             op=ALU.subtract, waits=[t_y[dc], t_omr, yx_use.get(("s2", dc % 2))])
                tx1 = prog.op(P_, "tensor_tensor", self.x1[dc][:, w * OT:(w + 1) * OT], xr[:], yr[:],
                              op=ALU.add, waits=[p1, p2])
                self.t_x1[(w, dc)] = tx1
                yx_use["yall"] = tok_max(yx_use.get("yall"), p2)
                yx_use["xoall"] = tok_max(yx_use.get("xoall"), p1)
                yx_use[("s", dc % 2)] = tx1
                yx_use[("s2", dc % 2)] = tx1
                p1l += [p1, p2]
                t_sq = prog.op(A_, "activation", self.cstg[dc % 2][:, 0:OT],
                               self.x1[dc][:, w * OT:(w + 1) * OT], AF.Square,
                               waits=[tx1, self.use_cstg.get(dc % 2)])
                sq_mm = prog.op(TE, "matmul", self.ps_s2[:, 0:OT], self.onesbf[:],
                                self.cstg[dc % 2][:, 0:OT],
                                start=(dc == 0), stop=(dc == DCN - 1),
                                waits=[t_sq] + ([self.use_ps2] if dc == 0 else []))
                self.use_cstg[dc % 2] = sq_mm
            o_use["o1"] = tok_max(*p1l)
            o_use["o2"] = o_use["o1"]
            sq_mm = self.pe_drain(sq_mm)
            t_es = prog.op(A_, "activation", row(R_TMP, OT), self.ps_s2[:, 0:OT], AF.Identity,
                           waits=[sq_mm, self.rows_use if w == 0 else None])
            self.use_ps2 = t_es
            tlf = prog.op(A_, "activation", row(R_TMP, OT), row(R_TMP, OT), AF.Ln, scale=1.0 / D,
                          bias=self.cbias[0:1, 2:3], waits=[t_es, self.t_cbias])
            tfac = prog.op(A_, "activation", row(R_TMP, OT), row(R_TMP, OT), AF.Exp, scale=-0.5, waits=[tlf])
            t_of = prog.op(TE, "matmul", self.ps_o3[:, 0:OT], self.ones[0:1, 0:128], row(R_TMP, OT),
                           waits=[tfac, o_use.get("o3")])
            t_of = self.pe_drain(t_of)
            lnt = []
            for dc in range(DCN):
                t_ln = prog.op(V, "scalar_tensor_tensor", self.x1ln[dc][:, w * OT:(w + 1) * OT],
                               self.x1[dc][:, w * OT:(w + 1) * OT], self.s_s2[:, dc:dc + 1],
                               self.ps_o3[:, 0:OT], op0=ALU.mult, op1=ALU.mult,
                               waits=[self.t_x1[(w, dc)], t_of])
                self.t_x1ln[(w, dc)] = t_ln
                lnt.append(t_ln)
            o_use["o3"] = tok_max(*lnt)
        if "x1" in self.dbg:
            tp = self.tap("x1", [D, TOK])
            for dc in range(DCN):
                prog.op(SY, "dma_start", out=tp[dc * 128:(dc + 1) * 128, :], in_=self.x1[dc][:],
                        waits=[tok_max(*[self.t_x1[(w, dc)] for w in range(NW)])])
        if "x1ln" in self.dbg:
            tp = self.tap("x1ln", [D, TOK], BF16)
            for dc in range(DCN):
                prog.op(SY, "dma_start", out=tp[dc * 128:(dc + 1) * 128, :], in_=self.x1ln[dc][:],
                        waits=[tok_max(*[self.t_x1ln[(w, dc)] for w in range(NW)])])

    def dequant_chunks(self):
        units = []
        for key, nat, outp, KCtot, MC, I in (
                ("u", self.dr["wun"], self.dr["wqu"], DCN, ICN, D),
                ("g", self.dr["wgn"], self.dr["wqg"], DCN, ICN, D),
                ("d", self.dr["wdn"], self.dr["wqd"], ICN, DCN, DI)):
            nsub = I // D
            for oc in range(MC):
                for s in range(nsub):
                    units.append((key, nat, outp, KCtot, s * DCN, oc, s == nsub - 1))
        return units

    def emit_dequant(self, unit):
        prog = self.prog
        key, nat, outp, KCtot, kc0, oc, last = unit
        nb = D // 32
        st = self.dq_state
        t_l = prog.op(SY, "dma_start", out=self.dq_w[:],
                      in_=nat[oc * 128:(oc + 1) * 128, kc0 * 128:kc0 * 128 + D],
                      waits=[st.get("w")])
        w3 = self.dq_w.ap().rearrange("p (b k) -> p b k", k=32)
        t_bm = prog.op(V, "reduce_max", self.dq_s[:, 0:nb], w3, axis=AX.X,
                       apply_absolute_value=True, waits=[t_l, st.get("s")])
        t_s = prog.op(V, "tensor_scalar", self.dq_s[:, 32:32 + nb], self.dq_s[:, 0:nb],
                      1.0 / 7.0, 1e-12, op0=ALU.mult, op1=ALU.max, waits=[t_bm, st.get("s2")])
        t_r = prog.op(V, "reciprocal", self.dq_s[:, 64:64 + nb], self.dq_s[:, 32:32 + nb],
                      waits=[t_s, st.get("r")])
        rb = self.dq_s.ap()[:, 64:64 + nb].unsqueeze(2).broadcast_to([128, nb, 32])
        sb_ = self.dq_s.ap()[:, 32:32 + nb].unsqueeze(2).broadcast_to([128, nb, 32])
        l3 = self.dq_l.ap().rearrange("p (b k) -> p b k", k=32)
        q3 = self.dq_q.ap().rearrange("p (b k) -> p b k", k=32)
        t_lv = prog.op(V, "tensor_tensor", l3, w3, rb, op=ALU.mult, waits=[t_r, st.get("l")])
        t_q = prog.op(P_, "tensor_tensor", q3, l3, sb_, op=ALU.mult, waits=[t_lv, st.get("q")])
        st["w"] = t_lv
        st["s"] = t_lv
        st["s2"] = tok_max(t_lv, t_q)
        st["r"] = t_lv
        st["l"] = t_q
        t_q = self.eng_drain(P_, t_q)
        evs = []
        for kc in range(DCN):
            t_t = prog.op(TE, "matmul", self.ps_t, self.dq_q[:, kc * 128:(kc + 1) * 128],
                          self.identb[:], is_transpose=True,
                          waits=[t_q, self.t_identb, st.get(("pt",))])
            t_t = self.pe_drain(t_t)
            t_e = prog.op(A_, "activation", self.dq_st[:, kc * 128:(kc + 1) * 128], self.ps_t,
                          AF.Identity, waits=[t_t, st.get(("st", kc))])
            st[("pt",)] = t_e
            evs.append(t_e)
            st[("st", kc)] = t_e
        rows0 = (oc * KCtot + kc0) * 128
        t_o = prog.op(SY, "dma_start",
                      out=outp[rows0:rows0 + DCN * 128, :].rearrange("(kc p) f -> p kc f", p=128),
                      in_=self.dq_st.ap().rearrange("p (kc f) -> p kc f", kc=DCN),
                      waits=[tok_max(*evs)])
        st["q"] = t_o
        for kc in range(DCN):
            st[("st", kc)] = tok_max(st[("st", kc)], t_o)
        if last:
            self.t_wq[(key, oc)] = t_o
        else:
            self.t_wq[(key, oc)] = tok_max(self.t_wq.get((key, oc)), t_o)

    def ph_attn(self):
        prog = self.prog
        row = self.row
        fence_a = prog.fence()
        self.dq_state = {}
        self.t_wq = {}
        self.t_identb = prog.op(SY, "dma_start", out=self.identb[:], in_=self.dr["identb"],
                                waits=[fence_a])
        dq_units = self.dequant_chunks()
        dq_pos = 0
        w1_use, w2_use, g1_use, ps_use, ev_use, xm_use, xo_use = {}, {}, {}, {}, {}, {}, {}
        self.t_xm = {}
        use_ps1l = self.use_ps1
        use_ps2l = self.use_ps2
        self.lnS_tok = []
        for q in range(NQ):
            x1ln_tok = self.eng_drain(V, *[self.t_x1ln[(w, dc)] for dc in range(DCN) for w in (2 * q, 2 * q + 1)])
            for fg in range(FC // 2):
                sl = fg % 2
                t_w = prog.op(G, "dma_start", out=slab_out(self.w1s[sl], 2, DCN), in_=slab_ap(self.dr["wp1"], fg * 2, 2, DCN),
                              waits=[w1_use.get(sl)] + ([fence_a] if (q == 0 and fg < 2) else []))
                gel = []
                for mc in range(2):
                    ffc = fg * 2 + mc
                    pb = [self.ps_a, self.ps_b][ffc % 2]
                    mml = None
                    for dc in range(DCN):
                        mml = prog.op(TE, "matmul", pb[:],
                                      self.w1s[sl][:, (mc * DCN + dc) * 128:(mc * DCN + dc + 1) * 128],
                                      self.x1ln[dc][:, q * QT:(q + 1) * QT],
                                      start=(dc == 0), stop=(dc == DCN - 1),
                                      waits=[t_w, x1ln_tok] + ([ps_use.get(ffc % 2)] if dc == 0 else []))
                    mml = self.pe_drain(mml)
                    t_g = prog.op(A_, "activation", self.g1bf[:, ffc * QT:(ffc + 1) * QT], pb[:],
                                  AF.Gelu, bias=self.s_b1[:, ffc:ffc + 1],
                                  waits=[mml, g1_use.get(ffc)])
                    ps_use[ffc % 2] = t_g
                    g1_use[ffc] = t_g
                    gel.append(t_g)
                w1_use[sl] = tok_max(*gel)
            g1_all = self.eng_drain(A_, *[g1_use[f] for f in range(FC)])
            mmA = mmB = None
            for dc in range(DCN):
                sl = dc % 2
                t_w = prog.op(G, "dma_start", out=slab_out(self.w2s[sl], 1, FC), in_=slab_ap(self.dr["wp2"], dc, 1, FC),
                              waits=[w2_use.get(sl)] + ([fence_a] if (q == 0 and dc < 2) else []))
                pb = [self.ps_a, self.ps_b][dc % 2]
                mml = None
                for kc in range(FC):
                    mml = prog.op(TE, "matmul", pb[:], self.w2s[sl][:, kc * 128:(kc + 1) * 128],
                                  self.g1bf[:, kc * QT:(kc + 1) * QT],
                                  start=(kc == 0), stop=(kc == FC - 1),
                                  waits=[t_w, g1_all] + ([ps_use.get(dc % 2)] if kc == 0 else []))
                w2_use[sl] = mml
                mml = self.pe_drain(mml)
                t_ev = prog.op(A_, "activation", self.evscr[sl][:], pb[:], AF.Identity,
                               bias=self.s_b2[:, dc:dc + 1], waits=[mml, ev_use.get(sl)])
                ps_use[dc % 2] = t_ev
                t_xo = prog.op(SY, "dma_start", out=self.xo2[sl][:],
                               in_=self.x_own[dc * 128:(dc + 1) * 128, q * QT:(q + 1) * QT],
                               waits=[xo_use.get(sl)] + ([fence_a] if (q == 0 and dc < 2) else []))
                x1tok = tok_max(self.t_x1ln[(2 * q, dc)], self.t_x1ln[(2 * q + 1, dc)])
                t_p = prog.op(P_, "tensor_tensor", self.xmscr[sl][:], self.evscr[sl][:],
                              self.x1[dc][:, q * QT:(q + 1) * QT], op=ALU.add,
                              waits=[t_ev, x1tok, xm_use.get(sl)])
                txm = prog.op(V, "tensor_tensor", self.x1[dc][:, q * QT:(q + 1) * QT],
                              self.xmscr[sl][:], self.xo2[sl][:], op=ALU.add,
                              waits=[t_p, t_xo])
                self.t_xm[(q, dc)] = txm
                ev_use[sl] = t_p
                xm_use[sl] = txm
                xo_use[sl] = txm
                t_cb = prog.op(A_, "activation", self.cstg[dc % 2][:], self.x1[dc][:, q * QT:(q + 1) * QT],
                               AF.Identity, waits=[txm, self.use_cstg.get(dc % 2)])
                mmA = prog.op(TE, "matmul", self.ps_s1[:], self.onesbf[:], self.cstg[dc % 2][:],
                              start=(dc == 0), stop=(dc == DCN - 1),
                              waits=[t_cb] + ([use_ps1l] if dc == 0 else []))
                t_sq = prog.op(A_, "activation", self.cstg[dc % 2][:], self.x1[dc][:, q * QT:(q + 1) * QT],
                               AF.Square, waits=[txm, mmA])
                mmB = prog.op(TE, "matmul", self.ps_s2[:], self.onesbf[:], self.cstg[dc % 2][:],
                              start=(dc == 0), stop=(dc == DCN - 1),
                              waits=[t_sq] + ([use_ps2l] if dc == 0 else []))
                self.use_cstg[dc % 2] = mmB
            t_mm = self.pe_drain(mmA, mmB)
            tE1 = prog.op(A_, "activation", self.rows[:, self.L_S1 + q * QT:self.L_S1 + (q + 1) * QT],
                          self.ps_s1[:], AF.Identity, waits=[t_mm, self.rows_use])
            tE2 = prog.op(A_, "activation", self.rows[:, self.L_S2 + q * QT:self.L_S2 + (q + 1) * QT],
                          self.ps_s2[:], AF.Identity, waits=[t_mm])
            use_ps1l, use_ps2l = tE1, tE2
            self.lnS_tok += [tE1, tE2]
            n_do = 18 if q < NQ - 1 else len(dq_units) - dq_pos
            for _ in range(n_do):
                if dq_pos < len(dq_units):
                    self.emit_dequant(dq_units[dq_pos])
                    dq_pos += 1
        if "xmid" in self.dbg:
            tp = self.tap("xmid", [D, TOK])
            for dc in range(DCN):
                prog.op(SY, "dma_start", out=tp[dc * 128:(dc + 1) * 128, :], in_=self.x1[dc][:],
                        waits=[tok_max(*[self.t_xm[(q, dc)] for q in range(NQ)])])

    def ph_ln2_m1_xq_pre(self):
        prog = self.prog
        L_S1, L_S2, L_M2R = self.L_S1, self.L_S2, self.L_M2R
        tS = tok_max(*self.lnS_tok)
        t1 = prog.op(V, "tensor_scalar", self.rows[:, L_S1:L_S1 + TOK], self.rows[:, L_S1:L_S1 + TOK],
                     1.0 / D, None, op0=ALU.mult, waits=[tS])
        t2 = prog.op(V, "tensor_scalar", self.rows[:, L_S2:L_S2 + TOK], self.rows[:, L_S2:L_S2 + TOK],
                     1.0 / D, None, op0=ALU.mult, waits=[tS])
        t3 = prog.op(V, "tensor_tensor", self.rows[:, L_M2R:L_M2R + TOK], self.rows[:, L_S1:L_S1 + TOK],
                     self.rows[:, L_S1:L_S1 + TOK], op=ALU.mult, waits=[t1])
        t4 = prog.op(V, "tensor_tensor", self.rows[:, L_S2:L_S2 + TOK], self.rows[:, L_S2:L_S2 + TOK],
                     self.rows[:, L_M2R:L_M2R + TOK], op=ALU.subtract, waits=[t2, t3])
        t5 = prog.op(A_, "activation", self.rows[:, L_S2:L_S2 + TOK], self.rows[:, L_S2:L_S2 + TOK],
                     AF.Ln, bias=self.cbias[0:1, 0:1], waits=[t4, self.t_cbias])
        t6 = prog.op(A_, "activation", self.rows[:, L_S2:L_S2 + TOK], self.rows[:, L_S2:L_S2 + TOK],
                     AF.Exp, scale=-0.5, waits=[t5])
        t7 = prog.op(V, "tensor_tensor", self.rows[:, L_M2R:L_M2R + TOK], self.rows[:, L_S1:L_S1 + TOK],
                     self.rows[:, L_S2:L_S2 + TOK], op=ALU.mult, waits=[t6, t3])
        self.t_ln2rows = tok_max(t6, t7)

    def _bcast_scales(self, key, dst, col0, n=2, extra=()):
        prog = self.prog
        t_d = prog.op(SY, "dma_start", out=self.m_in[key].ap(), in_=self.srow1[0:1, 0:2],
                      waits=[self.t_maxloc])
        t_c = prog.op(G, "collective_compute", "AllReduce", ALU.max, replica_groups=ALL8,
                      ins=[self.m_in[key].ap()], outs=[self.m_out[key].ap()], waits=[t_d])
        t_r = prog.op(SY, "dma_start", out=self.srow1[:, 4:6], in_=self.m_out[key].ap(), waits=[t_c])
        ts_ = prog.op(V, "tensor_scalar", self.srow1[:, 8:8 + n], self.srow1[:, 4:4 + n],
                      1.0 / QMAX, 1e-12, op0=ALU.mult, op1=ALU.max, waits=[t_r])
        tr_ = prog.op(V, "reciprocal", self.srow1[:, 8 + n:8 + 2 * n], self.srow1[:, 8:8 + n], waits=[ts_])
        t_o = prog.op(TE, "matmul", self.ps_o3[:, 0:2 * n], self.ones[0:1, 0:128],
                      self.srow1[0:1, 8:8 + 2 * n],
                      waits=[tr_, self.t_ones, self.ps_o3_use] + list(extra))
        t_o = self.pe_drain(t_o)
        t_e = prog.op(A_, "activation", dst[:, col0:col0 + 2 * n], self.ps_o3[:, 0:2 * n],
                      AF.Identity, waits=[t_o])
        self.ps_o3_use = t_e
        return t_e

    def ph_ln2_m1_xq(self):
        prog = self.prog
        L_S2, L_M2R = self.L_S2, self.L_M2R
        self.ps_o3_use = None
        fence = self.fence_ffn
        ln_use = {}
        o_use = {}
        mx = []
        ps_use = {}
        for q in range(NQ):
            t_or = prog.op(TE, "matmul", self.ps_o1[:], self.ones[0:1, 0:128],
                           self.rows[:, L_S2 + q * QT:L_S2 + (q + 1) * QT],
                           waits=[self.t_ln2rows, o_use.get("o1"), fence])
            t_om = prog.op(TE, "matmul", self.ps_o2[:], self.ones[0:1, 0:128],
                           self.rows[:, L_M2R + q * QT:L_M2R + (q + 1) * QT],
                           waits=[self.t_ln2rows, o_use.get("o2")])
            t_or = t_om = self.pe_drain(t_or, t_om)
            tq = []
            for dc in range(DCN):
                t1 = prog.op(V, "tensor_tensor", self.qf[dc % 2][:], self.x1[dc][:, q * QT:(q + 1) * QT],
                             self.ps_o1[:], op=ALU.mult,
                             waits=[self.t_xm[(q, dc)], t_or, ln_use.get(("qf", dc % 2)), fence])
                t2 = prog.op(V, "tensor_tensor", self.ln2[dc][:, q * QT:(q + 1) * QT], self.qf[dc % 2][:],
                             self.ps_o2[:], op=ALU.subtract, waits=[t1, t_om])
                ln_use[("qf", dc % 2)] = t2
                t3 = prog.op(V, "reduce_max", self.maxc[:, q * DCN + dc:q * DCN + dc + 1],
                             self.ln2[dc][:, q * QT:(q + 1) * QT], axis=AX.X,
                             apply_absolute_value=True, waits=[t2])
                ln_use[(q, dc)] = t2
                tq.append(t3)
                mx.append(t3)
            o_use["o1"] = tok_max(*tq)
            o_use["o2"] = o_use["o1"]
        t_mr = prog.op(V, "reduce_max", self.maxc[:, 32:33], self.maxc[:, 0:32], axis=AX.X,
                       waits=[tok_max(*mx)])
        t_pc = prog.op(G, "tensor_reduce", self.srow1[0:1, 0:1], self.maxc[:, 32:33], axis=AX.C,
                       op=ALU.max, waits=[t_mr])
        t_cp = prog.op(V, "tensor_scalar", self.srow1[:, 1:2], self.srow1[:, 0:1], 1.0, None,
                       op0=ALU.mult, waits=[t_pc])
        self.t_maxloc = tok_max(t_pc, t_cp)
        t_sc = self._bcast_scales("m1", self.scal, 0)
        self.t_xq = {}
        lvl_use = {}
        for q in range(NQ):
            for dc in range(DCN):
                r = dc % 4
                tl = prog.op(V, "tensor_scalar", self.ql[r][:], self.ln2[dc][:, q * QT:(q + 1) * QT],
                             self.scal[:, 2:3], None, op0=ALU.mult,
                             waits=[ln_use[(q, dc)], t_sc, lvl_use.get(r)])
                tb = prog.op(V, "tensor_scalar", self.xqb[dc][:, q * QT:(q + 1) * QT], self.ql[r][:],
                             self.scal[:, 0:1], None, op0=ALU.mult, waits=[tl])
                lvl_use[r] = tb
                self.t_xq[(q, dc)] = tb
        if "ln2" in self.dbg:
            tp = self.tap("ln2", [D, TOK])
            for dc in range(DCN):
                prog.op(SY, "dma_start", out=tp[dc * 128:(dc + 1) * 128, :], in_=self.ln2[dc][:],
                        waits=[tok_max(*[self.t_xq[(q, dc)] for q in range(NQ)])])
        if "xq" in self.dbg:
            tp = self.tap("xq", [D, TOK], BF16)
            for dc in range(DCN):
                prog.op(SY, "dma_start", out=tp[dc * 128:(dc + 1) * 128, :], in_=self.xqb[dc][:],
                        waits=[tok_max(*[self.t_xq[(q, dc)] for q in range(NQ)])])

    def ph_upgate(self):
        prog = self.prog
        sub = self.stop - 7 if self.stop < 9 else 99
        wu_use, wg_use, ps_use, st_use = {}, {}, {}, {}
        mx = []
        xq_all = self.eng_drain(V, *self.t_xq.values())
        for ig in range(ICN // 2):
            sl = ig % 2
            t_wu = prog.op(SY, "dma_start", out=slab_out(self.wus[sl], 2, DCN),
                           in_=slab_ap(self.dr["wqu"], ig * 2, 2, DCN),
                           waits=[wu_use.get(sl), self.t_wq[("u", ig * 2)], self.t_wq[("u", ig * 2 + 1)],
                                  self.fence_q if ig < 2 else None])
            t_wg = prog.op(SY, "dma_start", out=slab_out(self.wgs[sl], 2, DCN),
                           in_=slab_ap(self.dr["wqg"], ig * 2, 2, DCN),
                           waits=[wg_use.get(sl), self.t_wq[("g", ig * 2)], self.t_wq[("g", ig * 2 + 1)],
                                  self.fence_q if ig < 2 else None])
            lastu, lastg = [], []
            if sub < 0.4:
                wu_use[sl] = t_wu
                wg_use[sl] = t_wg
                continue
            for mc in range(2):
                ic = ig * 2 + mc
                for q in range(NQ):
                    mmu = mmg = None
                    for dc in range(DCN):
                        mmu = prog.op(TE, "matmul", self.ps_a[:],
                                      self.wus[sl][:, (mc * DCN + dc) * 128:(mc * DCN + dc + 1) * 128],
                                      self.xqb[dc][:, q * QT:(q + 1) * QT],
                                      start=(dc == 0), stop=(dc == DCN - 1),
                                      waits=[t_wu, xq_all] + ([ps_use.get(0)] if dc == 0 else []))
                    for dc in range(DCN):
                        mmg = prog.op(TE, "matmul", self.ps_b[:],
                                      self.wgs[sl][:, (mc * DCN + dc) * 128:(mc * DCN + dc + 1) * 128],
                                      self.xqb[dc][:, q * QT:(q + 1) * QT],
                                      start=(dc == 0), stop=(dc == DCN - 1),
                                      waits=[t_wg, xq_all] + ([ps_use.get(1)] if dc == 0 else []))
                    if sub < 0.6:
                        ev = prog.op(A_, "activation", self.stg[0][:], self.ps_a[:], AF.Identity,
                                     waits=[mmu, st_use.get("x"), self.fence_q])
                        ev2 = prog.op(A_, "activation", self.stg[1][:], self.ps_b[:], AF.Identity,
                                      waits=[mmg, st_use.get("y"), self.fence_q])
                        st_use["x"], st_use["y"] = ev, ev2
                        ps_use[0], ps_use[1] = ev, ev2
                        lastu.append(mmu)
                        lastg.append(mmg)
                        continue
                    t_dr = self.pe_drain(mmu, mmg)
                    mmu = tok_max(mmu, t_dr)
                    mmg = tok_max(mmg, t_dr)
                    su = self.stg[0]
                    sg = self.stg[1]
                    teu = prog.op(A_, "activation", su[:, q * QT:(q + 1) * QT], self.ps_a[:], AF.Identity,
                                  waits=[mmu, (st_use.get(0) if q == 0 else None), self.fence_q])
                    teg = prog.op(A_, "activation", sg[:, q * QT:(q + 1) * QT], self.ps_b[:], AF.Identity,
                                  waits=[mmg, (st_use.get(1) if q == 0 else None), self.fence_q])
                    cu = 64 + ic * NQ + q
                    cg = 160 + ic * NQ + q
                    tmu = prog.op(V, "reduce_max", self.maxc[:, cu:cu + 1], self.ps_a[:], axis=AX.X,
                                  apply_absolute_value=True, waits=[mmu, teu])
                    tmg = prog.op(V, "reduce_max", self.maxc[:, cg:cg + 1], self.ps_b[:], axis=AX.X,
                                  apply_absolute_value=True, waits=[mmg, teg])
                    ps_use[0] = tok_max(teu, tmu)
                    ps_use[1] = tok_max(teg, tmg)
                    mx += [tmu, tmg]
                    lastu.append(mmu)
                    lastg.append(mmg)
                    if sub < 0.8:
                        continue
                    if q == NQ - 1:
                        du = prog.op(SY, "dma_start", out=self.sp_up[ic * 128:(ic + 1) * 128, :],
                                     in_=su[:], waits=[teu])
                        dg = prog.op(SY, "dma_start", out=self.sp_gate[ic * 128:(ic + 1) * 128, :],
                                     in_=sg[:], waits=[teg])
                        st_use[0] = du
                        st_use[1] = dg
            wu_use[sl] = tok_max(*lastu)
            wg_use[sl] = tok_max(*lastg)
        if sub < 0.6:
            return
        tmx = tok_max(*mx)
        t1 = prog.op(V, "reduce_max", self.maxc[:, 33:34], self.maxc[:, 64:160], axis=AX.X, waits=[tmx])
        t2 = prog.op(V, "reduce_max", self.maxc[:, 34:35], self.maxc[:, 160:256], axis=AX.X, waits=[tmx])
        t3 = prog.op(G, "tensor_reduce", self.srow1[0:1, 0:1], self.maxc[:, 33:34], axis=AX.C,
                     op=ALU.max, waits=[t1])
        t4 = prog.op(G, "tensor_reduce", self.srow1[0:1, 1:2], self.maxc[:, 34:35], axis=AX.C,
                     op=ALU.max, waits=[t2])
        self.t_maxloc = tok_max(t3, t4)
        self.maxc_free = tok_max(t1, t2)
        self.t_scug = self._bcast_scales("mug", self.scal, 0, n=2, extra=[xq_all])
        if "gate" in self.dbg:
            tpg = self.tap("gate", [DI, TOK], BF16)
            tpu = self.tap("up", [DI, TOK], BF16)
            fs = prog.fence()
            prog.op(SY, "dma_start", out=tpg[:, :], in_=self.sp_gate.ap(), waits=[fs])
            prog.op(SY, "dma_start", out=tpu[:, :], in_=self.sp_up.ap(), waits=[fs])

    def _gate_chain(self, ic, q, r, want_t2, waits_extra=()):
        prog = self.prog
        tg = prog.op(SY, "dma_start", out=self.qa[r][:],
                     in_=self.sp_gate[ic * 128:(ic + 1) * 128, q * QT:(q + 1) * QT],
                     waits=[self.qa_use.get(r)] + list(waits_extra))
        tl = prog.op(V, "tensor_scalar", self.ql[r][:], self.qa[r][:], self.scal[:, 3:4], None,
                     op0=ALU.mult, waits=[tg, self.t_scug, self.ql_use.get(r)])
        self.qa_use[r] = tl
        tgq = prog.op(A_, "activation", self.qf[r][:], self.ql[r][:], AF.Identity,
                      scale=self.scal[:, 1:2], waits=[tl, self.qf_use.get(r)])
        tsg = prog.op(A_, "activation", self.qf[2 + r][:], self.ql[r][:], AF.Sigmoid,
                      scale=self.scal[:, 1:2], waits=[tl, self.qf_use.get(2 + r)])
        self.ql_use[r] = tsg
        tsm = prog.op(V, "tensor_scalar", self.qm[0][:], self.qf[2 + r][:], 65536.0, None,
                      op0=ALU.mult, waits=[tsg, self.qm_use.get(0)])
        tsq = prog.op(A_, "activation", self.qf[2 + r][:], self.qm[0][:], AF.Identity,
                      scale=1.0 / 65536.0, waits=[tsm])
        self.qm_use[0] = tsq
        return tgq, tsq

    def _signed_minmax_combine(self, waits=()):
        prog = self.prog
        tM = prog.op(V, "reduce_max", self.maxc[:, 50:51], self.maxc[:, 64:160], axis=AX.X,
                     waits=list(waits))
        tN = prog.op(V, "tensor_reduce", self.maxc[:, 51:52], self.maxc[:, 160:256], axis=AX.X,
                     op=ALU.min, waits=list(waits))
        tC = prog.op(V, "scalar_tensor_tensor", self.maxc[:, 52:53], self.maxc[:, 51:52], -1.0,
                     self.maxc[:, 50:51], op0=ALU.mult, op1=ALU.max, waits=[tM, tN])
        tP = prog.op(G, "tensor_reduce", self.srow1[0:1, 0:1], self.maxc[:, 52:53], axis=AX.C,
                     op=ALU.max, waits=[tC])
        tcp = prog.op(V, "tensor_scalar", self.srow1[:, 1:2], self.srow1[:, 0:1], 1.0, None,
                      op0=ALU.mult, waits=[tP])
        self.t_maxloc = tok_max(tP, tcp)
        self.maxc_free = tok_max(tM, tN)

    def ph_passA(self):
        prog = self.prog
        self.qa_use, self.ql_use, self.qf_use, self.qm_use = {}, {}, {}, {}
        last = []
        for ic in range(ICN):
            for q in range(NQ):
                idx = ic * NQ + q
                r = idx % 2
                tgq, tsq = self._gate_chain(ic, q, r, True)
                tt2 = prog.op(P_, "tensor_tensor", self.qf[4][:], self.qf[r][:], self.qf[2 + r][:],
                              op=ALU.mult, waits=[tgq, tsq, self.qf_use.get(4), self.maxc_free])
                self.qf_use[r] = tt2
                self.qf_use[2 + r] = tt2
                tmx = prog.op(V, "tensor_reduce", self.maxc[:, 64 + idx:65 + idx], self.qf[4][:],
                              axis=AX.X, op=ALU.max, waits=[tt2, self.maxc_free])
                tmn = prog.op(V, "tensor_reduce", self.maxc[:, 160 + idx:161 + idx], self.qf[4][:],
                              axis=AX.X, op=ALU.min, waits=[tt2])
                self.qf_use[4] = tok_max(tmx, tmn)
                last.append(tok_max(tmx, tmn))
        self._signed_minmax_combine(waits=[tok_max(*last)])
        self.t_sc2 = self._bcast_scales("mg2", self.scal, 4, n=2)

    def ph_passB(self):
        prog = self.prog
        lastB = []
        for ic in range(ICN):
            for q in range(NQ):
                r = (ic * NQ + q) % 2
                tgq, tsq = self._gate_chain(ic, q, r, True, waits_extra=[self.t_sc2])
                tt2 = prog.op(P_, "tensor_tensor", self.qf[4][:], self.qf[r][:], self.qf[2 + r][:],
                              op=ALU.mult, waits=[tgq, tsq, self.qf_use.get(4)])
                self.qf_use[r] = tt2
                self.qf_use[2 + r] = tt2
                tl2 = prog.op(V, "tensor_scalar", self.ql[2 + r][:], self.qf[4][:], self.scal[:, 6:7],
                              None, op0=ALU.mult, waits=[tt2, self.t_sc2, self.ql_use.get(2 + r)])
                self.qf_use[4] = tl2
                tg2 = prog.op(A_, "activation", self.qf[4][:], self.ql[2 + r][:], AF.Identity,
                              scale=self.scal[:, 4:5], waits=[tl2])
                self.ql_use[2 + r] = tg2
                tu = prog.op(SY, "dma_start", out=self.qa[2 + r][:],
                             in_=self.sp_up[ic * 128:(ic + 1) * 128, q * QT:(q + 1) * QT],
                             waits=[self.qa_use.get(2 + r)])
                tlu = prog.op(V, "tensor_scalar", self.ql[r][:], self.qa[2 + r][:], self.scal[:, 2:3],
                              None, op0=ALU.mult, waits=[tu, self.ql_use.get(r)])
                self.qa_use[2 + r] = tlu
                tuq = prog.op(A_, "activation", self.qf[5][:], self.ql[r][:], AF.Identity,
                              scale=self.scal[:, 0:1], waits=[tlu, self.qf_use.get(5)])
                self.ql_use[r] = tuq
                idx = ic * NQ + q
                tt3 = prog.op(P_, "tensor_tensor", self.qf[r][:], self.qf[4][:], self.qf[5][:],
                              op=ALU.mult, waits=[tg2, tuq, self.qf_use.get(r), self.maxc_free])
                self.qf_use[4] = tt3
                self.qf_use[5] = tt3
                tmx = prog.op(V, "tensor_reduce", self.maxc[:, 64 + idx:65 + idx], self.qf[r][:],
                              axis=AX.X, op=ALU.max, waits=[tt3, self.maxc_free])
                tmn = prog.op(V, "tensor_reduce", self.maxc[:, 160 + idx:161 + idx], self.qf[r][:],
                              axis=AX.X, op=ALU.min, waits=[tt3])
                self.qf_use[r] = tok_max(tmx, tmn)
                lastB.append(tok_max(tmx, tmn))
        self._signed_minmax_combine(waits=[tok_max(*lastB)])
        self.t_sco = self._bcast_scales("mo", self.scal2, 0, n=1)

    def ph_passC_down(self):
        prog = self.prog
        wd_use = {}
        obf_use = {}
        out_d = []
        ps_use = {}
        oxo_use = {}
        for q in range(NQ):
            ot = []
            for ic in range(ICN):
                r = (ic) % 2
                tgq, tsq = self._gate_chain(ic, q, r, True, waits_extra=[self.t_sco])
                tt2 = prog.op(P_, "tensor_tensor", self.qf[4][:], self.qf[r][:], self.qf[2 + r][:],
                              op=ALU.mult, waits=[tgq, tsq, self.qf_use.get(4)])
                self.qf_use[r] = tt2
                self.qf_use[2 + r] = tt2
                tl2 = prog.op(V, "tensor_scalar", self.ql[2 + r][:], self.qf[4][:], self.scal[:, 6:7],
                              None, op0=ALU.mult, waits=[tt2, self.ql_use.get(2 + r)])
                self.qf_use[4] = tl2
                tg2 = prog.op(A_, "activation", self.qf[4][:], self.ql[2 + r][:], AF.Identity,
                              scale=self.scal[:, 4:5], waits=[tl2])
                self.ql_use[2 + r] = tg2
                tu = prog.op(SY, "dma_start", out=self.qa[2 + r][:],
                             in_=self.sp_up[ic * 128:(ic + 1) * 128, q * QT:(q + 1) * QT],
                             waits=[self.qa_use.get(2 + r)])
                tlu = prog.op(V, "tensor_scalar", self.ql[r][:], self.qa[2 + r][:], self.scal[:, 2:3],
                              None, op0=ALU.mult, waits=[tu, self.ql_use.get(r)])
                self.qa_use[2 + r] = tlu
                tuq = prog.op(A_, "activation", self.qf[5][:], self.ql[r][:], AF.Identity,
                              scale=self.scal[:, 0:1], waits=[tlu, self.qf_use.get(5)])
                self.ql_use[r] = tuq
                tt3 = prog.op(P_, "tensor_tensor", self.qf[r][:], self.qf[4][:], self.qf[5][:],
                              op=ALU.mult, waits=[tg2, tuq, self.qf_use.get(r)])
                self.qf_use[4] = tt3
                self.qf_use[5] = tt3
                tlo = prog.op(V, "tensor_scalar", self.ql[2 + r][:], self.qf[r][:], self.scal2[:, 1:2],
                              None, op0=ALU.mult, waits=[tt3, self.t_sco, self.ql_use.get(2 + r)])
                self.qf_use[r] = tlo
                tob = prog.op(V, "tensor_scalar", self.obf[ic][:], self.ql[2 + r][:], self.scal2[:, 0:1],
                              None, op0=ALU.mult, waits=[tlo, obf_use.get(ic)])
                self.ql_use[2 + r] = tob
                ot.append(tob)
            o_all = self.eng_drain(V, *ot)
            for dc in range(DCN):
                sl = dc % 2
                t_w = prog.op(SY, "dma_start", out=slab_out(self.wds[sl], 1, ICN),
                              in_=slab_ap(self.dr["wqd"], dc, 1, ICN),
                              waits=[wd_use.get(sl), self.t_wq[("d", dc)],
                                     self.fence_q if (q == 0 and dc < 2) else None])
                pb = [self.ps_a, self.ps_b][dc % 2]
                mml = None
                for kc in range(ICN):
                    mml = prog.op(TE, "matmul", pb[:], self.wds[sl][:, kc * 128:(kc + 1) * 128],
                                  self.obf[kc][:], start=(kc == 0), stop=(kc == ICN - 1),
                                  waits=[t_w, o_all] + ([ps_use.get(dc % 2)] if kc == 0 else []))
                wd_use[sl] = mml
                for ic in range(ICN):
                    obf_use[ic] = tok_max(obf_use.get(ic), mml)
                mml = self.pe_drain(mml)
                t_ev = prog.op(A_, "activation", self.oxo[0][:], pb[:], AF.Identity,
                               waits=[mml, oxo_use.get(0)])
                ps_use[dc % 2] = t_ev
                t_fin = prog.op(V, "tensor_tensor", self.qf[dc % 2][:], self.oxo[0][:],
                                self.x1[dc][:, q * QT:(q + 1) * QT], op=ALU.add,
                                waits=[t_ev, self.t_xm[(q, dc)], self.qf_use.get(dc % 2)])
                oxo_use[0] = t_fin
                if "ffnout" in self.dbg:
                    if "ffnout" not in self.taps:
                        self.tap("ffnout", [D, TOK])
                    tfo = prog.op(SY, "dma_start",
                                  out=self.taps["ffnout"][dc * 128:(dc + 1) * 128, q * QT:(q + 1) * QT],
                                  in_=self.oxo[0][:], waits=[t_ev])
                    oxo_use[0] = tok_max(t_fin, tfo)
                self.qf_use[dc % 2] = t_fin
                out_d.append(prog.op(SY, "dma_start", out=self.out[dc * 128:(dc + 1) * 128, q * QT:(q + 1) * QT],
                                     in_=self.qf[dc % 2][:], waits=[t_fin]))
                self.qf_use[dc % 2] = tok_max(self.qf_use[dc % 2], out_d[-1])
        fin = tok_max(*out_d)
        prog.raw(SY, lambda e: None, waits=[fin])


def build_program(dbg=(), stop=99):
    b = Builder(dbg, stop=stop)
    nc = b.build()
    return nc, b


def _host_inputs(inputs, cid):
    b, h = cid // 2, cid % 2
    x = np.asarray(inputs["L0_attn_x"], np.float32)
    xT = np.ascontiguousarray(x[b].T)
    x_scan = np.ascontiguousarray(xT[h * DH:(h + 1) * DH])
    cols = np.concatenate([np.arange((2 * j + h) * OT, (2 * j + h) * OT + OT) for j in range(NW)])
    x_own = np.ascontiguousarray(xT[:, cols])

    def dhalf_lane(a):
        return np.ascontiguousarray(
            np.asarray(a, np.float32)[h * DH:(h + 1) * DH].reshape(SCN, 128, NS).transpose(1, 0, 2).reshape(128, SCN * NS))

    def dhalf_col(v):
        return np.ascontiguousarray(np.asarray(v, np.float32)[h * DH:(h + 1) * DH].reshape(SCN, 128).T)

    def full_col(v, n):
        return np.ascontiguousarray(np.asarray(v, np.float32).reshape(n, 128).T)

    dt16 = np.exp(np.linspace(np.log(0.001), np.log(0.1), NS)).astype(np.float32)
    dtb = np.ascontiguousarray(np.broadcast_to(np.tile(dt16, SCN), (128, SCN * NS)).astype(np.float32))

    return {
        "x_scan": x_scan,
        "x_own": x_own,
        "A_s": dhalf_lane(inputs["L0_attn_A_ssm"]),
        "B_s": dhalf_lane(inputs["L0_attn_B_ssm"]),
        "C_s": dhalf_lane(inputs["L0_attn_C_ssm"]),
        "dtb": dtb,
        "D_s": dhalf_col(inputs["L0_attn_D_ssm"]),
        "s1_s": dhalf_col(inputs["L0_attn_scale1"]),
        "s2c": full_col(inputs["L0_attn_scale2"], DCN),
        "b1c": full_col(inputs["L0_attn_b1"], FC),
        "b2c": full_col(inputs["L0_attn_b2"], DCN),
    }


_WCACHE = {}


def _host_weights(inputs):
    import ml_dtypes
    w1 = np.asarray(inputs["L0_attn_w1"], np.float32)
    w2 = np.asarray(inputs["L0_attn_w2"], np.float32)
    return {
        "wp1": pack_w(w1, DCN, FC),
        "wp2": pack_w(w2, FC, DCN),
        "wun": np.ascontiguousarray(np.asarray(inputs["L0_ffn_w_up"], np.float32)),
        "wgn": np.ascontiguousarray(np.asarray(inputs["L0_ffn_w_gate"], np.float32)),
        "wdn": np.ascontiguousarray(np.asarray(inputs["L0_ffn_w_down"], np.float32)),
        "identb": np.eye(128, dtype=np.float32).astype(ml_dtypes.bfloat16),
    }


def kernel(**inputs):
    nc, _b = build_program()
    wts = _host_weights(inputs)
    in_maps = []
    for cid in range(8):
        m = _host_inputs(inputs, cid)
        m.update(wts)
        in_maps.append(m)
    res = run_bass_kernel_spmd(nc, in_maps, core_ids=list(range(8)))
    outp = np.zeros((B, D, T), np.float32)
    for cid in range(8):
        b, h = cid // 2, cid % 2
        o = np.asarray(res.results[cid]["out"])
        for j in range(NW):
            outp[b][:, (2 * j + h) * OT:(2 * j + h) * OT + OT] = o[:, j * OT:(j + 1) * OT]
    return np.ascontiguousarray(outp.transpose(0, 2, 1))


# version 07
# speedup vs baseline: 1.1713x; 1.1091x over previous
import numpy as np
from contextlib import ExitStack

import concourse.bass as bass
import concourse.mybir as mybir
import bass_rust
from concourse.bass_utils import run_bass_kernel_spmd

F32 = mybir.dt.float32
BF16 = mybir.dt.bfloat16
I16 = mybir.dt.int16
I32 = mybir.dt.int32
U16 = mybir.dt.uint16
AF = mybir.ActivationFunctionType
ALU = mybir.AluOpType
AX = mybir.AxisListType

B, T, D, NS, DFF, DI = 4, 4096, 1024, 16, 4096, 3072
DH, SCN, DCN = 512, 4, 8
FC, ICN = DFF // 128, DI // 128
NW, WT, OT = 8, 512, 256
TOK, QT, NQ = 2048, 512, 4
QMAX = 32767.0
EPS_LN, EPS_RMS = 1e-5, 1e-6
PAIRS = [[0, 1], [2, 3], [4, 5], [6, 7]]
ALL8 = [list(range(8))]
ENGS = ("sync", "scalar", "vector", "gpsimd", "tensor")
V, A_, P_, G, SY, TE = "vector", "scalar", "gpsimd", "gpsimd", "sync", "tensor"


def tok_max(*tokens):
    outd = {}
    for t in tokens:
        if t is None:
            continue
        items = t.items() if isinstance(t, dict) else [t]
        for k, v in items:
            if outd.get(k, -1) < v:
                outd[k] = v
    return outd


class Prog:
    SEMS = ["sv", "ss", "sp", "st", "dy", "dq", "dg", "cc"]

    def __init__(self, nc):
        self.nc = nc
        self.buf = {e: [] for e in ENGS}
        self.cnt = {}
        self.sems = {}
        self.marks = {}

    def _semname(self, eng, meth):
        if meth == "collective_compute":
            return "cc"
        if meth is not None and "dma" in meth:
            return {"sync": "dy", "scalar": "dq", "gpsimd": "dg"}.get(eng, "dy")
        return {"vector": "sv", "scalar": "ss", "gpsimd": "sp", "tensor": "st", "sync": "dy"}[eng]

    def _filter(self, eng, waits):
        need = tok_max(*waits)
        fil = {}
        for k, v in need.items():
            if self.marks.get((eng, k), 0) < v:
                fil[k] = v
                self.marks[(eng, k)] = v
        return fil

    def op(self, eng, meth, *args, waits=(), argfn=None, **kw):
        fil = self._filter(eng, waits)
        semname = self._semname(eng, meth)
        inc = 16 if semname in ("dy", "dq", "dg") else 1
        self.cnt[semname] = self.cnt.get(semname, 0) + inc
        self.buf[eng].append((fil, meth, args, kw, semname, inc, argfn))
        return (semname, self.cnt[semname])

    def raw(self, eng, fn, waits=()):
        fil = self._filter(eng, waits)
        self.buf[eng].append((fil, None, None, None, None, 0, fn))

    def fence(self):
        return dict(self.cnt)

    def flush(self, block):
        for eng in ENGS:
            items = self.buf[eng]
            self.buf[eng] = []
            if not items:
                continue

            def make(items=items):
                def body(e):
                    for fil, meth, args, kw, semname, inc, argfn in items:
                        for k, v in fil.items():
                            e.wait_ge(self.sems[k], v)
                        if meth is None:
                            argfn(e)
                            continue
                        a, k2 = (args, kw) if argfn is None else argfn(e)
                        getattr(e, meth)(*a, **k2).then_inc(self.sems[semname], inc)
                return body
            getattr(block, eng)(make())


def pack_w(w, KC, MC):
    a = np.asarray(w, np.float32).reshape(KC, 128, MC, 128).transpose(2, 0, 1, 3)
    return np.ascontiguousarray(a.reshape(MC * KC * 128, 128))


def slab_ap(dram_ap, mc0, nmc, KC):
    rows = dram_ap[mc0 * KC * 128:(mc0 + nmc) * KC * 128, :]
    return rows.rearrange("(o kc p) f -> p o kc f", o=nmc, kc=KC, p=128)


def slab_out(tile, nmc, KC):
    return tile.ap().rearrange("p (o kc f) -> p o kc f", o=nmc, kc=KC)


class Builder:
    def __init__(self, dbg=(), stop=99):
        self.nc = bass.Bass("TRN2", target_bir_lowering=False)
        self.stop = stop
        self.dbg = list(dbg)
        self.prog = Prog(self.nc)
        self.dr = {}
        self.taps = {}
        self.hcell = {}

    def din(self, name, shape, dt=F32):
        self.dr[name] = self.nc.dram_tensor(name, list(shape), dt, kind="ExternalInput").ap()
        return self.dr[name]

    def dint(self, name, shape, dt=F32):
        self.dr[name] = self.nc.dram_tensor(name, list(shape), dt)
        return self.dr[name]

    def tap(self, name, shape, dt=F32):
        self.taps[name] = self.nc.dram_tensor("tap_" + name, list(shape), dt, kind="ExternalOutput").ap()
        return self.taps[name]

    def build(self):
        nc, prog = self.nc, self.prog
        self.x_scan = self.din("x_scan", [DH, T])
        self.x_own = self.din("x_own", [D, TOK])
        for n, s in (("A_s", [128, SCN * NS]), ("B_s", [128, SCN * NS]), ("C_s", [128, SCN * NS]),
                     ("dtb", [128, SCN * NS]), ("D_s", [128, SCN]), ("s1_s", [128, SCN]),
                     ("s2c", [128, DCN]), ("b1c", [128, FC]), ("b2c", [128, DCN])):
            self.din(n, s)
        for n, s in (("wp1", [FC * DCN * 128, 128]), ("wp2", [DCN * FC * 128, 128])):
            self.din(n, s)
        self.din("wun", [DI, D])
        self.din("wgn", [DI, D])
        self.din("wdn", [D, DI])
        self.dr["identb"] = self.nc.dram_tensor("identb", [128, 128], BF16, kind="ExternalInput").ap()
        self.dint("wqu", [ICN * DCN * 128, 128], BF16)
        self.dint("wqg", [ICN * DCN * 128, 128], BF16)
        self.dint("wqd", [DCN * ICN * 128, 128], BF16)
        self.out = self.nc.dram_tensor("out", [D, TOK], F32, kind="ExternalOutput").ap()
        self.stats_in = self.dint("stats_in", [2, T])
        self.stats_out = self.dint("stats_out", [4, T])
        self.yag_in = [self.dint(f"yag_in{w}", [DH, WT]) for w in range(NW)]
        self.yag_out = [self.dint(f"yag_out{w}", [D, WT]) for w in range(NW)]
        self.m_in = {k: self.dint(f"min_{k}", [1, 2]) for k in ("m1", "mug", "mg2", "mo")}
        self.m_out = {k: self.dint(f"mout_{k}", [1, 2]) for k in ("m1", "mug", "mg2", "mo")}
        self.sp_up = self.dint("sp_up", [DI, TOK], BF16)
        self.sp_gate = self.dint("sp_gate", [DI, TOK], BF16)

        with ExitStack() as ctx:
            self.ctx = ctx
            S = lambda st, n, sh, dt=F32: st.enter_context(nc.sbuf_tensor(n, list(sh), dt))
            self.S = S
            self.ones = S(ctx, "ones", [128, WT])
            self.onesbf = S(ctx, "onesbf", [128, 8], BF16)
            self.dAv = S(ctx, "dAv", [128, SCN * NS])
            self.CdB = S(ctx, "CdB", [128, SCN * NS])
            self.sD = S(ctx, "sD", [128, SCN])
            self.s_s2 = S(ctx, "s_s2", [128, DCN])
            self.s_b1 = S(ctx, "s_b1", [128, FC])
            self.s_b2 = S(ctx, "s_b2", [128, DCN])
            self.states = S(ctx, "states", [128, SCN * NS])
            self.x1 = [S(ctx, f"x1_{dc}", [128, TOK]) for dc in range(DCN)]
            self.rows = S(ctx, "rows", [1, 6144])
            self.maxc = S(ctx, "maxc", [128, 256])
            self.scal = S(ctx, "scal", [128, 8])
            self.scal2 = S(ctx, "scal2", [128, 2])
            self.srow1 = S(ctx, "srow1", [1, 32])
            self.cbias = S(ctx, "cbias", [128, 4])
            self.drt = S(ctx, "drt", [128, 8])
            for name in Prog.SEMS:
                prog.sems[name] = ctx.enter_context(nc.semaphore(name))
            self.block = ctx.enter_context(nc.Block())

            def mk_h(e):
                pid = e.partition_id()
                h = e.compute_val(pid % 2)
                reg = e.alloc_register("hreg")
                e.reg_mov(reg, h)
                self.hcell["hrv"] = bass_rust.RuntimeValue(reg, min_val=0, max_val=1)
            prog.raw(SY, mk_h)

            self.t_ones = prog.op(V, "memset", self.ones[:], 1.0)
            tb0 = prog.op(V, "memset", self.cbias[:, 0:1], EPS_LN)
            tb1 = prog.op(V, "memset", self.cbias[:, 1:2], EPS_LN * EPS_RMS)
            tb2 = prog.op(V, "memset", self.cbias[:, 2:3], EPS_RMS)
            self.t_cbias = tok_max(tb0, tb1, tb2)
            self.t_onesbf = prog.op(V, "memset", self.onesbf[:, 0:1], 1.0)
            self.t_st0 = prog.op(V, "memset", self.states[:], 0.0)

            self.ph_params()
            if self.stop >= 2:
                self.run_main()
            prog.flush(self.block)
        return nc

    def run_main(self):
        nc, prog = self.nc, self.prog
        ctx = self.ctx
        if True:
            with ExitStack() as rgx:
                self.x1ln = [self.S(rgx, f"x1ln_{dc}", [128, TOK], BF16) for dc in range(DCN)]
                self.cstg = [self.S(rgx, f"cstg{i}", [128, WT], BF16) for i in range(2)]
                with ExitStack() as psc:
                    self.alloc_psum(psc)
                    with ExitStack() as rg:
                        self.alloc_scan(rg)
                        self.ph_stats_u()
                        if self.stop >= 3:
                            self.ph_scan()
                        if self.stop >= 4:
                            self.ph_x1()
                        prog.flush(self.block)
                    if self.stop >= 5:
                        with ExitStack() as rga:
                            self.alloc_attn(rga)
                            self.ph_attn()
                            prog.flush(self.block)
                    if self.stop >= 6:
                        self.ph_ln2_m1_xq_pre()
                    prog.flush(self.block)
            if self.stop < 7:
                return
            self.fence_ffn = prog.fence()
            with ExitStack() as psc:
                self.alloc_psum(psc)
                with ExitStack() as rgfU:
                    self.xqb = [self.S(rgfU, f"xq_{dc}", [128, TOK], BF16) for dc in range(DCN)]
                    with ExitStack() as rgf1:
                        self.ln2 = [self.S(rgf1, f"ln2_{dc}", [128, TOK]) for dc in range(DCN)]
                        self.ql = [self.S(rgf1, f"qlx{i}", [128, QT], I16) for i in range(4)]
                        self.qf = [self.S(rgf1, f"qfx{i}", [128, QT]) for i in range(2)]
                        self.ph_ln2_m1_xq()
                        prog.flush(self.block)
                    self.fence_q = prog.fence()
                    with ExitStack() as rgf2:
                        self.alloc_ffn_u(rgf2)
                        if self.stop >= 8:
                            self.ph_upgate()
                        prog.flush(self.block)
                self.fence_p = prog.fence()
                with ExitStack() as rgfP:
                    self.alloc_ffn_p(rgfP)
                    if self.stop >= 9:
                        self.ph_passA()
                    if self.stop >= 10:
                        self.ph_passB()
                    if self.stop >= 11:
                        self.ph_passC_down()
                    prog.flush(self.block)

    _psum_gen = [0]

    def alloc_psum(self, psc):
        nc = self.nc
        g = self._psum_gen[0]
        self._psum_gen[0] += 1
        self.ps_a = psc.enter_context(nc.psum_tensor(f"ps_a{g}", [128, QT], F32))
        self.ps_b = psc.enter_context(nc.psum_tensor(f"ps_b{g}", [128, QT], F32))
        self.ps_o1 = psc.enter_context(nc.psum_tensor(f"ps_o1{g}", [128, WT], F32))
        self.ps_o2 = psc.enter_context(nc.psum_tensor(f"ps_o2{g}", [128, WT], F32))
        self.ps_o3 = psc.enter_context(nc.psum_tensor(f"ps_o3{g}", [128, OT], F32))
        self.ps_s1 = psc.enter_context(nc.psum_tensor(f"ps_s1{g}", [1, WT], F32))
        self.ps_s2 = psc.enter_context(nc.psum_tensor(f"ps_s2{g}", [1, WT], F32))
        self.ps_aux = psc.enter_context(nc.psum_tensor(f"ps_aux{g}", [128, 512], F32))
        self.ps_t = self.ps_aux.ap()[:, 0:256].bitcast(BF16)
        self.ps_dmy = self.ps_aux.ap()[:, 500:508]

    def alloc_scan(self, rg):
        S = self.S
        WP = 2 * WT
        self.z_t = [S(rg, f"z{i}", [128, WP]) for i in range(4)]
        self.dab = [S(rg, f"dab{i}", [128, WP]) for i in range(2)]
        self.prb = [S(rg, f"prb{i}", [128, WP]) for i in range(2)]
        self.s_out = [S(rg, f"sout{i}", [128, WP]) for i in range(2)]
        self.y_t = [S(rg, f"yt{i}", [128, WP]) for i in range(2)]
        self.yall = S(rg, "yall", [128, DCN * OT])
        self.xoall = S(rg, "xoall", [128, DCN * OT])
        self.xscr = [S(rg, f"xscr{i}", [128, OT]) for i in range(4)]
        self.gstat = [S(rg, f"gstat{i}", [128, 32]) for i in range(2)]
        self.gd = [S(rg, f"gd{i}", [128, 32]) for i in range(2)]
        self.rgf = S(rg, "rgf", [1, WP])
        self.rmg = S(rg, "rmg", [1, WP])
        self.rrow = S(rg, "rrow", [1, OT])
        self.mrow = S(rg, "mrow", [1, OT])
        self.ostat = [S(rg, f"ostat{i}", [32, 32]) for i in range(2)]
        self.od = [S(rg, f"od{i}", [32, 32]) for i in range(2)]


    def alloc_attn(self, rg):
        S = self.S
        self.dq_w = S(rg, "dq_w", [128, D])
        self.dq_l = S(rg, "dq_l", [128, D], I16)
        self.dq_q = S(rg, "dq_q", [128, D], BF16)
        self.dq_s = S(rg, "dq_s", [128, 3 * 32])
        self.dq_st = S(rg, "dq_st", [128, DCN * 128], BF16)
        self.identb = S(rg, "identb_sb", [128, 128], BF16)
        self.w1s = [S(rg, f"w1s{i}", [128, 2 * DCN * 128], BF16) for i in range(2)]
        self.w2s = [S(rg, f"w2s{i}", [128, FC * 128], BF16) for i in range(2)]
        self.g1bf = S(rg, "g1bf", [128, FC * QT], BF16)
        self.evscr = [S(rg, f"ev{i}", [128, QT]) for i in range(2)]
        self.xmscr = [S(rg, f"xm{i}", [128, QT]) for i in range(2)]
        self.xo2 = [S(rg, f"xo2_{i}", [128, QT]) for i in range(2)]

    def alloc_ffn_u(self, rg):
        S = self.S
        self.wus = [S(rg, f"wus{i}", [128, 2 * DCN * 128], BF16) for i in range(2)]
        self.wgs = [S(rg, f"wgs{i}", [128, 2 * DCN * 128], BF16) for i in range(2)]
        self.stg = [S(rg, f"stg{i}", [128, TOK], BF16) for i in range(2)]

    def alloc_ffn_p(self, rg):
        S = self.S
        self.qa = [S(rg, f"qa{i}", [128, QT], BF16) for i in range(8)]
        self.ql = [S(rg, f"ql{i}", [128, QT], I16) for i in range(8)]
        self.qm = [S(rg, f"qm{i}", [128, QT], U16) for i in range(2)]
        self.qf = [S(rg, f"qf{i}", [128, QT]) for i in range(10)]
        self.obf = [S(rg, f"obf{i}", [128, QT], BF16) for i in range(ICN)]
        self.wds = [S(rg, f"wds{i}", [128, ICN * 128], BF16) for i in range(2)]
        self.oxo = [S(rg, f"oxo{i}", [128, QT]) for i in range(2)]
        self.qdn = [S(rg, f"qdn{i}", [128, QT]) for i in range(2)]

    R_M, R_VAR, R_RSTD, R_GFAC, R_MG, R_MRSTD, R_TMP = (i * WT for i in range(7))
    R_SR4 = 7 * WT
    L_S1, L_S2, L_M2R = 0, TOK, 2 * TOK

    def row(self, off, n=WT):
        return self.rows[:, off:off + n]

    def eng_drain(self, eng, *toks):
        if eng == V:
            t = self.prog.op(V, "memset", self.drt[:, 0:4], 0.0, waits=list(toks))
        elif eng == A_:
            t = self.prog.op(A_, "activation", self.drt[:, 4:8], self.drt[:, 0:4], AF.Identity,
                             waits=list(toks))
        else:
            t = self.prog.op(P_, "memset", self.drt[:, 0:4], 0.0, waits=list(toks))
        return tok_max(t, *toks)

    def pe_drain(self, *toks):
        t = self.prog.op(TE, "matmul", self.ps_dmy, self.ones[0:1, 0:128],
                         self.ones[0:1, 0:8], waits=list(toks))
        return tok_max(t, *toks)

    def ph_params(self):
        prog, S = self.prog, self.S
        with ExitStack() as rg:
            sA = S(rg, "sA", [128, SCN * NS]); sB = S(rg, "sB", [128, SCN * NS])
            sC = S(rg, "sC", [128, SCN * NS]); sdt = S(rg, "sdt", [128, SCN * NS])
            sp1 = S(rg, "sp1", [128, SCN * NS]); s_s1 = S(rg, "s_s1", [128, SCN])
            dr = self.dr
            tc = tok_max(*[prog.op(SY, "dma_start", out=d[:], in_=dr[n]) for n, d in
                           (("A_s", sA), ("B_s", sB), ("C_s", sC), ("dtb", sdt),
                            ("D_s", self.sD), ("s1_s", s_s1), ("s2c", self.s_s2),
                            ("b1c", self.s_b1), ("b2c", self.s_b2))])
            t_expA = prog.op(A_, "activation", sp1[:], sA[:], AF.Exp, waits=[tc])
            t_r = prog.op(V, "reciprocal", self.CdB[:], sp1[:], waits=[t_expA])
            t_m = prog.op(V, "tensor_tensor", self.dAv[:], sp1[:], sdt[:], op=ALU.mult, waits=[t_expA, tc])
            self.t_dA = prog.op(A_, "activation", self.dAv[:], self.dAv[:], AF.Exp, scale=-1.0, waits=[t_m])
            t_1m = prog.op(V, "tensor_scalar", sp1[:], self.dAv[:], -1.0, 1.0,
                           op0=ALU.mult, op1=ALU.add, waits=[self.t_dA, t_r])
            t_d1 = prog.op(V, "tensor_tensor", self.CdB[:], self.CdB[:], sp1[:], op=ALU.mult, waits=[t_1m])
            t_d2 = prog.op(V, "tensor_tensor", self.CdB[:], self.CdB[:], sB[:], op=ALU.mult, waits=[t_d1])
            t_d3 = prog.op(V, "tensor_tensor", self.CdB[:], self.CdB[:], sC[:], op=ALU.mult, waits=[t_d2])
            tl = []
            for sc in range(SCN):
                tl.append(prog.op(V, "tensor_scalar", self.CdB[:, sc * NS:(sc + 1) * NS],
                                  self.CdB[:, sc * NS:(sc + 1) * NS], s_s1[:, sc:sc + 1], None,
                                  op0=ALU.mult, waits=[t_d3]))
            self.t_cdb = tok_max(*tl)
            self.t_Df = prog.op(V, "tensor_tensor", self.sD[:], self.sD[:], s_s1[:], op=ALU.mult, waits=[tc])
            self.fence_params = prog.fence()
            prog.flush(self.block)

    def ph_stats_u(self):
        prog = self.prog
        row, R_TMP, R_SR4 = self.row, 0, T
        self.t_xs = {}
        self.use_cstg = {}
        use_ps1 = use_ps2 = None
        stat_d = []
        xs_free = {}
        for w in range(NW):
            mm1 = mm2 = None
            for sc in range(SCN):
                xst = self.z_t[sc].ap()[:, (w % 2) * WT:(w % 2) * WT + WT]
                t_l = prog.op(SY, "dma_start", out=xst,
                              in_=self.x_scan[sc * 128:(sc + 1) * 128, w * WT:(w + 1) * WT],
                              waits=[xs_free.get((sc, w % 2)), self.fence_params if w == 0 else None])
                t_c = prog.op(A_, "activation", self.cstg[0][:], xst, AF.Identity,
                              waits=[t_l, self.use_cstg.get(0)])
                t_q = prog.op(A_, "activation", self.cstg[1][:], xst, AF.Square,
                              waits=[t_l, self.use_cstg.get(1)])
                xs_free[(sc, w % 2)] = tok_max(t_c, t_q)
                mm1 = prog.op(TE, "matmul", self.ps_s1[:], self.onesbf[:, 0:1], self.cstg[0][:],
                              start=(sc == 0), stop=(sc == SCN - 1),
                              waits=[t_c, self.t_onesbf] + ([use_ps1] if sc == 0 else []))
                mm2 = prog.op(TE, "matmul", self.ps_s2[:], self.onesbf[:, 0:1], self.cstg[1][:],
                              start=(sc == 0), stop=(sc == SCN - 1),
                              waits=[t_q, self.t_onesbf] + ([use_ps2] if sc == 0 else []))
                self.use_cstg[0] = mm1
                self.use_cstg[1] = mm2
            t_mm = self.pe_drain(mm1, mm2)
            t_e1 = prog.op(A_, "activation", row(R_TMP), self.ps_s1[:], AF.Identity,
                           waits=[t_mm] + ([stat_d[-2]] if stat_d else []))
            t_e2 = prog.op(A_, "activation", row(R_SR4), self.ps_s2[:], AF.Identity,
                           waits=[t_mm] + ([stat_d[-1]] if stat_d else []))
            use_ps1, use_ps2 = t_e1, t_e2
            stat_d.append(prog.op(SY, "dma_start", out=self.stats_in[0:1, w * WT:(w + 1) * WT],
                                  in_=row(R_TMP), waits=[t_e1]))
            stat_d.append(prog.op(SY, "dma_start", out=self.stats_in[1:2, w * WT:(w + 1) * WT],
                                  in_=row(R_SR4), waits=[t_e2]))
        self.xs_free = xs_free
        self.use_ps1, self.use_ps2 = use_ps1, use_ps2
        self.t_sag = prog.op(G, "collective_compute", "AllGather", ALU.bypass,
                             replica_groups=PAIRS, ins=[self.stats_in.ap()], outs=[self.stats_out.ap()],
                             waits=[tok_max(*stat_d)])
        self.rows_use = tok_max(*stat_d)

    def ph_scan(self):
        prog = self.prog
        WP = 2 * WT
        self.ywtok = [None] * NW
        stuse = {}
        dause, pruse, sduse = {}, {}, {}
        zuse = {}
        yfree = {}
        gstat_use, gd_use, rgf_use = {}, {}, {}
        t_gload, t_wb = {}, {}
        combo = 0

        def gload(wp):
            sl = wp % 2
            rr = []
            for r in range(4):
                rr.append(prog.op(SY, "dma_start", out=self.gstat[sl][:, r * 8:(r + 1) * 8],
                                  in_=self.stats_out[r:r + 1, wp * WP:(wp + 1) * WP],
                                  waits=[self.t_sag, gstat_use.get(sl)]))
            t_gload[wp] = tok_max(*rr)

        def derive(wp, extra=()):
            sl = wp % 2
            g, d = self.gstat[sl], self.gd[sl]
            tS1 = prog.op(V, "tensor_tensor", d[:, 0:8], g[:, 0:8], g[:, 16:24], op=ALU.add,
                          waits=[t_gload[wp], gd_use.get(sl)] + list(extra))
            tS2 = prog.op(V, "tensor_tensor", d[:, 8:16], g[:, 8:16], g[:, 24:32], op=ALU.add,
                          waits=[t_gload[wp], gd_use.get(sl)])
            gstat_use[sl] = tok_max(tS1, tS2)
            tm = prog.op(V, "tensor_scalar", d[:, 0:8], d[:, 0:8], 1.0 / D, None, op0=ALU.mult,
                         waits=[tS1])
            tex = prog.op(V, "tensor_scalar", d[:, 8:16], d[:, 8:16], 1.0 / D, None, op0=ALU.mult,
                          waits=[tS2])
            tm2 = prog.op(V, "tensor_tensor", d[:, 16:24], d[:, 0:8], d[:, 0:8], op=ALU.mult,
                          waits=[tm, tS2])
            tvar = prog.op(V, "tensor_tensor", d[:, 8:16], d[:, 8:16], d[:, 16:24], op=ALU.subtract,
                           waits=[tex, tm2])
            tl2 = prog.op(A_, "activation", d[:, 24:32], d[:, 8:16], AF.Ln, scale=1.0 + EPS_RMS,
                          bias=self.cbias[:, 1:2], waits=[tvar, self.t_cbias])
            tgf = prog.op(A_, "activation", d[:, 24:32], d[:, 24:32], AF.Exp, scale=-0.5, waits=[tl2])
            tmg = prog.op(V, "tensor_tensor", d[:, 8:16], d[:, 0:8], d[:, 24:32], op=ALU.mult,
                          waits=[tgf, tm])
            tgf_d = self.eng_drain(A_, tgf)
            tmg_d = self.eng_drain(V, tmg)
            t_gfr = prog.op(SY, "dma_start", out=self.rgf[0:1, :], in_=d[:, 24:32],
                            waits=[tgf_d, rgf_use.get("gf")])
            t_mgr = prog.op(SY, "dma_start", out=self.rmg[0:1, :], in_=d[:, 8:16],
                            waits=[tmg_d, rgf_use.get("mg")])
            gd_use[sl] = tok_max(t_gfr, t_mgr)
            t_wb[wp] = (t_gfr, t_mgr)

        def xload(wp, init=None):
            t_lx = {}
            for sc in range(SCN):
                t_lx[sc] = prog.op(SY if sc % 2 == 0 else G, "dma_start", out=self.z_t[sc][:],
                                   in_=self.x_scan[sc * 128:(sc + 1) * 128, wp * WP:(wp + 1) * WP],
                                   waits=[init[sc] if init is not None else zuse.get(("z", sc))])
            return t_lx

        gload(0)
        init0 = {sc: tok_max(self.xs_free.get((sc, 0)), self.xs_free.get((sc, 1)))
                 for sc in range(SCN)}
        t_lx = xload(0, init=init0)
        t_lx_next = {}
        derive(0, extra=list(t_lx.values()))

        for wp in range(NW // 2):
            if wp > 0:
                t_lx = t_lx_next
                t_lx_next = {}
            t_gfr, t_mgr = t_wb[wp]
            if wp == 0:
                sp1 = prog.op(SY, "dma_start", out=self.stats_in[0:1, 0:WP], in_=self.rgf[0:1, :],
                              waits=[t_gfr])
                sp2 = prog.op(SY, "dma_start", out=self.stats_in[1:2, 0:WP], in_=self.rmg[0:1, :],
                              waits=[t_mgr, sp1])
                t_gfr = tok_max(t_gfr, sp2)
                t_mgr = tok_max(t_mgr, sp2)
            z_parts = {}
            for h in range(2):
                t_og = prog.op(TE, "matmul", self.ps_o1[:], self.ones[0:1, 0:128],
                               self.rgf[0:1, h * WT:(h + 1) * WT],
                               waits=[t_gfr, self.t_ones, zuse.get("o1")])
                t_omg = prog.op(TE, "matmul", self.ps_o2[:], self.ones[0:1, 0:128],
                                self.rmg[0:1, h * WT:(h + 1) * WT],
                                waits=[t_mgr, self.t_ones, zuse.get("o2")])
                t_og = t_omg = self.pe_drain(t_og, t_omg)
                rgf_use["gf"] = t_og
                rgf_use["mg"] = t_og
                tcons = []
                for sc in range(SCN):
                    zh = self.z_t[sc].ap()[:, h * WT:(h + 1) * WT]
                    t1 = prog.op(V, "tensor_tensor", zh, zh, self.ps_o1[:],
                                 op=ALU.mult, waits=[t_lx[sc], t_og])
                    t2 = prog.op(V, "tensor_tensor", zh, zh, self.ps_o2[:], op=ALU.subtract,
                                 waits=[t1])
                    tcons.append(t2)
                    z_parts[(sc, h)] = t2
                zuse["o1"] = tok_max(*tcons)
                zuse["o2"] = zuse["o1"]
            tz_w = {sc: tok_max(z_parts[(sc, 0)], z_parts[(sc, 1)]) for sc in range(SCN)}
            if "z" in self.dbg:
                if "z" not in self.taps:
                    self.tap("z", [DH, T])
                for sc in range(SCN):
                    tz_d = prog.op(SY, "dma_start",
                                   out=self.taps["z"][sc * 128:(sc + 1) * 128, wp * WP:(wp + 1) * WP],
                                   in_=self.z_t[sc][:], waits=[tz_w[sc]])
                    tz_w[sc] = tok_max(tz_w[sc], tz_d)

            if wp + 1 < NW // 2:
                gload(wp + 1)
                derive(wp + 1)

            ybounce = []
            for sc in range(SCN):
                yt = self.y_t[sc % 2]
                acc_prev = None
                prb_toks = []
                for n in range(NS):
                    ci = sc * NS + n
                    k = combo % 2
                    t_pr = prog.op(A_, "activation", self.prb[k][:], self.z_t[sc][:], AF.Identity,
                                   scale=self.CdB[:, ci:ci + 1],
                                   waits=[tz_w[sc], self.t_cdb, pruse.get(k)])
                    prb_toks.append(t_pr)
                    t_da = prog.op(A_, "activation", self.dab[k][:], self.prb[k][:], AF.Identity,
                                   scale=0.0, bias=self.dAv[:, ci:ci + 1],
                                   waits=[t_pr, self.t_dA, dause.get(k)])
                    init = 0.0 if wp == 0 else self.states[:, ci:ci + 1]
                    t_sc = prog.op(V, "tensor_tensor_scan", self.s_out[k][:], self.dab[k][:],
                                   self.prb[k][:], init, op0=ALU.mult, op1=ALU.add,
                                   waits=[t_da, t_pr, sduse.get(k), self.t_st0] +
                                         ([stuse[ci]] if ci in stuse else []))
                    t_cp = prog.op(V, "tensor_scalar", self.states[:, ci:ci + 1],
                                   self.s_out[k][:, WP - 1:WP], 1.0, None, op0=ALU.mult,
                                   waits=[t_sc])
                    stuse[ci] = t_cp
                    dause[k] = t_sc
                    pruse[k] = t_sc
                    if n == 0:
                        t_ac = prog.op(V, "scalar_tensor_tensor", yt[:], self.z_t[sc][:],
                                       self.sD[:, sc:sc + 1], self.s_out[k][:],
                                       op0=ALU.mult, op1=ALU.add,
                                       waits=[t_sc, tz_w[sc], self.t_Df, yfree.get(sc % 2)])
                    else:
                        t_ac = prog.op(V, "tensor_tensor", yt[:], yt[:], self.s_out[k][:],
                                       op=ALU.add, waits=[t_sc, acc_prev])
                    sduse[k] = tok_max(t_ac, t_cp)
                    acc_prev = t_ac
                    combo += 1
                zuse[("z", sc)] = tok_max(acc_prev, tz_w[sc], *prb_toks)
                ybt = []
                for h in range(2):
                    w = 2 * wp + h
                    t_yd = prog.op(SY, "dma_start",
                                   out=self.yag_in[w][sc * 128:(sc + 1) * 128, :],
                                   in_=yt[:, h * WT:(h + 1) * WT], waits=[acc_prev])
                    ybt.append(t_yd)
                    ybounce.append((w, t_yd))
                yfree[sc % 2] = tok_max(*ybt)
                if wp + 1 < NW // 2:
                    t_lx_next[sc] = prog.op(
                        SY if sc % 2 == 0 else G, "dma_start", out=self.z_t[sc][:],
                        in_=self.x_scan[sc * 128:(sc + 1) * 128, (wp + 1) * WP:(wp + 2) * WP],
                        waits=[zuse.get(("z", sc))])
            for h in range(2):
                w = 2 * wp + h
                toks = [t for (ww, t) in ybounce if ww == w]
                self.ywtok[w] = prog.op(G, "collective_compute", "AllGather", ALU.bypass,
                                        replica_groups=PAIRS, ins=[self.yag_in[w].ap()],
                                        outs=[self.yag_out[w].ap()], waits=[tok_max(*toks)])
                if "yag" in self.dbg:
                    if "yag" not in self.taps:
                        self.tap("yag", [D, T])
                    prog.op(SY, "dma_start", out=self.taps["yag"][:, w * WT:(w + 1) * WT],
                            in_=self.yag_out[w].ap(), waits=[self.ywtok[w]])
        self.zuse_final = dict(zuse)

    def ph_x1(self):
        prog = self.prog
        row = self.row
        R_TMP = self.R_TMP
        self.t_x1 = {}
        self.t_x1ln = {}
        yx_use = {}
        ost_use, od_use, rr_use = {}, {}, {}
        rowtmp_use = {}
        t_trd = {}
        o_use = {"o1": self.zuse_final.get("o1"), "o2": self.zuse_final.get("o2")}

        def oload(w):
            sl = w % 2
            base = T + 1024 * sl
            def mk_r(w=w, base=base):
                def fn(e):
                    return ((), dict(
                        out=self.rows[:, base:base + 4 * OT].rearrange("p (r c) -> p r c", r=4),
                        in_=self.stats_out.rearrange("(a r) c -> a r c", a=1)[:, :, bass.ts(self.hcell["hrv"] + 2 * w, OT)]))
                return fn
            t_row = prog.op(SY, "dma_start", argfn=mk_r(),
                            waits=[self.t_sag, rowtmp_use.get(sl, self.rows_use)])
            rr = []
            for r in range(4):
                rr.append(prog.op(SY, "dma_start", out=self.ostat[sl][0:32, r * 8:(r + 1) * 8],
                                  in_=self.rows[0:1, base + r * OT:base + (r + 1) * OT],
                                  waits=[t_row, ost_use.get(sl)]))
            t_trd[w] = tok_max(*rr)
            rowtmp_use[sl] = t_trd[w]

        oload(0)
        for w in range(NW):
            sl = w % 2
            def mk_yread(w=w):
                def fn(e):
                    src_ap = self.yag_out[w].rearrange("(dc p) t -> p dc t", p=128)
                    return ((), dict(out=self.yall.ap().rearrange("p (dc t) -> p dc t", dc=DCN),
                                     in_=src_ap[:, :, bass.ts(self.hcell["hrv"], OT)]))
                return fn
            t_yall = prog.op(SY, "dma_start", argfn=mk_yread(),
                             waits=[self.ywtok[w], yx_use.get("yall")])
            t_xoall = prog.op(SY, "dma_start",
                              out=self.xoall.ap().rearrange("p (dc t) -> p dc t", dc=DCN),
                              in_=self.x_own.rearrange("(dc p) t -> p dc t", p=128)[:, :, w * OT:(w + 1) * OT],
                              waits=[yx_use.get("xoall")])
            t_y = {dc: t_yall for dc in range(DCN)}
            t_xo = {dc: t_xoall for dc in range(DCN)}
            ost, od = self.ostat[sl], self.od[sl]
            trd = t_trd[w]
            if w + 1 < NW:
                oload(w + 1)
            tS1 = prog.op(V, "tensor_tensor", od[:, 0:8], ost[:, 0:8], ost[:, 16:24],
                          op=ALU.add, waits=[trd, od_use.get(sl)] +
                          ([t_xoall] if w == 0 else []))
            tS2 = prog.op(V, "tensor_tensor", od[:, 8:16], ost[:, 8:16], ost[:, 24:32],
                          op=ALU.add, waits=[trd, od_use.get(sl)])
            ost_use[sl] = tok_max(tS1, tS2)
            tm = prog.op(V, "tensor_scalar", od[:, 0:8], od[:, 0:8], 1.0 / D, None,
                         op0=ALU.mult, waits=[tS1])
            tex = prog.op(V, "tensor_scalar", od[:, 8:16], od[:, 8:16], 1.0 / D, None,
                          op0=ALU.mult, waits=[tS2])
            tm2 = prog.op(V, "tensor_tensor", od[:, 16:24], od[:, 0:8], od[:, 0:8],
                          op=ALU.mult, waits=[tm, tS2])
            tvar = prog.op(V, "tensor_tensor", od[:, 8:16], od[:, 8:16], od[:, 16:24],
                           op=ALU.subtract, waits=[tex, tm2])
            tl1 = prog.op(A_, "activation", od[:, 24:32], od[:, 8:16], AF.Ln,
                          bias=self.cbias[0:32, 0:1], waits=[tvar, self.t_cbias])
            trstd = prog.op(A_, "activation", od[:, 24:32], od[:, 24:32], AF.Exp, scale=-0.5,
                            waits=[tl1])
            tmr = prog.op(V, "tensor_tensor", od[:, 8:16], od[:, 0:8], od[:, 24:32],
                          op=ALU.mult, waits=[trstd, tm])
            rrow = self.rrow.ap()[0:1, :]
            mrow = self.mrow.ap()[0:1, :]
            trstd_d = self.eng_drain(A_, trstd)
            tmr_d = self.eng_drain(V, tmr)
            t_rr = prog.op(SY, "dma_start", out=rrow, in_=od[:, 24:32],
                           waits=[trstd_d, rr_use.get("r")])
            t_mr = prog.op(SY, "dma_start", out=mrow, in_=od[:, 8:16],
                           waits=[tmr_d, rr_use.get("m")])
            od_use[sl] = tok_max(t_rr, t_mr)
            t_or = prog.op(TE, "matmul", self.ps_o1[:, 0:OT], self.ones[0:1, 0:128],
                           rrow, waits=[t_rr, o_use.get("o1")])
            t_omr = prog.op(TE, "matmul", self.ps_o2[:, 0:OT], self.ones[0:1, 0:128],
                            mrow, waits=[t_mr, o_use.get("o2")])
            t_or = t_omr = self.pe_drain(t_or, t_omr)
            rr_use["r"] = t_or
            rr_use["m"] = t_or
            sq_mm = None
            p1l = []
            for dc in range(DCN):
                xr = self.xscr[dc % 2]
                yr = self.xscr[2 + dc % 2]
                p1 = prog.op(V, "tensor_tensor", xr[:], self.xoall[:, dc * OT:(dc + 1) * OT],
                             self.ps_o1[:, 0:OT],
                             op=ALU.mult, waits=[t_xo[dc], t_or, yx_use.get(("s", dc % 2))])
                p2 = prog.op(V, "tensor_tensor", yr[:], self.yall[:, dc * OT:(dc + 1) * OT],
                             self.ps_o2[:, 0:OT],
                             op=ALU.subtract, waits=[t_y[dc], t_omr, yx_use.get(("s2", dc % 2))])
                tx1 = prog.op(P_, "tensor_tensor", self.x1[dc][:, w * OT:(w + 1) * OT], xr[:], yr[:],
                              op=ALU.add, waits=[p1, p2])
                self.t_x1[(w, dc)] = tx1
                yx_use["yall"] = tok_max(yx_use.get("yall"), p2)
                yx_use["xoall"] = tok_max(yx_use.get("xoall"), p1)
                yx_use[("s", dc % 2)] = tx1
                yx_use[("s2", dc % 2)] = tx1
                p1l += [p1, p2]
                t_sq = prog.op(A_, "activation", self.cstg[dc % 2][:, 0:OT],
                               self.x1[dc][:, w * OT:(w + 1) * OT], AF.Square,
                               waits=[tx1, self.use_cstg.get(dc % 2)])
                sq_mm = prog.op(TE, "matmul", self.ps_s2[:, 0:OT], self.onesbf[:, 0:1],
                                self.cstg[dc % 2][:, 0:OT],
                                start=(dc == 0), stop=(dc == DCN - 1),
                                waits=[t_sq] + ([self.use_ps2] if dc == 0 else []))
                self.use_cstg[dc % 2] = sq_mm
            o_use["o1"] = tok_max(*p1l)
            o_use["o2"] = o_use["o1"]
            sq_mm = self.pe_drain(sq_mm)
            t_es = prog.op(A_, "activation", row(R_TMP, OT), self.ps_s2[:, 0:OT], AF.Identity,
                           waits=[sq_mm, self.rows_use if w == 0 else None])
            self.use_ps2 = t_es
            tlf = prog.op(A_, "activation", row(R_TMP, OT), row(R_TMP, OT), AF.Ln, scale=1.0 / D,
                          bias=self.cbias[0:1, 2:3], waits=[t_es, self.t_cbias])
            tfac = prog.op(A_, "activation", row(R_TMP, OT), row(R_TMP, OT), AF.Exp, scale=-0.5, waits=[tlf])
            t_of = prog.op(TE, "matmul", self.ps_o3[:, 0:OT], self.ones[0:1, 0:128], row(R_TMP, OT),
                           waits=[tfac, o_use.get("o3")])
            t_of = self.pe_drain(t_of)
            lnt = []
            for dc in range(DCN):
                t_ln = prog.op(V, "scalar_tensor_tensor", self.x1ln[dc][:, w * OT:(w + 1) * OT],
                               self.x1[dc][:, w * OT:(w + 1) * OT], self.s_s2[:, dc:dc + 1],
                               self.ps_o3[:, 0:OT], op0=ALU.mult, op1=ALU.mult,
                               waits=[self.t_x1[(w, dc)], t_of])
                self.t_x1ln[(w, dc)] = t_ln
                lnt.append(t_ln)
            o_use["o3"] = tok_max(*lnt)
        if "x1" in self.dbg:
            tp = self.tap("x1", [D, TOK])
            for dc in range(DCN):
                prog.op(SY, "dma_start", out=tp[dc * 128:(dc + 1) * 128, :], in_=self.x1[dc][:],
                        waits=[tok_max(*[self.t_x1[(w, dc)] for w in range(NW)])])
        if "x1ln" in self.dbg:
            tp = self.tap("x1ln", [D, TOK], BF16)
            for dc in range(DCN):
                prog.op(SY, "dma_start", out=tp[dc * 128:(dc + 1) * 128, :], in_=self.x1ln[dc][:],
                        waits=[tok_max(*[self.t_x1ln[(w, dc)] for w in range(NW)])])

    def dequant_chunks(self):
        units = []
        for key, nat, outp, KCtot, MC, I in (
                ("u", self.dr["wun"], self.dr["wqu"], DCN, ICN, D),
                ("g", self.dr["wgn"], self.dr["wqg"], DCN, ICN, D),
                ("d", self.dr["wdn"], self.dr["wqd"], ICN, DCN, DI)):
            nsub = I // D
            for oc in range(MC):
                for s in range(nsub):
                    units.append((key, nat, outp, KCtot, s * DCN, oc, s == nsub - 1))
        return units

    def emit_dequant(self, unit):
        prog = self.prog
        key, nat, outp, KCtot, kc0, oc, last = unit
        nb = D // 32
        st = self.dq_state
        t_l = prog.op(SY, "dma_start", out=self.dq_w[:],
                      in_=nat[oc * 128:(oc + 1) * 128, kc0 * 128:kc0 * 128 + D],
                      waits=[st.get("w")])
        w3 = self.dq_w.ap().rearrange("p (b k) -> p b k", k=32)
        t_bm = prog.op(V, "reduce_max", self.dq_s[:, 0:nb], w3, axis=AX.X,
                       apply_absolute_value=True, waits=[t_l, st.get("s")])
        t_s = prog.op(V, "tensor_scalar", self.dq_s[:, 32:32 + nb], self.dq_s[:, 0:nb],
                      1.0 / 7.0, 1e-12, op0=ALU.mult, op1=ALU.max, waits=[t_bm, st.get("s2")])
        t_r = prog.op(V, "reciprocal", self.dq_s[:, 64:64 + nb], self.dq_s[:, 32:32 + nb],
                      waits=[t_s, st.get("r")])
        rb = self.dq_s.ap()[:, 64:64 + nb].unsqueeze(2).broadcast_to([128, nb, 32])
        sb_ = self.dq_s.ap()[:, 32:32 + nb].unsqueeze(2).broadcast_to([128, nb, 32])
        l3 = self.dq_l.ap().rearrange("p (b k) -> p b k", k=32)
        q3 = self.dq_q.ap().rearrange("p (b k) -> p b k", k=32)
        t_lv = prog.op(V, "tensor_tensor", l3, w3, rb, op=ALU.mult, waits=[t_r, st.get("l")])
        t_q = prog.op(P_, "tensor_tensor", q3, l3, sb_, op=ALU.mult, waits=[t_lv, st.get("q")])
        st["w"] = t_lv
        st["s"] = t_lv
        st["s2"] = tok_max(t_lv, t_q)
        st["r"] = t_lv
        st["l"] = t_q
        t_q = self.eng_drain(P_, t_q)
        evs = []
        for g4 in range(DCN // 4):
            t_t = None
            for j in range(4):
                kc = g4 * 4 + j
                t_t = prog.op(TE, "matmul", self.ps_t[:, j * 128:(j + 1) * 128],
                              self.dq_q[:, kc * 128:(kc + 1) * 128],
                              self.identb[:], is_transpose=True,
                              waits=[t_q, self.t_identb] + ([st.get(("pt",))] if j == 0 else []))
            t_t = self.pe_drain(t_t)
            t_e = prog.op(A_, "activation", self.dq_st[:, g4 * 512:(g4 + 1) * 512], self.ps_t,
                          AF.Identity, waits=[t_t, st.get(("st", g4))])
            st[("pt",)] = t_e
            evs.append(t_e)
            st[("st", g4)] = t_e
        rows0 = (oc * KCtot + kc0) * 128
        t_o = prog.op(SY, "dma_start",
                      out=outp[rows0:rows0 + DCN * 128, :].rearrange("(kc p) f -> p kc f", p=128),
                      in_=self.dq_st.ap().rearrange("p (kc f) -> p kc f", kc=DCN),
                      waits=[tok_max(*evs)])
        st["q"] = t_o
        for g4 in range(DCN // 4):
            st[("st", g4)] = tok_max(st[("st", g4)], t_o)
        if last:
            self.t_wq[(key, oc)] = t_o
        else:
            self.t_wq[(key, oc)] = tok_max(self.t_wq.get((key, oc)), t_o)

    def ph_attn(self):
        prog = self.prog
        row = self.row
        fence_a = prog.fence()
        self.dq_state = {}
        self.t_wq = {}
        self.t_identb = prog.op(SY, "dma_start", out=self.identb[:], in_=self.dr["identb"],
                                waits=[fence_a])
        dq_units = self.dequant_chunks()
        dq_pos = 0
        w1_use, w2_use, g1_use, ps_use, ev_use, xm_use, xo_use = {}, {}, {}, {}, {}, {}, {}
        self.t_xm = {}
        use_ps1l = self.use_ps1
        use_ps2l = self.use_ps2
        self.lnS_tok = []
        for q in range(NQ):
            x1ln_tok = self.eng_drain(V, *[self.t_x1ln[(w, dc)] for dc in range(DCN) for w in (2 * q, 2 * q + 1)])
            for fg in range(FC // 2):
                sl = fg % 2
                t_w = prog.op(G, "dma_start", out=slab_out(self.w1s[sl], 2, DCN), in_=slab_ap(self.dr["wp1"], fg * 2, 2, DCN),
                              waits=[w1_use.get(sl)] + ([fence_a] if (q == 0 and fg < 2) else []))
                gel = []
                for mc in range(2):
                    ffc = fg * 2 + mc
                    pb = [self.ps_a, self.ps_b][ffc % 2]
                    mml = None
                    for dc in range(DCN):
                        mml = prog.op(TE, "matmul", pb[:],
                                      self.w1s[sl][:, (mc * DCN + dc) * 128:(mc * DCN + dc + 1) * 128],
                                      self.x1ln[dc][:, q * QT:(q + 1) * QT],
                                      start=(dc == 0), stop=(dc == DCN - 1),
                                      waits=[t_w, x1ln_tok] + ([ps_use.get(ffc % 2)] if dc == 0 else []))
                    mml = self.pe_drain(mml)
                    t_g = prog.op(A_, "activation", self.g1bf[:, ffc * QT:(ffc + 1) * QT], pb[:],
                                  AF.Gelu, bias=self.s_b1[:, ffc:ffc + 1],
                                  waits=[mml, g1_use.get(ffc)])
                    ps_use[ffc % 2] = t_g
                    g1_use[ffc] = t_g
                    gel.append(t_g)
                w1_use[sl] = tok_max(*gel)
            g1_all = self.eng_drain(A_, *[g1_use[f] for f in range(FC)])
            mmA = mmB = None
            for dc in range(DCN):
                sl = dc % 2
                t_w = prog.op(G, "dma_start", out=slab_out(self.w2s[sl], 1, FC), in_=slab_ap(self.dr["wp2"], dc, 1, FC),
                              waits=[w2_use.get(sl)] + ([fence_a] if (q == 0 and dc < 2) else []))
                pb = [self.ps_a, self.ps_b][dc % 2]
                mml = None
                for kc in range(FC):
                    mml = prog.op(TE, "matmul", pb[:], self.w2s[sl][:, kc * 128:(kc + 1) * 128],
                                  self.g1bf[:, kc * QT:(kc + 1) * QT],
                                  start=(kc == 0), stop=(kc == FC - 1),
                                  waits=[t_w, g1_all] + ([ps_use.get(dc % 2)] if kc == 0 else []))
                w2_use[sl] = mml
                mml = self.pe_drain(mml)
                t_ev = prog.op(A_, "activation", self.evscr[sl][:], pb[:], AF.Identity,
                               bias=self.s_b2[:, dc:dc + 1], waits=[mml, ev_use.get(sl)])
                ps_use[dc % 2] = t_ev
                t_xo = prog.op(SY, "dma_start", out=self.xo2[sl][:],
                               in_=self.x_own[dc * 128:(dc + 1) * 128, q * QT:(q + 1) * QT],
                               waits=[xo_use.get(sl)] + ([fence_a] if (q == 0 and dc < 2) else []))
                x1tok = tok_max(self.t_x1ln[(2 * q, dc)], self.t_x1ln[(2 * q + 1, dc)])
                t_p = prog.op(P_, "tensor_tensor", self.xmscr[sl][:], self.evscr[sl][:],
                              self.x1[dc][:, q * QT:(q + 1) * QT], op=ALU.add,
                              waits=[t_ev, x1tok, xm_use.get(sl)])
                txm = prog.op(V, "tensor_tensor", self.x1[dc][:, q * QT:(q + 1) * QT],
                              self.xmscr[sl][:], self.xo2[sl][:], op=ALU.add,
                              waits=[t_p, t_xo])
                self.t_xm[(q, dc)] = txm
                ev_use[sl] = t_p
                xm_use[sl] = txm
                xo_use[sl] = txm
                t_cb = prog.op(A_, "activation", self.cstg[dc % 2][:], self.x1[dc][:, q * QT:(q + 1) * QT],
                               AF.Identity, waits=[txm, self.use_cstg.get(dc % 2)])
                mmA = prog.op(TE, "matmul", self.ps_s1[:], self.onesbf[:, 0:1], self.cstg[dc % 2][:],
                              start=(dc == 0), stop=(dc == DCN - 1),
                              waits=[t_cb] + ([use_ps1l] if dc == 0 else []))
                t_sq = prog.op(A_, "activation", self.cstg[dc % 2][:], self.x1[dc][:, q * QT:(q + 1) * QT],
                               AF.Square, waits=[txm, mmA])
                mmB = prog.op(TE, "matmul", self.ps_s2[:], self.onesbf[:, 0:1], self.cstg[dc % 2][:],
                              start=(dc == 0), stop=(dc == DCN - 1),
                              waits=[t_sq] + ([use_ps2l] if dc == 0 else []))
                self.use_cstg[dc % 2] = mmB
            t_mm = self.pe_drain(mmA, mmB)
            tE1 = prog.op(A_, "activation", self.rows[:, self.L_S1 + q * QT:self.L_S1 + (q + 1) * QT],
                          self.ps_s1[:], AF.Identity, waits=[t_mm, self.rows_use])
            tE2 = prog.op(A_, "activation", self.rows[:, self.L_S2 + q * QT:self.L_S2 + (q + 1) * QT],
                          self.ps_s2[:], AF.Identity, waits=[t_mm])
            use_ps1l, use_ps2l = tE1, tE2
            self.lnS_tok += [tE1, tE2]
            n_do = 18 if q < NQ - 1 else len(dq_units) - dq_pos
            for _ in range(n_do):
                if dq_pos < len(dq_units):
                    self.emit_dequant(dq_units[dq_pos])
                    dq_pos += 1
        if "xmid" in self.dbg:
            tp = self.tap("xmid", [D, TOK])
            for dc in range(DCN):
                prog.op(SY, "dma_start", out=tp[dc * 128:(dc + 1) * 128, :], in_=self.x1[dc][:],
                        waits=[tok_max(*[self.t_xm[(q, dc)] for q in range(NQ)])])

    def ph_ln2_m1_xq_pre(self):
        prog = self.prog
        L_S1, L_S2, L_M2R = self.L_S1, self.L_S2, self.L_M2R
        tS = tok_max(*self.lnS_tok)
        t1 = prog.op(V, "tensor_scalar", self.rows[:, L_S1:L_S1 + TOK], self.rows[:, L_S1:L_S1 + TOK],
                     1.0 / D, None, op0=ALU.mult, waits=[tS])
        t2 = prog.op(V, "tensor_scalar", self.rows[:, L_S2:L_S2 + TOK], self.rows[:, L_S2:L_S2 + TOK],
                     1.0 / D, None, op0=ALU.mult, waits=[tS])
        t3 = prog.op(V, "tensor_tensor", self.rows[:, L_M2R:L_M2R + TOK], self.rows[:, L_S1:L_S1 + TOK],
                     self.rows[:, L_S1:L_S1 + TOK], op=ALU.mult, waits=[t1])
        t4 = prog.op(V, "tensor_tensor", self.rows[:, L_S2:L_S2 + TOK], self.rows[:, L_S2:L_S2 + TOK],
                     self.rows[:, L_M2R:L_M2R + TOK], op=ALU.subtract, waits=[t2, t3])
        t5 = prog.op(A_, "activation", self.rows[:, L_S2:L_S2 + TOK], self.rows[:, L_S2:L_S2 + TOK],
                     AF.Ln, bias=self.cbias[0:1, 0:1], waits=[t4, self.t_cbias])
        t6 = prog.op(A_, "activation", self.rows[:, L_S2:L_S2 + TOK], self.rows[:, L_S2:L_S2 + TOK],
                     AF.Exp, scale=-0.5, waits=[t5])
        t7 = prog.op(V, "tensor_tensor", self.rows[:, L_M2R:L_M2R + TOK], self.rows[:, L_S1:L_S1 + TOK],
                     self.rows[:, L_S2:L_S2 + TOK], op=ALU.mult, waits=[t6, t3])
        self.t_ln2rows = tok_max(t6, t7)

    def _bcast_scales(self, key, dst, col0, n=2, extra=()):
        prog = self.prog
        t_d = prog.op(SY, "dma_start", out=self.m_in[key].ap(), in_=self.srow1[0:1, 0:2],
                      waits=[self.t_maxloc])
        t_c = prog.op(G, "collective_compute", "AllReduce", ALU.max, replica_groups=ALL8,
                      ins=[self.m_in[key].ap()], outs=[self.m_out[key].ap()], waits=[t_d])
        t_r = prog.op(SY, "dma_start", out=self.srow1[:, 4:6], in_=self.m_out[key].ap(), waits=[t_c])
        ts_ = prog.op(V, "tensor_scalar", self.srow1[:, 8:8 + n], self.srow1[:, 4:4 + n],
                      1.0 / QMAX, 1e-12, op0=ALU.mult, op1=ALU.max, waits=[t_r])
        tr_ = prog.op(V, "reciprocal", self.srow1[:, 8 + n:8 + 2 * n], self.srow1[:, 8:8 + n], waits=[ts_])
        t_o = prog.op(TE, "matmul", self.ps_o3[:, 0:2 * n], self.ones[0:1, 0:128],
                      self.srow1[0:1, 8:8 + 2 * n],
                      waits=[tr_, self.t_ones, self.ps_o3_use] + list(extra))
        t_o = self.pe_drain(t_o)
        t_e = prog.op(A_, "activation", dst[:, col0:col0 + 2 * n], self.ps_o3[:, 0:2 * n],
                      AF.Identity, waits=[t_o])
        self.ps_o3_use = t_e
        return t_e

    def ph_ln2_m1_xq(self):
        prog = self.prog
        L_S2, L_M2R = self.L_S2, self.L_M2R
        self.ps_o3_use = None
        fence = self.fence_ffn
        ln_use = {}
        o_use = {}
        mx = []
        ps_use = {}
        for q in range(NQ):
            t_or = prog.op(TE, "matmul", self.ps_o1[:], self.ones[0:1, 0:128],
                           self.rows[:, L_S2 + q * QT:L_S2 + (q + 1) * QT],
                           waits=[self.t_ln2rows, o_use.get("o1"), fence])
            t_om = prog.op(TE, "matmul", self.ps_o2[:], self.ones[0:1, 0:128],
                           self.rows[:, L_M2R + q * QT:L_M2R + (q + 1) * QT],
                           waits=[self.t_ln2rows, o_use.get("o2")])
            t_or = t_om = self.pe_drain(t_or, t_om)
            tq = []
            for dc in range(DCN):
                t1 = prog.op(V, "tensor_tensor", self.qf[dc % 2][:], self.x1[dc][:, q * QT:(q + 1) * QT],
                             self.ps_o1[:], op=ALU.mult,
                             waits=[self.t_xm[(q, dc)], t_or, ln_use.get(("qf", dc % 2)), fence])
                t2 = prog.op(V, "tensor_tensor", self.ln2[dc][:, q * QT:(q + 1) * QT], self.qf[dc % 2][:],
                             self.ps_o2[:], op=ALU.subtract, waits=[t1, t_om])
                ln_use[("qf", dc % 2)] = t2
                t3 = prog.op(V, "reduce_max", self.maxc[:, q * DCN + dc:q * DCN + dc + 1],
                             self.ln2[dc][:, q * QT:(q + 1) * QT], axis=AX.X,
                             apply_absolute_value=True, waits=[t2])
                ln_use[(q, dc)] = t2
                tq.append(t3)
                mx.append(t3)
            o_use["o1"] = tok_max(*tq)
            o_use["o2"] = o_use["o1"]
        t_mr = prog.op(V, "reduce_max", self.maxc[:, 32:33], self.maxc[:, 0:32], axis=AX.X,
                       waits=[tok_max(*mx)])
        t_pc = prog.op(G, "tensor_reduce", self.srow1[0:1, 0:1], self.maxc[:, 32:33], axis=AX.C,
                       op=ALU.max, waits=[t_mr])
        t_cp = prog.op(V, "tensor_scalar", self.srow1[:, 1:2], self.srow1[:, 0:1], 1.0, None,
                       op0=ALU.mult, waits=[t_pc])
        self.t_maxloc = tok_max(t_pc, t_cp)
        t_sc = self._bcast_scales("m1", self.scal, 0)
        self.t_xq = {}
        lvl_use = {}
        for q in range(NQ):
            for dc in range(DCN):
                r = dc % 4
                tl = prog.op(V, "tensor_scalar", self.ql[r][:], self.ln2[dc][:, q * QT:(q + 1) * QT],
                             self.scal[:, 2:3], None, op0=ALU.mult,
                             waits=[ln_use[(q, dc)], t_sc, lvl_use.get(r)])
                tb = prog.op(V, "tensor_scalar", self.xqb[dc][:, q * QT:(q + 1) * QT], self.ql[r][:],
                             self.scal[:, 0:1], None, op0=ALU.mult, waits=[tl])
                lvl_use[r] = tb
                self.t_xq[(q, dc)] = tb
        if "ln2" in self.dbg:
            tp = self.tap("ln2", [D, TOK])
            for dc in range(DCN):
                prog.op(SY, "dma_start", out=tp[dc * 128:(dc + 1) * 128, :], in_=self.ln2[dc][:],
                        waits=[tok_max(*[self.t_xq[(q, dc)] for q in range(NQ)])])
        if "xq" in self.dbg:
            tp = self.tap("xq", [D, TOK], BF16)
            for dc in range(DCN):
                prog.op(SY, "dma_start", out=tp[dc * 128:(dc + 1) * 128, :], in_=self.xqb[dc][:],
                        waits=[tok_max(*[self.t_xq[(q, dc)] for q in range(NQ)])])

    def ph_upgate(self):
        prog = self.prog
        sub = self.stop - 7 if self.stop < 9 else 99
        wu_use, wg_use, ps_use, st_use = {}, {}, {}, {}
        mx = []
        xq_all = self.eng_drain(V, *self.t_xq.values())
        for ig in range(ICN // 2):
            sl = ig % 2
            t_wu = prog.op(SY, "dma_start", out=slab_out(self.wus[sl], 2, DCN),
                           in_=slab_ap(self.dr["wqu"], ig * 2, 2, DCN),
                           waits=[wu_use.get(sl), self.t_wq[("u", ig * 2)], self.t_wq[("u", ig * 2 + 1)],
                                  self.fence_q if ig < 2 else None])
            t_wg = prog.op(SY, "dma_start", out=slab_out(self.wgs[sl], 2, DCN),
                           in_=slab_ap(self.dr["wqg"], ig * 2, 2, DCN),
                           waits=[wg_use.get(sl), self.t_wq[("g", ig * 2)], self.t_wq[("g", ig * 2 + 1)],
                                  self.fence_q if ig < 2 else None])
            lastu, lastg = [], []
            if sub < 0.4:
                wu_use[sl] = t_wu
                wg_use[sl] = t_wg
                continue
            for mc in range(2):
                ic = ig * 2 + mc
                for q in range(NQ):
                    k = (ic * NQ + q) % 2
                    pbu = [self.ps_a, self.ps_o1][k]
                    pbg = [self.ps_b, self.ps_o2][k]
                    mmu = mmg = None
                    for dc in range(DCN):
                        mmu = prog.op(TE, "matmul", pbu[:],
                                      self.wus[sl][:, (mc * DCN + dc) * 128:(mc * DCN + dc + 1) * 128],
                                      self.xqb[dc][:, q * QT:(q + 1) * QT],
                                      start=(dc == 0), stop=(dc == DCN - 1),
                                      waits=[t_wu, xq_all] + ([ps_use.get(("u", k))] if dc == 0 else []))
                    for dc in range(DCN):
                        mmg = prog.op(TE, "matmul", pbg[:],
                                      self.wgs[sl][:, (mc * DCN + dc) * 128:(mc * DCN + dc + 1) * 128],
                                      self.xqb[dc][:, q * QT:(q + 1) * QT],
                                      start=(dc == 0), stop=(dc == DCN - 1),
                                      waits=[t_wg, xq_all] + ([ps_use.get(("g", k))] if dc == 0 else []))
                    if sub < 0.6:
                        ev = prog.op(A_, "activation", self.stg[0][:], pbu[:], AF.Identity,
                                     waits=[mmu, st_use.get("x"), self.fence_q])
                        ev2 = prog.op(A_, "activation", self.stg[1][:], pbg[:], AF.Identity,
                                      waits=[mmg, st_use.get("y"), self.fence_q])
                        st_use["x"], st_use["y"] = ev, ev2
                        ps_use[("u", k)], ps_use[("g", k)] = ev, ev2
                        lastu.append(mmu)
                        lastg.append(mmg)
                        continue
                    t_dr = self.pe_drain(mmu, mmg)
                    mmu = tok_max(mmu, t_dr)
                    mmg = tok_max(mmg, t_dr)
                    su = self.stg[0]
                    sg = self.stg[1]
                    teu = prog.op(A_, "activation", su[:, q * QT:(q + 1) * QT], pbu[:], AF.Identity,
                                  waits=[mmu, (st_use.get(0) if q == 0 else None), self.fence_q])
                    teg = prog.op(A_, "activation", sg[:, q * QT:(q + 1) * QT], pbg[:], AF.Identity,
                                  waits=[mmg, (st_use.get(1) if q == 0 else None), self.fence_q])
                    cu = 64 + ic * NQ + q
                    cg = 160 + ic * NQ + q
                    tmu = prog.op(V, "reduce_max", self.maxc[:, cu:cu + 1], pbu[:], axis=AX.X,
                                  apply_absolute_value=True, waits=[mmu, teu])
                    tmg = prog.op(V, "reduce_max", self.maxc[:, cg:cg + 1], pbg[:], axis=AX.X,
                                  apply_absolute_value=True, waits=[mmg, teg])
                    ps_use[("u", k)] = tok_max(teu, tmu)
                    ps_use[("g", k)] = tok_max(teg, tmg)
                    mx += [tmu, tmg]
                    lastu.append(mmu)
                    lastg.append(mmg)
                    if sub < 0.8:
                        continue
                    if q == NQ - 1:
                        du = prog.op(SY, "dma_start", out=self.sp_up[ic * 128:(ic + 1) * 128, :],
                                     in_=su[:], waits=[teu])
                        dg = prog.op(SY, "dma_start", out=self.sp_gate[ic * 128:(ic + 1) * 128, :],
                                     in_=sg[:], waits=[teg])
                        st_use[0] = du
                        st_use[1] = dg
            wu_use[sl] = tok_max(*lastu)
            wg_use[sl] = tok_max(*lastg)
        if sub < 0.6:
            return
        tmx = tok_max(*mx)
        t1 = prog.op(V, "reduce_max", self.maxc[:, 33:34], self.maxc[:, 64:160], axis=AX.X, waits=[tmx])
        t2 = prog.op(V, "reduce_max", self.maxc[:, 34:35], self.maxc[:, 160:256], axis=AX.X, waits=[tmx])
        t3 = prog.op(G, "tensor_reduce", self.srow1[0:1, 0:1], self.maxc[:, 33:34], axis=AX.C,
                     op=ALU.max, waits=[t1])
        t4 = prog.op(G, "tensor_reduce", self.srow1[0:1, 1:2], self.maxc[:, 34:35], axis=AX.C,
                     op=ALU.max, waits=[t2])
        self.t_maxloc = tok_max(t3, t4)
        self.maxc_free = tok_max(t1, t2)
        self.t_scug = self._bcast_scales("mug", self.scal, 0, n=2, extra=[xq_all])
        if "gate" in self.dbg:
            tpg = self.tap("gate", [DI, TOK], BF16)
            tpu = self.tap("up", [DI, TOK], BF16)
            fs = prog.fence()
            prog.op(SY, "dma_start", out=tpg[:, :], in_=self.sp_gate.ap(), waits=[fs])
            prog.op(SY, "dma_start", out=tpu[:, :], in_=self.sp_up.ap(), waits=[fs])

    def _gate_chain(self, ic, q, idx, waits_extra=()):
        prog = self.prog
        a, b = idx % 4, idx % 2
        tg = prog.op(SY, "dma_start", out=self.qa[a][:],
                     in_=self.sp_gate[ic * 128:(ic + 1) * 128, q * QT:(q + 1) * QT],
                     waits=[self.qa_use.get(a)] + list(waits_extra))
        tl = prog.op(V, "tensor_scalar", self.ql[a][:], self.qa[a][:], self.scal[:, 3:4], None,
                     op0=ALU.mult, waits=[tg, self.t_scug, self.ql_use.get(a)])
        self.qa_use[a] = tl
        tgq = prog.op(A_, "activation", self.qf[a][:], self.ql[a][:], AF.Identity,
                      scale=self.scal[:, 1:2], waits=[tl, self.qf_use.get(a)])
        tsg = prog.op(A_, "activation", self.qf[4 + b][:], self.ql[a][:], AF.Sigmoid,
                      scale=self.scal[:, 1:2], waits=[tl, self.qf_use.get(4 + b)])
        self.ql_use[a] = tok_max(tgq, tsg)
        tsm = prog.op(V, "tensor_scalar", self.qm[b][:], self.qf[4 + b][:], 65536.0, None,
                      op0=ALU.mult, waits=[tsg, self.qm_use.get(b)])
        tsq = prog.op(A_, "activation", self.qf[4 + b][:], self.qm[b][:], AF.Identity,
                      scale=1.0 / 65536.0, waits=[tsm])
        self.qm_use[b] = tsq
        return tgq, tsq

    def _signed_minmax_combine(self, waits=()):
        prog = self.prog
        tM = prog.op(V, "reduce_max", self.maxc[:, 50:51], self.maxc[:, 64:160], axis=AX.X,
                     waits=list(waits))
        tN = prog.op(V, "tensor_reduce", self.maxc[:, 51:52], self.maxc[:, 160:256], axis=AX.X,
                     op=ALU.min, waits=list(waits))
        tC = prog.op(V, "scalar_tensor_tensor", self.maxc[:, 52:53], self.maxc[:, 51:52], -1.0,
                     self.maxc[:, 50:51], op0=ALU.mult, op1=ALU.max, waits=[tM, tN])
        tP = prog.op(G, "tensor_reduce", self.srow1[0:1, 0:1], self.maxc[:, 52:53], axis=AX.C,
                     op=ALU.max, waits=[tC])
        tcp = prog.op(V, "tensor_scalar", self.srow1[:, 1:2], self.srow1[:, 0:1], 1.0, None,
                      op0=ALU.mult, waits=[tP])
        self.t_maxloc = tok_max(tP, tcp)
        self.maxc_free = tok_max(tM, tN)

    def ph_passA(self):
        prog = self.prog
        self.qa_use, self.ql_use, self.qf_use, self.qm_use = {}, {}, {}, {}
        last = []
        for ic in range(ICN):
            for q in range(NQ):
                idx = ic * NQ + q
                a, b = idx % 4, idx % 2
                tgq, tsq = self._gate_chain(ic, q, idx,
                                            waits_extra=[self.fence_p] if idx < 8 else ())
                tt2 = prog.op(P_, "tensor_tensor", self.qf[6 + b][:], self.qf[a][:], self.qf[4 + b][:],
                              op=ALU.mult, waits=[tgq, tsq, self.qf_use.get(6 + b), self.maxc_free])
                self.qf_use[a] = tt2
                self.qf_use[4 + b] = tt2
                tmx = prog.op(V, "tensor_reduce", self.maxc[:, 64 + idx:65 + idx], self.qf[6 + b][:],
                              axis=AX.X, op=ALU.max, waits=[tt2, self.maxc_free])
                tmn = prog.op(V, "tensor_reduce", self.maxc[:, 160 + idx:161 + idx], self.qf[6 + b][:],
                              axis=AX.X, op=ALU.min, waits=[tt2])
                self.qf_use[6 + b] = tok_max(tmx, tmn)
                last.append(tok_max(tmx, tmn))
        self._signed_minmax_combine(waits=[tok_max(*last)])
        self.t_sc2 = self._bcast_scales("mg2", self.scal, 4, n=2)

    def ph_passB(self):
        prog = self.prog
        lastB = []
        for ic in range(ICN):
            for q in range(NQ):
                idx = ic * NQ + q
                a, b = idx % 4, idx % 2
                tgq, tsq = self._gate_chain(ic, q, idx, waits_extra=[self.t_sc2])
                tt2 = prog.op(P_, "tensor_tensor", self.qf[6 + b][:], self.qf[a][:], self.qf[4 + b][:],
                              op=ALU.mult, waits=[tgq, tsq, self.qf_use.get(6 + b)])
                self.qf_use[a] = tt2
                self.qf_use[4 + b] = tt2
                tl2 = prog.op(V, "tensor_scalar", self.ql[4 + b][:], self.qf[6 + b][:], self.scal[:, 6:7],
                              None, op0=ALU.mult, waits=[tt2, self.t_sc2, self.ql_use.get(4 + b)])
                self.qf_use[6 + b] = tl2
                tg2 = prog.op(A_, "activation", self.qf[6 + b][:], self.ql[4 + b][:], AF.Identity,
                              scale=self.scal[:, 4:5], waits=[tl2])
                self.ql_use[4 + b] = tg2
                tu = prog.op(SY, "dma_start", out=self.qa[4 + a][:],
                             in_=self.sp_up[ic * 128:(ic + 1) * 128, q * QT:(q + 1) * QT],
                             waits=[self.qa_use.get(4 + a), self.fence_p if idx < 4 else None])
                tlu = prog.op(V, "tensor_scalar", self.ql[6 + b][:], self.qa[4 + a][:], self.scal[:, 2:3],
                              None, op0=ALU.mult, waits=[tu, self.ql_use.get(6 + b)])
                self.qa_use[4 + a] = tlu
                tuq = prog.op(A_, "activation", self.qf[8 + b][:], self.ql[6 + b][:], AF.Identity,
                              scale=self.scal[:, 0:1], waits=[tlu, self.qf_use.get(8 + b)])
                self.ql_use[6 + b] = tuq
                tt3 = prog.op(P_, "tensor_tensor", self.qf[a][:], self.qf[6 + b][:], self.qf[8 + b][:],
                              op=ALU.mult, waits=[tg2, tuq, self.qf_use.get(a), self.maxc_free])
                self.qf_use[6 + b] = tt3
                self.qf_use[8 + b] = tt3
                tmx = prog.op(V, "tensor_reduce", self.maxc[:, 64 + idx:65 + idx], self.qf[a][:],
                              axis=AX.X, op=ALU.max, waits=[tt3, self.maxc_free])
                tmn = prog.op(V, "tensor_reduce", self.maxc[:, 160 + idx:161 + idx], self.qf[a][:],
                              axis=AX.X, op=ALU.min, waits=[tt3])
                self.qf_use[a] = tok_max(tmx, tmn)
                lastB.append(tok_max(tmx, tmn))
        self._signed_minmax_combine(waits=[tok_max(*lastB)])
        self.t_sco = self._bcast_scales("mo", self.scal2, 0, n=1)

    def ph_passC_down(self):
        prog = self.prog
        wd_use = {}
        obf_use = {}
        out_d = []
        ps_use = {}
        oxo_use = {}
        for q in range(NQ):
            ot = []
            for ic in range(ICN):
                idx = q * ICN + ic
                a, b = idx % 4, idx % 2
                tgq, tsq = self._gate_chain(ic, q, idx, waits_extra=[self.t_sco])
                tt2 = prog.op(P_, "tensor_tensor", self.qf[6 + b][:], self.qf[a][:], self.qf[4 + b][:],
                              op=ALU.mult, waits=[tgq, tsq, self.qf_use.get(6 + b)])
                self.qf_use[a] = tt2
                self.qf_use[4 + b] = tt2
                tl2 = prog.op(V, "tensor_scalar", self.ql[4 + b][:], self.qf[6 + b][:], self.scal[:, 6:7],
                              None, op0=ALU.mult, waits=[tt2, self.ql_use.get(4 + b)])
                self.qf_use[6 + b] = tl2
                tg2 = prog.op(A_, "activation", self.qf[6 + b][:], self.ql[4 + b][:], AF.Identity,
                              scale=self.scal[:, 4:5], waits=[tl2])
                self.ql_use[4 + b] = tg2
                tu = prog.op(SY, "dma_start", out=self.qa[4 + a][:],
                             in_=self.sp_up[ic * 128:(ic + 1) * 128, q * QT:(q + 1) * QT],
                             waits=[self.qa_use.get(4 + a)])
                tlu = prog.op(V, "tensor_scalar", self.ql[6 + b][:], self.qa[4 + a][:], self.scal[:, 2:3],
                              None, op0=ALU.mult, waits=[tu, self.ql_use.get(6 + b)])
                self.qa_use[4 + a] = tlu
                tuq = prog.op(A_, "activation", self.qf[8 + b][:], self.ql[6 + b][:], AF.Identity,
                              scale=self.scal[:, 0:1], waits=[tlu, self.qf_use.get(8 + b)])
                self.ql_use[6 + b] = tuq
                tt3 = prog.op(P_, "tensor_tensor", self.qf[a][:], self.qf[6 + b][:], self.qf[8 + b][:],
                              op=ALU.mult, waits=[tg2, tuq, self.qf_use.get(a)])
                self.qf_use[6 + b] = tt3
                self.qf_use[8 + b] = tt3
                tlo = prog.op(V, "tensor_scalar", self.ql[4 + b][:], self.qf[a][:], self.scal2[:, 1:2],
                              None, op0=ALU.mult, waits=[tt3, self.t_sco, self.ql_use.get(4 + b)])
                self.qf_use[a] = tlo
                tob = prog.op(V, "tensor_scalar", self.obf[ic][:], self.ql[4 + b][:], self.scal2[:, 0:1],
                              None, op0=ALU.mult, waits=[tlo, obf_use.get(ic)])
                self.ql_use[4 + b] = tob
                ot.append(tob)
            o_all = self.eng_drain(V, *ot)
            qdn_use = getattr(self, "qdn_use", {})
            self.qdn_use = qdn_use
            for dc in range(DCN):
                sl = dc % 2
                t_w = prog.op(SY, "dma_start", out=slab_out(self.wds[sl], 1, ICN),
                              in_=slab_ap(self.dr["wqd"], dc, 1, ICN),
                              waits=[wd_use.get(sl), self.t_wq[("d", dc)],
                                     self.fence_p if (q == 0 and dc < 2) else None])
                pb = [self.ps_a, self.ps_b, self.ps_o1, self.ps_o2][dc % 4]
                mml = None
                for kc in range(ICN):
                    mml = prog.op(TE, "matmul", pb[:], self.wds[sl][:, kc * 128:(kc + 1) * 128],
                                  self.obf[kc][:], start=(kc == 0), stop=(kc == ICN - 1),
                                  waits=[t_w, o_all] + ([ps_use.get(dc % 4)] if kc == 0 else []))
                wd_use[sl] = mml
                for ic in range(ICN):
                    obf_use[ic] = tok_max(obf_use.get(ic), mml)
                mml = self.pe_drain(mml)
                t_ev = prog.op(A_, "activation", self.oxo[dc % 2][:], pb[:], AF.Identity,
                               waits=[mml, oxo_use.get(dc % 2)])
                ps_use[dc % 4] = t_ev
                t_fin = prog.op(V, "tensor_tensor", self.qdn[dc % 2][:], self.oxo[dc % 2][:],
                                self.x1[dc][:, q * QT:(q + 1) * QT], op=ALU.add,
                                waits=[t_ev, self.t_xm[(q, dc)], qdn_use.get(dc % 2)])
                oxo_use[dc % 2] = t_fin
                if "ffnout" in self.dbg:
                    if "ffnout" not in self.taps:
                        self.tap("ffnout", [D, TOK])
                    tfo = prog.op(SY, "dma_start",
                                  out=self.taps["ffnout"][dc * 128:(dc + 1) * 128, q * QT:(q + 1) * QT],
                                  in_=self.oxo[dc % 2][:], waits=[t_ev])
                    oxo_use[dc % 2] = tok_max(t_fin, tfo)
                out_d.append(prog.op(SY, "dma_start", out=self.out[dc * 128:(dc + 1) * 128, q * QT:(q + 1) * QT],
                                     in_=self.qdn[dc % 2][:], waits=[t_fin]))
                qdn_use[dc % 2] = tok_max(t_fin, out_d[-1])
        fin = tok_max(*out_d)
        prog.raw(SY, lambda e: None, waits=[fin])


def build_program(dbg=(), stop=99):
    b = Builder(dbg, stop=stop)
    nc = b.build()
    return nc, b


def _host_inputs(inputs, cid):
    b, h = cid // 2, cid % 2
    x = np.asarray(inputs["L0_attn_x"], np.float32)
    xT = np.ascontiguousarray(x[b].T)
    x_scan = np.ascontiguousarray(xT[h * DH:(h + 1) * DH])
    cols = np.concatenate([np.arange((2 * j + h) * OT, (2 * j + h) * OT + OT) for j in range(NW)])
    x_own = np.ascontiguousarray(xT[:, cols])

    def dhalf_lane(a):
        return np.ascontiguousarray(
            np.asarray(a, np.float32)[h * DH:(h + 1) * DH].reshape(SCN, 128, NS).transpose(1, 0, 2).reshape(128, SCN * NS))

    def dhalf_col(v):
        return np.ascontiguousarray(np.asarray(v, np.float32)[h * DH:(h + 1) * DH].reshape(SCN, 128).T)

    def full_col(v, n):
        return np.ascontiguousarray(np.asarray(v, np.float32).reshape(n, 128).T)

    dt16 = np.exp(np.linspace(np.log(0.001), np.log(0.1), NS)).astype(np.float32)
    dtb = np.ascontiguousarray(np.broadcast_to(np.tile(dt16, SCN), (128, SCN * NS)).astype(np.float32))

    return {
        "x_scan": x_scan,
        "x_own": x_own,
        "A_s": dhalf_lane(inputs["L0_attn_A_ssm"]),
        "B_s": dhalf_lane(inputs["L0_attn_B_ssm"]),
        "C_s": dhalf_lane(inputs["L0_attn_C_ssm"]),
        "dtb": dtb,
        "D_s": dhalf_col(inputs["L0_attn_D_ssm"]),
        "s1_s": dhalf_col(inputs["L0_attn_scale1"]),
        "s2c": full_col(inputs["L0_attn_scale2"], DCN),
        "b1c": full_col(inputs["L0_attn_b1"], FC),
        "b2c": full_col(inputs["L0_attn_b2"], DCN),
    }


_WCACHE = {}


def _host_weights(inputs):
    import ml_dtypes
    w1 = np.asarray(inputs["L0_attn_w1"], np.float32)
    w2 = np.asarray(inputs["L0_attn_w2"], np.float32)
    return {
        "wp1": pack_w(w1, DCN, FC),
        "wp2": pack_w(w2, FC, DCN),
        "wun": np.ascontiguousarray(np.asarray(inputs["L0_ffn_w_up"], np.float32)),
        "wgn": np.ascontiguousarray(np.asarray(inputs["L0_ffn_w_gate"], np.float32)),
        "wdn": np.ascontiguousarray(np.asarray(inputs["L0_ffn_w_down"], np.float32)),
        "identb": np.eye(128, dtype=np.float32).astype(ml_dtypes.bfloat16),
    }


def kernel(**inputs):
    nc, _b = build_program()
    wts = _host_weights(inputs)
    in_maps = []
    for cid in range(8):
        m = _host_inputs(inputs, cid)
        m.update(wts)
        in_maps.append(m)
    res = run_bass_kernel_spmd(nc, in_maps, core_ids=list(range(8)))
    outp = np.zeros((B, D, T), np.float32)
    for cid in range(8):
        b, h = cid // 2, cid % 2
        o = np.asarray(res.results[cid]["out"])
        for j in range(NW):
            outp[b][:, (2 * j + h) * OT:(2 * j + h) * OT + OT] = o[:, j * OT:(j + 1) * OT]
    return np.ascontiguousarray(outp.transpose(0, 2, 1))


# version 08
# speedup vs baseline: 1.3624x; 1.1631x over previous
import numpy as np
from contextlib import ExitStack

import concourse.bass as bass
import concourse.mybir as mybir
import bass_rust
from concourse.bass_utils import run_bass_kernel_spmd

F32 = mybir.dt.float32
BF16 = mybir.dt.bfloat16
I16 = mybir.dt.int16
I32 = mybir.dt.int32
U16 = mybir.dt.uint16
AF = mybir.ActivationFunctionType
ALU = mybir.AluOpType
AX = mybir.AxisListType

B, T, D, NS, DFF, DI = 4, 4096, 1024, 16, 4096, 3072
DH, SCN, DCN = 512, 4, 8
FC, ICN = DFF // 128, DI // 128
NW, WT, OT = 8, 512, 256
TOK, QT, NQ = 2048, 512, 4
QMAX = 32767.0
EPS_LN, EPS_RMS = 1e-5, 1e-6
PAIRS = [[0, 1], [2, 3], [4, 5], [6, 7]]
ALL8 = [list(range(8))]
ENGS = ("sync", "scalar", "vector", "gpsimd", "tensor")
V, A_, P_, G, SY, TE = "vector", "scalar", "gpsimd", "gpsimd", "sync", "tensor"


def tok_max(*tokens):
    outd = {}
    for t in tokens:
        if t is None:
            continue
        items = t.items() if isinstance(t, dict) else [t]
        for k, v in items:
            if outd.get(k, -1) < v:
                outd[k] = v
    return outd


class Prog:
    SEMS = ["sv", "ss", "sp", "st", "dy", "dq", "dg", "cc"]

    def __init__(self, nc):
        self.nc = nc
        self.buf = {e: [] for e in ENGS}
        self.cnt = {}
        self.sems = {}
        self.marks = {}

    def _semname(self, eng, meth):
        if meth == "collective_compute":
            return "cc"
        if meth is not None and "dma" in meth:
            return {"sync": "dy", "scalar": "dq", "gpsimd": "dg"}.get(eng, "dy")
        return {"vector": "sv", "scalar": "ss", "gpsimd": "sp", "tensor": "st", "sync": "dy"}[eng]

    def _filter(self, eng, waits):
        need = tok_max(*waits)
        fil = {}
        for k, v in need.items():
            if self.marks.get((eng, k), 0) < v:
                fil[k] = v
                self.marks[(eng, k)] = v
        return fil

    def op(self, eng, meth, *args, waits=(), argfn=None, **kw):
        fil = self._filter(eng, waits)
        semname = self._semname(eng, meth)
        inc = 16 if semname in ("dy", "dq", "dg") else 1
        self.cnt[semname] = self.cnt.get(semname, 0) + inc
        self.buf[eng].append((fil, meth, args, kw, semname, inc, argfn))
        return (semname, self.cnt[semname])

    def raw(self, eng, fn, waits=()):
        fil = self._filter(eng, waits)
        self.buf[eng].append((fil, None, None, None, None, 0, fn))

    def fence(self):
        return dict(self.cnt)

    def flush(self, block):
        for eng in ENGS:
            items = self.buf[eng]
            self.buf[eng] = []
            if not items:
                continue

            def make(items=items):
                def body(e):
                    for fil, meth, args, kw, semname, inc, argfn in items:
                        for k, v in fil.items():
                            e.wait_ge(self.sems[k], v)
                        if meth is None:
                            argfn(e)
                            continue
                        a, k2 = (args, kw) if argfn is None else argfn(e)
                        getattr(e, meth)(*a, **k2).then_inc(self.sems[semname], inc)
                return body
            getattr(block, eng)(make())


def pack_w(w, KC, MC):
    a = np.asarray(w, np.float32).reshape(KC, 128, MC, 128).transpose(2, 0, 1, 3)
    return np.ascontiguousarray(a.reshape(MC * KC * 128, 128))


def slab_ap(dram_ap, mc0, nmc, KC):
    rows = dram_ap[mc0 * KC * 128:(mc0 + nmc) * KC * 128, :]
    return rows.rearrange("(o kc p) f -> p o kc f", o=nmc, kc=KC, p=128)


def slab_out(tile, nmc, KC):
    return tile.ap().rearrange("p (o kc f) -> p o kc f", o=nmc, kc=KC)


class Builder:
    def __init__(self, dbg=(), stop=99):
        self.nc = bass.Bass("TRN2", target_bir_lowering=False)
        self.stop = stop
        self.dbg = list(dbg)
        self.prog = Prog(self.nc)
        self.dr = {}
        self.taps = {}
        self.hcell = {}

    def din(self, name, shape, dt=F32):
        self.dr[name] = self.nc.dram_tensor(name, list(shape), dt, kind="ExternalInput").ap()
        return self.dr[name]

    def dint(self, name, shape, dt=F32):
        self.dr[name] = self.nc.dram_tensor(name, list(shape), dt)
        return self.dr[name]

    def tap(self, name, shape, dt=F32):
        self.taps[name] = self.nc.dram_tensor("tap_" + name, list(shape), dt, kind="ExternalOutput").ap()
        return self.taps[name]

    def build(self):
        nc, prog = self.nc, self.prog
        self.x_scan = self.din("x_scan", [DH, T])
        self.x_own = self.din("x_own", [D, TOK])
        for n, s in (("A_s", [128, SCN * NS]), ("B_s", [128, SCN * NS]), ("C_s", [128, SCN * NS]),
                     ("dtb", [128, SCN * NS]), ("D_s", [128, SCN]), ("s1_s", [128, SCN]),
                     ("s2c", [128, DCN]), ("b1c", [128, FC]), ("b2c", [128, DCN])):
            self.din(n, s)
        for n, s in (("wp1", [FC * DCN * 128, 128]), ("wp2", [DCN * FC * 128, 128])):
            self.din(n, s)
        self.din("wun", [DI, D])
        self.din("wgn", [DI, D])
        self.din("wdn", [D, DI])
        self.dr["identb"] = self.nc.dram_tensor("identb", [128, 128], BF16, kind="ExternalInput").ap()
        self.dint("wqu", [ICN * DCN * 128, 128], BF16)
        self.dint("wqg", [ICN * DCN * 128, 128], BF16)
        self.dint("wqd", [DCN * ICN * 128, 128], BF16)
        self.out = self.nc.dram_tensor("out", [D, TOK], F32, kind="ExternalOutput").ap()
        self.stats_in = self.dint("stats_in", [2, T])
        self.stats_out = self.dint("stats_out", [4, T])
        self.yag_in = [self.dint(f"yag_in{w}", [DH, WT]) for w in range(NW)]
        self.yag_out = [self.dint(f"yag_out{w}", [D, WT]) for w in range(NW)]
        self.m_in = {k: self.dint(f"min_{k}", [1, 2]) for k in ("m1", "mug", "mg2", "mo")}
        self.m_out = {k: self.dint(f"mout_{k}", [1, 2]) for k in ("m1", "mug", "mg2", "mo")}
        self.sp_up = self.dint("sp_up", [DI, TOK], BF16)
        self.sp_gate = self.dint("sp_gate", [DI, TOK], BF16)

        with ExitStack() as ctx:
            self.ctx = ctx
            S = lambda st, n, sh, dt=F32: st.enter_context(nc.sbuf_tensor(n, list(sh), dt))
            self.S = S
            self.ones = S(ctx, "ones", [128, WT])
            self.onesbf = S(ctx, "onesbf", [128, 8], BF16)
            self.dAv = S(ctx, "dAv", [128, SCN * NS])
            self.CdB = S(ctx, "CdB", [128, SCN * NS])
            self.sD = S(ctx, "sD", [128, SCN])
            self.s_s2 = S(ctx, "s_s2", [128, DCN])
            self.s_b1 = S(ctx, "s_b1", [128, FC])
            self.s_b2 = S(ctx, "s_b2", [128, DCN])
            self.states = S(ctx, "states", [128, SCN * NS])
            self.x1 = [S(ctx, f"x1_{dc}", [128, TOK]) for dc in range(DCN)]
            self.rows = S(ctx, "rows", [1, 6144])
            self.maxc = S(ctx, "maxc", [128, 256])
            self.scal = S(ctx, "scal", [128, 8])
            self.scal2 = S(ctx, "scal2", [128, 2])
            self.srow1 = S(ctx, "srow1", [1, 32])
            self.cbias = S(ctx, "cbias", [128, 4])
            self.drt = S(ctx, "drt", [128, 8])
            for name in Prog.SEMS:
                prog.sems[name] = ctx.enter_context(nc.semaphore(name))
            self.block = ctx.enter_context(nc.Block())

            def mk_h(e):
                pid = e.partition_id()
                h = e.compute_val(pid % 2)
                reg = e.alloc_register("hreg")
                e.reg_mov(reg, h)
                self.hcell["hrv"] = bass_rust.RuntimeValue(reg, min_val=0, max_val=1)
            prog.raw(SY, mk_h)

            self.t_ones = prog.op(V, "memset", self.ones[:], 1.0)
            tb0 = prog.op(V, "memset", self.cbias[:, 0:1], EPS_LN)
            tb1 = prog.op(V, "memset", self.cbias[:, 1:2], EPS_LN * EPS_RMS)
            tb2 = prog.op(V, "memset", self.cbias[:, 2:3], EPS_RMS)
            self.t_cbias = tok_max(tb0, tb1, tb2)
            self.t_onesbf = prog.op(V, "memset", self.onesbf[:, 0:1], 1.0)
            self.t_st0 = prog.op(V, "memset", self.states[:], 0.0)

            self.ph_params()
            if self.stop >= 2:
                self.run_main()
            prog.flush(self.block)
        return nc

    def run_main(self):
        nc, prog = self.nc, self.prog
        ctx = self.ctx
        if True:
            with ExitStack() as rgx:
                self.x1ln = [self.S(rgx, f"x1ln_{dc}", [128, TOK], BF16) for dc in range(DCN)]
                self.cstg = [self.S(rgx, f"cstg{i}", [128, WT], BF16) for i in range(2)]
                with ExitStack() as psc:
                    self.alloc_psum(psc)
                    with ExitStack() as rg:
                        self.alloc_scan(rg)
                        self.ph_stats_u()
                        if self.stop >= 3:
                            self.ph_scan()
                        if self.stop >= 4:
                            self.ph_x1()
                        prog.flush(self.block)
                    if self.stop >= 5:
                        with ExitStack() as rga:
                            self.alloc_attn(rga)
                            self.ph_attn()
                            prog.flush(self.block)
                    if self.stop >= 6:
                        self.ph_ln2_m1_xq_pre()
                    prog.flush(self.block)
            if self.stop < 7:
                return
            self.fence_ffn = prog.fence()
            with ExitStack() as psc:
                self.alloc_psum(psc)
                with ExitStack() as rgfU:
                    self.xqb = [self.S(rgfU, f"xq_{dc}", [128, TOK], BF16) for dc in range(DCN)]
                    with ExitStack() as rgf1:
                        self.ln2 = [self.S(rgf1, f"ln2_{dc}", [128, TOK]) for dc in range(DCN)]
                        self.ql = [self.S(rgf1, f"qlx{i}", [128, QT], I16) for i in range(4)]
                        self.qf = [self.S(rgf1, f"qfx{i}", [128, QT]) for i in range(2)]
                        self.ph_ln2_m1_xq()
                        prog.flush(self.block)
                    self.fence_q = prog.fence()
                    with ExitStack() as rgf2:
                        self.alloc_ffn_u(rgf2)
                        if self.stop >= 8:
                            self.ph_upgate()
                        prog.flush(self.block)
                self.fence_p = prog.fence()
                with ExitStack() as rgfP:
                    self.alloc_ffn_p(rgfP)
                    if self.stop >= 9:
                        self.ph_passA()
                    if self.stop >= 10:
                        self.ph_passB()
                    if self.stop >= 11:
                        self.ph_passC_down()
                    prog.flush(self.block)

    _psum_gen = [0]

    def alloc_psum(self, psc):
        nc = self.nc
        g = self._psum_gen[0]
        self._psum_gen[0] += 1
        self.ps_a = psc.enter_context(nc.psum_tensor(f"ps_a{g}", [128, QT], F32))
        self.ps_b = psc.enter_context(nc.psum_tensor(f"ps_b{g}", [128, QT], F32))
        self.ps_o1 = psc.enter_context(nc.psum_tensor(f"ps_o1{g}", [128, WT], F32))
        self.ps_o2 = psc.enter_context(nc.psum_tensor(f"ps_o2{g}", [128, WT], F32))
        self.ps_o3 = psc.enter_context(nc.psum_tensor(f"ps_o3{g}", [128, OT], F32))
        self.ps_s1 = psc.enter_context(nc.psum_tensor(f"ps_s1{g}", [1, WT], F32))
        self.ps_s2 = psc.enter_context(nc.psum_tensor(f"ps_s2{g}", [1, WT], F32))
        self.ps_aux = psc.enter_context(nc.psum_tensor(f"ps_aux{g}", [128, 512], F32))
        self.ps_t = self.ps_aux.ap()[:, 0:256].bitcast(BF16)
        self.ps_dmy = self.ps_aux.ap()[:, 500:508]

    def alloc_scan(self, rg):
        S = self.S
        WP = 2 * WT
        self.z_t = [S(rg, f"z{i}", [128, WP]) for i in range(4)]
        self.dab = [S(rg, f"dab{i}", [128, WP]) for i in range(2)]
        self.prb = [S(rg, f"prb{i}", [128, WP]) for i in range(2)]
        self.s_out = [S(rg, f"sout{i}", [128, WP]) for i in range(2)]
        self.y_t = [S(rg, f"yt{i}", [128, WP]) for i in range(2)]
        self.yall = S(rg, "yall", [128, DCN * OT])
        self.xoall = S(rg, "xoall", [128, DCN * OT])
        self.xscr = [S(rg, f"xscr{i}", [128, OT]) for i in range(4)]
        self.gstat = [S(rg, f"gstat{i}", [128, 32]) for i in range(2)]
        self.gd = [S(rg, f"gd{i}", [128, 32]) for i in range(2)]
        self.rgf = S(rg, "rgf", [1, WP])
        self.rmg = S(rg, "rmg", [1, WP])
        self.rrow = S(rg, "rrow", [1, OT])
        self.mrow = S(rg, "mrow", [1, OT])
        self.ostat = [S(rg, f"ostat{i}", [32, 32]) for i in range(2)]
        self.od = [S(rg, f"od{i}", [32, 32]) for i in range(2)]


    def alloc_attn(self, rg):
        S = self.S
        self.dq_w = S(rg, "dq_w", [128, D])
        self.dq_l = S(rg, "dq_l", [128, D], I16)
        self.dq_q = S(rg, "dq_q", [128, D], BF16)
        self.dq_s = S(rg, "dq_s", [128, 3 * 32])
        self.dq_st = S(rg, "dq_st", [128, DCN * 128], BF16)
        self.identb = S(rg, "identb_sb", [128, 128], BF16)
        self.w1s = [S(rg, f"w1s{i}", [128, 2 * DCN * 128], BF16) for i in range(2)]
        self.w2s = [S(rg, f"w2s{i}", [128, FC * 128], BF16) for i in range(2)]
        self.g1bf = S(rg, "g1bf", [128, FC * QT], BF16)
        self.evscr = [S(rg, f"ev{i}", [128, QT]) for i in range(2)]
        self.xmscr = [S(rg, f"xm{i}", [128, QT]) for i in range(2)]
        self.xo2 = [S(rg, f"xo2_{i}", [128, QT]) for i in range(2)]

    def alloc_ffn_u(self, rg):
        S = self.S
        self.wus = [S(rg, f"wus{i}", [128, 2 * DCN * 128], BF16) for i in range(2)]
        self.wgs = [S(rg, f"wgs{i}", [128, 2 * DCN * 128], BF16) for i in range(2)]
        self.stg = [S(rg, f"stg{i}", [128, TOK], BF16) for i in range(2)]

    def alloc_ffn_p(self, rg):
        S = self.S
        self.qa = [S(rg, f"qa{i}", [128, QT], BF16) for i in range(8)]
        self.ql = [S(rg, f"ql{i}", [128, QT], I16) for i in range(12)]
        self.qm = [S(rg, f"qm{i}", [128, QT], U16) for i in range(4)]
        self.qf = [S(rg, f"qf{i}", [128, QT]) for i in range(16)]
        self.obf = [S(rg, f"obf{i}", [128, QT], BF16) for i in range(ICN)]
        self.wds = [S(rg, f"wds{i}", [128, ICN * 128], BF16) for i in range(2)]
        self.oxo = [S(rg, f"oxo{i}", [128, QT]) for i in range(2)]
        self.qdn = [S(rg, f"qdn{i}", [128, QT]) for i in range(2)]

    R_M, R_VAR, R_RSTD, R_GFAC, R_MG, R_MRSTD, R_TMP = (i * WT for i in range(7))
    R_SR4 = 7 * WT
    L_S1, L_S2, L_M2R = 0, TOK, 2 * TOK

    def row(self, off, n=WT):
        return self.rows[:, off:off + n]

    def eng_drain(self, eng, *toks):
        if eng == V:
            t = self.prog.op(V, "memset", self.drt[:, 0:4], 0.0, waits=list(toks))
        elif eng == A_:
            t = self.prog.op(A_, "activation", self.drt[:, 4:8], self.drt[:, 0:4], AF.Identity,
                             waits=list(toks))
        else:
            t = self.prog.op(P_, "memset", self.drt[:, 0:4], 0.0, waits=list(toks))
        return tok_max(t, *toks)

    def pe_drain(self, *toks):
        t = self.prog.op(TE, "matmul", self.ps_dmy, self.ones[0:1, 0:128],
                         self.ones[0:1, 0:8], waits=list(toks))
        return tok_max(t, *toks)

    def ph_params(self):
        prog, S = self.prog, self.S
        with ExitStack() as rg:
            sA = S(rg, "sA", [128, SCN * NS]); sB = S(rg, "sB", [128, SCN * NS])
            sC = S(rg, "sC", [128, SCN * NS]); sdt = S(rg, "sdt", [128, SCN * NS])
            sp1 = S(rg, "sp1", [128, SCN * NS]); s_s1 = S(rg, "s_s1", [128, SCN])
            dr = self.dr
            tc = tok_max(*[prog.op(SY, "dma_start", out=d[:], in_=dr[n]) for n, d in
                           (("A_s", sA), ("B_s", sB), ("C_s", sC), ("dtb", sdt),
                            ("D_s", self.sD), ("s1_s", s_s1), ("s2c", self.s_s2),
                            ("b1c", self.s_b1), ("b2c", self.s_b2))])
            t_expA = prog.op(A_, "activation", sp1[:], sA[:], AF.Exp, waits=[tc])
            t_r = prog.op(V, "reciprocal", self.CdB[:], sp1[:], waits=[t_expA])
            t_m = prog.op(V, "tensor_tensor", self.dAv[:], sp1[:], sdt[:], op=ALU.mult, waits=[t_expA, tc])
            self.t_dA = prog.op(A_, "activation", self.dAv[:], self.dAv[:], AF.Exp, scale=-1.0, waits=[t_m])
            t_1m = prog.op(V, "tensor_scalar", sp1[:], self.dAv[:], -1.0, 1.0,
                           op0=ALU.mult, op1=ALU.add, waits=[self.t_dA, t_r])
            t_d1 = prog.op(V, "tensor_tensor", self.CdB[:], self.CdB[:], sp1[:], op=ALU.mult, waits=[t_1m])
            t_d2 = prog.op(V, "tensor_tensor", self.CdB[:], self.CdB[:], sB[:], op=ALU.mult, waits=[t_d1])
            t_d3 = prog.op(V, "tensor_tensor", self.CdB[:], self.CdB[:], sC[:], op=ALU.mult, waits=[t_d2])
            tl = []
            for sc in range(SCN):
                tl.append(prog.op(V, "tensor_scalar", self.CdB[:, sc * NS:(sc + 1) * NS],
                                  self.CdB[:, sc * NS:(sc + 1) * NS], s_s1[:, sc:sc + 1], None,
                                  op0=ALU.mult, waits=[t_d3]))
            self.t_cdb = tok_max(*tl)
            self.t_Df = prog.op(V, "tensor_tensor", self.sD[:], self.sD[:], s_s1[:], op=ALU.mult, waits=[tc])
            self.fence_params = prog.fence()
            prog.flush(self.block)

    def ph_stats_u(self):
        prog = self.prog
        row, R_TMP, R_SR4 = self.row, 0, T
        self.t_xs = {}
        self.use_cstg = {}
        use_ps1 = use_ps2 = None
        stat_d = []
        xs_free = {}
        for w in range(NW):
            mm1 = mm2 = None
            for sc in range(SCN):
                xst = self.z_t[sc].ap()[:, (w % 2) * WT:(w % 2) * WT + WT]
                t_l = prog.op(SY, "dma_start", out=xst,
                              in_=self.x_scan[sc * 128:(sc + 1) * 128, w * WT:(w + 1) * WT],
                              waits=[xs_free.get((sc, w % 2)), self.fence_params if w == 0 else None])
                t_c = prog.op(A_, "activation", self.cstg[0][:], xst, AF.Identity,
                              waits=[t_l, self.use_cstg.get(0)])
                t_q = prog.op(A_, "activation", self.cstg[1][:], xst, AF.Square,
                              waits=[t_l, self.use_cstg.get(1)])
                xs_free[(sc, w % 2)] = tok_max(t_c, t_q)
                mm1 = prog.op(TE, "matmul", self.ps_s1[:], self.onesbf[:, 0:1], self.cstg[0][:],
                              start=(sc == 0), stop=(sc == SCN - 1),
                              waits=[t_c, self.t_onesbf] + ([use_ps1] if sc == 0 else []))
                mm2 = prog.op(TE, "matmul", self.ps_s2[:], self.onesbf[:, 0:1], self.cstg[1][:],
                              start=(sc == 0), stop=(sc == SCN - 1),
                              waits=[t_q, self.t_onesbf] + ([use_ps2] if sc == 0 else []))
                self.use_cstg[0] = mm1
                self.use_cstg[1] = mm2
            t_mm = self.pe_drain(mm1, mm2)
            t_e1 = prog.op(A_, "activation", row(R_TMP), self.ps_s1[:], AF.Identity,
                           waits=[t_mm] + ([stat_d[-2]] if stat_d else []))
            t_e2 = prog.op(A_, "activation", row(R_SR4), self.ps_s2[:], AF.Identity,
                           waits=[t_mm] + ([stat_d[-1]] if stat_d else []))
            use_ps1, use_ps2 = t_e1, t_e2
            stat_d.append(prog.op(SY, "dma_start", out=self.stats_in[0:1, w * WT:(w + 1) * WT],
                                  in_=row(R_TMP), waits=[t_e1]))
            stat_d.append(prog.op(SY, "dma_start", out=self.stats_in[1:2, w * WT:(w + 1) * WT],
                                  in_=row(R_SR4), waits=[t_e2]))
        self.xs_free = xs_free
        self.use_ps1, self.use_ps2 = use_ps1, use_ps2
        self.t_sag = prog.op(G, "collective_compute", "AllGather", ALU.bypass,
                             replica_groups=PAIRS, ins=[self.stats_in.ap()], outs=[self.stats_out.ap()],
                             waits=[tok_max(*stat_d)])
        self.rows_use = tok_max(*stat_d)

    def ph_scan(self):
        prog = self.prog
        WP = 2 * WT
        self.ywtok = [None] * NW
        stuse = {}
        dause, pruse, sduse = {}, {}, {}
        zuse = {}
        yfree = {}
        gstat_use, gd_use, rgf_use = {}, {}, {}
        t_gload, t_wb = {}, {}
        combo = 0

        def gload(wp):
            sl = wp % 2
            rr = []
            for r in range(4):
                rr.append(prog.op(SY, "dma_start", out=self.gstat[sl][:, r * 8:(r + 1) * 8],
                                  in_=self.stats_out[r:r + 1, wp * WP:(wp + 1) * WP],
                                  waits=[self.t_sag, gstat_use.get(sl)]))
            t_gload[wp] = tok_max(*rr)

        def derive(wp, extra=()):
            sl = wp % 2
            g, d = self.gstat[sl], self.gd[sl]
            tS1 = prog.op(V, "tensor_tensor", d[:, 0:8], g[:, 0:8], g[:, 16:24], op=ALU.add,
                          waits=[t_gload[wp], gd_use.get(sl)] + list(extra))
            tS2 = prog.op(V, "tensor_tensor", d[:, 8:16], g[:, 8:16], g[:, 24:32], op=ALU.add,
                          waits=[t_gload[wp], gd_use.get(sl)])
            gstat_use[sl] = tok_max(tS1, tS2)
            tm = prog.op(V, "tensor_scalar", d[:, 0:8], d[:, 0:8], 1.0 / D, None, op0=ALU.mult,
                         waits=[tS1])
            tex = prog.op(V, "tensor_scalar", d[:, 8:16], d[:, 8:16], 1.0 / D, None, op0=ALU.mult,
                          waits=[tS2])
            tm2 = prog.op(V, "tensor_tensor", d[:, 16:24], d[:, 0:8], d[:, 0:8], op=ALU.mult,
                          waits=[tm, tS2])
            tvar = prog.op(V, "tensor_tensor", d[:, 8:16], d[:, 8:16], d[:, 16:24], op=ALU.subtract,
                           waits=[tex, tm2])
            tl2 = prog.op(A_, "activation", d[:, 24:32], d[:, 8:16], AF.Ln, scale=1.0 + EPS_RMS,
                          bias=self.cbias[:, 1:2], waits=[tvar, self.t_cbias])
            tgf = prog.op(A_, "activation", d[:, 24:32], d[:, 24:32], AF.Exp, scale=-0.5, waits=[tl2])
            tmg = prog.op(V, "tensor_tensor", d[:, 8:16], d[:, 0:8], d[:, 24:32], op=ALU.mult,
                          waits=[tgf, tm])
            tgf_d = self.eng_drain(A_, tgf)
            tmg_d = self.eng_drain(V, tmg)
            t_gfr = prog.op(SY, "dma_start", out=self.rgf[0:1, :], in_=d[:, 24:32],
                            waits=[tgf_d, rgf_use.get("gf")])
            t_mgr = prog.op(SY, "dma_start", out=self.rmg[0:1, :], in_=d[:, 8:16],
                            waits=[tmg_d, rgf_use.get("mg")])
            gd_use[sl] = tok_max(t_gfr, t_mgr)
            t_wb[wp] = (t_gfr, t_mgr)

        def xload(wp, init=None):
            t_lx = {}
            for sc in range(SCN):
                t_lx[sc] = prog.op(SY if sc % 2 == 0 else G, "dma_start", out=self.z_t[sc][:],
                                   in_=self.x_scan[sc * 128:(sc + 1) * 128, wp * WP:(wp + 1) * WP],
                                   waits=[init[sc] if init is not None else zuse.get(("z", sc))])
            return t_lx

        gload(0)
        init0 = {sc: tok_max(self.xs_free.get((sc, 0)), self.xs_free.get((sc, 1)))
                 for sc in range(SCN)}
        t_lx = xload(0, init=init0)
        t_lx_next = {}
        derive(0, extra=list(t_lx.values()))

        for wp in range(NW // 2):
            if wp > 0:
                t_lx = t_lx_next
                t_lx_next = {}
            t_gfr, t_mgr = t_wb[wp]
            if wp == 0:
                sp1 = prog.op(SY, "dma_start", out=self.stats_in[0:1, 0:WP], in_=self.rgf[0:1, :],
                              waits=[t_gfr])
                sp2 = prog.op(SY, "dma_start", out=self.stats_in[1:2, 0:WP], in_=self.rmg[0:1, :],
                              waits=[t_mgr, sp1])
                t_gfr = tok_max(t_gfr, sp2)
                t_mgr = tok_max(t_mgr, sp2)
            z_parts = {}
            for h in range(2):
                t_og = prog.op(TE, "matmul", self.ps_o1[:], self.ones[0:1, 0:128],
                               self.rgf[0:1, h * WT:(h + 1) * WT],
                               waits=[t_gfr, self.t_ones, zuse.get("o1")])
                t_omg = prog.op(TE, "matmul", self.ps_o2[:], self.ones[0:1, 0:128],
                                self.rmg[0:1, h * WT:(h + 1) * WT],
                                waits=[t_mgr, self.t_ones, zuse.get("o2")])
                t_og = t_omg = self.pe_drain(t_og, t_omg)
                rgf_use["gf"] = t_og
                rgf_use["mg"] = t_og
                tcons = []
                for sc in range(SCN):
                    zh = self.z_t[sc].ap()[:, h * WT:(h + 1) * WT]
                    t1 = prog.op(V, "tensor_tensor", zh, zh, self.ps_o1[:],
                                 op=ALU.mult, waits=[t_lx[sc], t_og])
                    t2 = prog.op(V, "tensor_tensor", zh, zh, self.ps_o2[:], op=ALU.subtract,
                                 waits=[t1])
                    tcons.append(t2)
                    z_parts[(sc, h)] = t2
                zuse["o1"] = tok_max(*tcons)
                zuse["o2"] = zuse["o1"]
            tz_w = {sc: tok_max(z_parts[(sc, 0)], z_parts[(sc, 1)]) for sc in range(SCN)}
            if "z" in self.dbg:
                if "z" not in self.taps:
                    self.tap("z", [DH, T])
                for sc in range(SCN):
                    tz_d = prog.op(SY, "dma_start",
                                   out=self.taps["z"][sc * 128:(sc + 1) * 128, wp * WP:(wp + 1) * WP],
                                   in_=self.z_t[sc][:], waits=[tz_w[sc]])
                    tz_w[sc] = tok_max(tz_w[sc], tz_d)

            if wp + 1 < NW // 2:
                gload(wp + 1)
                derive(wp + 1)

            ybounce = []
            for sc in range(SCN):
                yt = self.y_t[sc % 2]
                acc_prev = None
                prb_toks = []
                for n in range(NS):
                    ci = sc * NS + n
                    k = combo % 2
                    t_pr = prog.op(A_, "activation", self.prb[k][:], self.z_t[sc][:], AF.Identity,
                                   scale=self.CdB[:, ci:ci + 1],
                                   waits=[tz_w[sc], self.t_cdb, pruse.get(k)])
                    prb_toks.append(t_pr)
                    t_da = prog.op(A_, "activation", self.dab[k][:], self.prb[k][:], AF.Identity,
                                   scale=0.0, bias=self.dAv[:, ci:ci + 1],
                                   waits=[t_pr, self.t_dA, dause.get(k)])
                    init = 0.0 if wp == 0 else self.states[:, ci:ci + 1]
                    t_sc = prog.op(V, "tensor_tensor_scan", self.s_out[k][:], self.dab[k][:],
                                   self.prb[k][:], init, op0=ALU.mult, op1=ALU.add,
                                   waits=[t_da, t_pr, sduse.get(k), self.t_st0] +
                                         ([stuse[ci]] if ci in stuse else []))
                    t_cp = prog.op(V, "tensor_scalar", self.states[:, ci:ci + 1],
                                   self.s_out[k][:, WP - 1:WP], 1.0, None, op0=ALU.mult,
                                   waits=[t_sc])
                    stuse[ci] = t_cp
                    dause[k] = t_sc
                    pruse[k] = t_sc
                    if n == 0:
                        t_ac = prog.op(V, "scalar_tensor_tensor", yt[:], self.z_t[sc][:],
                                       self.sD[:, sc:sc + 1], self.s_out[k][:],
                                       op0=ALU.mult, op1=ALU.add,
                                       waits=[t_sc, tz_w[sc], self.t_Df, yfree.get(sc % 2)])
                    else:
                        t_ac = prog.op(V, "tensor_tensor", yt[:], yt[:], self.s_out[k][:],
                                       op=ALU.add, waits=[t_sc, acc_prev])
                    sduse[k] = tok_max(t_ac, t_cp)
                    acc_prev = t_ac
                    combo += 1
                zuse[("z", sc)] = tok_max(acc_prev, tz_w[sc], *prb_toks)
                ybt = []
                for h in range(2):
                    w = 2 * wp + h
                    t_yd = prog.op(SY, "dma_start",
                                   out=self.yag_in[w][sc * 128:(sc + 1) * 128, :],
                                   in_=yt[:, h * WT:(h + 1) * WT], waits=[acc_prev])
                    ybt.append(t_yd)
                    ybounce.append((w, t_yd))
                yfree[sc % 2] = tok_max(*ybt)
                if wp + 1 < NW // 2:
                    t_lx_next[sc] = prog.op(
                        SY if sc % 2 == 0 else G, "dma_start", out=self.z_t[sc][:],
                        in_=self.x_scan[sc * 128:(sc + 1) * 128, (wp + 1) * WP:(wp + 2) * WP],
                        waits=[zuse.get(("z", sc))])
            for h in range(2):
                w = 2 * wp + h
                toks = [t for (ww, t) in ybounce if ww == w]
                self.ywtok[w] = prog.op(G, "collective_compute", "AllGather", ALU.bypass,
                                        replica_groups=PAIRS, ins=[self.yag_in[w].ap()],
                                        outs=[self.yag_out[w].ap()], waits=[tok_max(*toks)])
                if "yag" in self.dbg:
                    if "yag" not in self.taps:
                        self.tap("yag", [D, T])
                    prog.op(SY, "dma_start", out=self.taps["yag"][:, w * WT:(w + 1) * WT],
                            in_=self.yag_out[w].ap(), waits=[self.ywtok[w]])
        self.zuse_final = dict(zuse)

    def ph_x1(self):
        prog = self.prog
        row = self.row
        R_TMP = self.R_TMP
        self.t_x1 = {}
        self.t_x1ln = {}
        yx_use = {}
        ost_use, od_use, rr_use = {}, {}, {}
        rowtmp_use = {}
        t_trd = {}
        o_use = {"o1": self.zuse_final.get("o1"), "o2": self.zuse_final.get("o2")}

        def oload(w):
            sl = w % 2
            base = T + 1024 * sl
            def mk_r(w=w, base=base):
                def fn(e):
                    return ((), dict(
                        out=self.rows[:, base:base + 4 * OT].rearrange("p (r c) -> p r c", r=4),
                        in_=self.stats_out.rearrange("(a r) c -> a r c", a=1)[:, :, bass.ts(self.hcell["hrv"] + 2 * w, OT)]))
                return fn
            t_row = prog.op(SY, "dma_start", argfn=mk_r(),
                            waits=[self.t_sag, rowtmp_use.get(sl, self.rows_use)])
            rr = []
            for r in range(4):
                rr.append(prog.op(SY, "dma_start", out=self.ostat[sl][0:32, r * 8:(r + 1) * 8],
                                  in_=self.rows[0:1, base + r * OT:base + (r + 1) * OT],
                                  waits=[t_row, ost_use.get(sl)]))
            t_trd[w] = tok_max(*rr)
            rowtmp_use[sl] = t_trd[w]

        oload(0)
        for w in range(NW):
            sl = w % 2
            def mk_yread(w=w):
                def fn(e):
                    src_ap = self.yag_out[w].rearrange("(dc p) t -> p dc t", p=128)
                    return ((), dict(out=self.yall.ap().rearrange("p (dc t) -> p dc t", dc=DCN),
                                     in_=src_ap[:, :, bass.ts(self.hcell["hrv"], OT)]))
                return fn
            t_yall = prog.op(SY, "dma_start", argfn=mk_yread(),
                             waits=[self.ywtok[w], yx_use.get("yall")])
            t_xoall = prog.op(SY, "dma_start",
                              out=self.xoall.ap().rearrange("p (dc t) -> p dc t", dc=DCN),
                              in_=self.x_own.rearrange("(dc p) t -> p dc t", p=128)[:, :, w * OT:(w + 1) * OT],
                              waits=[yx_use.get("xoall")])
            t_y = {dc: t_yall for dc in range(DCN)}
            t_xo = {dc: t_xoall for dc in range(DCN)}
            ost, od = self.ostat[sl], self.od[sl]
            trd = t_trd[w]
            if w + 1 < NW:
                oload(w + 1)
            tS1 = prog.op(V, "tensor_tensor", od[:, 0:8], ost[:, 0:8], ost[:, 16:24],
                          op=ALU.add, waits=[trd, od_use.get(sl)] +
                          ([t_xoall] if w == 0 else []))
            tS2 = prog.op(V, "tensor_tensor", od[:, 8:16], ost[:, 8:16], ost[:, 24:32],
                          op=ALU.add, waits=[trd, od_use.get(sl)])
            ost_use[sl] = tok_max(tS1, tS2)
            tm = prog.op(V, "tensor_scalar", od[:, 0:8], od[:, 0:8], 1.0 / D, None,
                         op0=ALU.mult, waits=[tS1])
            tex = prog.op(V, "tensor_scalar", od[:, 8:16], od[:, 8:16], 1.0 / D, None,
                          op0=ALU.mult, waits=[tS2])
            tm2 = prog.op(V, "tensor_tensor", od[:, 16:24], od[:, 0:8], od[:, 0:8],
                          op=ALU.mult, waits=[tm, tS2])
            tvar = prog.op(V, "tensor_tensor", od[:, 8:16], od[:, 8:16], od[:, 16:24],
                           op=ALU.subtract, waits=[tex, tm2])
            tl1 = prog.op(A_, "activation", od[:, 24:32], od[:, 8:16], AF.Ln,
                          bias=self.cbias[0:32, 0:1], waits=[tvar, self.t_cbias])
            trstd = prog.op(A_, "activation", od[:, 24:32], od[:, 24:32], AF.Exp, scale=-0.5,
                            waits=[tl1])
            tmr = prog.op(V, "tensor_tensor", od[:, 8:16], od[:, 0:8], od[:, 24:32],
                          op=ALU.mult, waits=[trstd, tm])
            rrow = self.rrow.ap()[0:1, :]
            mrow = self.mrow.ap()[0:1, :]
            trstd_d = self.eng_drain(A_, trstd)
            tmr_d = self.eng_drain(V, tmr)
            t_rr = prog.op(SY, "dma_start", out=rrow, in_=od[:, 24:32],
                           waits=[trstd_d, rr_use.get("r")])
            t_mr = prog.op(SY, "dma_start", out=mrow, in_=od[:, 8:16],
                           waits=[tmr_d, rr_use.get("m")])
            od_use[sl] = tok_max(t_rr, t_mr)
            t_or = prog.op(TE, "matmul", self.ps_o1[:, 0:OT], self.ones[0:1, 0:128],
                           rrow, waits=[t_rr, o_use.get("o1")])
            t_omr = prog.op(TE, "matmul", self.ps_o2[:, 0:OT], self.ones[0:1, 0:128],
                            mrow, waits=[t_mr, o_use.get("o2")])
            t_or = t_omr = self.pe_drain(t_or, t_omr)
            rr_use["r"] = t_or
            rr_use["m"] = t_or
            sq_mm = None
            p1l = []
            for dc in range(DCN):
                xr = self.xscr[dc % 2]
                yr = self.xscr[2 + dc % 2]
                p1 = prog.op(V, "tensor_tensor", xr[:], self.xoall[:, dc * OT:(dc + 1) * OT],
                             self.ps_o1[:, 0:OT],
                             op=ALU.mult, waits=[t_xo[dc], t_or, yx_use.get(("s", dc % 2))])
                p2 = prog.op(V, "tensor_tensor", yr[:], self.yall[:, dc * OT:(dc + 1) * OT],
                             self.ps_o2[:, 0:OT],
                             op=ALU.subtract, waits=[t_y[dc], t_omr, yx_use.get(("s2", dc % 2))])
                tx1 = prog.op(P_, "tensor_tensor", self.x1[dc][:, w * OT:(w + 1) * OT], xr[:], yr[:],
                              op=ALU.add, waits=[p1, p2])
                self.t_x1[(w, dc)] = tx1
                yx_use["yall"] = tok_max(yx_use.get("yall"), p2)
                yx_use["xoall"] = tok_max(yx_use.get("xoall"), p1)
                yx_use[("s", dc % 2)] = tx1
                yx_use[("s2", dc % 2)] = tx1
                p1l += [p1, p2]
                t_sq = prog.op(A_, "activation", self.cstg[dc % 2][:, 0:OT],
                               self.x1[dc][:, w * OT:(w + 1) * OT], AF.Square,
                               waits=[tx1, self.use_cstg.get(dc % 2)])
                sq_mm = prog.op(TE, "matmul", self.ps_s2[:, 0:OT], self.onesbf[:, 0:1],
                                self.cstg[dc % 2][:, 0:OT],
                                start=(dc == 0), stop=(dc == DCN - 1),
                                waits=[t_sq] + ([self.use_ps2] if dc == 0 else []))
                self.use_cstg[dc % 2] = sq_mm
            o_use["o1"] = tok_max(*p1l)
            o_use["o2"] = o_use["o1"]
            sq_mm = self.pe_drain(sq_mm)
            t_es = prog.op(A_, "activation", row(R_TMP, OT), self.ps_s2[:, 0:OT], AF.Identity,
                           waits=[sq_mm, self.rows_use if w == 0 else None])
            self.use_ps2 = t_es
            tlf = prog.op(A_, "activation", row(R_TMP, OT), row(R_TMP, OT), AF.Ln, scale=1.0 / D,
                          bias=self.cbias[0:1, 2:3], waits=[t_es, self.t_cbias])
            tfac = prog.op(A_, "activation", row(R_TMP, OT), row(R_TMP, OT), AF.Exp, scale=-0.5, waits=[tlf])
            t_of = prog.op(TE, "matmul", self.ps_o3[:, 0:OT], self.ones[0:1, 0:128], row(R_TMP, OT),
                           waits=[tfac, o_use.get("o3")])
            t_of = self.pe_drain(t_of)
            lnt = []
            for dc in range(DCN):
                t_ln = prog.op(V, "scalar_tensor_tensor", self.x1ln[dc][:, w * OT:(w + 1) * OT],
                               self.x1[dc][:, w * OT:(w + 1) * OT], self.s_s2[:, dc:dc + 1],
                               self.ps_o3[:, 0:OT], op0=ALU.mult, op1=ALU.mult,
                               waits=[self.t_x1[(w, dc)], t_of])
                self.t_x1ln[(w, dc)] = t_ln
                lnt.append(t_ln)
            o_use["o3"] = tok_max(*lnt)
        if "x1" in self.dbg:
            tp = self.tap("x1", [D, TOK])
            for dc in range(DCN):
                prog.op(SY, "dma_start", out=tp[dc * 128:(dc + 1) * 128, :], in_=self.x1[dc][:],
                        waits=[tok_max(*[self.t_x1[(w, dc)] for w in range(NW)])])
        if "x1ln" in self.dbg:
            tp = self.tap("x1ln", [D, TOK], BF16)
            for dc in range(DCN):
                prog.op(SY, "dma_start", out=tp[dc * 128:(dc + 1) * 128, :], in_=self.x1ln[dc][:],
                        waits=[tok_max(*[self.t_x1ln[(w, dc)] for w in range(NW)])])

    def dequant_chunks(self):
        units = []
        for key, nat, outp, KCtot, MC, I in (
                ("u", self.dr["wun"], self.dr["wqu"], DCN, ICN, D),
                ("g", self.dr["wgn"], self.dr["wqg"], DCN, ICN, D),
                ("d", self.dr["wdn"], self.dr["wqd"], ICN, DCN, DI)):
            nsub = I // D
            for oc in range(MC):
                for s in range(nsub):
                    units.append((key, nat, outp, KCtot, s * DCN, oc, s == nsub - 1))
        return units

    def emit_dequant(self, unit):
        prog = self.prog
        key, nat, outp, KCtot, kc0, oc, last = unit
        nb = D // 32
        st = self.dq_state
        t_l = prog.op(SY, "dma_start", out=self.dq_w[:],
                      in_=nat[oc * 128:(oc + 1) * 128, kc0 * 128:kc0 * 128 + D],
                      waits=[st.get("w")])
        w3 = self.dq_w.ap().rearrange("p (b k) -> p b k", k=32)
        t_bm = prog.op(V, "reduce_max", self.dq_s[:, 0:nb], w3, axis=AX.X,
                       apply_absolute_value=True, waits=[t_l, st.get("s")])
        t_s = prog.op(V, "tensor_scalar", self.dq_s[:, 32:32 + nb], self.dq_s[:, 0:nb],
                      1.0 / 7.0, 1e-12, op0=ALU.mult, op1=ALU.max, waits=[t_bm, st.get("s2")])
        t_r = prog.op(V, "reciprocal", self.dq_s[:, 64:64 + nb], self.dq_s[:, 32:32 + nb],
                      waits=[t_s, st.get("r")])
        rb = self.dq_s.ap()[:, 64:64 + nb].unsqueeze(2).broadcast_to([128, nb, 32])
        sb_ = self.dq_s.ap()[:, 32:32 + nb].unsqueeze(2).broadcast_to([128, nb, 32])
        l3 = self.dq_l.ap().rearrange("p (b k) -> p b k", k=32)
        q3 = self.dq_q.ap().rearrange("p (b k) -> p b k", k=32)
        t_lv = prog.op(V, "tensor_tensor", l3, w3, rb, op=ALU.mult, waits=[t_r, st.get("l")])
        t_q = prog.op(P_, "tensor_tensor", q3, l3, sb_, op=ALU.mult, waits=[t_lv, st.get("q")])
        st["w"] = t_lv
        st["s"] = t_lv
        st["s2"] = tok_max(t_lv, t_q)
        st["r"] = t_lv
        st["l"] = t_q
        t_q = self.eng_drain(P_, t_q)
        evs = []
        for g4 in range(DCN // 4):
            t_t = None
            for j in range(4):
                kc = g4 * 4 + j
                t_t = prog.op(TE, "matmul", self.ps_t[:, j * 128:(j + 1) * 128],
                              self.dq_q[:, kc * 128:(kc + 1) * 128],
                              self.identb[:], is_transpose=True,
                              waits=[t_q, self.t_identb] + ([st.get(("pt",))] if j == 0 else []))
            t_t = self.pe_drain(t_t)
            t_e = prog.op(A_, "activation", self.dq_st[:, g4 * 512:(g4 + 1) * 512], self.ps_t,
                          AF.Identity, waits=[t_t, st.get(("st", g4))])
            st[("pt",)] = t_e
            evs.append(t_e)
            st[("st", g4)] = t_e
        rows0 = (oc * KCtot + kc0) * 128
        t_o = prog.op(SY, "dma_start",
                      out=outp[rows0:rows0 + DCN * 128, :].rearrange("(kc p) f -> p kc f", p=128),
                      in_=self.dq_st.ap().rearrange("p (kc f) -> p kc f", kc=DCN),
                      waits=[tok_max(*evs)])
        st["q"] = t_o
        for g4 in range(DCN // 4):
            st[("st", g4)] = tok_max(st[("st", g4)], t_o)
        if last:
            self.t_wq[(key, oc)] = t_o
        else:
            self.t_wq[(key, oc)] = tok_max(self.t_wq.get((key, oc)), t_o)

    def ph_attn(self):
        prog = self.prog
        row = self.row
        fence_a = prog.fence()
        self.dq_state = {}
        self.t_wq = {}
        self.t_identb = prog.op(SY, "dma_start", out=self.identb[:], in_=self.dr["identb"],
                                waits=[fence_a])
        dq_units = self.dequant_chunks()
        dq_pos = 0
        w1_use, w2_use, g1_use, ps_use, ev_use, xm_use, xo_use = {}, {}, {}, {}, {}, {}, {}
        self.t_xm = {}
        use_ps1l = self.use_ps1
        use_ps2l = self.use_ps2
        self.lnS_tok = []
        for q in range(NQ):
            x1ln_tok = self.eng_drain(V, *[self.t_x1ln[(w, dc)] for dc in range(DCN) for w in (2 * q, 2 * q + 1)])
            for fg in range(FC // 2):
                sl = fg % 2
                t_w = prog.op(G, "dma_start", out=slab_out(self.w1s[sl], 2, DCN), in_=slab_ap(self.dr["wp1"], fg * 2, 2, DCN),
                              waits=[w1_use.get(sl)] + ([fence_a] if (q == 0 and fg < 2) else []))
                gel = []
                for mc in range(2):
                    ffc = fg * 2 + mc
                    pb = [self.ps_a, self.ps_b][ffc % 2]
                    mml = None
                    for dc in range(DCN):
                        mml = prog.op(TE, "matmul", pb[:],
                                      self.w1s[sl][:, (mc * DCN + dc) * 128:(mc * DCN + dc + 1) * 128],
                                      self.x1ln[dc][:, q * QT:(q + 1) * QT],
                                      start=(dc == 0), stop=(dc == DCN - 1),
                                      waits=[t_w, x1ln_tok] + ([ps_use.get(ffc % 2)] if dc == 0 else []))
                    mml = self.pe_drain(mml)
                    t_g = prog.op(A_, "activation", self.g1bf[:, ffc * QT:(ffc + 1) * QT], pb[:],
                                  AF.Gelu, bias=self.s_b1[:, ffc:ffc + 1],
                                  waits=[mml, g1_use.get(ffc)])
                    ps_use[ffc % 2] = t_g
                    g1_use[ffc] = t_g
                    gel.append(t_g)
                w1_use[sl] = tok_max(*gel)
            g1_all = self.eng_drain(A_, *[g1_use[f] for f in range(FC)])
            mmA = mmB = None
            for dc in range(DCN):
                sl = dc % 2
                t_w = prog.op(G, "dma_start", out=slab_out(self.w2s[sl], 1, FC), in_=slab_ap(self.dr["wp2"], dc, 1, FC),
                              waits=[w2_use.get(sl)] + ([fence_a] if (q == 0 and dc < 2) else []))
                pb = [self.ps_a, self.ps_b][dc % 2]
                mml = None
                for kc in range(FC):
                    mml = prog.op(TE, "matmul", pb[:], self.w2s[sl][:, kc * 128:(kc + 1) * 128],
                                  self.g1bf[:, kc * QT:(kc + 1) * QT],
                                  start=(kc == 0), stop=(kc == FC - 1),
                                  waits=[t_w, g1_all] + ([ps_use.get(dc % 2)] if kc == 0 else []))
                w2_use[sl] = mml
                mml = self.pe_drain(mml)
                t_ev = prog.op(A_, "activation", self.evscr[sl][:], pb[:], AF.Identity,
                               bias=self.s_b2[:, dc:dc + 1], waits=[mml, ev_use.get(sl)])
                ps_use[dc % 2] = t_ev
                t_xo = prog.op(SY, "dma_start", out=self.xo2[sl][:],
                               in_=self.x_own[dc * 128:(dc + 1) * 128, q * QT:(q + 1) * QT],
                               waits=[xo_use.get(sl)] + ([fence_a] if (q == 0 and dc < 2) else []))
                x1tok = tok_max(self.t_x1ln[(2 * q, dc)], self.t_x1ln[(2 * q + 1, dc)])
                t_p = prog.op(P_, "tensor_tensor", self.xmscr[sl][:], self.evscr[sl][:],
                              self.x1[dc][:, q * QT:(q + 1) * QT], op=ALU.add,
                              waits=[t_ev, x1tok, xm_use.get(sl)])
                txm = prog.op(V, "tensor_tensor", self.x1[dc][:, q * QT:(q + 1) * QT],
                              self.xmscr[sl][:], self.xo2[sl][:], op=ALU.add,
                              waits=[t_p, t_xo])
                self.t_xm[(q, dc)] = txm
                ev_use[sl] = t_p
                xm_use[sl] = txm
                xo_use[sl] = txm
                t_cb = prog.op(A_, "activation", self.cstg[dc % 2][:], self.x1[dc][:, q * QT:(q + 1) * QT],
                               AF.Identity, waits=[txm, self.use_cstg.get(dc % 2)])
                mmA = prog.op(TE, "matmul", self.ps_s1[:], self.onesbf[:, 0:1], self.cstg[dc % 2][:],
                              start=(dc == 0), stop=(dc == DCN - 1),
                              waits=[t_cb] + ([use_ps1l] if dc == 0 else []))
                t_sq = prog.op(A_, "activation", self.cstg[dc % 2][:], self.x1[dc][:, q * QT:(q + 1) * QT],
                               AF.Square, waits=[txm, mmA])
                mmB = prog.op(TE, "matmul", self.ps_s2[:], self.onesbf[:, 0:1], self.cstg[dc % 2][:],
                              start=(dc == 0), stop=(dc == DCN - 1),
                              waits=[t_sq] + ([use_ps2l] if dc == 0 else []))
                self.use_cstg[dc % 2] = mmB
            t_mm = self.pe_drain(mmA, mmB)
            tE1 = prog.op(A_, "activation", self.rows[:, self.L_S1 + q * QT:self.L_S1 + (q + 1) * QT],
                          self.ps_s1[:], AF.Identity, waits=[t_mm, self.rows_use])
            tE2 = prog.op(A_, "activation", self.rows[:, self.L_S2 + q * QT:self.L_S2 + (q + 1) * QT],
                          self.ps_s2[:], AF.Identity, waits=[t_mm])
            use_ps1l, use_ps2l = tE1, tE2
            self.lnS_tok += [tE1, tE2]
            n_do = 18 if q < NQ - 1 else len(dq_units) - dq_pos
            for _ in range(n_do):
                if dq_pos < len(dq_units):
                    self.emit_dequant(dq_units[dq_pos])
                    dq_pos += 1
        if "xmid" in self.dbg:
            tp = self.tap("xmid", [D, TOK])
            for dc in range(DCN):
                prog.op(SY, "dma_start", out=tp[dc * 128:(dc + 1) * 128, :], in_=self.x1[dc][:],
                        waits=[tok_max(*[self.t_xm[(q, dc)] for q in range(NQ)])])

    def ph_ln2_m1_xq_pre(self):
        prog = self.prog
        L_S1, L_S2, L_M2R = self.L_S1, self.L_S2, self.L_M2R
        tS = tok_max(*self.lnS_tok)
        t1 = prog.op(V, "tensor_scalar", self.rows[:, L_S1:L_S1 + TOK], self.rows[:, L_S1:L_S1 + TOK],
                     1.0 / D, None, op0=ALU.mult, waits=[tS])
        t2 = prog.op(V, "tensor_scalar", self.rows[:, L_S2:L_S2 + TOK], self.rows[:, L_S2:L_S2 + TOK],
                     1.0 / D, None, op0=ALU.mult, waits=[tS])
        t3 = prog.op(V, "tensor_tensor", self.rows[:, L_M2R:L_M2R + TOK], self.rows[:, L_S1:L_S1 + TOK],
                     self.rows[:, L_S1:L_S1 + TOK], op=ALU.mult, waits=[t1])
        t4 = prog.op(V, "tensor_tensor", self.rows[:, L_S2:L_S2 + TOK], self.rows[:, L_S2:L_S2 + TOK],
                     self.rows[:, L_M2R:L_M2R + TOK], op=ALU.subtract, waits=[t2, t3])
        t5 = prog.op(A_, "activation", self.rows[:, L_S2:L_S2 + TOK], self.rows[:, L_S2:L_S2 + TOK],
                     AF.Ln, bias=self.cbias[0:1, 0:1], waits=[t4, self.t_cbias])
        t6 = prog.op(A_, "activation", self.rows[:, L_S2:L_S2 + TOK], self.rows[:, L_S2:L_S2 + TOK],
                     AF.Exp, scale=-0.5, waits=[t5])
        t7 = prog.op(V, "tensor_tensor", self.rows[:, L_M2R:L_M2R + TOK], self.rows[:, L_S1:L_S1 + TOK],
                     self.rows[:, L_S2:L_S2 + TOK], op=ALU.mult, waits=[t6, t3])
        self.t_ln2rows = tok_max(t6, t7)

    def _bcast_scales(self, key, dst, col0, n=2, extra=()):
        prog = self.prog
        t_d = prog.op(SY, "dma_start", out=self.m_in[key].ap(), in_=self.srow1[0:1, 0:2],
                      waits=[self.t_maxloc])
        t_c = prog.op(G, "collective_compute", "AllReduce", ALU.max, replica_groups=ALL8,
                      ins=[self.m_in[key].ap()], outs=[self.m_out[key].ap()], waits=[t_d])
        t_r = prog.op(SY, "dma_start", out=self.srow1[:, 4:6], in_=self.m_out[key].ap(), waits=[t_c])
        ts_ = prog.op(V, "tensor_scalar", self.srow1[:, 8:8 + n], self.srow1[:, 4:4 + n],
                      1.0 / QMAX, 1e-12, op0=ALU.mult, op1=ALU.max, waits=[t_r])
        tr_ = prog.op(V, "reciprocal", self.srow1[:, 8 + n:8 + 2 * n], self.srow1[:, 8:8 + n], waits=[ts_])
        t_o = prog.op(TE, "matmul", self.ps_o3[:, 0:2 * n], self.ones[0:1, 0:128],
                      self.srow1[0:1, 8:8 + 2 * n],
                      waits=[tr_, self.t_ones, self.ps_o3_use] + list(extra))
        t_o = self.pe_drain(t_o)
        t_e = prog.op(A_, "activation", dst[:, col0:col0 + 2 * n], self.ps_o3[:, 0:2 * n],
                      AF.Identity, waits=[t_o])
        self.ps_o3_use = t_e
        return t_e

    def ph_ln2_m1_xq(self):
        prog = self.prog
        L_S2, L_M2R = self.L_S2, self.L_M2R
        self.ps_o3_use = None
        fence = self.fence_ffn
        ln_use = {}
        o_use = {}
        mx = []
        ps_use = {}
        for q in range(NQ):
            t_or = prog.op(TE, "matmul", self.ps_o1[:], self.ones[0:1, 0:128],
                           self.rows[:, L_S2 + q * QT:L_S2 + (q + 1) * QT],
                           waits=[self.t_ln2rows, o_use.get("o1"), fence])
            t_om = prog.op(TE, "matmul", self.ps_o2[:], self.ones[0:1, 0:128],
                           self.rows[:, L_M2R + q * QT:L_M2R + (q + 1) * QT],
                           waits=[self.t_ln2rows, o_use.get("o2")])
            t_or = t_om = self.pe_drain(t_or, t_om)
            tq = []
            for dc in range(DCN):
                t1 = prog.op(V, "tensor_tensor", self.qf[dc % 2][:], self.x1[dc][:, q * QT:(q + 1) * QT],
                             self.ps_o1[:], op=ALU.mult,
                             waits=[self.t_xm[(q, dc)], t_or, ln_use.get(("qf", dc % 2)), fence])
                t2 = prog.op(V, "tensor_tensor", self.ln2[dc][:, q * QT:(q + 1) * QT], self.qf[dc % 2][:],
                             self.ps_o2[:], op=ALU.subtract, waits=[t1, t_om])
                ln_use[("qf", dc % 2)] = t2
                t3 = prog.op(V, "reduce_max", self.maxc[:, q * DCN + dc:q * DCN + dc + 1],
                             self.ln2[dc][:, q * QT:(q + 1) * QT], axis=AX.X,
                             apply_absolute_value=True, waits=[t2])
                ln_use[(q, dc)] = t2
                tq.append(t3)
                mx.append(t3)
            o_use["o1"] = tok_max(*tq)
            o_use["o2"] = o_use["o1"]
        t_mr = prog.op(V, "reduce_max", self.maxc[:, 32:33], self.maxc[:, 0:32], axis=AX.X,
                       waits=[tok_max(*mx)])
        t_pc = prog.op(G, "tensor_reduce", self.srow1[0:1, 0:1], self.maxc[:, 32:33], axis=AX.C,
                       op=ALU.max, waits=[t_mr])
        t_cp = prog.op(V, "tensor_scalar", self.srow1[:, 1:2], self.srow1[:, 0:1], 1.0, None,
                       op0=ALU.mult, waits=[t_pc])
        self.t_maxloc = tok_max(t_pc, t_cp)
        t_sc = self._bcast_scales("m1", self.scal, 0)
        self.t_xq = {}
        lvl_use = {}
        for q in range(NQ):
            for dc in range(DCN):
                r = dc % 4
                tl = prog.op(V, "tensor_scalar", self.ql[r][:], self.ln2[dc][:, q * QT:(q + 1) * QT],
                             self.scal[:, 2:3], None, op0=ALU.mult,
                             waits=[ln_use[(q, dc)], t_sc, lvl_use.get(r)])
                tb = prog.op(V, "tensor_scalar", self.xqb[dc][:, q * QT:(q + 1) * QT], self.ql[r][:],
                             self.scal[:, 0:1], None, op0=ALU.mult, waits=[tl])
                lvl_use[r] = tb
                self.t_xq[(q, dc)] = tb
        if "ln2" in self.dbg:
            tp = self.tap("ln2", [D, TOK])
            for dc in range(DCN):
                prog.op(SY, "dma_start", out=tp[dc * 128:(dc + 1) * 128, :], in_=self.ln2[dc][:],
                        waits=[tok_max(*[self.t_xq[(q, dc)] for q in range(NQ)])])
        if "xq" in self.dbg:
            tp = self.tap("xq", [D, TOK], BF16)
            for dc in range(DCN):
                prog.op(SY, "dma_start", out=tp[dc * 128:(dc + 1) * 128, :], in_=self.xqb[dc][:],
                        waits=[tok_max(*[self.t_xq[(q, dc)] for q in range(NQ)])])

    def ph_upgate(self):
        prog = self.prog
        sub = self.stop - 7 if self.stop < 9 else 99
        wu_use, wg_use, ps_use, st_use = {}, {}, {}, {}
        mx = []
        xq_all = self.eng_drain(V, *self.t_xq.values())
        for ig in range(ICN // 2):
            sl = ig % 2
            t_wu = prog.op(SY, "dma_start", out=slab_out(self.wus[sl], 2, DCN),
                           in_=slab_ap(self.dr["wqu"], ig * 2, 2, DCN),
                           waits=[wu_use.get(sl), self.t_wq[("u", ig * 2)], self.t_wq[("u", ig * 2 + 1)],
                                  self.fence_q if ig < 2 else None])
            t_wg = prog.op(SY, "dma_start", out=slab_out(self.wgs[sl], 2, DCN),
                           in_=slab_ap(self.dr["wqg"], ig * 2, 2, DCN),
                           waits=[wg_use.get(sl), self.t_wq[("g", ig * 2)], self.t_wq[("g", ig * 2 + 1)],
                                  self.fence_q if ig < 2 else None])
            lastu, lastg = [], []
            if sub < 0.4:
                wu_use[sl] = t_wu
                wg_use[sl] = t_wg
                continue
            for mc in range(2):
                ic = ig * 2 + mc
                for q in range(NQ):
                    k = (ic * NQ + q) % 2
                    pbu = [self.ps_a, self.ps_o1][k]
                    pbg = [self.ps_b, self.ps_o2][k]
                    mmu = mmg = None
                    for dc in range(DCN):
                        mmu = prog.op(TE, "matmul", pbu[:],
                                      self.wus[sl][:, (mc * DCN + dc) * 128:(mc * DCN + dc + 1) * 128],
                                      self.xqb[dc][:, q * QT:(q + 1) * QT],
                                      start=(dc == 0), stop=(dc == DCN - 1),
                                      waits=[t_wu, xq_all] + ([ps_use.get(("u", k))] if dc == 0 else []))
                    for dc in range(DCN):
                        mmg = prog.op(TE, "matmul", pbg[:],
                                      self.wgs[sl][:, (mc * DCN + dc) * 128:(mc * DCN + dc + 1) * 128],
                                      self.xqb[dc][:, q * QT:(q + 1) * QT],
                                      start=(dc == 0), stop=(dc == DCN - 1),
                                      waits=[t_wg, xq_all] + ([ps_use.get(("g", k))] if dc == 0 else []))
                    if sub < 0.6:
                        ev = prog.op(A_, "activation", self.stg[0][:], pbu[:], AF.Identity,
                                     waits=[mmu, st_use.get("x"), self.fence_q])
                        ev2 = prog.op(A_, "activation", self.stg[1][:], pbg[:], AF.Identity,
                                      waits=[mmg, st_use.get("y"), self.fence_q])
                        st_use["x"], st_use["y"] = ev, ev2
                        ps_use[("u", k)], ps_use[("g", k)] = ev, ev2
                        lastu.append(mmu)
                        lastg.append(mmg)
                        continue
                    t_dr = self.pe_drain(mmu, mmg)
                    mmu = tok_max(mmu, t_dr)
                    mmg = tok_max(mmg, t_dr)
                    su = self.stg[0]
                    sg = self.stg[1]
                    teu = prog.op(A_, "activation", su[:, q * QT:(q + 1) * QT], pbu[:], AF.Identity,
                                  waits=[mmu, (st_use.get(0) if q == 0 else None), self.fence_q])
                    teg = prog.op(A_, "activation", sg[:, q * QT:(q + 1) * QT], pbg[:], AF.Identity,
                                  waits=[mmg, (st_use.get(1) if q == 0 else None), self.fence_q])
                    cu = 64 + ic * NQ + q
                    cg = 160 + ic * NQ + q
                    tmu = prog.op(V, "reduce_max", self.maxc[:, cu:cu + 1], pbu[:], axis=AX.X,
                                  apply_absolute_value=True, waits=[mmu, teu])
                    tmg = prog.op(V, "reduce_max", self.maxc[:, cg:cg + 1], pbg[:], axis=AX.X,
                                  apply_absolute_value=True, waits=[mmg, teg])
                    ps_use[("u", k)] = tok_max(teu, tmu)
                    ps_use[("g", k)] = tok_max(teg, tmg)
                    mx += [tmu, tmg]
                    lastu.append(mmu)
                    lastg.append(mmg)
                    if sub < 0.8:
                        continue
                    if q == NQ - 1:
                        du = prog.op(SY, "dma_start", out=self.sp_up[ic * 128:(ic + 1) * 128, :],
                                     in_=su[:], waits=[teu])
                        dg = prog.op(SY, "dma_start", out=self.sp_gate[ic * 128:(ic + 1) * 128, :],
                                     in_=sg[:], waits=[teg])
                        st_use[0] = du
                        st_use[1] = dg
            wu_use[sl] = tok_max(*lastu)
            wg_use[sl] = tok_max(*lastg)
        if sub < 0.6:
            return
        tmx = tok_max(*mx)
        t1 = prog.op(V, "reduce_max", self.maxc[:, 33:34], self.maxc[:, 64:160], axis=AX.X, waits=[tmx])
        t2 = prog.op(V, "reduce_max", self.maxc[:, 34:35], self.maxc[:, 160:256], axis=AX.X, waits=[tmx])
        t3 = prog.op(G, "tensor_reduce", self.srow1[0:1, 0:1], self.maxc[:, 33:34], axis=AX.C,
                     op=ALU.max, waits=[t1])
        t4 = prog.op(G, "tensor_reduce", self.srow1[0:1, 1:2], self.maxc[:, 34:35], axis=AX.C,
                     op=ALU.max, waits=[t2])
        self.t_maxloc = tok_max(t3, t4)
        self.maxc_free = tok_max(t1, t2)
        self.t_scug = self._bcast_scales("mug", self.scal, 0, n=2, extra=[xq_all])
        if "gate" in self.dbg:
            tpg = self.tap("gate", [DI, TOK], BF16)
            tpu = self.tap("up", [DI, TOK], BF16)
            fs = prog.fence()
            prog.op(SY, "dma_start", out=tpg[:, :], in_=self.sp_gate.ap(), waits=[fs])
            prog.op(SY, "dma_start", out=tpu[:, :], in_=self.sp_up.ap(), waits=[fs])

    def _chain_a(self, ic, q, idx, waits_extra=()):
        prog = self.prog
        a = idx % 4
        tg = prog.op(SY, "dma_start", out=self.qa[a][:],
                     in_=self.sp_gate[ic * 128:(ic + 1) * 128, q * QT:(q + 1) * QT],
                     waits=[self.qa_use.get(a)] + list(waits_extra))
        tl = prog.op(V, "tensor_scalar", self.ql[a][:], self.qa[a][:], self.scal[:, 3:4], None,
                     op0=ALU.mult, waits=[tg, self.t_scug, self.ql_use.get(a)])
        self.qa_use[a] = tl
        tgq = prog.op(A_, "activation", self.qf[a][:], self.ql[a][:], AF.Identity,
                      scale=self.scal[:, 1:2], waits=[tl, self.qf_use.get(a)])
        tsg = prog.op(A_, "activation", self.qf[4 + a][:], self.ql[a][:], AF.Sigmoid,
                      scale=self.scal[:, 1:2], waits=[tl, self.qf_use.get(4 + a)])
        self.ql_use[a] = tok_max(tgq, tsg)
        return {"tgq": tgq, "tsg": tsg}

    def _chain_b(self, ic, q, idx, st):
        prog = self.prog
        a = idx % 4
        tsm = prog.op(V, "tensor_scalar", self.qm[a][:], self.qf[4 + a][:], 65536.0, None,
                      op0=ALU.mult, waits=[st["tsg"], self.qm_use.get(a)])
        tsq = prog.op(A_, "activation", self.qf[4 + a][:], self.qm[a][:], AF.Identity,
                      scale=1.0 / 65536.0, waits=[tsm])
        self.qm_use[a] = tsq
        st["tsq"] = tsq

    def _signed_minmax_combine(self, waits=()):
        prog = self.prog
        tM = prog.op(V, "reduce_max", self.maxc[:, 50:51], self.maxc[:, 64:160], axis=AX.X,
                     waits=list(waits))
        tN = prog.op(V, "tensor_reduce", self.maxc[:, 51:52], self.maxc[:, 160:256], axis=AX.X,
                     op=ALU.min, waits=list(waits))
        tC = prog.op(V, "scalar_tensor_tensor", self.maxc[:, 52:53], self.maxc[:, 51:52], -1.0,
                     self.maxc[:, 50:51], op0=ALU.mult, op1=ALU.max, waits=[tM, tN])
        tP = prog.op(G, "tensor_reduce", self.srow1[0:1, 0:1], self.maxc[:, 52:53], axis=AX.C,
                     op=ALU.max, waits=[tC])
        tcp = prog.op(V, "tensor_scalar", self.srow1[:, 1:2], self.srow1[:, 0:1], 1.0, None,
                      op0=ALU.mult, waits=[tP])
        self.t_maxloc = tok_max(tP, tcp)
        self.maxc_free = tok_max(tM, tN)

    def ph_passA(self):
        prog = self.prog
        self.qa_use, self.ql_use, self.qf_use, self.qm_use = {}, {}, {}, {}
        last = []
        N = ICN * NQ
        states = {}

        def segC(idx, st):
            a = idx % 4
            tt2 = prog.op(P_, "tensor_tensor", self.qf[8 + a][:], self.qf[a][:], self.qf[4 + a][:],
                          op=ALU.mult, waits=[st["tgq"], st["tsq"], self.qf_use.get(8 + a),
                                              self.maxc_free])
            self.qf_use[a] = tt2
            self.qf_use[4 + a] = tt2
            tmx = prog.op(V, "tensor_reduce", self.maxc[:, 64 + idx:65 + idx], self.qf[8 + a][:],
                          axis=AX.X, op=ALU.max, waits=[tt2, self.maxc_free])
            tmn = prog.op(V, "tensor_reduce", self.maxc[:, 160 + idx:161 + idx], self.qf[8 + a][:],
                          axis=AX.X, op=ALU.min, waits=[tt2])
            self.qf_use[8 + a] = tok_max(tmx, tmn)
            last.append(tok_max(tmx, tmn))

        for t in range(N + 2):
            if 0 <= t - 2 < N:
                segC(t - 2, states.pop(t - 2))
            if 0 <= t - 1 < N:
                ic, q = divmod(t - 1, NQ)
                self._chain_b(ic, q, t - 1, states[t - 1])
            if t < N:
                ic, q = divmod(t, NQ)
                states[t] = self._chain_a(ic, q, t,
                                          waits_extra=[self.fence_p] if t < 8 else ())
        self._signed_minmax_combine(waits=[tok_max(*last)])
        self.t_sc2 = self._bcast_scales("mg2", self.scal, 4, n=2)

    def _up_segA(self, ic, q, idx, extra):
        st = self._chain_a(ic, q, idx, waits_extra=extra)
        a = idx % 4
        st["tu"] = self.prog.op(SY, "dma_start", out=self.qa[4 + a][:],
                                in_=self.sp_up[ic * 128:(ic + 1) * 128, q * QT:(q + 1) * QT],
                                waits=[self.qa_use.get(4 + a)])
        return st

    def _up_segB(self, ic, q, idx, st):
        a = idx % 4
        self._chain_b(ic, q, idx, st)
        st["tlu"] = self.prog.op(V, "tensor_scalar", self.ql[8 + a][:], self.qa[4 + a][:],
                                 self.scal[:, 2:3], None, op0=ALU.mult,
                                 waits=[st["tu"], self.ql_use.get(8 + a)])
        self.qa_use[4 + a] = st["tlu"]

    def _up_segC(self, idx, st):
        prog = self.prog
        a = idx % 4
        tt2 = prog.op(P_, "tensor_tensor", self.qf[8 + a][:], self.qf[a][:], self.qf[4 + a][:],
                      op=ALU.mult, waits=[st["tgq"], st["tsq"], self.qf_use.get(8 + a)])
        self.qf_use[a] = tt2
        self.qf_use[4 + a] = tt2
        tl2 = prog.op(V, "tensor_scalar", self.ql[4 + a][:], self.qf[8 + a][:], self.scal[:, 6:7],
                      None, op0=ALU.mult, waits=[tt2, self.t_sc2, self.ql_use.get(4 + a)])
        self.qf_use[8 + a] = tl2
        tg2 = prog.op(A_, "activation", self.qf[8 + a][:], self.ql[4 + a][:], AF.Identity,
                      scale=self.scal[:, 4:5], waits=[tl2])
        self.ql_use[4 + a] = tg2
        tuq = prog.op(A_, "activation", self.qf[12 + a][:], self.ql[8 + a][:], AF.Identity,
                      scale=self.scal[:, 0:1], waits=[st["tlu"], self.qf_use.get(12 + a)])
        self.ql_use[8 + a] = tuq
        st["tg2"], st["tuq"] = tg2, tuq

    def ph_passB(self):
        prog = self.prog
        lastB = []
        N = ICN * NQ
        states = {}

        def segD(idx, st):
            a = idx % 4
            tt3 = prog.op(P_, "tensor_tensor", self.qf[a][:], self.qf[8 + a][:], self.qf[12 + a][:],
                          op=ALU.mult, waits=[st["tg2"], st["tuq"], self.qf_use.get(a),
                                              self.maxc_free])
            self.qf_use[8 + a] = tt3
            self.qf_use[12 + a] = tt3
            tmx = prog.op(V, "tensor_reduce", self.maxc[:, 64 + idx:65 + idx], self.qf[a][:],
                          axis=AX.X, op=ALU.max, waits=[tt3, self.maxc_free])
            tmn = prog.op(V, "tensor_reduce", self.maxc[:, 160 + idx:161 + idx], self.qf[a][:],
                          axis=AX.X, op=ALU.min, waits=[tt3])
            self.qf_use[a] = tok_max(tmx, tmn)
            lastB.append(tok_max(tmx, tmn))

        for t in range(N + 3):
            if 0 <= t - 3 < N:
                segD(t - 3, states.pop(t - 3))
            if 0 <= t - 2 < N:
                self._up_segC(t - 2, states[t - 2])
            if 0 <= t - 1 < N:
                ic, q = divmod(t - 1, NQ)
                self._up_segB(ic, q, t - 1, states[t - 1])
            if t < N:
                ic, q = divmod(t, NQ)
                states[t] = self._up_segA(ic, q, t, [self.t_sc2])
        self._signed_minmax_combine(waits=[tok_max(*lastB)])
        self.t_sco = self._bcast_scales("mo", self.scal2, 0, n=1)

    def ph_passC_down(self):
        prog = self.prog
        wd_use, obf_use, ps_use, oxo_use, qdn_use = {}, {}, {}, {}, {}
        out_d = []
        states = {}
        ot = {q: [] for q in range(NQ)}
        N = NQ * ICN

        def segD(idx, st):
            q, ic = divmod(idx, ICN)
            a = idx % 4
            tt3 = prog.op(P_, "tensor_tensor", self.qf[a][:], self.qf[8 + a][:], self.qf[12 + a][:],
                          op=ALU.mult, waits=[st["tg2"], st["tuq"], self.qf_use.get(a)])
            self.qf_use[8 + a] = tt3
            self.qf_use[12 + a] = tt3
            tlo = prog.op(V, "tensor_scalar", self.ql[4 + a][:], self.qf[a][:], self.scal2[:, 1:2],
                          None, op0=ALU.mult, waits=[tt3, self.t_sco, self.ql_use.get(4 + a)])
            self.qf_use[a] = tlo
            tob = prog.op(V, "tensor_scalar", self.obf[ic][:], self.ql[4 + a][:], self.scal2[:, 0:1],
                          None, op0=ALU.mult, waits=[tlo, obf_use.get(ic)])
            self.ql_use[4 + a] = tob
            ot[q].append(tob)

        def emit_down(q):
            o_all = self.eng_drain(V, *ot[q])
            for dc in range(DCN):
                sl = dc % 2
                t_w = prog.op(SY, "dma_start", out=slab_out(self.wds[sl], 1, ICN),
                              in_=slab_ap(self.dr["wqd"], dc, 1, ICN),
                              waits=[wd_use.get(sl), self.t_wq[("d", dc)],
                                     self.fence_p if (q == 0 and dc < 2) else None])
                pb = [self.ps_a, self.ps_b, self.ps_o1, self.ps_o2][dc % 4]
                mml = None
                for kc in range(ICN):
                    mml = prog.op(TE, "matmul", pb[:], self.wds[sl][:, kc * 128:(kc + 1) * 128],
                                  self.obf[kc][:], start=(kc == 0), stop=(kc == ICN - 1),
                                  waits=[t_w, o_all] + ([ps_use.get(dc % 4)] if kc == 0 else []))
                wd_use[sl] = mml
                for ic in range(ICN):
                    obf_use[ic] = tok_max(obf_use.get(ic), mml)
                mml = self.pe_drain(mml)
                t_ev = prog.op(A_, "activation", self.oxo[dc % 2][:], pb[:], AF.Identity,
                               waits=[mml, oxo_use.get(dc % 2)])
                ps_use[dc % 4] = t_ev
                t_fin = prog.op(V, "tensor_tensor", self.qdn[dc % 2][:], self.oxo[dc % 2][:],
                                self.x1[dc][:, q * QT:(q + 1) * QT], op=ALU.add,
                                waits=[t_ev, self.t_xm[(q, dc)], qdn_use.get(dc % 2)])
                oxo_use[dc % 2] = t_fin
                if "ffnout" in self.dbg:
                    if "ffnout" not in self.taps:
                        self.tap("ffnout", [D, TOK])
                    tfo = prog.op(SY, "dma_start",
                                  out=self.taps["ffnout"][dc * 128:(dc + 1) * 128, q * QT:(q + 1) * QT],
                                  in_=self.oxo[dc % 2][:], waits=[t_ev])
                    oxo_use[dc % 2] = tok_max(t_fin, tfo)
                out_d.append(prog.op(SY, "dma_start",
                                     out=self.out[dc * 128:(dc + 1) * 128, q * QT:(q + 1) * QT],
                                     in_=self.qdn[dc % 2][:], waits=[t_fin]))
                qdn_use[dc % 2] = tok_max(t_fin, out_d[-1])

        for t in range(N + 3):
            if 0 <= t - 3 < N:
                segD(t - 3, states.pop(t - 3))
                qd, icd = divmod(t - 3, ICN)
                if icd == ICN - 1:
                    emit_down(qd)
            if 0 <= t - 2 < N:
                self._up_segC(t - 2, states[t - 2])
            if 0 <= t - 1 < N:
                q, ic = divmod(t - 1, ICN)
                self._up_segB(ic, q, t - 1, states[t - 1])
            if t < N:
                q, ic = divmod(t, ICN)
                states[t] = self._up_segA(ic, q, t, [self.t_sco])
        fin = tok_max(*out_d)
        prog.raw(SY, lambda e: None, waits=[fin])


def build_program(dbg=(), stop=99):
    b = Builder(dbg, stop=stop)
    nc = b.build()
    return nc, b


def _host_inputs(inputs, cid):
    b, h = cid // 2, cid % 2
    x = np.asarray(inputs["L0_attn_x"], np.float32)
    xT = np.ascontiguousarray(x[b].T)
    x_scan = np.ascontiguousarray(xT[h * DH:(h + 1) * DH])
    cols = np.concatenate([np.arange((2 * j + h) * OT, (2 * j + h) * OT + OT) for j in range(NW)])
    x_own = np.ascontiguousarray(xT[:, cols])

    def dhalf_lane(a):
        return np.ascontiguousarray(
            np.asarray(a, np.float32)[h * DH:(h + 1) * DH].reshape(SCN, 128, NS).transpose(1, 0, 2).reshape(128, SCN * NS))

    def dhalf_col(v):
        return np.ascontiguousarray(np.asarray(v, np.float32)[h * DH:(h + 1) * DH].reshape(SCN, 128).T)

    def full_col(v, n):
        return np.ascontiguousarray(np.asarray(v, np.float32).reshape(n, 128).T)

    dt16 = np.exp(np.linspace(np.log(0.001), np.log(0.1), NS)).astype(np.float32)
    dtb = np.ascontiguousarray(np.broadcast_to(np.tile(dt16, SCN), (128, SCN * NS)).astype(np.float32))

    return {
        "x_scan": x_scan,
        "x_own": x_own,
        "A_s": dhalf_lane(inputs["L0_attn_A_ssm"]),
        "B_s": dhalf_lane(inputs["L0_attn_B_ssm"]),
        "C_s": dhalf_lane(inputs["L0_attn_C_ssm"]),
        "dtb": dtb,
        "D_s": dhalf_col(inputs["L0_attn_D_ssm"]),
        "s1_s": dhalf_col(inputs["L0_attn_scale1"]),
        "s2c": full_col(inputs["L0_attn_scale2"], DCN),
        "b1c": full_col(inputs["L0_attn_b1"], FC),
        "b2c": full_col(inputs["L0_attn_b2"], DCN),
    }


_WCACHE = {}


def _host_weights(inputs):
    import ml_dtypes
    w1 = np.asarray(inputs["L0_attn_w1"], np.float32)
    w2 = np.asarray(inputs["L0_attn_w2"], np.float32)
    return {
        "wp1": pack_w(w1, DCN, FC),
        "wp2": pack_w(w2, FC, DCN),
        "wun": np.ascontiguousarray(np.asarray(inputs["L0_ffn_w_up"], np.float32)),
        "wgn": np.ascontiguousarray(np.asarray(inputs["L0_ffn_w_gate"], np.float32)),
        "wdn": np.ascontiguousarray(np.asarray(inputs["L0_ffn_w_down"], np.float32)),
        "identb": np.eye(128, dtype=np.float32).astype(ml_dtypes.bfloat16),
    }


def kernel(**inputs):
    nc, _b = build_program()
    wts = _host_weights(inputs)
    in_maps = []
    for cid in range(8):
        m = _host_inputs(inputs, cid)
        m.update(wts)
        in_maps.append(m)
    res = run_bass_kernel_spmd(nc, in_maps, core_ids=list(range(8)))
    outp = np.zeros((B, D, T), np.float32)
    for cid in range(8):
        b, h = cid // 2, cid % 2
        o = np.asarray(res.results[cid]["out"])
        for j in range(NW):
            outp[b][:, (2 * j + h) * OT:(2 * j + h) * OT + OT] = o[:, j * OT:(j + 1) * OT]
    return np.ascontiguousarray(outp.transpose(0, 2, 1))


# version 09
# speedup vs baseline: 1.4605x; 1.0720x over previous
import numpy as np
from contextlib import ExitStack

import concourse.bass as bass
import concourse.mybir as mybir
import bass_rust
from concourse.bass_utils import run_bass_kernel_spmd

F32 = mybir.dt.float32
BF16 = mybir.dt.bfloat16
I16 = mybir.dt.int16
I32 = mybir.dt.int32
U16 = mybir.dt.uint16
AF = mybir.ActivationFunctionType
ALU = mybir.AluOpType
AX = mybir.AxisListType

B, T, D, NS, DFF, DI = 4, 4096, 1024, 16, 4096, 3072
DH, SCN, DCN = 512, 4, 8
FC, ICN = DFF // 128, DI // 128
NW, WT, OT = 8, 512, 256
TOK, QT, NQ = 2048, 512, 4
QMAX = 32767.0
EPS_LN, EPS_RMS = 1e-5, 1e-6
PAIRS = [[0, 1], [2, 3], [4, 5], [6, 7]]
ALL8 = [list(range(8))]
ENGS = ("sync", "scalar", "vector", "gpsimd", "tensor")
V, A_, P_, G, SY, TE = "vector", "scalar", "gpsimd", "gpsimd", "sync", "tensor"


def tok_max(*tokens):
    outd = {}
    for t in tokens:
        if t is None:
            continue
        items = t.items() if isinstance(t, dict) else [t]
        for k, v in items:
            if outd.get(k, -1) < v:
                outd[k] = v
    return outd


class Prog:
    SEMS = ["sv", "ss", "sp", "st", "dy", "dq", "dg", "cc"]

    def __init__(self, nc):
        self.nc = nc
        self.buf = {e: [] for e in ENGS}
        self.cnt = {}
        self.sems = {}
        self.marks = {}

    def _semname(self, eng, meth):
        if meth == "collective_compute":
            return "cc"
        if meth is not None and "dma" in meth:
            return {"sync": "dy", "scalar": "dq", "gpsimd": "dg"}.get(eng, "dy")
        return {"vector": "sv", "scalar": "ss", "gpsimd": "sp", "tensor": "st", "sync": "dy"}[eng]

    def _filter(self, eng, waits):
        need = tok_max(*waits)
        fil = {}
        for k, v in need.items():
            if self.marks.get((eng, k), 0) < v:
                fil[k] = v
                self.marks[(eng, k)] = v
        return fil

    def op(self, eng, meth, *args, waits=(), argfn=None, **kw):
        fil = self._filter(eng, waits)
        semname = self._semname(eng, meth)
        inc = 16 if semname in ("dy", "dq", "dg") else 1
        self.cnt[semname] = self.cnt.get(semname, 0) + inc
        self.buf[eng].append((fil, meth, args, kw, semname, inc, argfn))
        return (semname, self.cnt[semname])

    def raw(self, eng, fn, waits=()):
        fil = self._filter(eng, waits)
        self.buf[eng].append((fil, None, None, None, None, 0, fn))

    def fence(self):
        return dict(self.cnt)

    def flush(self, block):
        for eng in ENGS:
            items = self.buf[eng]
            self.buf[eng] = []
            if not items:
                continue

            def make(items=items):
                def body(e):
                    for fil, meth, args, kw, semname, inc, argfn in items:
                        for k, v in fil.items():
                            e.wait_ge(self.sems[k], v)
                        if meth is None:
                            argfn(e)
                            continue
                        a, k2 = (args, kw) if argfn is None else argfn(e)
                        getattr(e, meth)(*a, **k2).then_inc(self.sems[semname], inc)
                return body
            getattr(block, eng)(make())


def pack_w(w, KC, MC):
    a = np.asarray(w, np.float32).reshape(KC, 128, MC, 128).transpose(2, 0, 1, 3)
    return np.ascontiguousarray(a.reshape(MC * KC * 128, 128))


def slab_ap(dram_ap, mc0, nmc, KC):
    rows = dram_ap[mc0 * KC * 128:(mc0 + nmc) * KC * 128, :]
    return rows.rearrange("(o kc p) f -> p o kc f", o=nmc, kc=KC, p=128)


def slab_out(tile, nmc, KC):
    return tile.ap().rearrange("p (o kc f) -> p o kc f", o=nmc, kc=KC)


class Builder:
    def __init__(self, dbg=(), stop=99):
        self.nc = bass.Bass("TRN2", target_bir_lowering=False)
        self.stop = stop
        self.dbg = list(dbg)
        self.prog = Prog(self.nc)
        self.dr = {}
        self.taps = {}
        self.hcell = {}

    def din(self, name, shape, dt=F32):
        self.dr[name] = self.nc.dram_tensor(name, list(shape), dt, kind="ExternalInput").ap()
        return self.dr[name]

    def dint(self, name, shape, dt=F32):
        self.dr[name] = self.nc.dram_tensor(name, list(shape), dt)
        return self.dr[name]

    def tap(self, name, shape, dt=F32):
        self.taps[name] = self.nc.dram_tensor("tap_" + name, list(shape), dt, kind="ExternalOutput").ap()
        return self.taps[name]

    def build(self):
        nc, prog = self.nc, self.prog
        self.x_scan = self.din("x_scan", [DH, T])
        self.x_own = self.din("x_own", [D, TOK])
        for n, s in (("A_s", [128, SCN * NS]), ("B_s", [128, SCN * NS]), ("C_s", [128, SCN * NS]),
                     ("dtb", [128, SCN * NS]), ("D_s", [128, SCN]), ("s1_s", [128, SCN]),
                     ("s2c", [128, DCN]), ("b1c", [128, FC]), ("b2c", [128, DCN])):
            self.din(n, s)
        for n, s in (("wp1", [FC * DCN * 128, 128]), ("wp2", [DCN * FC * 128, 128])):
            self.din(n, s)
        self.din("wun", [DI, D])
        self.din("wgn", [DI, D])
        self.din("wdn", [D, DI])
        self.dr["identb"] = self.nc.dram_tensor("identb", [128, 128], BF16, kind="ExternalInput").ap()
        self.dint("wqu", [ICN * DCN * 128, 128], BF16)
        self.dint("wqg", [ICN * DCN * 128, 128], BF16)
        self.dint("wqd", [DCN * ICN * 128, 128], BF16)
        self.out = self.nc.dram_tensor("out", [D, TOK], F32, kind="ExternalOutput").ap()
        self.stats_in = self.dint("stats_in", [2, T])
        self.stats_out = self.dint("stats_out", [4, T])
        self.yag_in = [self.dint(f"yag_in{w}", [DH, WT]) for w in range(NW)]
        self.yag_out = [self.dint(f"yag_out{w}", [D, WT]) for w in range(NW)]
        self.m_in = {k: self.dint(f"min_{k}", [1, 2]) for k in ("m1", "mug", "mg2", "mo")}
        self.m_out = {k: self.dint(f"mout_{k}", [1, 2]) for k in ("m1", "mug", "mg2", "mo")}
        self.sp_up = self.dint("sp_up", [DI, TOK], BF16)
        self.sp_gate = self.dint("sp_gate", [DI, TOK], BF16)

        with ExitStack() as ctx:
            self.ctx = ctx
            S = lambda st, n, sh, dt=F32: st.enter_context(nc.sbuf_tensor(n, list(sh), dt))
            self.S = S
            self.ones = S(ctx, "ones", [128, WT])
            self.onesbf = S(ctx, "onesbf", [128, 8], BF16)
            self.dAv = S(ctx, "dAv", [128, SCN * NS])
            self.CdB = S(ctx, "CdB", [128, SCN * NS])
            self.sD = S(ctx, "sD", [128, SCN])
            self.s_s2 = S(ctx, "s_s2", [128, DCN])
            self.s_b1 = S(ctx, "s_b1", [128, FC])
            self.s_b2 = S(ctx, "s_b2", [128, DCN])
            self.states = S(ctx, "states", [128, SCN * NS])
            self.x1 = [S(ctx, f"x1_{dc}", [128, TOK]) for dc in range(DCN)]
            self.rows = S(ctx, "rows", [1, 6144])
            self.maxc = S(ctx, "maxc", [128, 256])
            self.scal = S(ctx, "scal", [128, 8])
            self.scal2 = S(ctx, "scal2", [128, 2])
            self.srow1 = S(ctx, "srow1", [1, 32])
            self.cbias = S(ctx, "cbias", [128, 4])
            self.drt = S(ctx, "drt", [128, 8])
            for name in Prog.SEMS:
                prog.sems[name] = ctx.enter_context(nc.semaphore(name))
            self.block = ctx.enter_context(nc.Block())

            def mk_h(e):
                pid = e.partition_id()
                h = e.compute_val(pid % 2)
                reg = e.alloc_register("hreg")
                e.reg_mov(reg, h)
                self.hcell["hrv"] = bass_rust.RuntimeValue(reg, min_val=0, max_val=1)
            prog.raw(SY, mk_h)

            self.t_ones = prog.op(V, "memset", self.ones[:], 1.0)
            tb0 = prog.op(V, "memset", self.cbias[:, 0:1], EPS_LN)
            tb1 = prog.op(V, "memset", self.cbias[:, 1:2], EPS_LN * EPS_RMS)
            tb2 = prog.op(V, "memset", self.cbias[:, 2:3], EPS_RMS)
            self.t_cbias = tok_max(tb0, tb1, tb2)
            self.t_onesbf = prog.op(V, "memset", self.onesbf[:, 0:1], 1.0)
            self.t_st0 = prog.op(V, "memset", self.states[:], 0.0)

            self.ph_params()
            if self.stop >= 2:
                self.run_main()
            prog.flush(self.block)
        return nc

    def run_main(self):
        nc, prog = self.nc, self.prog
        ctx = self.ctx
        if True:
            with ExitStack() as rgx:
                self.x1ln = [self.S(rgx, f"x1ln_{dc}", [128, TOK], BF16) for dc in range(DCN)]
                self.cstg = [self.S(rgx, f"cstg{i}", [128, WT], BF16) for i in range(2)]
                with ExitStack() as psc:
                    self.alloc_psum(psc)
                    with ExitStack() as rg:
                        self.alloc_scan(rg)
                        self.ph_stats_u()
                        if self.stop >= 3:
                            self.ph_scan()
                        if self.stop >= 4:
                            self.ph_x1()
                        prog.flush(self.block)
                    if self.stop >= 5:
                        with ExitStack() as rga:
                            self.alloc_attn(rga)
                            self.ph_attn()
                            prog.flush(self.block)
                    if self.stop >= 6:
                        self.ph_ln2_m1_xq_pre()
                    prog.flush(self.block)
            if self.stop < 7:
                return
            self.fence_ffn = prog.fence()
            with ExitStack() as psc:
                self.alloc_psum(psc)
                with ExitStack() as rgfU:
                    self.xqb = [self.S(rgfU, f"xq_{dc}", [128, TOK], BF16) for dc in range(DCN)]
                    with ExitStack() as rgf1:
                        self.ln2 = [self.S(rgf1, f"ln2_{dc}", [128, TOK]) for dc in range(DCN)]
                        self.ql = [self.S(rgf1, f"qlx{i}", [128, QT], I16) for i in range(4)]
                        self.qf = [self.S(rgf1, f"qfx{i}", [128, QT]) for i in range(2)]
                        self.ph_ln2_m1_xq()
                        prog.flush(self.block)
                    self.fence_q = prog.fence()
                    with ExitStack() as rgf2:
                        self.alloc_ffn_u(rgf2)
                        if self.stop >= 8:
                            self.ph_upgate()
                        prog.flush(self.block)
                self.fence_p = prog.fence()
                with ExitStack() as rgfP:
                    self.alloc_ffn_p(rgfP)
                    if self.stop >= 9:
                        self.ph_passA()
                    if self.stop >= 10:
                        self.ph_passB()
                    if self.stop >= 11:
                        self.ph_passC_down()
                    prog.flush(self.block)

    _psum_gen = [0]

    def alloc_psum(self, psc):
        nc = self.nc
        g = self._psum_gen[0]
        self._psum_gen[0] += 1
        self.ps_a = psc.enter_context(nc.psum_tensor(f"ps_a{g}", [128, QT], F32))
        self.ps_b = psc.enter_context(nc.psum_tensor(f"ps_b{g}", [128, QT], F32))
        self.ps_o1 = psc.enter_context(nc.psum_tensor(f"ps_o1{g}", [128, WT], F32))
        self.ps_o2 = psc.enter_context(nc.psum_tensor(f"ps_o2{g}", [128, WT], F32))
        self.ps_o3 = psc.enter_context(nc.psum_tensor(f"ps_o3{g}", [128, OT], F32))
        self.ps_s1 = psc.enter_context(nc.psum_tensor(f"ps_s1{g}", [1, WT], F32))
        self.ps_s2 = psc.enter_context(nc.psum_tensor(f"ps_s2{g}", [1, WT], F32))
        self.ps_aux = psc.enter_context(nc.psum_tensor(f"ps_aux{g}", [128, 512], F32))
        self.ps_t = self.ps_aux.ap()[:, 0:256].bitcast(BF16)
        self.ps_dmy = self.ps_aux.ap()[:, 500:508]

    def alloc_scan(self, rg):
        S = self.S
        WP = 2 * WT
        self.z_t = [S(rg, f"z{i}", [128, WP]) for i in range(4)]
        self.dab = [S(rg, f"dab{i}", [128, WP]) for i in range(2)]
        self.prb = [S(rg, f"prb{i}", [128, WP]) for i in range(2)]
        self.s_out = [S(rg, f"sout{i}", [128, WP]) for i in range(2)]
        self.y_t = [S(rg, f"yt{i}", [128, WP]) for i in range(2)]
        self.yall = S(rg, "yall", [128, DCN * OT])
        self.xoall = S(rg, "xoall", [128, DCN * OT])
        self.xscr = [S(rg, f"xscr{i}", [128, OT]) for i in range(4)]
        self.gstat = [S(rg, f"gstat{i}", [128, 32]) for i in range(2)]
        self.gd = [S(rg, f"gd{i}", [128, 32]) for i in range(2)]
        self.rgf = S(rg, "rgf", [1, WP])
        self.rmg = S(rg, "rmg", [1, WP])
        self.rrow = S(rg, "rrow", [1, OT])
        self.mrow = S(rg, "mrow", [1, OT])
        self.ostat = [S(rg, f"ostat{i}", [32, 32]) for i in range(2)]
        self.od = [S(rg, f"od{i}", [32, 32]) for i in range(2)]


    def alloc_attn(self, rg):
        S = self.S
        self.dq_w = [S(rg, f"dq_w{i}", [128, D]) for i in range(1)]
        self.dq_l = [S(rg, f"dq_l{i}", [128, D], I16) for i in range(1)]
        self.dq_q = [S(rg, f"dq_q{i}", [128, D], BF16) for i in range(2)]
        self.dq_s = [S(rg, f"dq_s{i}", [128, 3 * 32]) for i in range(2)]
        self.dq_st = [S(rg, f"dq_st{i}", [128, DCN * 128], BF16) for i in range(1)]
        self.identb = S(rg, "identb_sb", [128, 128], BF16)
        self.w1s = [S(rg, f"w1s{i}", [128, 2 * DCN * 128], BF16) for i in range(3)]
        self.w2s = [S(rg, f"w2s{i}", [128, FC * 128], BF16) for i in range(2)]
        self.g1bf = S(rg, "g1bf", [128, FC * QT], BF16)
        self.evscr = [S(rg, f"ev{i}", [128, QT]) for i in range(2)]
        self.xo2 = [S(rg, f"xo2_{i}", [128, QT]) for i in range(2)]

    def alloc_ffn_u(self, rg):
        S = self.S
        self.wus = [S(rg, f"wus{i}", [128, 2 * DCN * 128], BF16) for i in range(2)]
        self.wgs = [S(rg, f"wgs{i}", [128, 2 * DCN * 128], BF16) for i in range(2)]
        self.stg = [S(rg, f"stg{i}", [128, TOK], BF16) for i in range(2)]

    def alloc_ffn_p(self, rg):
        S = self.S
        self.qa = [S(rg, f"qa{i}", [128, QT], BF16) for i in range(8)]
        self.ql = [S(rg, f"ql{i}", [128, QT], I16) for i in range(12)]
        self.qm = [S(rg, f"qm{i}", [128, QT], U16) for i in range(4)]
        self.qf = [S(rg, f"qf{i}", [128, QT]) for i in range(16)]
        self.obf = [S(rg, f"obf{i}", [128, QT], BF16) for i in range(ICN)]
        self.wds = [S(rg, f"wds{i}", [128, ICN * 128], BF16) for i in range(2)]
        self.oxo = [S(rg, f"oxo{i}", [128, QT]) for i in range(2)]
        self.qdn = [S(rg, f"qdn{i}", [128, QT]) for i in range(2)]

    R_M, R_VAR, R_RSTD, R_GFAC, R_MG, R_MRSTD, R_TMP = (i * WT for i in range(7))
    R_SR4 = 7 * WT
    L_S1, L_S2, L_M2R = 0, TOK, 2 * TOK

    def row(self, off, n=WT):
        return self.rows[:, off:off + n]

    def eng_drain(self, eng, *toks):
        if eng == V:
            t = self.prog.op(V, "memset", self.drt[:, 0:4], 0.0, waits=list(toks))
        elif eng == A_:
            t = self.prog.op(A_, "activation", self.drt[:, 4:8], self.drt[:, 0:4], AF.Identity,
                             waits=list(toks))
        else:
            t = self.prog.op(P_, "memset", self.drt[:, 0:4], 0.0, waits=list(toks))
        return tok_max(t, *toks)

    def pe_drain(self, *toks):
        t = self.prog.op(TE, "matmul", self.ps_dmy, self.ones[0:1, 0:128],
                         self.ones[0:1, 0:8], waits=list(toks))
        return tok_max(t, *toks)

    def ph_params(self):
        prog, S = self.prog, self.S
        with ExitStack() as rg:
            sA = S(rg, "sA", [128, SCN * NS]); sB = S(rg, "sB", [128, SCN * NS])
            sC = S(rg, "sC", [128, SCN * NS]); sdt = S(rg, "sdt", [128, SCN * NS])
            sp1 = S(rg, "sp1", [128, SCN * NS]); s_s1 = S(rg, "s_s1", [128, SCN])
            dr = self.dr
            tc = tok_max(*[prog.op(SY, "dma_start", out=d[:], in_=dr[n]) for n, d in
                           (("A_s", sA), ("B_s", sB), ("C_s", sC), ("dtb", sdt),
                            ("D_s", self.sD), ("s1_s", s_s1), ("s2c", self.s_s2),
                            ("b1c", self.s_b1), ("b2c", self.s_b2))])
            t_expA = prog.op(A_, "activation", sp1[:], sA[:], AF.Exp, waits=[tc])
            t_r = prog.op(V, "reciprocal", self.CdB[:], sp1[:], waits=[t_expA])
            t_m = prog.op(V, "tensor_tensor", self.dAv[:], sp1[:], sdt[:], op=ALU.mult, waits=[t_expA, tc])
            self.t_dA = prog.op(A_, "activation", self.dAv[:], self.dAv[:], AF.Exp, scale=-1.0, waits=[t_m])
            t_1m = prog.op(V, "tensor_scalar", sp1[:], self.dAv[:], -1.0, 1.0,
                           op0=ALU.mult, op1=ALU.add, waits=[self.t_dA, t_r])
            t_d1 = prog.op(V, "tensor_tensor", self.CdB[:], self.CdB[:], sp1[:], op=ALU.mult, waits=[t_1m])
            t_d2 = prog.op(V, "tensor_tensor", self.CdB[:], self.CdB[:], sB[:], op=ALU.mult, waits=[t_d1])
            t_d3 = prog.op(V, "tensor_tensor", self.CdB[:], self.CdB[:], sC[:], op=ALU.mult, waits=[t_d2])
            tl = []
            for sc in range(SCN):
                tl.append(prog.op(V, "tensor_scalar", self.CdB[:, sc * NS:(sc + 1) * NS],
                                  self.CdB[:, sc * NS:(sc + 1) * NS], s_s1[:, sc:sc + 1], None,
                                  op0=ALU.mult, waits=[t_d3]))
            self.t_cdb = tok_max(*tl)
            self.t_Df = prog.op(V, "tensor_tensor", self.sD[:], self.sD[:], s_s1[:], op=ALU.mult, waits=[tc])
            self.fence_params = prog.fence()
            prog.flush(self.block)

    def ph_stats_u(self):
        prog = self.prog
        row, R_TMP, R_SR4 = self.row, 0, T
        self.t_xs = {}
        self.use_cstg = {}
        use_ps1 = use_ps2 = None
        stat_d = []
        xs_free = {}
        for w in range(NW):
            mm1 = mm2 = None
            for sc in range(SCN):
                xst = self.z_t[sc].ap()[:, (w % 2) * WT:(w % 2) * WT + WT]
                t_l = prog.op(SY, "dma_start", out=xst,
                              in_=self.x_scan[sc * 128:(sc + 1) * 128, w * WT:(w + 1) * WT],
                              waits=[xs_free.get((sc, w % 2)), self.fence_params if w == 0 else None])
                t_c = prog.op(A_, "activation", self.cstg[0][:], xst, AF.Identity,
                              waits=[t_l, self.use_cstg.get(0)])
                t_q = prog.op(A_, "activation", self.cstg[1][:], xst, AF.Square,
                              waits=[t_l, self.use_cstg.get(1)])
                xs_free[(sc, w % 2)] = tok_max(t_c, t_q)
                mm1 = prog.op(TE, "matmul", self.ps_s1[:], self.onesbf[:, 0:1], self.cstg[0][:],
                              start=(sc == 0), stop=(sc == SCN - 1),
                              waits=[t_c, self.t_onesbf] + ([use_ps1] if sc == 0 else []))
                mm2 = prog.op(TE, "matmul", self.ps_s2[:], self.onesbf[:, 0:1], self.cstg[1][:],
                              start=(sc == 0), stop=(sc == SCN - 1),
                              waits=[t_q, self.t_onesbf] + ([use_ps2] if sc == 0 else []))
                self.use_cstg[0] = mm1
                self.use_cstg[1] = mm2
            t_mm = self.pe_drain(mm1, mm2)
            t_e1 = prog.op(A_, "activation", row(R_TMP), self.ps_s1[:], AF.Identity,
                           waits=[t_mm] + ([stat_d[-2]] if stat_d else []))
            t_e2 = prog.op(A_, "activation", row(R_SR4), self.ps_s2[:], AF.Identity,
                           waits=[t_mm] + ([stat_d[-1]] if stat_d else []))
            use_ps1, use_ps2 = t_e1, t_e2
            stat_d.append(prog.op(SY, "dma_start", out=self.stats_in[0:1, w * WT:(w + 1) * WT],
                                  in_=row(R_TMP), waits=[t_e1]))
            stat_d.append(prog.op(SY, "dma_start", out=self.stats_in[1:2, w * WT:(w + 1) * WT],
                                  in_=row(R_SR4), waits=[t_e2]))
        self.xs_free = xs_free
        self.use_ps1, self.use_ps2 = use_ps1, use_ps2
        self.t_sag = prog.op(G, "collective_compute", "AllGather", ALU.bypass,
                             replica_groups=PAIRS, ins=[self.stats_in.ap()], outs=[self.stats_out.ap()],
                             waits=[tok_max(*stat_d)])
        self.rows_use = tok_max(*stat_d)

    def ph_scan(self):
        prog = self.prog
        WP = 2 * WT
        self.ywtok = [None] * NW
        stuse = {}
        dause, pruse, sduse = {}, {}, {}
        zuse = {}
        yfree = {}
        gstat_use, gd_use, rgf_use = {}, {}, {}
        t_gload, t_wb = {}, {}
        combo = 0

        def gload(wp):
            sl = wp % 2
            rr = []
            for r in range(4):
                rr.append(prog.op(SY, "dma_start", out=self.gstat[sl][:, r * 8:(r + 1) * 8],
                                  in_=self.stats_out[r:r + 1, wp * WP:(wp + 1) * WP],
                                  waits=[self.t_sag, gstat_use.get(sl)]))
            t_gload[wp] = tok_max(*rr)

        def derive(wp, extra=()):
            sl = wp % 2
            g, d = self.gstat[sl], self.gd[sl]
            tS1 = prog.op(V, "tensor_tensor", d[:, 0:8], g[:, 0:8], g[:, 16:24], op=ALU.add,
                          waits=[t_gload[wp], gd_use.get(sl)] + list(extra))
            tS2 = prog.op(V, "tensor_tensor", d[:, 8:16], g[:, 8:16], g[:, 24:32], op=ALU.add,
                          waits=[t_gload[wp], gd_use.get(sl)])
            gstat_use[sl] = tok_max(tS1, tS2)
            tm = prog.op(V, "tensor_scalar", d[:, 0:8], d[:, 0:8], 1.0 / D, None, op0=ALU.mult,
                         waits=[tS1])
            tex = prog.op(V, "tensor_scalar", d[:, 8:16], d[:, 8:16], 1.0 / D, None, op0=ALU.mult,
                          waits=[tS2])
            tm2 = prog.op(V, "tensor_tensor", d[:, 16:24], d[:, 0:8], d[:, 0:8], op=ALU.mult,
                          waits=[tm, tS2])
            tvar = prog.op(V, "tensor_tensor", d[:, 8:16], d[:, 8:16], d[:, 16:24], op=ALU.subtract,
                           waits=[tex, tm2])
            tl2 = prog.op(A_, "activation", d[:, 24:32], d[:, 8:16], AF.Ln, scale=1.0 + EPS_RMS,
                          bias=self.cbias[:, 1:2], waits=[tvar, self.t_cbias])
            tgf = prog.op(A_, "activation", d[:, 24:32], d[:, 24:32], AF.Exp, scale=-0.5, waits=[tl2])
            tmg = prog.op(V, "tensor_tensor", d[:, 8:16], d[:, 0:8], d[:, 24:32], op=ALU.mult,
                          waits=[tgf, tm])
            tgf_d = self.eng_drain(A_, tgf)
            tmg_d = self.eng_drain(V, tmg)
            t_gfr = prog.op(SY, "dma_start", out=self.rgf[0:1, :], in_=d[:, 24:32],
                            waits=[tgf_d, rgf_use.get("gf")])
            t_mgr = prog.op(SY, "dma_start", out=self.rmg[0:1, :], in_=d[:, 8:16],
                            waits=[tmg_d, rgf_use.get("mg")])
            gd_use[sl] = tok_max(t_gfr, t_mgr)
            t_wb[wp] = (t_gfr, t_mgr)

        def xload(wp, init=None):
            t_lx = {}
            for sc in range(SCN):
                t_lx[sc] = prog.op(SY if sc % 2 == 0 else G, "dma_start", out=self.z_t[sc][:],
                                   in_=self.x_scan[sc * 128:(sc + 1) * 128, wp * WP:(wp + 1) * WP],
                                   waits=[init[sc] if init is not None else zuse.get(("z", sc))])
            return t_lx

        gload(0)
        init0 = {sc: tok_max(self.xs_free.get((sc, 0)), self.xs_free.get((sc, 1)))
                 for sc in range(SCN)}
        t_lx = xload(0, init=init0)
        t_lx_next = {}
        derive(0, extra=list(t_lx.values()))

        for wp in range(NW // 2):
            if wp > 0:
                t_lx = t_lx_next
                t_lx_next = {}
            t_gfr, t_mgr = t_wb[wp]
            if wp == 0:
                sp1 = prog.op(SY, "dma_start", out=self.stats_in[0:1, 0:WP], in_=self.rgf[0:1, :],
                              waits=[t_gfr])
                sp2 = prog.op(SY, "dma_start", out=self.stats_in[1:2, 0:WP], in_=self.rmg[0:1, :],
                              waits=[t_mgr, sp1])
                t_gfr = tok_max(t_gfr, sp2)
                t_mgr = tok_max(t_mgr, sp2)
            z_parts = {}
            for h in range(2):
                t_og = prog.op(TE, "matmul", self.ps_o1[:], self.ones[0:1, 0:128],
                               self.rgf[0:1, h * WT:(h + 1) * WT],
                               waits=[t_gfr, self.t_ones, zuse.get("o1")])
                t_omg = prog.op(TE, "matmul", self.ps_o2[:], self.ones[0:1, 0:128],
                                self.rmg[0:1, h * WT:(h + 1) * WT],
                                waits=[t_mgr, self.t_ones, zuse.get("o2")])
                t_og = t_omg = self.pe_drain(t_og, t_omg)
                rgf_use["gf"] = t_og
                rgf_use["mg"] = t_og
                tcons = []
                for sc in range(SCN):
                    zh = self.z_t[sc].ap()[:, h * WT:(h + 1) * WT]
                    t1 = prog.op(V, "tensor_tensor", zh, zh, self.ps_o1[:],
                                 op=ALU.mult, waits=[t_lx[sc], t_og])
                    t2 = prog.op(V, "tensor_tensor", zh, zh, self.ps_o2[:], op=ALU.subtract,
                                 waits=[t1])
                    tcons.append(t2)
                    z_parts[(sc, h)] = t2
                zuse["o1"] = tok_max(*tcons)
                zuse["o2"] = zuse["o1"]
            tz_w = {sc: tok_max(z_parts[(sc, 0)], z_parts[(sc, 1)]) for sc in range(SCN)}
            if "z" in self.dbg:
                if "z" not in self.taps:
                    self.tap("z", [DH, T])
                for sc in range(SCN):
                    tz_d = prog.op(SY, "dma_start",
                                   out=self.taps["z"][sc * 128:(sc + 1) * 128, wp * WP:(wp + 1) * WP],
                                   in_=self.z_t[sc][:], waits=[tz_w[sc]])
                    tz_w[sc] = tok_max(tz_w[sc], tz_d)

            if wp + 1 < NW // 2:
                gload(wp + 1)
                derive(wp + 1)

            ybounce = []
            for sc in range(SCN):
                yt = self.y_t[sc % 2]
                acc_prev = None
                prb_toks = []
                for n in range(NS):
                    ci = sc * NS + n
                    k = combo % 2
                    t_pr = prog.op(A_, "activation", self.prb[k][:], self.z_t[sc][:], AF.Identity,
                                   scale=self.CdB[:, ci:ci + 1],
                                   waits=[tz_w[sc], self.t_cdb, pruse.get(k)])
                    prb_toks.append(t_pr)
                    t_da = prog.op(A_, "activation", self.dab[k][:], self.prb[k][:], AF.Identity,
                                   scale=0.0, bias=self.dAv[:, ci:ci + 1],
                                   waits=[t_pr, self.t_dA, dause.get(k)])
                    init = 0.0 if wp == 0 else self.states[:, ci:ci + 1]
                    t_sc = prog.op(V, "tensor_tensor_scan", self.s_out[k][:], self.dab[k][:],
                                   self.prb[k][:], init, op0=ALU.mult, op1=ALU.add,
                                   waits=[t_da, t_pr, sduse.get(k), self.t_st0] +
                                         ([stuse[ci]] if ci in stuse else []))
                    t_cp = prog.op(V, "tensor_scalar", self.states[:, ci:ci + 1],
                                   self.s_out[k][:, WP - 1:WP], 1.0, None, op0=ALU.mult,
                                   waits=[t_sc])
                    stuse[ci] = t_cp
                    dause[k] = t_sc
                    pruse[k] = t_sc
                    if n == 0:
                        t_ac = prog.op(V, "scalar_tensor_tensor", yt[:], self.z_t[sc][:],
                                       self.sD[:, sc:sc + 1], self.s_out[k][:],
                                       op0=ALU.mult, op1=ALU.add,
                                       waits=[t_sc, tz_w[sc], self.t_Df, yfree.get(sc % 2)])
                    else:
                        t_ac = prog.op(V, "tensor_tensor", yt[:], yt[:], self.s_out[k][:],
                                       op=ALU.add, waits=[t_sc, acc_prev])
                    sduse[k] = tok_max(t_ac, t_cp)
                    acc_prev = t_ac
                    combo += 1
                zuse[("z", sc)] = tok_max(acc_prev, tz_w[sc], *prb_toks)
                ybt = []
                for h in range(2):
                    w = 2 * wp + h
                    t_yd = prog.op(SY, "dma_start",
                                   out=self.yag_in[w][sc * 128:(sc + 1) * 128, :],
                                   in_=yt[:, h * WT:(h + 1) * WT], waits=[acc_prev])
                    ybt.append(t_yd)
                    ybounce.append((w, t_yd))
                yfree[sc % 2] = tok_max(*ybt)
                if wp + 1 < NW // 2:
                    t_lx_next[sc] = prog.op(
                        SY if sc % 2 == 0 else G, "dma_start", out=self.z_t[sc][:],
                        in_=self.x_scan[sc * 128:(sc + 1) * 128, (wp + 1) * WP:(wp + 2) * WP],
                        waits=[zuse.get(("z", sc))])
            for h in range(2):
                w = 2 * wp + h
                toks = [t for (ww, t) in ybounce if ww == w]
                self.ywtok[w] = prog.op(G, "collective_compute", "AllGather", ALU.bypass,
                                        replica_groups=PAIRS, ins=[self.yag_in[w].ap()],
                                        outs=[self.yag_out[w].ap()], waits=[tok_max(*toks)])
                if "yag" in self.dbg:
                    if "yag" not in self.taps:
                        self.tap("yag", [D, T])
                    prog.op(SY, "dma_start", out=self.taps["yag"][:, w * WT:(w + 1) * WT],
                            in_=self.yag_out[w].ap(), waits=[self.ywtok[w]])
        self.zuse_final = dict(zuse)

    def ph_x1(self):
        prog = self.prog
        row = self.row
        R_TMP = self.R_TMP
        self.t_x1 = {}
        self.t_x1ln = {}
        yx_use = {}
        ost_use, od_use, rr_use = {}, {}, {}
        rowtmp_use = {}
        t_trd = {}
        o_use = {"o1": self.zuse_final.get("o1"), "o2": self.zuse_final.get("o2")}

        def oload(w):
            sl = w % 2
            base = T + 1024 * sl
            def mk_r(w=w, base=base):
                def fn(e):
                    return ((), dict(
                        out=self.rows[:, base:base + 4 * OT].rearrange("p (r c) -> p r c", r=4),
                        in_=self.stats_out.rearrange("(a r) c -> a r c", a=1)[:, :, bass.ts(self.hcell["hrv"] + 2 * w, OT)]))
                return fn
            t_row = prog.op(SY, "dma_start", argfn=mk_r(),
                            waits=[self.t_sag, rowtmp_use.get(sl, self.rows_use)])
            rr = []
            for r in range(4):
                rr.append(prog.op(SY, "dma_start", out=self.ostat[sl][0:32, r * 8:(r + 1) * 8],
                                  in_=self.rows[0:1, base + r * OT:base + (r + 1) * OT],
                                  waits=[t_row, ost_use.get(sl)]))
            t_trd[w] = tok_max(*rr)
            rowtmp_use[sl] = t_trd[w]

        oload(0)
        for w in range(NW):
            sl = w % 2
            def mk_yread(w=w):
                def fn(e):
                    src_ap = self.yag_out[w].rearrange("(dc p) t -> p dc t", p=128)
                    return ((), dict(out=self.yall.ap().rearrange("p (dc t) -> p dc t", dc=DCN),
                                     in_=src_ap[:, :, bass.ts(self.hcell["hrv"], OT)]))
                return fn
            t_yall = prog.op(SY, "dma_start", argfn=mk_yread(),
                             waits=[self.ywtok[w], yx_use.get("yall")])
            t_xoall = prog.op(SY, "dma_start",
                              out=self.xoall.ap().rearrange("p (dc t) -> p dc t", dc=DCN),
                              in_=self.x_own.rearrange("(dc p) t -> p dc t", p=128)[:, :, w * OT:(w + 1) * OT],
                              waits=[yx_use.get("xoall")])
            t_y = {dc: t_yall for dc in range(DCN)}
            t_xo = {dc: t_xoall for dc in range(DCN)}
            ost, od = self.ostat[sl], self.od[sl]
            trd = t_trd[w]
            if w + 1 < NW:
                oload(w + 1)
            tS1 = prog.op(V, "tensor_tensor", od[:, 0:8], ost[:, 0:8], ost[:, 16:24],
                          op=ALU.add, waits=[trd, od_use.get(sl)] +
                          ([t_xoall] if w == 0 else []))
            tS2 = prog.op(V, "tensor_tensor", od[:, 8:16], ost[:, 8:16], ost[:, 24:32],
                          op=ALU.add, waits=[trd, od_use.get(sl)])
            ost_use[sl] = tok_max(tS1, tS2)
            tm = prog.op(V, "tensor_scalar", od[:, 0:8], od[:, 0:8], 1.0 / D, None,
                         op0=ALU.mult, waits=[tS1])
            tex = prog.op(V, "tensor_scalar", od[:, 8:16], od[:, 8:16], 1.0 / D, None,
                          op0=ALU.mult, waits=[tS2])
            tm2 = prog.op(V, "tensor_tensor", od[:, 16:24], od[:, 0:8], od[:, 0:8],
                          op=ALU.mult, waits=[tm, tS2])
            tvar = prog.op(V, "tensor_tensor", od[:, 8:16], od[:, 8:16], od[:, 16:24],
                           op=ALU.subtract, waits=[tex, tm2])
            tl1 = prog.op(A_, "activation", od[:, 24:32], od[:, 8:16], AF.Ln,
                          bias=self.cbias[0:32, 0:1], waits=[tvar, self.t_cbias])
            trstd = prog.op(A_, "activation", od[:, 24:32], od[:, 24:32], AF.Exp, scale=-0.5,
                            waits=[tl1])
            tmr = prog.op(V, "tensor_tensor", od[:, 8:16], od[:, 0:8], od[:, 24:32],
                          op=ALU.mult, waits=[trstd, tm])
            rrow = self.rrow.ap()[0:1, :]
            mrow = self.mrow.ap()[0:1, :]
            trstd_d = self.eng_drain(A_, trstd)
            tmr_d = self.eng_drain(V, tmr)
            t_rr = prog.op(SY, "dma_start", out=rrow, in_=od[:, 24:32],
                           waits=[trstd_d, rr_use.get("r")])
            t_mr = prog.op(SY, "dma_start", out=mrow, in_=od[:, 8:16],
                           waits=[tmr_d, rr_use.get("m")])
            od_use[sl] = tok_max(t_rr, t_mr)
            t_or = prog.op(TE, "matmul", self.ps_o1[:, 0:OT], self.ones[0:1, 0:128],
                           rrow, waits=[t_rr, o_use.get("o1")])
            t_omr = prog.op(TE, "matmul", self.ps_o2[:, 0:OT], self.ones[0:1, 0:128],
                            mrow, waits=[t_mr, o_use.get("o2")])
            t_or = t_omr = self.pe_drain(t_or, t_omr)
            rr_use["r"] = t_or
            rr_use["m"] = t_or
            sq_mm = None
            p1l = []
            for dc in range(DCN):
                xr = self.xscr[dc % 2]
                yr = self.xscr[2 + dc % 2]
                p1 = prog.op(V, "tensor_tensor", xr[:], self.xoall[:, dc * OT:(dc + 1) * OT],
                             self.ps_o1[:, 0:OT],
                             op=ALU.mult, waits=[t_xo[dc], t_or, yx_use.get(("s", dc % 2))])
                p2 = prog.op(V, "tensor_tensor", yr[:], self.yall[:, dc * OT:(dc + 1) * OT],
                             self.ps_o2[:, 0:OT],
                             op=ALU.subtract, waits=[t_y[dc], t_omr, yx_use.get(("s2", dc % 2))])
                tx1 = prog.op(P_, "tensor_tensor", self.x1[dc][:, w * OT:(w + 1) * OT], xr[:], yr[:],
                              op=ALU.add, waits=[p1, p2])
                self.t_x1[(w, dc)] = tx1
                yx_use["yall"] = tok_max(yx_use.get("yall"), p2)
                yx_use["xoall"] = tok_max(yx_use.get("xoall"), p1)
                yx_use[("s", dc % 2)] = tx1
                yx_use[("s2", dc % 2)] = tx1
                p1l += [p1, p2]
                t_sq = prog.op(A_, "activation", self.cstg[dc % 2][:, 0:OT],
                               self.x1[dc][:, w * OT:(w + 1) * OT], AF.Square,
                               waits=[tx1, self.use_cstg.get(dc % 2)])
                sq_mm = prog.op(TE, "matmul", self.ps_s2[:, 0:OT], self.onesbf[:, 0:1],
                                self.cstg[dc % 2][:, 0:OT],
                                start=(dc == 0), stop=(dc == DCN - 1),
                                waits=[t_sq] + ([self.use_ps2] if dc == 0 else []))
                self.use_cstg[dc % 2] = sq_mm
            o_use["o1"] = tok_max(*p1l)
            o_use["o2"] = o_use["o1"]
            sq_mm = self.pe_drain(sq_mm)
            t_es = prog.op(A_, "activation", row(R_TMP, OT), self.ps_s2[:, 0:OT], AF.Identity,
                           waits=[sq_mm, self.rows_use if w == 0 else None])
            self.use_ps2 = t_es
            tlf = prog.op(A_, "activation", row(R_TMP, OT), row(R_TMP, OT), AF.Ln, scale=1.0 / D,
                          bias=self.cbias[0:1, 2:3], waits=[t_es, self.t_cbias])
            tfac = prog.op(A_, "activation", row(R_TMP, OT), row(R_TMP, OT), AF.Exp, scale=-0.5, waits=[tlf])
            t_of = prog.op(TE, "matmul", self.ps_o3[:, 0:OT], self.ones[0:1, 0:128], row(R_TMP, OT),
                           waits=[tfac, o_use.get("o3")])
            t_of = self.pe_drain(t_of)
            lnt = []
            for dc in range(DCN):
                t_ln = prog.op(V, "scalar_tensor_tensor", self.x1ln[dc][:, w * OT:(w + 1) * OT],
                               self.x1[dc][:, w * OT:(w + 1) * OT], self.s_s2[:, dc:dc + 1],
                               self.ps_o3[:, 0:OT], op0=ALU.mult, op1=ALU.mult,
                               waits=[self.t_x1[(w, dc)], t_of])
                self.t_x1ln[(w, dc)] = t_ln
                lnt.append(t_ln)
            o_use["o3"] = tok_max(*lnt)
        self.t_x1_psfree = tok_max(o_use.get("o1"), o_use.get("o2"))
        if "x1" in self.dbg:
            tp = self.tap("x1", [D, TOK])
            for dc in range(DCN):
                prog.op(SY, "dma_start", out=tp[dc * 128:(dc + 1) * 128, :], in_=self.x1[dc][:],
                        waits=[tok_max(*[self.t_x1[(w, dc)] for w in range(NW)])])
        if "x1ln" in self.dbg:
            tp = self.tap("x1ln", [D, TOK], BF16)
            for dc in range(DCN):
                prog.op(SY, "dma_start", out=tp[dc * 128:(dc + 1) * 128, :], in_=self.x1ln[dc][:],
                        waits=[tok_max(*[self.t_x1ln[(w, dc)] for w in range(NW)])])

    def dequant_chunks(self):
        units = []
        for key, nat, outp, KCtot, MC, I in (
                ("u", self.dr["wun"], self.dr["wqu"], DCN, ICN, D),
                ("g", self.dr["wgn"], self.dr["wqg"], DCN, ICN, D),
                ("d", self.dr["wdn"], self.dr["wqd"], ICN, DCN, DI)):
            nsub = I // D
            for oc in range(MC):
                for s in range(nsub):
                    units.append((key, nat, outp, KCtot, s * DCN, oc, s == nsub - 1))
        return units

    def emit_dequant(self, unit):
        prog = self.prog
        key, nat, outp, KCtot, kc0, oc, last = unit
        nb = D // 32
        st = self.dq_state
        s = st.get("slot", 0)
        st["slot"] = s ^ 1
        dq_w, dq_l, dq_q, dq_s, dq_st = (self.dq_w[0], self.dq_l[0], self.dq_q[s],
                                         self.dq_s[s], self.dq_st[0])
        t_l = prog.op(SY, "dma_start", out=dq_w[:],
                      in_=nat[oc * 128:(oc + 1) * 128, kc0 * 128:kc0 * 128 + D],
                      waits=[st.get(("w", 0))])
        w3 = dq_w.ap().rearrange("p (b k) -> p b k", k=32)
        t_bm = prog.op(V, "reduce_max", dq_s[:, 0:nb], w3, axis=AX.X,
                       apply_absolute_value=True, waits=[t_l, st.get(("s", s))])
        t_s = prog.op(V, "tensor_scalar", dq_s[:, 32:32 + nb], dq_s[:, 0:nb],
                      1.0 / 7.0, 1e-12, op0=ALU.mult, op1=ALU.max, waits=[t_bm, st.get(("s2", s))])
        t_r = prog.op(V, "reciprocal", dq_s[:, 64:64 + nb], dq_s[:, 32:32 + nb],
                      waits=[t_s, st.get(("r", s))])
        rb = dq_s.ap()[:, 64:64 + nb].unsqueeze(2).broadcast_to([128, nb, 32])
        sb_ = dq_s.ap()[:, 32:32 + nb].unsqueeze(2).broadcast_to([128, nb, 32])
        l3 = dq_l.ap().rearrange("p (b k) -> p b k", k=32)
        q3 = dq_q.ap().rearrange("p (b k) -> p b k", k=32)
        t_lv = prog.op(V, "tensor_tensor", l3, w3, rb, op=ALU.mult, waits=[t_r, st.get(("l", 0))])
        t_q = prog.op(V, "tensor_tensor", q3, l3, sb_, op=ALU.mult, waits=[t_lv, st.get(("q", s))])
        st[("w", 0)] = t_lv
        st[("s", s)] = t_lv
        st[("s2", s)] = tok_max(t_lv, t_q)
        st[("r", s)] = t_lv
        st[("l", 0)] = t_q
        t_q = self.eng_drain(V, t_q)
        evs = []
        for g4 in range(DCN // 4):
            t_t = None
            for j in range(4):
                kc = g4 * 4 + j
                t_t = prog.op(TE, "matmul", self.ps_t[:, j * 128:(j + 1) * 128],
                              dq_q[:, kc * 128:(kc + 1) * 128],
                              self.identb[:], is_transpose=True,
                              waits=[t_q, self.t_identb] + ([st.get(("pt",))] if j == 0 else []))
            t_t = self.pe_drain(t_t)
            t_e = prog.op(A_, "activation", dq_st[:, g4 * 512:(g4 + 1) * 512], self.ps_t,
                          AF.Identity, waits=[t_t, st.get(("st", 0, g4))])
            st[("pt",)] = t_e
            evs.append(t_e)
            st[("st", 0, g4)] = t_e
        rows0 = (oc * KCtot + kc0) * 128
        t_o = prog.op(SY, "dma_start",
                      out=outp[rows0:rows0 + DCN * 128, :].rearrange("(kc p) f -> p kc f", p=128),
                      in_=dq_st.ap().rearrange("p (kc f) -> p kc f", kc=DCN),
                      waits=[tok_max(*evs)])
        st[("q", s)] = t_o
        for g4 in range(DCN // 4):
            st[("st", 0, g4)] = tok_max(st[("st", 0, g4)], t_o)
        if last:
            self.t_wq[(key, oc)] = t_o
        else:
            self.t_wq[(key, oc)] = tok_max(self.t_wq.get((key, oc)), t_o)

    def ph_attn(self):
        prog = self.prog
        row = self.row
        fence_a = prog.fence()
        self.dq_state = {}
        self.t_wq = {}
        self.t_identb = prog.op(SY, "dma_start", out=self.identb[:], in_=self.dr["identb"],
                                waits=[fence_a])
        dq_units = self.dequant_chunks()
        dq_pos = 0
        w1_use, w2_use, g1_use, ev_use, xm_use, xo_use = {}, {}, {}, {}, {}, {}
        ps_use = {2: self.t_x1_psfree, 3: self.t_x1_psfree}
        banks4 = [self.ps_a, self.ps_b, self.ps_o1, self.ps_o2]
        self.t_xm = {}
        use_ps1l = self.use_ps1
        use_ps2l = self.use_ps2
        self.lnS_tok = []
        for q in range(NQ):
            x1ln_tok = self.eng_drain(V, *[self.t_x1ln[(w, dc)] for dc in range(DCN) for w in (2 * q, 2 * q + 1)])
            for fg in range(FC // 2):
                sl = fg % 3
                t_w = prog.op(G, "dma_start", out=slab_out(self.w1s[sl], 2, DCN), in_=slab_ap(self.dr["wp1"], fg * 2, 2, DCN),
                              waits=[w1_use.get(sl)] + ([fence_a] if (q == 0 and fg < 3) else []))
                gel = []
                for mc in range(2):
                    ffc = fg * 2 + mc
                    pb = banks4[ffc % 4]
                    mml = None
                    for dc in range(DCN):
                        mml = prog.op(TE, "matmul", pb[:],
                                      self.w1s[sl][:, (mc * DCN + dc) * 128:(mc * DCN + dc + 1) * 128],
                                      self.x1ln[dc][:, q * QT:(q + 1) * QT],
                                      start=(dc == 0), stop=(dc == DCN - 1),
                                      waits=[t_w, x1ln_tok] + ([ps_use.get(ffc % 4)] if dc == 0 else []))
                    mml = self.pe_drain(mml)
                    t_g = prog.op(A_, "activation", self.g1bf[:, ffc * QT:(ffc + 1) * QT], pb[:],
                                  AF.Gelu, bias=self.s_b1[:, ffc:ffc + 1],
                                  waits=[mml, g1_use.get(ffc)])
                    ps_use[ffc % 4] = t_g
                    g1_use[ffc] = self.eng_drain(A_, t_g)
                    gel.append(t_g)
                w1_use[sl] = tok_max(*gel)
            mmA = mmB = None
            for dc in range(DCN):
                sl = dc % 2
                t_w = prog.op(G, "dma_start", out=slab_out(self.w2s[sl], 1, FC), in_=slab_ap(self.dr["wp2"], dc, 1, FC),
                              waits=[w2_use.get(sl)] + ([fence_a] if (q == 0 and dc < 2) else []))
                pb = banks4[dc % 4]
                mml = None
                for kc in range(FC):
                    mml = prog.op(TE, "matmul", pb[:], self.w2s[sl][:, kc * 128:(kc + 1) * 128],
                                  self.g1bf[:, kc * QT:(kc + 1) * QT],
                                  start=(kc == 0), stop=(kc == FC - 1),
                                  waits=[t_w, g1_use[kc]] + ([ps_use.get(dc % 4)] if kc == 0 else []))
                w2_use[sl] = mml
                mml = self.pe_drain(mml)
                t_ev = prog.op(A_, "activation", self.evscr[dc % 2][:], pb[:], AF.Identity,
                               bias=self.s_b2[:, dc:dc + 1], waits=[mml, ev_use.get(dc % 2)])
                ps_use[dc % 4] = t_ev
                s2 = dc % 2
                t_xo = prog.op(SY, "dma_start", out=self.xo2[s2][:],
                               in_=self.x_own[dc * 128:(dc + 1) * 128, q * QT:(q + 1) * QT],
                               waits=[xo_use.get(s2)] + ([fence_a] if (q == 0 and dc < 2) else []))
                x1tok = tok_max(self.t_x1ln[(2 * q, dc)], self.t_x1ln[(2 * q + 1, dc)])
                t_p = prog.op(V, "tensor_tensor", self.evscr[s2][:], self.evscr[s2][:],
                              self.x1[dc][:, q * QT:(q + 1) * QT], op=ALU.add,
                              waits=[t_ev, x1tok, xm_use.get(s2)])
                txm = prog.op(V, "tensor_tensor", self.x1[dc][:, q * QT:(q + 1) * QT],
                              self.evscr[s2][:], self.xo2[s2][:], op=ALU.add,
                              waits=[t_p, t_xo])
                self.t_xm[(q, dc)] = txm
                ev_use[s2] = txm
                xm_use[s2] = txm
                xo_use[s2] = txm
                t_cb = prog.op(A_, "activation", self.cstg[dc % 2][:], self.x1[dc][:, q * QT:(q + 1) * QT],
                               AF.Identity, waits=[txm, self.use_cstg.get(dc % 2)])
                mmA = prog.op(TE, "matmul", self.ps_s1[:], self.onesbf[:, 0:1], self.cstg[dc % 2][:],
                              start=(dc == 0), stop=(dc == DCN - 1),
                              waits=[t_cb] + ([use_ps1l] if dc == 0 else []))
                t_sq = prog.op(A_, "activation", self.cstg[dc % 2][:], self.x1[dc][:, q * QT:(q + 1) * QT],
                               AF.Square, waits=[txm, mmA])
                mmB = prog.op(TE, "matmul", self.ps_s2[:], self.onesbf[:, 0:1], self.cstg[dc % 2][:],
                              start=(dc == 0), stop=(dc == DCN - 1),
                              waits=[t_sq] + ([use_ps2l] if dc == 0 else []))
                self.use_cstg[dc % 2] = mmB
            t_mm = self.pe_drain(mmA, mmB)
            tE1 = prog.op(A_, "activation", self.rows[:, self.L_S1 + q * QT:self.L_S1 + (q + 1) * QT],
                          self.ps_s1[:], AF.Identity, waits=[t_mm, self.rows_use])
            tE2 = prog.op(A_, "activation", self.rows[:, self.L_S2 + q * QT:self.L_S2 + (q + 1) * QT],
                          self.ps_s2[:], AF.Identity, waits=[t_mm])
            use_ps1l, use_ps2l = tE1, tE2
            self.lnS_tok += [tE1, tE2]
            n_do = 18 if q < NQ - 1 else len(dq_units) - dq_pos
            for _ in range(n_do):
                if dq_pos < len(dq_units):
                    self.emit_dequant(dq_units[dq_pos])
                    dq_pos += 1
        if "xmid" in self.dbg:
            tp = self.tap("xmid", [D, TOK])
            for dc in range(DCN):
                prog.op(SY, "dma_start", out=tp[dc * 128:(dc + 1) * 128, :], in_=self.x1[dc][:],
                        waits=[tok_max(*[self.t_xm[(q, dc)] for q in range(NQ)])])

    def ph_ln2_m1_xq_pre(self):
        prog = self.prog
        L_S1, L_S2, L_M2R = self.L_S1, self.L_S2, self.L_M2R
        tS = tok_max(*self.lnS_tok)
        t1 = prog.op(V, "tensor_scalar", self.rows[:, L_S1:L_S1 + TOK], self.rows[:, L_S1:L_S1 + TOK],
                     1.0 / D, None, op0=ALU.mult, waits=[tS])
        t2 = prog.op(V, "tensor_scalar", self.rows[:, L_S2:L_S2 + TOK], self.rows[:, L_S2:L_S2 + TOK],
                     1.0 / D, None, op0=ALU.mult, waits=[tS])
        t3 = prog.op(V, "tensor_tensor", self.rows[:, L_M2R:L_M2R + TOK], self.rows[:, L_S1:L_S1 + TOK],
                     self.rows[:, L_S1:L_S1 + TOK], op=ALU.mult, waits=[t1])
        t4 = prog.op(V, "tensor_tensor", self.rows[:, L_S2:L_S2 + TOK], self.rows[:, L_S2:L_S2 + TOK],
                     self.rows[:, L_M2R:L_M2R + TOK], op=ALU.subtract, waits=[t2, t3])
        t5 = prog.op(A_, "activation", self.rows[:, L_S2:L_S2 + TOK], self.rows[:, L_S2:L_S2 + TOK],
                     AF.Ln, bias=self.cbias[0:1, 0:1], waits=[t4, self.t_cbias])
        t6 = prog.op(A_, "activation", self.rows[:, L_S2:L_S2 + TOK], self.rows[:, L_S2:L_S2 + TOK],
                     AF.Exp, scale=-0.5, waits=[t5])
        t7 = prog.op(V, "tensor_tensor", self.rows[:, L_M2R:L_M2R + TOK], self.rows[:, L_S1:L_S1 + TOK],
                     self.rows[:, L_S2:L_S2 + TOK], op=ALU.mult, waits=[t6, t3])
        self.t_ln2rows = tok_max(t6, t7)

    def _bcast_scales(self, key, dst, col0, n=2, extra=()):
        prog = self.prog
        t_d = prog.op(SY, "dma_start", out=self.m_in[key].ap(), in_=self.srow1[0:1, 0:2],
                      waits=[self.t_maxloc])
        t_c = prog.op(G, "collective_compute", "AllReduce", ALU.max, replica_groups=ALL8,
                      ins=[self.m_in[key].ap()], outs=[self.m_out[key].ap()], waits=[t_d])
        t_r = prog.op(SY, "dma_start", out=self.srow1[:, 4:6], in_=self.m_out[key].ap(), waits=[t_c])
        ts_ = prog.op(V, "tensor_scalar", self.srow1[:, 8:8 + n], self.srow1[:, 4:4 + n],
                      1.0 / QMAX, 1e-12, op0=ALU.mult, op1=ALU.max, waits=[t_r])
        tr_ = prog.op(V, "reciprocal", self.srow1[:, 8 + n:8 + 2 * n], self.srow1[:, 8:8 + n], waits=[ts_])
        t_o = prog.op(TE, "matmul", self.ps_o3[:, 0:2 * n], self.ones[0:1, 0:128],
                      self.srow1[0:1, 8:8 + 2 * n],
                      waits=[tr_, self.t_ones, self.ps_o3_use] + list(extra))
        t_o = self.pe_drain(t_o)
        t_e = prog.op(A_, "activation", dst[:, col0:col0 + 2 * n], self.ps_o3[:, 0:2 * n],
                      AF.Identity, waits=[t_o])
        self.ps_o3_use = t_e
        return t_e

    def ph_ln2_m1_xq(self):
        prog = self.prog
        L_S2, L_M2R = self.L_S2, self.L_M2R
        self.ps_o3_use = None
        fence = self.fence_ffn
        ln_use = {}
        o_use = {}
        mx = []
        ps_use = {}
        for q in range(NQ):
            t_or = prog.op(TE, "matmul", self.ps_o1[:], self.ones[0:1, 0:128],
                           self.rows[:, L_S2 + q * QT:L_S2 + (q + 1) * QT],
                           waits=[self.t_ln2rows, o_use.get("o1"), fence])
            t_om = prog.op(TE, "matmul", self.ps_o2[:], self.ones[0:1, 0:128],
                           self.rows[:, L_M2R + q * QT:L_M2R + (q + 1) * QT],
                           waits=[self.t_ln2rows, o_use.get("o2")])
            t_or = t_om = self.pe_drain(t_or, t_om)
            tq = []
            for dc in range(DCN):
                t1 = prog.op(V, "tensor_tensor", self.qf[dc % 2][:], self.x1[dc][:, q * QT:(q + 1) * QT],
                             self.ps_o1[:], op=ALU.mult,
                             waits=[self.t_xm[(q, dc)], t_or, ln_use.get(("qf", dc % 2)), fence])
                t2 = prog.op(V, "tensor_tensor", self.ln2[dc][:, q * QT:(q + 1) * QT], self.qf[dc % 2][:],
                             self.ps_o2[:], op=ALU.subtract, waits=[t1, t_om])
                ln_use[("qf", dc % 2)] = t2
                t3 = prog.op(V, "reduce_max", self.maxc[:, q * DCN + dc:q * DCN + dc + 1],
                             self.ln2[dc][:, q * QT:(q + 1) * QT], axis=AX.X,
                             apply_absolute_value=True, waits=[t2])
                ln_use[(q, dc)] = t2
                tq.append(t3)
                mx.append(t3)
            o_use["o1"] = tok_max(*tq)
            o_use["o2"] = o_use["o1"]
        t_mr = prog.op(V, "reduce_max", self.maxc[:, 32:33], self.maxc[:, 0:32], axis=AX.X,
                       waits=[tok_max(*mx)])
        t_pc = prog.op(G, "tensor_reduce", self.srow1[0:1, 0:1], self.maxc[:, 32:33], axis=AX.C,
                       op=ALU.max, waits=[t_mr])
        t_cp = prog.op(V, "tensor_scalar", self.srow1[:, 1:2], self.srow1[:, 0:1], 1.0, None,
                       op0=ALU.mult, waits=[t_pc])
        self.t_maxloc = tok_max(t_pc, t_cp)
        t_sc = self._bcast_scales("m1", self.scal, 0)
        self.t_xq = {}
        lvl_use = {}
        for q in range(NQ):
            for dc in range(DCN):
                r = dc % 4
                tl = prog.op(V, "tensor_scalar", self.ql[r][:], self.ln2[dc][:, q * QT:(q + 1) * QT],
                             self.scal[:, 2:3], None, op0=ALU.mult,
                             waits=[ln_use[(q, dc)], t_sc, lvl_use.get(r)])
                tb = prog.op(V, "tensor_scalar", self.xqb[dc][:, q * QT:(q + 1) * QT], self.ql[r][:],
                             self.scal[:, 0:1], None, op0=ALU.mult, waits=[tl])
                lvl_use[r] = tb
                self.t_xq[(q, dc)] = tb
        if "ln2" in self.dbg:
            tp = self.tap("ln2", [D, TOK])
            for dc in range(DCN):
                prog.op(SY, "dma_start", out=tp[dc * 128:(dc + 1) * 128, :], in_=self.ln2[dc][:],
                        waits=[tok_max(*[self.t_xq[(q, dc)] for q in range(NQ)])])
        if "xq" in self.dbg:
            tp = self.tap("xq", [D, TOK], BF16)
            for dc in range(DCN):
                prog.op(SY, "dma_start", out=tp[dc * 128:(dc + 1) * 128, :], in_=self.xqb[dc][:],
                        waits=[tok_max(*[self.t_xq[(q, dc)] for q in range(NQ)])])

    def ph_upgate(self):
        prog = self.prog
        sub = self.stop - 7 if self.stop < 9 else 99
        wu_use, wg_use, ps_use, st_use = {}, {}, {}, {}
        mx = []
        xq_all = self.eng_drain(V, *self.t_xq.values())
        for ig in range(ICN // 2):
            sl = ig % 2
            t_wu = prog.op(SY, "dma_start", out=slab_out(self.wus[sl], 2, DCN),
                           in_=slab_ap(self.dr["wqu"], ig * 2, 2, DCN),
                           waits=[wu_use.get(sl), self.t_wq[("u", ig * 2)], self.t_wq[("u", ig * 2 + 1)],
                                  self.fence_q if ig < 2 else None])
            t_wg = prog.op(SY, "dma_start", out=slab_out(self.wgs[sl], 2, DCN),
                           in_=slab_ap(self.dr["wqg"], ig * 2, 2, DCN),
                           waits=[wg_use.get(sl), self.t_wq[("g", ig * 2)], self.t_wq[("g", ig * 2 + 1)],
                                  self.fence_q if ig < 2 else None])
            lastu, lastg = [], []
            if sub < 0.4:
                wu_use[sl] = t_wu
                wg_use[sl] = t_wg
                continue
            for mc in range(2):
                ic = ig * 2 + mc
                for q in range(NQ):
                    k = (ic * NQ + q) % 2
                    pbu = [self.ps_a, self.ps_o1][k]
                    pbg = [self.ps_b, self.ps_o2][k]
                    mmu = mmg = None
                    for dc in range(DCN):
                        mmu = prog.op(TE, "matmul", pbu[:],
                                      self.wus[sl][:, (mc * DCN + dc) * 128:(mc * DCN + dc + 1) * 128],
                                      self.xqb[dc][:, q * QT:(q + 1) * QT],
                                      start=(dc == 0), stop=(dc == DCN - 1),
                                      waits=[t_wu, xq_all] + ([ps_use.get(("u", k))] if dc == 0 else []))
                    for dc in range(DCN):
                        mmg = prog.op(TE, "matmul", pbg[:],
                                      self.wgs[sl][:, (mc * DCN + dc) * 128:(mc * DCN + dc + 1) * 128],
                                      self.xqb[dc][:, q * QT:(q + 1) * QT],
                                      start=(dc == 0), stop=(dc == DCN - 1),
                                      waits=[t_wg, xq_all] + ([ps_use.get(("g", k))] if dc == 0 else []))
                    if sub < 0.6:
                        ev = prog.op(A_, "activation", self.stg[0][:], pbu[:], AF.Identity,
                                     waits=[mmu, st_use.get("x"), self.fence_q])
                        ev2 = prog.op(A_, "activation", self.stg[1][:], pbg[:], AF.Identity,
                                      waits=[mmg, st_use.get("y"), self.fence_q])
                        st_use["x"], st_use["y"] = ev, ev2
                        ps_use[("u", k)], ps_use[("g", k)] = ev, ev2
                        lastu.append(mmu)
                        lastg.append(mmg)
                        continue
                    t_dr = self.pe_drain(mmu, mmg)
                    mmu = tok_max(mmu, t_dr)
                    mmg = tok_max(mmg, t_dr)
                    su = self.stg[0]
                    sg = self.stg[1]
                    teu = prog.op(A_, "activation", su[:, q * QT:(q + 1) * QT], pbu[:], AF.Identity,
                                  waits=[mmu, (st_use.get(0) if q == 0 else None), self.fence_q])
                    teg = prog.op(A_, "activation", sg[:, q * QT:(q + 1) * QT], pbg[:], AF.Identity,
                                  waits=[mmg, (st_use.get(1) if q == 0 else None), self.fence_q])
                    cu = 64 + ic * NQ + q
                    cg = 160 + ic * NQ + q
                    tmu = prog.op(V, "reduce_max", self.maxc[:, cu:cu + 1], pbu[:], axis=AX.X,
                                  apply_absolute_value=True, waits=[mmu, teu])
                    tmg = prog.op(V, "reduce_max", self.maxc[:, cg:cg + 1], pbg[:], axis=AX.X,
                                  apply_absolute_value=True, waits=[mmg, teg])
                    ps_use[("u", k)] = tok_max(teu, tmu)
                    ps_use[("g", k)] = tok_max(teg, tmg)
                    mx += [tmu, tmg]
                    lastu.append(mmu)
                    lastg.append(mmg)
                    if sub < 0.8:
                        continue
                    if q == NQ - 1:
                        du = prog.op(SY, "dma_start", out=self.sp_up[ic * 128:(ic + 1) * 128, :],
                                     in_=su[:], waits=[teu])
                        dg = prog.op(SY, "dma_start", out=self.sp_gate[ic * 128:(ic + 1) * 128, :],
                                     in_=sg[:], waits=[teg])
                        st_use[0] = du
                        st_use[1] = dg
            wu_use[sl] = tok_max(*lastu)
            wg_use[sl] = tok_max(*lastg)
        if sub < 0.6:
            return
        tmx = tok_max(*mx)
        t1 = prog.op(V, "reduce_max", self.maxc[:, 33:34], self.maxc[:, 64:160], axis=AX.X, waits=[tmx])
        t2 = prog.op(V, "reduce_max", self.maxc[:, 34:35], self.maxc[:, 160:256], axis=AX.X, waits=[tmx])
        t3 = prog.op(G, "tensor_reduce", self.srow1[0:1, 0:1], self.maxc[:, 33:34], axis=AX.C,
                     op=ALU.max, waits=[t1])
        t4 = prog.op(G, "tensor_reduce", self.srow1[0:1, 1:2], self.maxc[:, 34:35], axis=AX.C,
                     op=ALU.max, waits=[t2])
        self.t_maxloc = tok_max(t3, t4)
        self.maxc_free = tok_max(t1, t2)
        self.t_scug = self._bcast_scales("mug", self.scal, 0, n=2, extra=[xq_all])
        if "gate" in self.dbg:
            tpg = self.tap("gate", [DI, TOK], BF16)
            tpu = self.tap("up", [DI, TOK], BF16)
            fs = prog.fence()
            prog.op(SY, "dma_start", out=tpg[:, :], in_=self.sp_gate.ap(), waits=[fs])
            prog.op(SY, "dma_start", out=tpu[:, :], in_=self.sp_up.ap(), waits=[fs])

    def _chain_a(self, ic, q, idx, waits_extra=()):
        prog = self.prog
        a = idx % 4
        tg = prog.op(SY, "dma_start", out=self.qa[a][:],
                     in_=self.sp_gate[ic * 128:(ic + 1) * 128, q * QT:(q + 1) * QT],
                     waits=[self.qa_use.get(a)] + list(waits_extra))
        tl = prog.op(V, "tensor_scalar", self.ql[a][:], self.qa[a][:], self.scal[:, 3:4], None,
                     op0=ALU.mult, waits=[tg, self.t_scug, self.ql_use.get(a)])
        self.qa_use[a] = tl
        tgq = prog.op(A_, "activation", self.qf[a][:], self.ql[a][:], AF.Identity,
                      scale=self.scal[:, 1:2], waits=[tl, self.qf_use.get(a)])
        tsg = prog.op(A_, "activation", self.qf[4 + a][:], self.ql[a][:], AF.Sigmoid,
                      scale=self.scal[:, 1:2], waits=[tl, self.qf_use.get(4 + a)])
        self.ql_use[a] = tok_max(tgq, tsg)
        return {"tgq": tgq, "tsg": tsg}

    def _chain_b(self, ic, q, idx, st):
        prog = self.prog
        a = idx % 4
        tsm = prog.op(V, "tensor_scalar", self.qm[a][:], self.qf[4 + a][:], 65536.0, None,
                      op0=ALU.mult, waits=[st["tsg"], self.qm_use.get(a)])
        tsq = prog.op(A_, "activation", self.qf[4 + a][:], self.qm[a][:], AF.Identity,
                      scale=1.0 / 65536.0, waits=[tsm])
        self.qm_use[a] = tsq
        st["tsq"] = tsq

    def _signed_minmax_combine(self, waits=()):
        prog = self.prog
        tM = prog.op(V, "reduce_max", self.maxc[:, 50:51], self.maxc[:, 64:160], axis=AX.X,
                     waits=list(waits))
        tN = prog.op(V, "tensor_reduce", self.maxc[:, 51:52], self.maxc[:, 160:256], axis=AX.X,
                     op=ALU.min, waits=list(waits))
        tC = prog.op(V, "scalar_tensor_tensor", self.maxc[:, 52:53], self.maxc[:, 51:52], -1.0,
                     self.maxc[:, 50:51], op0=ALU.mult, op1=ALU.max, waits=[tM, tN])
        tP = prog.op(G, "tensor_reduce", self.srow1[0:1, 0:1], self.maxc[:, 52:53], axis=AX.C,
                     op=ALU.max, waits=[tC])
        tcp = prog.op(V, "tensor_scalar", self.srow1[:, 1:2], self.srow1[:, 0:1], 1.0, None,
                      op0=ALU.mult, waits=[tP])
        self.t_maxloc = tok_max(tP, tcp)
        self.maxc_free = tok_max(tM, tN)

    def ph_passA(self):
        prog = self.prog
        self.qa_use, self.ql_use, self.qf_use, self.qm_use = {}, {}, {}, {}
        last = []
        N = ICN * NQ
        states = {}

        def segC(idx, st):
            a = idx % 4
            tt2 = prog.op(P_, "tensor_tensor", self.qf[8 + a][:], self.qf[a][:], self.qf[4 + a][:],
                          op=ALU.mult, waits=[st["tgq"], st["tsq"], self.qf_use.get(8 + a),
                                              self.maxc_free])
            self.qf_use[a] = tt2
            self.qf_use[4 + a] = tt2
            tmx = prog.op(V, "tensor_reduce", self.maxc[:, 64 + idx:65 + idx], self.qf[8 + a][:],
                          axis=AX.X, op=ALU.max, waits=[tt2, self.maxc_free])
            tmn = prog.op(V, "tensor_reduce", self.maxc[:, 160 + idx:161 + idx], self.qf[8 + a][:],
                          axis=AX.X, op=ALU.min, waits=[tt2])
            self.qf_use[8 + a] = tok_max(tmx, tmn)
            last.append(tok_max(tmx, tmn))

        for t in range(N + 2):
            if 0 <= t - 2 < N:
                segC(t - 2, states.pop(t - 2))
            if 0 <= t - 1 < N:
                ic, q = divmod(t - 1, NQ)
                self._chain_b(ic, q, t - 1, states[t - 1])
            if t < N:
                ic, q = divmod(t, NQ)
                states[t] = self._chain_a(ic, q, t,
                                          waits_extra=[self.fence_p] if t < 8 else ())
        self._signed_minmax_combine(waits=[tok_max(*last)])
        self.t_sc2 = self._bcast_scales("mg2", self.scal, 4, n=2)

    def _up_segA(self, ic, q, idx, extra):
        st = self._chain_a(ic, q, idx, waits_extra=extra)
        a = idx % 4
        st["tu"] = self.prog.op(SY, "dma_start", out=self.qa[4 + a][:],
                                in_=self.sp_up[ic * 128:(ic + 1) * 128, q * QT:(q + 1) * QT],
                                waits=[self.qa_use.get(4 + a)])
        return st

    def _up_segB(self, ic, q, idx, st):
        a = idx % 4
        self._chain_b(ic, q, idx, st)
        st["tlu"] = self.prog.op(V, "tensor_scalar", self.ql[8 + a][:], self.qa[4 + a][:],
                                 self.scal[:, 2:3], None, op0=ALU.mult,
                                 waits=[st["tu"], self.ql_use.get(8 + a)])
        self.qa_use[4 + a] = st["tlu"]

    def _up_segC(self, idx, st):
        prog = self.prog
        a = idx % 4
        tt2 = prog.op(P_, "tensor_tensor", self.qf[8 + a][:], self.qf[a][:], self.qf[4 + a][:],
                      op=ALU.mult, waits=[st["tgq"], st["tsq"], self.qf_use.get(8 + a)])
        self.qf_use[a] = tt2
        self.qf_use[4 + a] = tt2
        tl2 = prog.op(V, "tensor_scalar", self.ql[4 + a][:], self.qf[8 + a][:], self.scal[:, 6:7],
                      None, op0=ALU.mult, waits=[tt2, self.t_sc2, self.ql_use.get(4 + a)])
        self.qf_use[8 + a] = tl2
        tg2 = prog.op(A_, "activation", self.qf[8 + a][:], self.ql[4 + a][:], AF.Identity,
                      scale=self.scal[:, 4:5], waits=[tl2])
        self.ql_use[4 + a] = tg2
        tuq = prog.op(A_, "activation", self.qf[12 + a][:], self.ql[8 + a][:], AF.Identity,
                      scale=self.scal[:, 0:1], waits=[st["tlu"], self.qf_use.get(12 + a)])
        self.ql_use[8 + a] = tuq
        st["tg2"], st["tuq"] = tg2, tuq

    def ph_passB(self):
        prog = self.prog
        lastB = []
        N = ICN * NQ
        states = {}

        def segD(idx, st):
            a = idx % 4
            tt3 = prog.op(P_, "tensor_tensor", self.qf[a][:], self.qf[8 + a][:], self.qf[12 + a][:],
                          op=ALU.mult, waits=[st["tg2"], st["tuq"], self.qf_use.get(a),
                                              self.maxc_free])
            self.qf_use[8 + a] = tt3
            self.qf_use[12 + a] = tt3
            tmx = prog.op(V, "tensor_reduce", self.maxc[:, 64 + idx:65 + idx], self.qf[a][:],
                          axis=AX.X, op=ALU.max, waits=[tt3, self.maxc_free])
            tmn = prog.op(V, "tensor_reduce", self.maxc[:, 160 + idx:161 + idx], self.qf[a][:],
                          axis=AX.X, op=ALU.min, waits=[tt3])
            self.qf_use[a] = tok_max(tmx, tmn)
            lastB.append(tok_max(tmx, tmn))

        for t in range(N + 3):
            if 0 <= t - 3 < N:
                segD(t - 3, states.pop(t - 3))
            if 0 <= t - 2 < N:
                self._up_segC(t - 2, states[t - 2])
            if 0 <= t - 1 < N:
                ic, q = divmod(t - 1, NQ)
                self._up_segB(ic, q, t - 1, states[t - 1])
            if t < N:
                ic, q = divmod(t, NQ)
                states[t] = self._up_segA(ic, q, t, [self.t_sc2])
        self._signed_minmax_combine(waits=[tok_max(*lastB)])
        self.t_sco = self._bcast_scales("mo", self.scal2, 0, n=1)

    def ph_passC_down(self):
        prog = self.prog
        wd_use, obf_use, ps_use, oxo_use, qdn_use = {}, {}, {}, {}, {}
        out_d = []
        states = {}
        ot = {q: [] for q in range(NQ)}
        N = NQ * ICN

        def segD(idx, st):
            q, ic = divmod(idx, ICN)
            a = idx % 4
            tt3 = prog.op(P_, "tensor_tensor", self.qf[a][:], self.qf[8 + a][:], self.qf[12 + a][:],
                          op=ALU.mult, waits=[st["tg2"], st["tuq"], self.qf_use.get(a)])
            self.qf_use[8 + a] = tt3
            self.qf_use[12 + a] = tt3
            tlo = prog.op(V, "tensor_scalar", self.ql[4 + a][:], self.qf[a][:], self.scal2[:, 1:2],
                          None, op0=ALU.mult, waits=[tt3, self.t_sco, self.ql_use.get(4 + a)])
            self.qf_use[a] = tlo
            tob = prog.op(V, "tensor_scalar", self.obf[ic][:], self.ql[4 + a][:], self.scal2[:, 0:1],
                          None, op0=ALU.mult, waits=[tlo, obf_use.get(ic)])
            self.ql_use[4 + a] = tob
            ot[q].append(tob)

        def emit_down(q):
            o_all = self.eng_drain(V, *ot[q])
            for dc in range(DCN):
                sl = dc % 2
                t_w = prog.op(SY, "dma_start", out=slab_out(self.wds[sl], 1, ICN),
                              in_=slab_ap(self.dr["wqd"], dc, 1, ICN),
                              waits=[wd_use.get(sl), self.t_wq[("d", dc)],
                                     self.fence_p if (q == 0 and dc < 2) else None])
                pb = [self.ps_a, self.ps_b, self.ps_o1, self.ps_o2][dc % 4]
                mml = None
                for kc in range(ICN):
                    mml = prog.op(TE, "matmul", pb[:], self.wds[sl][:, kc * 128:(kc + 1) * 128],
                                  self.obf[kc][:], start=(kc == 0), stop=(kc == ICN - 1),
                                  waits=[t_w, o_all] + ([ps_use.get(dc % 4)] if kc == 0 else []))
                wd_use[sl] = mml
                for ic in range(ICN):
                    obf_use[ic] = tok_max(obf_use.get(ic), mml)
                mml = self.pe_drain(mml)
                t_ev = prog.op(A_, "activation", self.oxo[dc % 2][:], pb[:], AF.Identity,
                               waits=[mml, oxo_use.get(dc % 2)])
                ps_use[dc % 4] = t_ev
                t_fin = prog.op(V, "tensor_tensor", self.qdn[dc % 2][:], self.oxo[dc % 2][:],
                                self.x1[dc][:, q * QT:(q + 1) * QT], op=ALU.add,
                                waits=[t_ev, self.t_xm[(q, dc)], qdn_use.get(dc % 2)])
                oxo_use[dc % 2] = t_fin
                if "ffnout" in self.dbg:
                    if "ffnout" not in self.taps:
                        self.tap("ffnout", [D, TOK])
                    tfo = prog.op(SY, "dma_start",
                                  out=self.taps["ffnout"][dc * 128:(dc + 1) * 128, q * QT:(q + 1) * QT],
                                  in_=self.oxo[dc % 2][:], waits=[t_ev])
                    oxo_use[dc % 2] = tok_max(t_fin, tfo)
                out_d.append(prog.op(SY, "dma_start",
                                     out=self.out[dc * 128:(dc + 1) * 128, q * QT:(q + 1) * QT],
                                     in_=self.qdn[dc % 2][:], waits=[t_fin]))
                qdn_use[dc % 2] = tok_max(t_fin, out_d[-1])

        for t in range(N + 3):
            if 0 <= t - 3 < N:
                segD(t - 3, states.pop(t - 3))
                qd, icd = divmod(t - 3, ICN)
                if icd == ICN - 1:
                    emit_down(qd)
            if 0 <= t - 2 < N:
                self._up_segC(t - 2, states[t - 2])
            if 0 <= t - 1 < N:
                q, ic = divmod(t - 1, ICN)
                self._up_segB(ic, q, t - 1, states[t - 1])
            if t < N:
                q, ic = divmod(t, ICN)
                states[t] = self._up_segA(ic, q, t, [self.t_sco])
        fin = tok_max(*out_d)
        prog.raw(SY, lambda e: None, waits=[fin])


def build_program(dbg=(), stop=99):
    b = Builder(dbg, stop=stop)
    nc = b.build()
    return nc, b


def _host_inputs(inputs, cid):
    b, h = cid // 2, cid % 2
    x = np.asarray(inputs["L0_attn_x"], np.float32)
    xT = np.ascontiguousarray(x[b].T)
    x_scan = np.ascontiguousarray(xT[h * DH:(h + 1) * DH])
    cols = np.concatenate([np.arange((2 * j + h) * OT, (2 * j + h) * OT + OT) for j in range(NW)])
    x_own = np.ascontiguousarray(xT[:, cols])

    def dhalf_lane(a):
        return np.ascontiguousarray(
            np.asarray(a, np.float32)[h * DH:(h + 1) * DH].reshape(SCN, 128, NS).transpose(1, 0, 2).reshape(128, SCN * NS))

    def dhalf_col(v):
        return np.ascontiguousarray(np.asarray(v, np.float32)[h * DH:(h + 1) * DH].reshape(SCN, 128).T)

    def full_col(v, n):
        return np.ascontiguousarray(np.asarray(v, np.float32).reshape(n, 128).T)

    dt16 = np.exp(np.linspace(np.log(0.001), np.log(0.1), NS)).astype(np.float32)
    dtb = np.ascontiguousarray(np.broadcast_to(np.tile(dt16, SCN), (128, SCN * NS)).astype(np.float32))

    return {
        "x_scan": x_scan,
        "x_own": x_own,
        "A_s": dhalf_lane(inputs["L0_attn_A_ssm"]),
        "B_s": dhalf_lane(inputs["L0_attn_B_ssm"]),
        "C_s": dhalf_lane(inputs["L0_attn_C_ssm"]),
        "dtb": dtb,
        "D_s": dhalf_col(inputs["L0_attn_D_ssm"]),
        "s1_s": dhalf_col(inputs["L0_attn_scale1"]),
        "s2c": full_col(inputs["L0_attn_scale2"], DCN),
        "b1c": full_col(inputs["L0_attn_b1"], FC),
        "b2c": full_col(inputs["L0_attn_b2"], DCN),
    }


_WCACHE = {}


def _host_weights(inputs):
    import ml_dtypes
    w1 = np.asarray(inputs["L0_attn_w1"], np.float32)
    w2 = np.asarray(inputs["L0_attn_w2"], np.float32)
    return {
        "wp1": pack_w(w1, DCN, FC),
        "wp2": pack_w(w2, FC, DCN),
        "wun": np.ascontiguousarray(np.asarray(inputs["L0_ffn_w_up"], np.float32)),
        "wgn": np.ascontiguousarray(np.asarray(inputs["L0_ffn_w_gate"], np.float32)),
        "wdn": np.ascontiguousarray(np.asarray(inputs["L0_ffn_w_down"], np.float32)),
        "identb": np.eye(128, dtype=np.float32).astype(ml_dtypes.bfloat16),
    }


def kernel(**inputs):
    nc, _b = build_program()
    wts = _host_weights(inputs)
    in_maps = []
    for cid in range(8):
        m = _host_inputs(inputs, cid)
        m.update(wts)
        in_maps.append(m)
    res = run_bass_kernel_spmd(nc, in_maps, core_ids=list(range(8)))
    outp = np.zeros((B, D, T), np.float32)
    for cid in range(8):
        b, h = cid // 2, cid % 2
        o = np.asarray(res.results[cid]["out"])
        for j in range(NW):
            outp[b][:, (2 * j + h) * OT:(2 * j + h) * OT + OT] = o[:, j * OT:(j + 1) * OT]
    return np.ascontiguousarray(outp.transpose(0, 2, 1))
